# version 04
# baseline (speedup 1.0000x reference)
import math
import contextlib
import numpy as np
import ml_dtypes
import concourse.bass as bass
import concourse.mybir as mybir
from concourse.bass_utils import run_bass_kernel_spmd

F32 = mybir.dt.float32
BF16 = mybir.dt.bfloat16
AF = mybir.ActivationFunctionType
ALU = mybir.AluOpType
AX = mybir.AxisListType

N_TOK = 8192
D = 2048
DH = 128
NCORES = 8
TOK = N_TOK // NCORES
HL = 2
E = 8
NDC = D // 128
NHC = 11
NSC = 22
ATT_SCALE = DH ** -0.5
NEG = -1e30

ENGINES = ("sync", "gpsimd", "tensor", "vector", "scalar")
TAPS = False


class Sched:
    def __init__(self):
        self.ops = {e: [] for e in ENGINES}
        self.cnt = {}

    def val(self, sem):
        return self.cnt.get(sem, 0)

    SELF = {"vector": "vS", "scalar": "sS"}

    def op(self, eng, fn, waits=(), incs=()):
        for ss, v in waits:
            assert v <= self.val(ss), f"wait on future value: {ss} {v} > {self.val(ss)}"
        sse = self.SELF.get(eng)
        n_orig = len(tuple(incs))
        if sse is not None and not any(ss == sse for ss, _ in incs):
            incs = tuple(incs) + ((sse, 1),)
        self.ops[eng].append((tuple(waits), fn, tuple(incs)))
        out = []
        for i, (ss, d) in enumerate(incs):
            self.cnt[ss] = self.cnt.get(ss, 0) + d
            if i < n_orig or n_orig == 0:
                out.append(self.cnt[ss])
        return out[0] if len(out) == 1 else out

    def emit(self, eng_name, eng, sems):
        last = {}
        sse = self.SELF.get(eng_name)
        self_c = 0
        for waits, fn, incs in self.ops[eng_name]:
            if sse is not None and self_c > 0:
                eng.wait_ge(sems[sse], self_c)
                last[sse] = max(last.get(sse, 0), self_c)
            for ss, v in waits:
                if last.get(ss, 0) < v:
                    eng.wait_ge(sems[ss], v)
                    last[ss] = v
            inst = fn(eng)
            for ss, d in incs:
                if ss.startswith("cc"):
                    inst.then_inc(sems[ss])
                else:
                    inst.then_inc(sems[ss], d)
                if ss == sse:
                    self_c += d


def build_program(taps=False):
    nc = bass.Bass("TRN2", target_bir_lowering=False)
    dt = nc.dram_tensor

    x_own = dt("x_own", [TOK, D], F32, kind="ExternalInput").ap()
    wq_own = dt("wq_own", [128, NDC * 256], F32, kind="ExternalInput").ap()
    wk_own = dt("wk_own", [128, NDC * 256], F32, kind="ExternalInput").ap()
    wv_own = dt("wv_own", [128, NDC * 256], F32, kind="ExternalInput").ap()
    b_qkv = dt("b_qkv", [128, 12], F32, kind="ExternalInput").ap()
    gate_w_a = dt("gate_w_a", [128, NDC * E], F32, kind="ExternalInput").ap()
    wg_a = dt("wg_a", [E, NHC, 128, NDC * 128], F32, kind="ExternalInput").ap()
    wu_a = dt("wu_a", [E, NHC, 128, NDC * 128], F32, kind="ExternalInput").ap()
    wd_a = dt("wd_a", [E, NDC, 128, NHC * 128], F32, kind="ExternalInput").ap()
    swg_a = dt("swg_a", [NSC, 128, NDC * 128], F32, kind="ExternalInput").ap()
    swu_a = dt("swu_a", [NSC, 128, NDC * 128], F32, kind="ExternalInput").ap()
    swd_a = dt("swd_a", [NDC, 128, NSC * 128], F32, kind="ExternalInput").ap()
    wo_a = dt("wo_a", [NDC, 128, NDC * 128], F32, kind="ExternalInput").ap()
    bo_t = dt("bo_t", [128, NDC], F32, kind="ExternalInput").ap()
    cos_t = dt("cos_t", [128, N_TOK], F32, kind="ExternalInput").ap()
    sin_t = dt("sin_t", [128, N_TOK], F32, kind="ExternalInput").ap()
    mask_d = dt("mask_d", [128, 1024], F32, kind="ExternalInput").ap()
    identbf_d = dt("identbf_d", [128, 128], BF16, kind="ExternalInput").ap()
    identf_d = dt("identf_d", [128, 128], F32, kind="ExternalInput").ap()
    onesrow_d = dt("onesrow_d", [1, 128], F32, kind="ExternalInput").ap()
    onescol_d = dt("onescol_d", [128, 1], BF16, kind="ExternalInput").ap()
    onescolf_d = dt("onescolf_d", [128, 1], F32, kind="ExternalInput").ap()
    onehot_d = dt("onehot_d", [8, E * 128], F32, kind="ExternalInput").ap()
    fconst_d = dt("fconst_d", [128, 4], F32, kind="ExternalInput").ap()
    out_own = dt("out_own", [TOK, D], F32, kind="ExternalOutput").ap()

    xn_bounce = dt("xn_bounce", [D, TOK], BF16)
    xn_gath = dt("xn_gath", [NCORES * D, TOK], BF16)
    a_bounce = dt("a_bounce", [HL * 128, N_TOK], BF16)
    a_gath = dt("a_gath", [NCORES * HL * 128, N_TOK], BF16)
    q_dram = dt("q_dram", [HL, 128, N_TOK], BF16)
    k_dram = dt("k_dram", [HL, 128, N_TOK], BF16)
    v_dram = dt("v_dram", [HL, N_TOK, 128], BF16)
    if taps:
        tap_xnT = dt("tap_xnT", [128, NDC * TOK], BF16, kind="ExternalOutput").ap()
        tap_AT = dt("tap_AT", [128, HL * N_TOK], BF16, kind="ExternalOutput").ap()
        tap_fa = dt("tap_fa", [128, NDC * TOK], BF16, kind="ExternalOutput").ap()
        tap_comb = dt("tap_comb", [8, TOK], F32, kind="ExternalOutput").ap()
        tap_mhat = dt("tap_mhat", [128, HL], F32, kind="ExternalOutput").ap()
        tap_kqm = dt("tap_kqm", [1, 8], F32, kind="ExternalOutput").ap()
        tap_st8 = dt("tap_st8", [128, 40], F32, kind="ExternalOutput").ap()
        tap_q = dt("tap_q", [HL, 128, N_TOK], BF16, kind="ExternalOutput").ap()
        tap_k = dt("tap_k", [HL, 128, N_TOK], BF16, kind="ExternalOutput").ap()
        tap_v = dt("tap_v", [HL, N_TOK, 128], BF16, kind="ExternalOutput").ap()

    s = Sched()

    with contextlib.ExitStack() as es:
        sb = lambda name, shape, dtp: es.enter_context(nc.sbuf_tensor(name, shape, dtp))
        psmk = lambda name, shape, dtp: es.enter_context(nc.psum_tensor(name, shape, dtp))

        xnT = sb("xnT", [128, NDC * TOK], BF16)
        AT = sb("AT", [128, HL * N_TOK], BF16)
        fa = sb("fa", [128, NDC * TOK], BF16)
        comb_rows = sb("comb_rows", [8, TOK], F32)
        identbf = sb("identbf", [128, 128], BF16)
        identf = sb("identf", [128, 128], F32)
        onesrow = sb("onesrow", [1, 128], F32)
        onescol = sb("onescol", [128, 1], BF16)
        onescolf = sb("onescolf", [128, 1], F32)
        bqkv = sb("bqkv", [128, 12], F32)
        bot = sb("bot", [128, NDC], F32)
        gw = sb("gw", [128, NDC * E], BF16)
        mhat = sb("mhat", [128, HL], F32)
        fco = sb("fco", [128, 4], F32)
        st8 = sb("st8", [128, 40], F32)
        kqm = sb("kqm", [1, 8], F32)
        a16 = sb("a16", [128, 29184], BF16)
        a32 = sb("a32", [128, 4608], F32)

        pfh = [psmk(f"pf{i}", [128, 512], F32) for i in range(6)]
        pf = [h[:, :] for h in pfh]
        pt = [psmk(f"pt{i}", [128, 1024], BF16) for i in range(2)]
        bank_free = [None] * 6
        pt_free = [None] * 2

        sems = {}
        for nm in ("dS", "dG", "cc1", "cc2", "tS", "vS", "sS"):
            sems[nm] = es.enter_context(nc.semaphore(nm))

        def sync_dma(dst, src, waits=()):
            return s.op("sync", lambda e, d=dst, sr=src: e.dma_start(out=d, in_=sr),
                        waits=waits, incs=(("dS", 16),))

        SERIAL_GP = True

        def gp_dma(dst, src, waits=(), accum=False):
            if accum:
                r = s.op("gpsimd",
                         lambda e, d=dst, sr=src: e.dma_start(out=d, in_=sr, accum_op=ALU.add),
                         waits=waits, incs=(("dG", 16),))
            else:
                r = s.op("gpsimd", lambda e, d=dst, sr=src: e.dma_start(out=d, in_=sr),
                         waits=waits, incs=(("dG", 16),))
            if SERIAL_GP:
                s.op("gpsimd", lambda e: None, waits=(("dG", r),))
            return r

        def bar():
            return {nm: s.val(nm) for nm in ("dS", "dG", "tS", "vS", "sS")}

        def wall(bc, only=None):
            return tuple((nm, v) for nm, v in bc.items()
                         if v > 0 and (only is None or nm in only))

        def bwait(i, extra=()):
            w = list(extra)
            if bank_free[i] is not None:
                w.append(bank_free[i])
            return tuple(w)

        sync_dma(identbf[:, :], identbf_d)
        sync_dma(identf[:, :], identf_d)
        sync_dma(onesrow[:, :], onesrow_d)
        sync_dma(onescol[:, :], onescol_d)
        sync_dma(onescolf[:, :], onescolf_d)
        sync_dma(bqkv[:, :], b_qkv)
        sync_dma(bot[:, :], bo_t)
        sync_dma(fco[:, :], fconst_d)
        c_consts = s.val("dS")
        sync_dma(out_own, x_own)
        c_xcopy = s.val("dS")
        gp_dma(gw[:, :], gate_w_a)
        c_gw = s.val("dG")

        NTT = TOK // 128
        o_xt = [0, 2048]
        o_xn = [0, 2048]
        x_free = [None, None]
        xn_free = [None, None]
        ln_dma = []
        for tt in range(NTT):
            b = tt % 2
            w = [("dS", c_consts)]
            if x_free[b] is not None:
                w.append(x_free[b])
            sync_dma(a32[:, o_xt[b]:o_xt[b] + 2048], x_own[tt * 128:(tt + 1) * 128, :], waits=tuple(w))
            ln_dma.append(s.val("dS"))
            if tt == 1:
                break
        for tt in range(NTT):
            b = tt % 2
            if tt >= 2:
                w2 = [("dS", c_consts), x_free[b]]
                sync_dma(a32[:, o_xt[b]:o_xt[b] + 2048], x_own[tt * 128:(tt + 1) * 128, :],
                         waits=tuple(w2))
                ln_dma.append(s.val("dS"))
            xt = a32[:, o_xt[b]:o_xt[b] + 2048]
            xn = a16[:, o_xn[b]:o_xn[b] + 2048]
            w = [("dS", ln_dma[tt])]
            if xn_free[b] is not None:
                w.append(xn_free[b])
            B0 = (tt % 2) * 16
            stc = lambda i: st8[:, B0 + i:B0 + i + 1]
            s.op("vector", lambda e, xt=xt, stc=stc: e.reduce_sum(stc(0), xt, axis=AX.X),
                 waits=tuple(w))
            xsq = a16[:, 4096 + o_xn[b]:4096 + o_xn[b] + 2048]
            cssq = s.op("scalar", lambda e, xt=xt, xsq=xsq: e.activation(
                xsq, xt, AF.Square), waits=tuple(w), incs=(("sS", 1),))
            s.op("vector", lambda e, xsq=xsq, stc=stc: e.reduce_sum(stc(1), xsq, axis=AX.X),
                 waits=(("sS", cssq),))
            s.op("vector", lambda e, stc=stc: e.tensor_scalar_mul(stc(2), stc(0), 1.0 / D))
            s.op("vector", lambda e, stc=stc: e.tensor_mul(stc(3), stc(2), stc(2)))
            cv = s.op("vector", lambda e, stc=stc: e.scalar_tensor_tensor(
                stc(4), stc(1), 1.0 / D, stc(3), ALU.mult, ALU.subtract),
                incs=(("vS", 1),))
            s.op("scalar", lambda e, stc=stc: e.activation(stc(5), stc(4), AF.Ln, bias=fco[:, 0:1]),
                 waits=(("vS", cv),))
            cs = s.op("scalar", lambda e, stc=stc: e.activation(stc(6), stc(5), AF.Exp, scale=-0.5),
                      incs=(("sS", 1),))
            cv2 = s.op("vector", lambda e, stc=stc: e.tensor_mul(stc(8), stc(2), stc(6)),
                       waits=(("sS", cs),))
            cv2 = s.op("vector", lambda e, stc=stc: e.tensor_scalar_mul(stc(9), stc(8), -1.0),
                       incs=(("vS", 1),))
            cs2 = s.op("scalar", lambda e, xt=xt, xn=xn, stc=stc: e.activation(
                xn, xt, AF.Identity, bias=stc(9), scale=stc(6)),
                waits=(("vS", cv2),), incs=(("sS", 1),))
            x_free[b] = ("sS", cs2)
            lastv = None
            for dc in range(NDC):
                pb = pt[dc % 2]
                wt = [("sS", cs2)]
                if pt_free[dc % 2] is not None:
                    wt.append(pt_free[dc % 2])
                ct = s.op("tensor", lambda e, pb=pb, xn=xn, dc=dc: e.transpose(
                    pb[:, 0:128], xn[:, dc * 128:(dc + 1) * 128], identbf[:, :]),
                    waits=tuple(wt), incs=(("tS", 1),))
                lastv = s.op("vector", lambda e, pb=pb, dc=dc, tt=tt: e.tensor_copy(
                    xnT[:, dc * TOK + tt * 128: dc * TOK + (tt + 1) * 128], pb[:, 0:128]),
                    waits=(("tS", ct),), incs=(("vS", 1),))
                pt_free[dc % 2] = ("vS", lastv)
            xn_free[b] = ("vS", lastv)
        c_ln_v = s.val("vS")
        sync_dma(xn_bounce.ap().rearrange("(c p) t -> p c t", p=128),
                 xnT[:, :].rearrange("p (c t) -> p c t", c=NDC), waits=(("vS", c_ln_v),))
        c_xnb = s.val("dS")
        if taps:
            sync_dma(tap_xnT, xnT[:, :])
            sync_dma(tap_st8, st8[:, :])
        s.op("gpsimd", lambda e: e.collective_compute(
            "AllGather", ALU.bypass, replica_groups=[list(range(NCORES))],
            ins=[xn_bounce.ap().opt()], outs=[xn_gath.ap().opt()]),
            waits=(("dS", c_xnb),), incs=(("cc1", 1),))

        bc0 = bar()
        for tt in range(NTT):
            bi = tt % 2
            wfirst = bwait(bi, wall(bc0) + (("dG", c_gw),))
            for dc in range(NDC):
                sp = (dc == NDC - 1)
                inc = (("tS", 1),) if sp else ()
                s.op("tensor", lambda e, bi=bi, dc=dc, tt=tt, sp=sp: e.matmul(
                    pf[bi][:, 0:8], xnT[:, dc * TOK + tt * 128: dc * TOK + (tt + 1) * 128],
                    gw[:, dc * 8:(dc + 1) * 8], start=(dc == 0), stop=sp,
                    skip_group_check=True),
                    waits=wfirst if dc == 0 else (), incs=inc)
                if dc == 0:
                    wfirst = ()
            ct = s.val("tS")
            B0 = (tt % 2) * 16
            stc = lambda i: st8[:, B0 + i:B0 + i + 1]
            ga = (tt % 2) * 128
            s.op("vector", lambda e, bi=bi, stc=stc: e.reduce_max(stc(0), pf[bi][:, 0:8], axis=AX.X),
                 waits=(("tS", ct),))
            cm = s.op("vector", lambda e, stc=stc: e.tensor_scalar_mul(stc(1), stc(0), -1.0),
                      incs=(("vS", 1),))
            ce = s.op("scalar", lambda e, bi=bi, stc=stc, ga=ga: e.activation(
                a32[:, ga + 0:ga + 8], pf[bi][:, 0:8], AF.Exp, bias=stc(1), accum_out=stc(2)),
                waits=(("vS", cm),), incs=(("sS", 1),))
            bank_free[bi] = ("sS", ce)
            cr = s.op("vector", lambda e, stc=stc: e.reciprocal(stc(3), stc(2)),
                      waits=(("sS", ce),), incs=(("vS", 1),))
            cn = s.op("scalar", lambda e, stc=stc, ga=ga: e.activation(
                a32[:, ga + 32:ga + 40], a32[:, ga + 0:ga + 8], AF.Identity, scale=stc(3)),
                waits=(("vS", cr),), incs=(("sS", 1),))
            scs = a32[:, ga + 32:ga + 40]
            s.op("vector", lambda e, scs=scs, stc=stc: e.reduce_max(stc(4), scs, axis=AX.X),
                 waits=(("sS", cn),))
            s.op("vector", lambda e, scs=scs, stc=stc, ga=ga: e.scalar_tensor_tensor(
                a32[:, ga + 64:ga + 72], scs, stc(4), scs, ALU.is_lt, ALU.mult))
            s.op("vector", lambda e, stc=stc, ga=ga: e.reduce_max(stc(5), a32[:, ga + 64:ga + 72], axis=AX.X))
            cv = s.op("vector", lambda e, scs=scs, stc=stc, ga=ga: e.scalar_tensor_tensor(
                a32[:, ga + 96:ga + 104], scs, stc(5), scs, ALU.is_ge, ALU.mult),
                incs=(("vS", 1),))
            ctr = s.op("tensor", lambda e, ga=ga: e.transpose(
                pf[2][0:8, 0:128], a32[:, ga + 96:ga + 104], identf[:, :]),
                waits=bwait(2, (("vS", cv),)), incs=(("tS", 1),))
            cvc = s.op("vector", lambda e, tt=tt: e.tensor_copy(
                comb_rows[0:8, tt * 128:(tt + 1) * 128], pf[2][0:8, 0:128]),
                waits=(("tS", ctr),), incs=(("vS", 1),))
            bank_free[2] = ("vS", cvc)
        c_comb_v = s.val("vS")

        o_wgb = [0, 2048]
        o_wub = [4096, 6144]
        o_wdb = [8192, 9600]
        o_G = 11008
        o_sil = [22272, 22784]
        o_t1 = [23296, 23808]
        o_cB = [0, 512]
        o_oh = 1024
        bc1 = bar()
        sync_dma(a32[0:8, o_oh:o_oh + E * 128], onehot_d, waits=wall(bc1, ("vS", "sS")))
        c_oh = s.val("dS")

        wgu_free = [None, None]
        wd_free = [None, None]
        sil_free = [None, None]
        moe_dma = {}

        def w_chunk(src, off, ln, waits):
            gp_dma(a16[:, off:off + ln], src, waits=waits)
            return s.val("dG")

        for (ei, hci, b) in ((0, 0, 0), (0, 1, 1)):
            moe_dma[("g", ei, hci)] = w_chunk(wg_a[ei, hci], o_wgb[b], NDC * 128,
                                              wall(bc1, ("dG", "tS")))
            moe_dma[("u", ei, hci)] = w_chunk(wu_a[ei, hci], o_wub[b], NDC * 128, ())

        first_moe = wall(bc1) + (("vS", c_comb_v), ("dS", c_oh))

        def gateup_phase(nchunks, dma_of, prefetch, use_comb=True, first=()):
            lastG = None
            tS0 = ("tS", s.val("tS"))
            gbar = wall(bar())
            for hc in range(nchunks):
                bks = [(4 * hc + i) % 6 for i in range(4)]
                wdma = dma_of(hc)
                for tc in range(2):
                    gb, ub = bks[2 * tc], bks[2 * tc + 1]
                    wfirst = list(wdma) + list(bwait(gb))
                    if hc == 0 and tc == 0:
                        wfirst += list(first) + list(gbar)
                    for dc in range(NDC):
                        s.op("tensor", lambda e, gb=gb, dc=dc, tc=tc, hc=hc: e.matmul(
                            pf[gb], a16[:, o_wgb[hc % 2] + dc * 128:o_wgb[hc % 2] + (dc + 1) * 128],
                            xnT[:, dc * TOK + tc * 512: dc * TOK + tc * 512 + 512],
                            start=(dc == 0), stop=(dc == NDC - 1), skip_group_check=True),
                            waits=tuple(wfirst) if dc == 0 else (),
                            incs=(("tS", 1),) if dc == NDC - 1 else ())
                        wfirst = []
                    ctg = s.val("tS")
                    wfirst = list(bwait(ub))
                    for dc in range(NDC):
                        s.op("tensor", lambda e, ub=ub, dc=dc, tc=tc, hc=hc: e.matmul(
                            pf[ub], a16[:, o_wub[hc % 2] + dc * 128:o_wub[hc % 2] + (dc + 1) * 128],
                            xnT[:, dc * TOK + tc * 512: dc * TOK + tc * 512 + 512],
                            start=(dc == 0), stop=(dc == NDC - 1), skip_group_check=True),
                            waits=tuple(wfirst) if dc == 0 else (),
                            incs=(("tS", 1),) if dc == NDC - 1 else ())
                        wfirst = []
                    ctu = s.val("tS")
                    sb_, tb_ = o_sil[tc], o_t1[tc]
                    wsil = [("tS", ctg)]
                    if sil_free[tc] is not None:
                        wsil.append(sil_free[tc])
                    cs = s.op("scalar", lambda e, gb=gb, sb_=sb_: e.activation(
                        a16[:, sb_:sb_ + 512], pf[gb], AF.Silu),
                        waits=tuple(wsil), incs=(("sS", 1),))
                    bank_free[gb] = ("sS", cs)
                    if use_comb:
                        cv1 = s.op("vector", lambda e, ub=ub, tb_=tb_, tc=tc: e.tensor_tensor(
                            a16[:, tb_:tb_ + 512], pf[ub], a32[:, o_cB[tc]:o_cB[tc] + 512],
                            ALU.mult), waits=(("tS", ctu),), incs=(("vS", 1),))
                        bank_free[ub] = ("vS", cv1)
                        lastG = s.op("vector", lambda e, sb_=sb_, tb_=tb_, hc=hc, tc=tc: e.tensor_mul(
                            a16[:, o_G + hc * 1024 + tc * 512:o_G + hc * 1024 + tc * 512 + 512],
                            a16[:, tb_:tb_ + 512], a16[:, sb_:sb_ + 512]),
                            waits=(("sS", cs), tS0), incs=(("vS", 1),))
                        sil_free[tc] = ("vS", lastG)
                    else:
                        lastG = s.op("vector", lambda e, ub=ub, sb_=sb_, hc=hc, tc=tc: e.tensor_tensor(
                            a16[:, o_G + hc * 1024 + tc * 512:o_G + hc * 1024 + tc * 512 + 512],
                            pf[ub], a16[:, sb_:sb_ + 512], ALU.mult),
                            waits=(("tS", ctu), ("sS", cs), tS0), incs=(("vS", 1),))
                        bank_free[ub] = ("vS", lastG)
                        sil_free[tc] = ("vS", lastG)
                wgu_free[hc % 2] = ("tS", ctu)
                prefetch(hc, wgu_free[hc % 2])
            return lastG

        def down_phase(nchunks, getwd, wd_len, first_copy=False):
            cG = s.val("vS")
            dbar = wall(bar())
            dma_d = {}
            for d2c in range(2):
                w = [wd_free[d2c % 2]] if wd_free[d2c % 2] is not None else []
                dma_d[d2c] = getwd(d2c, d2c % 2, tuple(w))
            for d2c in range(NDC):
                bwf = o_wdb[d2c % 2]
                ctd_last = None
                for tc in range(2):
                    dbk = 4 + tc
                    w = [("dG", dma_d[d2c]), ("vS", cG)] + list(bwait(dbk))
                    if d2c == 0 and tc == 0:
                        w += list(dbar)
                    for hc in range(nchunks):
                        s.op("tensor", lambda e, dbk=dbk, bwf=bwf, hc=hc, tc=tc: e.matmul(
                            pf[dbk], a16[:, bwf + hc * 128:bwf + (hc + 1) * 128],
                            a16[:, o_G + hc * 1024 + tc * 512:o_G + hc * 1024 + tc * 512 + 512],
                            start=(hc == 0), stop=(hc == nchunks - 1), skip_group_check=True),
                            waits=tuple(w) if hc == 0 else (),
                            incs=(("tS", 1),) if hc == nchunks - 1 else ())
                        w = []
                    ctd = s.val("tS")
                    ctd_last = ctd
                    if first_copy:
                        cv = s.op("vector", lambda e, dbk=dbk, d2c=d2c, tc=tc: e.tensor_copy(
                            fa[:, d2c * TOK + tc * 512: d2c * TOK + tc * 512 + 512], pf[dbk]),
                            waits=(("tS", ctd),), incs=(("vS", 1),))
                    else:
                        cv = s.op("vector", lambda e, dbk=dbk, d2c=d2c, tc=tc: e.tensor_add(
                            fa[:, d2c * TOK + tc * 512: d2c * TOK + tc * 512 + 512],
                            pf[dbk], fa[:, d2c * TOK + tc * 512: d2c * TOK + tc * 512 + 512]),
                            waits=(("tS", ctd),), incs=(("vS", 1),))
                    bank_free[dbk] = ("vS", cv)
                wd_free[d2c % 2] = ("tS", ctd_last)
                if d2c + 2 < NDC:
                    dma_d[d2c + 2] = getwd(d2c + 2, d2c % 2, (wd_free[d2c % 2],))

        for e_i in range(E):
            for tc in range(2):
                w = list(bwait(4))
                if e_i == 0 and tc == 0:
                    w += list(first_moe)
                ct = s.op("tensor", lambda e, e_i=e_i, tc=tc: e.matmul(
                    pf[4], a32[0:8, o_oh + e_i * 128:o_oh + (e_i + 1) * 128],
                    comb_rows[0:8, tc * 512:(tc + 1) * 512], start=True, stop=True,
                    skip_group_check=True),
                    waits=tuple(w), incs=(("tS", 1),))
                cv = s.op("vector", lambda e, tc=tc: e.tensor_copy(
                    a32[:, o_cB[tc]:o_cB[tc] + 512], pf[4]),
                    waits=(("tS", ct),), incs=(("vS", 1),))
                bank_free[4] = ("vS", cv)

            def dma_of(hc, e_i=e_i):
                return [("dG", moe_dma[("g", e_i, hc)]), ("dG", moe_dma[("u", e_i, hc)])]

            def prefetch(hc, free_evt, e_i=e_i):
                if hc + 2 < NHC:
                    nh = hc + 2
                    if ("g", e_i, nh) not in moe_dma:
                        moe_dma[("g", e_i, nh)] = w_chunk(wg_a[e_i, nh], o_wgb[nh % 2],
                                                          NDC * 128, (free_evt,))
                        moe_dma[("u", e_i, nh)] = w_chunk(wu_a[e_i, nh], o_wub[nh % 2],
                                                          NDC * 128, ())
                elif hc == NHC - 1 and e_i + 1 < E:
                    for nh in (0, 1):
                        moe_dma[("g", e_i + 1, nh)] = w_chunk(
                            wg_a[e_i + 1, nh], o_wgb[nh % 2], NDC * 128,
                            (wgu_free[nh % 2],))
                        moe_dma[("u", e_i + 1, nh)] = w_chunk(
                            wu_a[e_i + 1, nh], o_wub[nh % 2], NDC * 128, ())

            gateup_phase(NHC, dma_of, prefetch, use_comb=True,
                         first=first_moe if e_i == 0 else ())

            def getwd(d2c, b, w, e_i=e_i):
                return w_chunk(wd_a[e_i, d2c], o_wdb[b], NHC * 128, w)

            down_phase(NHC, getwd, NHC * 128, first_copy=(e_i == 0))

        bc2 = bar()
        o_xg = [0, 4096, 8192]
        o_wq, o_wk, o_wv = 12288, 16384, 20480
        o_qs = [24576, 25088]
        o_ks = [25600, 26112]
        o_vs = [26624, 27136]
        o_vt = [27648, 28160]
        o_cos = [0, 512]
        o_sin = [1024, 1536]
        o_rt1, o_rt2 = 2048, 2560
        gp_dma(a16[:, o_wq:o_wq + NDC * 256], wq_own, waits=wall({"tS": bc2["tS"], "dG": bc2["dG"]}))
        gp_dma(a16[:, o_wk:o_wk + NDC * 256], wk_own)
        gp_dma(a16[:, o_wv:o_wv + NDC * 256], wv_own)
        c_wqkv = s.val("dG")
        cvz = s.op("vector", lambda e: e.memset(kqm[0:1, 0:8], 0.0), waits=wall(bc2, ("sS",)),
                   incs=(("vS", 1),))
        c_kqm0 = s.val("vS")

        NU = 2 * NTT * 2
        xg_dma = {}
        xg_used = {}

        def xg_load(u, waits):
            tc, half = u // 2, u % 2
            r = tc // 2
            l0 = (tc % 2) * 512
            rows0 = r * D + half * 1024
            src = xn_gath.ap()[rows0:rows0 + 1024, l0:l0 + 512].rearrange("(c p) t -> p c t", p=128)
            off = o_xg[u % 3]
            sync_dma(a16[:, off:off + 4096].rearrange("p (c t) -> p c t", c=8), src, waits=waits)
            xg_dma[u] = s.val("dS")

        for u in range(3):
            xg_load(u, wall(bc2, ("vS", "tS")) + (("cc1", 1),))

        bank_map = [("q", 0, 0), ("k", 0, 1), ("v", 0, 2), ("q", 1, 3), ("k", 1, 4), ("v", 1, 5)]
        cos_free = [None, None]
        qs_free = [None, None]
        ks_free = [None, None]
        vs_free = [None, None]
        vt_free = [None, None]
        norm_red = None

        for tc in range(2 * NTT):
            b2 = tc % 2
            w = [("dS", c_consts)]
            if cos_free[b2] is not None:
                w.append(cos_free[b2])
            else:
                w += list(wall(bc2, ("vS", "tS")))
            sync_dma(a32[:, o_cos[b2]:o_cos[b2] + 512], cos_t[:, tc * 512:(tc + 1) * 512],
                     waits=tuple(w))
            sync_dma(a32[:, o_sin[b2]:o_sin[b2] + 512], sin_t[:, tc * 512:(tc + 1) * 512])
            c_cs = s.val("dS")
            for half in range(2):
                u = tc * 2 + half
                whalf = [("dG", c_wqkv), ("dS", xg_dma[u])]
                for wnm, h, bk in bank_map:
                    woff = {"q": o_wq, "k": o_wk, "v": o_wv}[wnm]
                    wfirst = list(whalf)
                    if half == 0:
                        wfirst += list(bwait(bk))
                    for dci in range(8):
                        dc = half * 8 + dci
                        s.op("tensor", lambda e, bk=bk, woff=woff, dc=dc, h=h, u=u: e.matmul(
                            pf[bk], a16[:, woff + dc * 256 + h * 128: woff + dc * 256 + h * 128 + 128],
                            a16[:, o_xg[u % 3] + (dc % 8) * 512: o_xg[u % 3] + (dc % 8) * 512 + 512],
                            start=(dc == 0), stop=(dc == NDC - 1), skip_group_check=True),
                            waits=tuple(wfirst) if dci == 0 else (),
                            incs=(("tS", 1),) if (dci == 7) else ())
                        wfirst = []
                xg_used[u] = s.val("tS")
                if u + 3 < NU:
                    xg_load(u + 3, (("tS", xg_used[u]),))
            ct_all = s.val("tS")
            for wnm, h, bk in bank_map:
                bcol = {"q": 0, "k": 2, "v": 4}[wnm] + h
                if wnm in ("q", "k"):
                    stg = (o_qs if wnm == "q" else o_ks)[b2]
                    sfree = (qs_free if wnm == "q" else ks_free)[b2]
                    wq_ = [("tS", ct_all), ("dS", c_cs)]
                    s.op("vector", lambda e, bk=bk, bcol=bcol, b2=b2: e.scalar_tensor_tensor(
                        a32[:, o_rt1:o_rt1 + 512], pf[bk], bqkv[:, bcol:bcol + 1],
                        a32[:, o_cos[b2]:o_cos[b2] + 512], ALU.add, ALU.mult),
                        waits=tuple(wq_))
                    s.op("vector", lambda e, bk=bk, bcol=bcol, b2=b2: e.scalar_tensor_tensor(
                        a32[0:64, o_rt2:o_rt2 + 512], pf[bk][64:128, :],
                        bqkv[0:64, 6 + bcol:7 + bcol],
                        a32[0:64, o_sin[b2]:o_sin[b2] + 512], ALU.add, ALU.mult))
                    cv = s.op("vector", lambda e, bk=bk, bcol=bcol, b2=b2: e.scalar_tensor_tensor(
                        a32[64:128, o_rt2:o_rt2 + 512], pf[bk][0:64, :],
                        bqkv[64:128, 6 + bcol:7 + bcol],
                        a32[64:128, o_sin[b2]:o_sin[b2] + 512], ALU.add, ALU.mult),
                        incs=(("vS", 1),))
                    cos_free[b2] = ("vS", cv)
                    wadd = [("vS", cv)]
                    if sfree is not None:
                        wadd.append(sfree)
                    cvr = s.op("vector", lambda e, stg=stg: e.tensor_add(
                        a16[:, stg:stg + 512], a32[:, o_rt1:o_rt1 + 512],
                        a32[:, o_rt2:o_rt2 + 512]), waits=tuple(wadd), incs=(("vS", 1),))
                    dst = (q_dram if wnm == "q" else k_dram).ap()[h, :, tc * 512:(tc + 1) * 512]
                    sync_dma(dst, a16[:, stg:stg + 512], waits=(("vS", cvr),))
                    cdma = s.val("dS")
                    cq2 = s.op("vector", lambda e, stg=stg: e.tensor_mul(
                        a32[:, 3072:3584], a16[:, stg:stg + 512], a16[:, stg:stg + 512]),
                        incs=(("vS", 1),))
                    if wnm == "q":
                        qs_free[b2] = ("dS", cdma)
                    else:
                        ks_free[b2] = ("dS", cdma)
                    ctn = s.op("tensor", lambda e, bk=bk: e.matmul(
                        pf[bk][0:1, :], onescolf[:, :], a32[:, 3072:3584],
                        start=True, stop=True, skip_group_check=True),
                        waits=(("vS", cq2),), incs=(("tS", 1),))
                    wred = [("tS", ctn), ("vS", c_kqm0)]
                    if norm_red is not None:
                        wred.append(norm_red)
                    s.op("vector", lambda e, bk=bk: e.reduce_max(
                        st8[0:1, 10:11], pf[bk][0:1, :], axis=AX.X), waits=tuple(wred))
                    kcol = (0 if wnm == "q" else 2) + h
                    cvn = s.op("vector", lambda e, kcol=kcol: e.tensor_max(
                        kqm[0:1, kcol:kcol + 1], kqm[0:1, kcol:kcol + 1], st8[0:1, 10:11]),
                        incs=(("vS", 1),))
                    norm_red = ("vS", cvn)
                    bank_free[bk] = ("vS", cvn)
                else:
                    wv_ = [("tS", ct_all)]
                    if vs_free[b2] is not None:
                        wv_.append(vs_free[b2])
                    cse = s.op("scalar", lambda e, bk=bk, bcol=bcol, b2=b2: e.activation(
                        a16[:, o_vs[b2]:o_vs[b2] + 512], pf[bk], AF.Identity,
                        bias=bqkv[:, bcol:bcol + 1]), waits=tuple(wv_), incs=(("sS", 1),))
                    bank_free[bk] = ("sS", cse)
                    lastv = None
                    for sub in range(4):
                        pb = pt[sub % 2]
                        wt = [("sS", cse)]
                        if pt_free[sub % 2] is not None:
                            wt.append(pt_free[sub % 2])
                        ctv = s.op("tensor", lambda e, pb=pb, sub=sub, b2=b2: e.transpose(
                            pb[:, 0:128], a16[:, o_vs[b2] + sub * 128:o_vs[b2] + (sub + 1) * 128],
                            identbf[:, :]), waits=tuple(wt), incs=(("tS", 1),))
                        wcp = [("tS", ctv)]
                        if sub == 0 and vt_free[b2] is not None:
                            wcp.append(vt_free[b2])
                        lastv = s.op("vector", lambda e, pb=pb, sub=sub, b2=b2: e.tensor_copy(
                            a16[:, o_vt[b2] + sub * 128:o_vt[b2] + (sub + 1) * 128], pb[:, 0:128]),
                            waits=tuple(wcp), incs=(("vS", 1),))
                        pt_free[sub % 2] = ("vS", lastv)
                    dst = v_dram.ap()[h, tc * 512:(tc + 1) * 512, :].rearrange(
                        "(c p) d -> p c d", p=128)
                    sync_dma(dst, a16[:, o_vt[b2]:o_vt[b2] + 512].rearrange("p (c d) -> p c d", c=4),
                             waits=(("vS", lastv),))
                    vt_free[b2] = ("dS", s.val("dS"))
                    vs_free[b2] = ("tS", ctv)
        c_qkv_sp = s.val("dS")
        for h in range(HL):
            cv = s.op("vector", lambda e, h=h: e.tensor_mul(
                st8[0:1, 11:12], kqm[0:1, h:h + 1], kqm[0:1, 2 + h:3 + h]),
                waits=(norm_red,), incs=(("vS", 1),))
            s.op("scalar", lambda e: e.activation(st8[0:1, 12:13], st8[0:1, 11:12], AF.Ln),
                 waits=(("vS", cv),))
            cs = s.op("scalar", lambda e: e.activation(
                st8[0:1, 13:14], st8[0:1, 12:13], AF.Exp, scale=0.5, bias=fco[0:1, 1:2]),
                incs=(("sS", 1),))
            cv = s.op("vector", lambda e: e.tensor_scalar_mul(st8[0:1, 14:15], st8[0:1, 13:14], -1.0),
                      waits=(("sS", cs),), incs=(("vS", 1),))
            ctb = s.op("tensor", lambda e: e.matmul(
                pf[5][:, 0:1], onesrow[:, :], st8[0:1, 14:15],
                start=True, stop=True, skip_group_check=True),
                waits=bwait(5, (("vS", cv),)), incs=(("tS", 1),))
            cvm = s.op("vector", lambda e, h=h: e.tensor_copy(mhat[:, h:h + 1], pf[5][:, 0:1]),
                       waits=(("tS", ctb),), incs=(("vS", 1),))
            bank_free[5] = ("vS", cvm)
            norm_red = ("vS", cvm)
        c_mhat_v = s.val("vS")

        sh_dma = {}

        def sh_dma_of_factory(half):
            def dma_of(hc):
                ghc = half * 11 + hc
                return [("dG", sh_dma[("g", ghc)]), ("dG", sh_dma[("u", ghc)])]
            return dma_of

        def sh_prefetch_factory(half):
            def prefetch(hc, free_evt):
                nh = hc + 2
                if nh < 11 and ("g", half * 11 + nh) not in sh_dma:
                    sh_dma[("g", half * 11 + nh)] = w_chunk(
                        swg_a[half * 11 + nh], o_wgb[hc % 2], NDC * 128, (free_evt,))
                    sh_dma[("u", half * 11 + nh)] = w_chunk(
                        swu_a[half * 11 + nh], o_wub[hc % 2], NDC * 128, ())
            return prefetch

        def sh_getwd_factory(half):
            def getwd(d2c, b, w):
                return w_chunk(swd_a[d2c][:, half * 1408:(half + 1) * 1408], o_wdb[b],
                               11 * 128, w)
            return getwd

        bcP4 = bar()
        for hci, b in ((0, 0), (1, 1)):
            pw = wall(bcP4, ("tS", "dG")) if hci == 0 else ()
            sh_dma[("g", hci)] = w_chunk(swg_a[hci], o_wgb[b], NDC * 128, pw)
            sh_dma[("u", hci)] = w_chunk(swu_a[hci], o_wub[b], NDC * 128, ())
        gateup_phase(11, sh_dma_of_factory(0), sh_prefetch_factory(0), use_comb=False)
        down_phase(11, sh_getwd_factory(0), 11 * 128)

        bc3 = bar()
        o_q = [0, 512]
        o_K = [1024, 1536]
        o_V = [2048, 2560]
        o_PT = [3072, 3584]
        o_mask = 0
        o_B = 1024
        o_rl = 1536
        sync_dma(a32[:, o_mask:o_mask + 1024], mask_d, waits=wall(bc3, ("vS", "sS", "tS")))
        c_mask = s.val("dS")

        q_dma = {}
        q_freed = [None, None]
        kv_dma = {}
        kv_freed = [None, None]
        PT_freed = [None, None]
        ucount = 0
        pv_last = None

        def kv_load(h, qb, kb, waits=()):
            bb = kb % 2
            w = list(waits)
            if kv_freed[bb] is not None:
                w.append(kv_freed[bb])
            sync_dma(a16[:, o_K[bb]:o_K[bb] + 512], k_dram.ap()[h, :, kb * 512:(kb + 1) * 512],
                     waits=tuple(w))
            sync_dma(a16[:, o_V[bb]:o_V[bb] + 512].rearrange("p (c d) -> p c d", c=4),
                     v_dram.ap()[h, kb * 512:(kb + 1) * 512, :].rearrange("(c p) d -> p c d", p=128))
            kv_dma[(h, qb, kb)] = s.val("dS")

        first_att = (("dS", c_mask), ("vS", c_mhat_v)) + wall(bc3, ("sS",))
        for h in range(HL):
            for qb in range(16):
                qbb = qb % 2
                w = []
                if q_freed[qbb] is not None:
                    w.append(q_freed[qbb])
                sync_dma(a16[:, o_q[qbb]:o_q[qbb] + 512],
                         q_dram.ap()[h, :, qb * 512:(qb + 1) * 512], waits=tuple(w))
                q_dma[(h, qb)] = s.val("dS")
                for kb in range(min(2, qb + 1)):
                    kv_load(h, qb, kb)
                for kb in range(qb + 1):
                    diag = (kb == qb)
                    for jc in range(4):
                        u = ucount
                        ucount += 1
                        sbki = u % 2
                        wS = [("dS", kv_dma[(h, qb, kb)]), ("dS", q_dma[(h, qb)])]
                        wS += list(bwait(sbki))
                        if u == 0:
                            wS += list(first_att)
                        ctS = s.op("tensor", lambda e, sbki=sbki, kb=kb, jc=jc, qbb=qbb: e.matmul(
                            pf[sbki], a16[:, o_K[kb % 2] + jc * 128:o_K[kb % 2] + (jc + 1) * 128],
                            a16[:, o_q[qbb]:o_q[qbb] + 512],
                            start=True, stop=True, skip_group_check=True),
                            waits=tuple(wS), incs=(("tS", 1),))
                        wE = [("tS", ctS)]
                        if diag:
                            cmsk = s.op("vector", lambda e, sbki=sbki, jc=jc: e.tensor_add(
                                pf[sbki], pf[sbki],
                                a32[:, o_mask + 512 - 128 * jc:o_mask + 1024 - 128 * jc]),
                                waits=(("tS", ctS),), incs=(("vS", 1),))
                            wE.append(("vS", cmsk))
                        if PT_freed[u % 2] is not None:
                            wE.append(PT_freed[u % 2])
                        cE = s.op("scalar", lambda e, sbki=sbki, u=u, h=h: e.activation(
                            a16[:, o_PT[u % 2]:o_PT[u % 2] + 512], pf[sbki], AF.Exp,
                            bias=mhat[:, h:h + 1], scale=ATT_SCALE),
                            waits=tuple(wE), incs=(("sS", 1),))
                        bank_free[sbki] = ("sS", cE)
                        wPV = [("sS", cE)]
                        if kb == 0 and jc == 0:
                            wPV += list(bwait(2)) + list(bwait(3))
                        s.op("tensor", lambda e, kb=kb, jc=jc, u=u,
                             fst=(kb == 0 and jc == 0), lst=(kb == qb and jc == 3): e.matmul(
                            pf[2], a16[:, o_V[kb % 2] + jc * 128:o_V[kb % 2] + (jc + 1) * 128],
                            a16[:, o_PT[u % 2]:o_PT[u % 2] + 512],
                            start=fst, stop=lst, skip_group_check=True),
                            waits=tuple(wPV))
                        ctL = s.op("tensor", lambda e, u=u,
                                   fst=(kb == 0 and jc == 0), lst=(kb == qb and jc == 3): e.matmul(
                            pf[3][0:1, :], onescol[:, :], a16[:, o_PT[u % 2]:o_PT[u % 2] + 512],
                            start=fst, stop=lst, skip_group_check=True),
                            incs=(("tS", 1),))
                        PT_freed[u % 2] = ("tS", ctL)
                        pv_last = ctL
                    if kb + 2 <= qb:
                        kv_load(h, qb, kb + 2, (("tS", pv_last),))
                    kv_freed[kb % 2] = ("tS", pv_last)
                q_freed[qbb] = ("tS", pv_last)
                crl = s.op("vector", lambda e: e.reciprocal(
                    a32[0:1, o_rl:o_rl + 512], pf[3][0:1, :]),
                    waits=(("tS", pv_last),), incs=(("vS", 1),))
                bank_free[3] = ("vS", crl)
                ctB = s.op("tensor", lambda e: e.matmul(
                    pf[4], onesrow[:, :], a32[0:1, o_rl:o_rl + 512],
                    start=True, stop=True, skip_group_check=True),
                    waits=bwait(4, (("vS", crl),)), incs=(("tS", 1),))
                cvB = s.op("vector", lambda e: e.tensor_copy(a32[:, o_B:o_B + 512], pf[4]),
                           waits=(("tS", ctB),), incs=(("vS", 1),))
                bank_free[4] = ("vS", cvB)
                cvO = s.op("vector", lambda e, h=h, qb=qb: e.tensor_tensor(
                    AT[:, h * N_TOK + qb * 512: h * N_TOK + (qb + 1) * 512],
                    pf[2], a32[:, o_B:o_B + 512], ALU.mult),
                    waits=(("vS", cvB),), incs=(("vS", 1),))
                bank_free[2] = ("vS", cvO)
        c_att_v = s.val("vS")
        sync_dma(a_bounce.ap().rearrange("(c p) t -> p c t", p=128),
                 AT[:, :].rearrange("p (c t) -> p c t", c=HL), waits=(("vS", c_att_v),))
        c_ab = s.val("dS")
        s.op("gpsimd", lambda e: e.collective_compute(
            "AllGather", ALU.bypass, replica_groups=[list(range(NCORES))],
            ins=[a_bounce.ap().opt()], outs=[a_gath.ap().opt()]),
            waits=(("dS", c_ab),), incs=(("cc2", 1),))

        bcP6 = bar()
        for hci, b in ((0, 0), (1, 1)):
            pw = wall(bcP6, ("tS", "dG")) if hci == 0 else ()
            sh_dma[("g", 11 + hci)] = w_chunk(swg_a[11 + hci], o_wgb[b], NDC * 128, pw)
            sh_dma[("u", 11 + hci)] = w_chunk(swu_a[11 + hci], o_wub[b], NDC * 128, ())
        gateup_phase(11, sh_dma_of_factory(1), sh_prefetch_factory(1), use_comb=False)
        down_phase(11, sh_getwd_factory(1), 11 * 128)

        bc4 = bar()
        o_wo = [0, 2048]
        o_ag = [4096, 12288]
        o_ssb = [0, 512]
        o_st4 = [1024, 1536]
        wo_dma = {}
        wo_used = {}
        ag_dma = {}
        ag_free = [None, None]
        ssb_free = [None, None]
        st4_free = [None, None]
        pairs = [(tc, d2c) for tc in range(2) for d2c in range(NDC)]

        def wo_load(i, waits=()):
            bb = i % 2
            w = list(waits)
            if i - 2 in wo_used:
                w.append(wo_used[i - 2])
            gp_dma(a16[:, o_wo[bb]:o_wo[bb] + NDC * 128], wo_a[pairs[i][1]], waits=tuple(w))
            wo_dma[i] = s.val("dG")

        for i in range(2):
            wo_load(i, wall(bc4, ("tS", "dG")) if i == 0 else ())

        def ag_load(tc, pid_mul, waits):
            bb = tc % 2
            w = list(waits)
            if ag_free[bb] is not None:
                w.append(ag_free[bb])

            def fn(e, tc=tc, bb=bb):
                pid = e.partition_id()
                src = a_gath.ap()[:, bass.ts(pid, TOK)][:, tc * 512:(tc + 1) * 512]
                src = src.rearrange("(c p) t -> p c t", p=128)
                return e.dma_start(out=a16[:, o_ag[bb]:o_ag[bb] + NDC * 512].rearrange(
                    "p (c t) -> p c t", c=NDC), in_=src)
            s.op("sync", fn, waits=tuple(w), incs=(("dS", 16),))
            ag_dma[tc] = s.val("dS")

        ag_load(0, TOK, (("cc2", 1),) + wall(bc4, ("tS",)))
        ag_load(1, TOK, ())

        for i, (tc, d2c) in enumerate(pairs):
            agb = tc % 2
            if True:
                wbk = i % 2
                wfirst = [("dG", wo_dma[i]), ("dS", ag_dma[tc])] + list(bwait(wbk))
                for ic in range(NDC):
                    s.op("tensor", lambda e, wbk=wbk, ic=ic, agb=agb: e.matmul(
                        pf[wbk], a16[:, o_wo[wbk] + ic * 128:o_wo[wbk] + (ic + 1) * 128],
                        a16[:, o_ag[agb] + ic * 512:o_ag[agb] + (ic + 1) * 512],
                        start=(ic == 0), stop=(ic == NDC - 1), skip_group_check=True),
                        waits=tuple(wfirst) if ic == 0 else (),
                        incs=(("tS", 1),) if ic == NDC - 1 else ())
                    wfirst = []
                ctw = s.val("tS")
                wo_used[i] = ("tS", ctw)
                if i + 2 < len(pairs):
                    wo_load(i + 2)
                wsb = [("tS", ctw)]
                if ssb_free[d2c % 2] is not None:
                    wsb.append(ssb_free[d2c % 2])
                csb = s.op("vector", lambda e, wbk=wbk, d2c=d2c, tc=tc: e.scalar_tensor_tensor(
                    a32[:, o_ssb[d2c % 2]:o_ssb[d2c % 2] + 512], pf[wbk], bot[:, d2c:d2c + 1],
                    fa[:, d2c * TOK + tc * 512: d2c * TOK + tc * 512 + 512], ALU.add, ALU.add),
                    waits=tuple(wsb), incs=(("vS", 1),))
                bank_free[wbk] = ("vS", csb)
                lastc = None
                for sub in range(4):
                    tb = 2 + (sub % 2)
                    ctt = s.op("tensor", lambda e, tb=tb, d2c=d2c, sub=sub: e.transpose(
                        pf[tb][:, 0:128], a32[:, o_ssb[d2c % 2] + sub * 128:
                                              o_ssb[d2c % 2] + (sub + 1) * 128], identf[:, :]),
                        waits=bwait(tb, (("vS", csb),)), incs=(("tS", 1),))
                    wcp = [("tS", ctt)]
                    if sub == 0 and st4_free[d2c % 2] is not None:
                        wcp.append(st4_free[d2c % 2])
                    lastc = s.op("vector", lambda e, tb=tb, d2c=d2c, sub=sub: e.tensor_copy(
                        a32[:, o_st4[d2c % 2] + sub * 128:o_st4[d2c % 2] + (sub + 1) * 128],
                        pf[tb][:, 0:128]), waits=tuple(wcp), incs=(("vS", 1),))
                    bank_free[tb] = ("vS", lastc)
                ssb_free[d2c % 2] = ("vS", lastc)
                dst = out_own[tc * 512:(tc + 1) * 512, d2c * 128:(d2c + 1) * 128].rearrange(
                    "(c p) d -> p c d", p=128)
                gp_dma(dst, a32[:, o_st4[d2c % 2]:o_st4[d2c % 2] + 512].rearrange(
                    "p (c d) -> p c d", c=4),
                    waits=(("vS", lastc), ("dS", c_xcopy)), accum=True)
                st4_free[d2c % 2] = ("dG", s.val("dG"))
            ag_free[agb] = ("tS", s.val("tS"))


        if taps:
            bfin = bar()
            sync_dma(tap_AT, AT[:, :], waits=wall(bfin))
            sync_dma(tap_fa, fa[:, :])
            sync_dma(tap_comb, comb_rows[:, :])
            sync_dma(tap_mhat, mhat[:, :])
            sync_dma(tap_kqm, kqm[0:1, 0:8])
            sync_dma(tap_q, q_dram.ap())
            sync_dma(tap_k, k_dram.ap())
            sync_dma(tap_v, v_dram.ap())

        fin = bar()
        s.op("sync", lambda e: None, waits=wall(fin) + (("cc1", 1), ("cc2", 1)))

        with nc.Block() as block:

            @block.sync
            def _(eng):
                s.emit("sync", eng, sems)

            @block.gpsimd
            def _(eng):
                s.emit("gpsimd", eng, sems)

            @block.tensor
            def _(eng):
                s.emit("tensor", eng, sems)

            @block.vector
            def _(eng):
                s.emit("vector", eng, sems)

            @block.scalar
            def _(eng):
                s.emit("scalar", eng, sems)

    return nc


_CACHE = {}


def _host_consts():
    inv = (1.0 / 10000.0 ** (np.arange(0, DH, 2, dtype=np.float32) / DH)).astype(np.float32)
    seq = np.arange(N_TOK, dtype=np.float32)
    fr = seq[:, None] * inv[None, :]
    pos = np.concatenate([fr, fr], axis=1).astype(np.float32)
    cos = np.cos(pos).T.copy()
    sinf = np.sin(pos)
    sign = np.concatenate([-np.ones(64, np.float32), np.ones(64, np.float32)])
    sin = (sinf * sign[None, :]).T.copy()
    mask = np.where(np.arange(1024)[None, :] >= np.arange(128)[:, None] + 512,
                    np.float32(0.0), np.float32(NEG)).astype(np.float32)
    onehot = np.zeros((8, E * 128), np.float32)
    for e in range(E):
        onehot[e, e * 128:(e + 1) * 128] = 1.0
    fconst = np.zeros((128, 4), np.float32)
    fconst[:, 0] = 1e-5
    fconst[:, 1] = math.log(1.02 * ATT_SCALE)
    return {
        "cos_t": cos, "sin_t": sin, "mask_d": mask,
        "identbf_d": np.eye(128, dtype=ml_dtypes.bfloat16),
        "identf_d": np.eye(128, dtype=np.float32),
        "onesrow_d": np.ones((1, 128), np.float32),
        "onescol_d": np.ones((128, 1), ml_dtypes.bfloat16),
        "onescolf_d": np.ones((128, 1), np.float32),
        "onehot_d": onehot, "fconst_d": fconst,
    }


def _pack(w, rows_chunks, cols_chunks):
    R, C = rows_chunks, cols_chunks
    return np.ascontiguousarray(
        w.reshape(R, 128, C, 128).transpose(2, 1, 0, 3).reshape(C, 128, R * 128))


def kernel(L0_attn_x, L0_attn_Wq, L0_attn_bq, L0_attn_Wk, L0_attn_bk, L0_attn_Wv,
           L0_attn_bv, L0_attn_Wo, L0_attn_bo, L0_ffn_gate_w, L0_ffn_w_gate,
           L0_ffn_w_up, L0_ffn_w_down, L0_ffn_sw_gate, L0_ffn_sw_up, L0_ffn_sw_down):
    key = ("prog", TAPS)
    if key not in _CACHE:
        _CACHE[key] = build_program(taps=TAPS)
    nc = _CACHE[key]

    x = np.asarray(L0_attn_x, dtype=np.float32).reshape(N_TOK, D)
    f32 = lambda a: np.asarray(a, dtype=np.float32)
    Wq, Wk, Wv, Wo = f32(L0_attn_Wq), f32(L0_attn_Wk), f32(L0_attn_Wv), f32(L0_attn_Wo)
    bq, bk, bv, bo = f32(L0_attn_bq), f32(L0_attn_bk), f32(L0_attn_bv), f32(L0_attn_bo)
    gate_w = f32(L0_ffn_gate_w)
    w_gate, w_up, w_down = f32(L0_ffn_w_gate), f32(L0_ffn_w_up), f32(L0_ffn_w_down)
    sw_gate, sw_up, sw_down = f32(L0_ffn_sw_gate), f32(L0_ffn_sw_up), f32(L0_ffn_sw_down)

    consts = _host_consts()
    gate_w_a = np.ascontiguousarray(
        gate_w.reshape(NDC, 128, E).transpose(1, 0, 2).reshape(128, NDC * E))
    wg_a = np.ascontiguousarray(
        w_gate.reshape(E, NDC, 128, NHC, 128).transpose(0, 3, 2, 1, 4).reshape(E, NHC, 128, NDC * 128))
    wu_a = np.ascontiguousarray(
        w_up.reshape(E, NDC, 128, NHC, 128).transpose(0, 3, 2, 1, 4).reshape(E, NHC, 128, NDC * 128))
    wd_a = np.ascontiguousarray(
        w_down.reshape(E, NHC, 128, NDC, 128).transpose(0, 3, 2, 1, 4).reshape(E, NDC, 128, NHC * 128))
    swg_a = _pack(sw_gate, NDC, NSC)
    swu_a = _pack(sw_up, NDC, NSC)
    swd_a = _pack(sw_down, NSC, NDC)
    wo_a = _pack(Wo, NDC, NDC)
    bo_t = np.ascontiguousarray(bo.reshape(NDC, 128).T)

    shared = dict(consts)
    shared.update({
        "gate_w_a": gate_w_a, "wg_a": wg_a, "wu_a": wu_a, "wd_a": wd_a,
        "swg_a": swg_a, "swu_a": swu_a, "swd_a": swd_a, "wo_a": wo_a, "bo_t": bo_t,
    })

    in_maps = []
    for c in range(NCORES):
        m = dict(shared)
        m["x_own"] = np.ascontiguousarray(x[c * TOK:(c + 1) * TOK, :])
        hsl = slice(c * 256, (c + 1) * 256)
        for nm, W in (("wq_own", Wq), ("wk_own", Wk), ("wv_own", Wv)):
            m[nm] = np.ascontiguousarray(
                W[:, hsl].reshape(NDC, 128, 256).transpose(1, 0, 2).reshape(128, NDC * 256))
        barr = np.zeros((128, 12), np.float32)
        for i, b in enumerate((bq, bk, bv)):
            bl = b[hsl].reshape(2, 128)
            barr[:, 2 * i + 0] = bl[0]
            barr[:, 2 * i + 1] = bl[1]
        barr[:, 6:12] = np.concatenate([barr[64:128, 0:6], barr[0:64, 0:6]], axis=0)
        m["b_qkv"] = barr
        in_maps.append(m)

    res = run_bass_kernel_spmd(nc, in_maps, core_ids=list(range(NCORES)))
    out = np.concatenate([res.results[c]["out_own"] for c in range(NCORES)], axis=0)
    if TAPS:
        kernel.last_results = res
    return out.reshape(1, N_TOK, D)


# version 05
# speedup vs baseline: 1.0360x; 1.0360x over previous
import math
import contextlib
import numpy as np
import ml_dtypes
import concourse.bass as bass
import concourse.mybir as mybir
from concourse.bass_utils import run_bass_kernel_spmd

F32 = mybir.dt.float32
BF16 = mybir.dt.bfloat16
AF = mybir.ActivationFunctionType
ALU = mybir.AluOpType
AX = mybir.AxisListType

N_TOK = 8192
D = 2048
DH = 128
NCORES = 8
TOK = N_TOK // NCORES
HL = 2
E = 8
NDC = D // 128
NHC = 11
NSC = 22
ATT_SCALE = DH ** -0.5
NEG = -1e30

ENGINES = ("sync", "gpsimd", "tensor", "vector", "scalar")
TAPS = False


DMA_SLOTS = 8


class Sched:
    def __init__(self):
        self.ops = {e: [] for e in ENGINES}
        self.cnt = {}
        self.dma_idx = {"sync": 0, "gpsimd": 0}
        self.dma_hist = {"dS": [], "dG": []}

    def dma(self, eng, fn, waits=()):
        vname = {"sync": "dS", "gpsimd": "dG"}[eng]
        j = self.dma_idx[eng]
        slot = j % DMA_SLOTS
        g = j // DMA_SLOTS + 1
        sname = f"{vname}{slot}"
        w = list(waits)
        if g > 1:
            w.append((sname, 16 * (g - 1)))
        self.dma_idx[eng] = j + 1
        self.dma_hist[vname].append(sname)
        self.cnt[vname] = self.cnt.get(vname, 0) + 16
        self.op(eng, fn, waits=tuple(w), incs=((sname, 16),))
        return self.cnt[vname]

    def _xlate(self, ss, v):
        if ss not in ("dS", "dG"):
            return ((ss, v),)
        k = v // 16
        hist = self.dma_hist[ss][:k]
        per = {}
        for sn in hist:
            per[sn] = per.get(sn, 0) + 16
        return tuple(per.items())

    def val(self, sem):
        return self.cnt.get(sem, 0)

    SELF = {"vector": "vS", "scalar": "sS"}

    def op(self, eng, fn, waits=(), incs=()):
        for ss, v in waits:
            if not (ss[0] == "d" and len(ss) > 2):
                assert v <= self.val(ss), f"wait on future: {ss} {v} > {self.val(ss)}"
        sse = self.SELF.get(eng)
        n_orig = len(tuple(incs))
        if sse is not None and not any(ss == sse for ss, _ in incs):
            incs = tuple(incs) + ((sse, 1),)
        self.ops[eng].append((tuple(waits), fn, tuple(incs)))
        out = []
        for i, (ss, d) in enumerate(incs):
            self.cnt[ss] = self.cnt.get(ss, 0) + d
            if i < n_orig or n_orig == 0:
                out.append(self.cnt[ss])
        return out[0] if len(out) == 1 else out

    def emit(self, eng_name, eng, sems):
        last = {}
        sse = self.SELF.get(eng_name)
        self_c = 0
        for waits, fn, incs in self.ops[eng_name]:
            if sse is not None and self_c > 0:
                eng.wait_ge(sems[sse], self_c)
                last[sse] = max(last.get(sse, 0), self_c)
            for ss0, v0 in waits:
                for ss, v in self._xlate(ss0, v0):
                    if last.get(ss, 0) < v:
                        eng.wait_ge(sems[ss], v)
                        last[ss] = v
            inst = fn(eng)
            for ss, d in incs:
                if ss.startswith("cc"):
                    inst.then_inc(sems[ss])
                else:
                    inst.then_inc(sems[ss], d)
                if ss == sse:
                    self_c += d


def build_program(taps=False):
    nc = bass.Bass("TRN2", target_bir_lowering=False)
    dt = nc.dram_tensor

    x_own = dt("x_own", [TOK, D], F32, kind="ExternalInput").ap()
    wq_own = dt("wq_own", [128, NDC * 256], F32, kind="ExternalInput").ap()
    wk_own = dt("wk_own", [128, NDC * 256], F32, kind="ExternalInput").ap()
    wv_own = dt("wv_own", [128, NDC * 256], F32, kind="ExternalInput").ap()
    b_qkv = dt("b_qkv", [128, 12], F32, kind="ExternalInput").ap()
    gate_w_a = dt("gate_w_a", [128, NDC * E], F32, kind="ExternalInput").ap()
    wg_a = dt("wg_a", [E, NHC, 128, NDC * 128], F32, kind="ExternalInput").ap()
    wu_a = dt("wu_a", [E, NHC, 128, NDC * 128], F32, kind="ExternalInput").ap()
    wd_a = dt("wd_a", [E, NDC, 128, NHC * 128], F32, kind="ExternalInput").ap()
    swg_a = dt("swg_a", [NSC, 128, NDC * 128], F32, kind="ExternalInput").ap()
    swu_a = dt("swu_a", [NSC, 128, NDC * 128], F32, kind="ExternalInput").ap()
    swd_a = dt("swd_a", [NDC, 128, NSC * 128], F32, kind="ExternalInput").ap()
    wo_a = dt("wo_a", [NDC, 128, NDC * 128], F32, kind="ExternalInput").ap()
    bo_t = dt("bo_t", [128, NDC], F32, kind="ExternalInput").ap()
    cos_t = dt("cos_t", [128, N_TOK], F32, kind="ExternalInput").ap()
    sin_t = dt("sin_t", [128, N_TOK], F32, kind="ExternalInput").ap()
    mask_d = dt("mask_d", [128, 1024], F32, kind="ExternalInput").ap()
    identbf_d = dt("identbf_d", [128, 128], BF16, kind="ExternalInput").ap()
    identf_d = dt("identf_d", [128, 128], F32, kind="ExternalInput").ap()
    onesrow_d = dt("onesrow_d", [1, 128], F32, kind="ExternalInput").ap()
    onescol_d = dt("onescol_d", [128, 1], BF16, kind="ExternalInput").ap()
    onescolf_d = dt("onescolf_d", [128, 1], F32, kind="ExternalInput").ap()
    onehot_d = dt("onehot_d", [8, E * 128], F32, kind="ExternalInput").ap()
    fconst_d = dt("fconst_d", [128, 4], F32, kind="ExternalInput").ap()
    out_own = dt("out_own", [TOK, D], F32, kind="ExternalOutput").ap()

    xn_bounce = dt("xn_bounce", [D, TOK], BF16)
    xn_gath = dt("xn_gath", [NCORES * D, TOK], BF16)
    a_bounce = dt("a_bounce", [HL * 128, N_TOK], BF16)
    a_gath = dt("a_gath", [NCORES * HL * 128, N_TOK], BF16)
    q_dram = dt("q_dram", [HL, 128, N_TOK], BF16)
    k_dram = dt("k_dram", [HL, 128, N_TOK], BF16)
    v_dram = dt("v_dram", [HL, N_TOK, 128], BF16)
    if taps:
        tap_xnT = dt("tap_xnT", [128, NDC * TOK], BF16, kind="ExternalOutput").ap()
        tap_AT = dt("tap_AT", [128, HL * N_TOK], BF16, kind="ExternalOutput").ap()
        tap_fa = dt("tap_fa", [128, NDC * TOK], BF16, kind="ExternalOutput").ap()
        tap_comb = dt("tap_comb", [8, TOK], F32, kind="ExternalOutput").ap()
        tap_mhat = dt("tap_mhat", [128, HL], F32, kind="ExternalOutput").ap()
        tap_kqm = dt("tap_kqm", [1, 8], F32, kind="ExternalOutput").ap()
        tap_st8 = dt("tap_st8", [128, 40], F32, kind="ExternalOutput").ap()
        tap_q = dt("tap_q", [HL, 128, N_TOK], BF16, kind="ExternalOutput").ap()
        tap_k = dt("tap_k", [HL, 128, N_TOK], BF16, kind="ExternalOutput").ap()
        tap_v = dt("tap_v", [HL, N_TOK, 128], BF16, kind="ExternalOutput").ap()

    s = Sched()

    with contextlib.ExitStack() as es:
        sb = lambda name, shape, dtp: es.enter_context(nc.sbuf_tensor(name, shape, dtp))
        psmk = lambda name, shape, dtp: es.enter_context(nc.psum_tensor(name, shape, dtp))

        xnT = sb("xnT", [128, NDC * TOK], BF16)
        AT = sb("AT", [128, HL * N_TOK], BF16)
        fa = sb("fa", [128, NDC * TOK], BF16)
        comb_rows = sb("comb_rows", [8, TOK], F32)
        identbf = sb("identbf", [128, 128], BF16)
        identf = sb("identf", [128, 128], F32)
        onesrow = sb("onesrow", [1, 128], F32)
        onescol = sb("onescol", [128, 1], BF16)
        onescolf = sb("onescolf", [128, 1], F32)
        bqkv = sb("bqkv", [128, 12], F32)
        bot = sb("bot", [128, NDC], F32)
        gw = sb("gw", [128, NDC * E], BF16)
        mhat = sb("mhat", [128, HL], F32)
        fco = sb("fco", [128, 4], F32)
        st8 = sb("st8", [128, 40], F32)
        kqm = sb("kqm", [1, 8], F32)
        a16 = sb("a16", [128, 29184], BF16)
        a32 = sb("a32", [128, 4608], F32)

        pfh = [psmk(f"pf{i}", [128, 512], F32) for i in range(6)]
        pf = [h[:, :] for h in pfh]
        pt = [psmk(f"pt{i}", [128, 1024], BF16) for i in range(2)]
        bank_free = [None] * 6
        pt_free = [None] * 2

        sems = {}
        for nm in (["cc1", "cc2", "tS", "vS", "sS"]
                   + [f"dS{i}" for i in range(DMA_SLOTS)]
                   + [f"dG{i}" for i in range(DMA_SLOTS)]):
            sems[nm] = es.enter_context(nc.semaphore(nm))

        def sync_dma(dst, src, waits=()):
            return s.dma("sync", lambda e, d=dst, sr=src: e.dma_start(out=d, in_=sr),
                         waits=waits)

        def gp_dma(dst, src, waits=(), accum=False):
            if accum:
                return s.dma("gpsimd",
                             lambda e, d=dst, sr=src: e.dma_start(out=d, in_=sr, accum_op=ALU.add),
                             waits=waits)
            return s.dma("gpsimd", lambda e, d=dst, sr=src: e.dma_start(out=d, in_=sr),
                         waits=waits)

        def bar():
            return {nm: s.val(nm) for nm in ("dS", "dG", "tS", "vS", "sS")}

        def wall(bc, only=None):
            return tuple((nm, v) for nm, v in bc.items()
                         if v > 0 and (only is None or nm in only))

        def bwait(i, extra=()):
            w = list(extra)
            if bank_free[i] is not None:
                w.append(bank_free[i])
            return tuple(w)

        sync_dma(identbf[:, :], identbf_d)
        sync_dma(identf[:, :], identf_d)
        sync_dma(onesrow[:, :], onesrow_d)
        sync_dma(onescol[:, :], onescol_d)
        sync_dma(onescolf[:, :], onescolf_d)
        sync_dma(bqkv[:, :], b_qkv)
        sync_dma(bot[:, :], bo_t)
        sync_dma(fco[:, :], fconst_d)
        c_consts = s.val("dS")
        sync_dma(out_own, x_own)
        c_xcopy = s.val("dS")
        gp_dma(gw[:, :], gate_w_a)
        c_gw = s.val("dG")

        NTT = TOK // 128
        o_xt = [0, 2048]
        o_xn = [0, 2048]
        x_free = [None, None]
        xn_free = [None, None]
        ln_dma = []
        for tt in range(NTT):
            b = tt % 2
            w = [("dS", c_consts)]
            if x_free[b] is not None:
                w.append(x_free[b])
            sync_dma(a32[:, o_xt[b]:o_xt[b] + 2048], x_own[tt * 128:(tt + 1) * 128, :], waits=tuple(w))
            ln_dma.append(s.val("dS"))
            if tt == 1:
                break
        for tt in range(NTT):
            b = tt % 2
            if tt >= 2:
                w2 = [("dS", c_consts), x_free[b]]
                sync_dma(a32[:, o_xt[b]:o_xt[b] + 2048], x_own[tt * 128:(tt + 1) * 128, :],
                         waits=tuple(w2))
                ln_dma.append(s.val("dS"))
            xt = a32[:, o_xt[b]:o_xt[b] + 2048]
            xn = a16[:, o_xn[b]:o_xn[b] + 2048]
            w = [("dS", ln_dma[tt])]
            if xn_free[b] is not None:
                w.append(xn_free[b])
            B0 = (tt % 2) * 16
            stc = lambda i: st8[:, B0 + i:B0 + i + 1]
            s.op("vector", lambda e, xt=xt, stc=stc: e.reduce_sum(stc(0), xt, axis=AX.X),
                 waits=tuple(w))
            xsq = a16[:, 4096 + o_xn[b]:4096 + o_xn[b] + 2048]
            cssq = s.op("scalar", lambda e, xt=xt, xsq=xsq: e.activation(
                xsq, xt, AF.Square), waits=tuple(w), incs=(("sS", 1),))
            s.op("vector", lambda e, xsq=xsq, stc=stc: e.reduce_sum(stc(1), xsq, axis=AX.X),
                 waits=(("sS", cssq),))
            s.op("vector", lambda e, stc=stc: e.tensor_scalar_mul(stc(2), stc(0), 1.0 / D))
            s.op("vector", lambda e, stc=stc: e.tensor_mul(stc(3), stc(2), stc(2)))
            cv = s.op("vector", lambda e, stc=stc: e.scalar_tensor_tensor(
                stc(4), stc(1), 1.0 / D, stc(3), ALU.mult, ALU.subtract),
                incs=(("vS", 1),))
            s.op("scalar", lambda e, stc=stc: e.activation(stc(5), stc(4), AF.Ln, bias=fco[:, 0:1]),
                 waits=(("vS", cv),))
            cs = s.op("scalar", lambda e, stc=stc: e.activation(stc(6), stc(5), AF.Exp, scale=-0.5),
                      incs=(("sS", 1),))
            cv2 = s.op("vector", lambda e, stc=stc: e.tensor_mul(stc(8), stc(2), stc(6)),
                       waits=(("sS", cs),))
            cv2 = s.op("vector", lambda e, stc=stc: e.tensor_scalar_mul(stc(9), stc(8), -1.0),
                       incs=(("vS", 1),))
            cs2 = s.op("scalar", lambda e, xt=xt, xn=xn, stc=stc: e.activation(
                xn, xt, AF.Identity, bias=stc(9), scale=stc(6)),
                waits=(("vS", cv2),), incs=(("sS", 1),))
            x_free[b] = ("sS", cs2)
            lastv = None
            for dc in range(NDC):
                pb = pt[dc % 2]
                wt = [("sS", cs2)]
                if pt_free[dc % 2] is not None:
                    wt.append(pt_free[dc % 2])
                ct = s.op("tensor", lambda e, pb=pb, xn=xn, dc=dc: e.transpose(
                    pb[:, 0:128], xn[:, dc * 128:(dc + 1) * 128], identbf[:, :]),
                    waits=tuple(wt), incs=(("tS", 1),))
                lastv = s.op("vector", lambda e, pb=pb, dc=dc, tt=tt: e.tensor_copy(
                    xnT[:, dc * TOK + tt * 128: dc * TOK + (tt + 1) * 128], pb[:, 0:128]),
                    waits=(("tS", ct),), incs=(("vS", 1),))
                pt_free[dc % 2] = ("vS", lastv)
            xn_free[b] = ("vS", lastv)
        c_ln_v = s.val("vS")
        sync_dma(xn_bounce.ap().rearrange("(c p) t -> p c t", p=128),
                 xnT[:, :].rearrange("p (c t) -> p c t", c=NDC), waits=(("vS", c_ln_v),))
        c_xnb = s.val("dS")
        if taps:
            sync_dma(tap_xnT, xnT[:, :])
            sync_dma(tap_st8, st8[:, :])
        s.op("gpsimd", lambda e: e.collective_compute(
            "AllGather", ALU.bypass, replica_groups=[list(range(NCORES))],
            ins=[xn_bounce.ap().opt()], outs=[xn_gath.ap().opt()]),
            waits=(("dS", c_xnb),), incs=(("cc1", 1),))

        bc0 = bar()
        for tt in range(NTT):
            bi = tt % 2
            wfirst = bwait(bi, wall(bc0) + (("dG", c_gw),))
            for dc in range(NDC):
                sp = (dc == NDC - 1)
                inc = (("tS", 1),) if sp else ()
                s.op("tensor", lambda e, bi=bi, dc=dc, tt=tt, sp=sp: e.matmul(
                    pf[bi][:, 0:8], xnT[:, dc * TOK + tt * 128: dc * TOK + (tt + 1) * 128],
                    gw[:, dc * 8:(dc + 1) * 8], start=(dc == 0), stop=sp,
                    skip_group_check=True),
                    waits=wfirst if dc == 0 else (), incs=inc)
                if dc == 0:
                    wfirst = ()
            ct = s.val("tS")
            B0 = (tt % 2) * 16
            stc = lambda i: st8[:, B0 + i:B0 + i + 1]
            ga = (tt % 2) * 128
            s.op("vector", lambda e, bi=bi, stc=stc: e.reduce_max(stc(0), pf[bi][:, 0:8], axis=AX.X),
                 waits=(("tS", ct),))
            cm = s.op("vector", lambda e, stc=stc: e.tensor_scalar_mul(stc(1), stc(0), -1.0),
                      incs=(("vS", 1),))
            ce = s.op("scalar", lambda e, bi=bi, stc=stc, ga=ga: e.activation(
                a32[:, ga + 0:ga + 8], pf[bi][:, 0:8], AF.Exp, bias=stc(1), accum_out=stc(2)),
                waits=(("vS", cm),), incs=(("sS", 1),))
            bank_free[bi] = ("sS", ce)
            cr = s.op("vector", lambda e, stc=stc: e.reciprocal(stc(3), stc(2)),
                      waits=(("sS", ce),), incs=(("vS", 1),))
            cn = s.op("scalar", lambda e, stc=stc, ga=ga: e.activation(
                a32[:, ga + 32:ga + 40], a32[:, ga + 0:ga + 8], AF.Identity, scale=stc(3)),
                waits=(("vS", cr),), incs=(("sS", 1),))
            scs = a32[:, ga + 32:ga + 40]
            s.op("vector", lambda e, scs=scs, stc=stc: e.reduce_max(stc(4), scs, axis=AX.X),
                 waits=(("sS", cn),))
            s.op("vector", lambda e, scs=scs, stc=stc, ga=ga: e.scalar_tensor_tensor(
                a32[:, ga + 64:ga + 72], scs, stc(4), scs, ALU.is_lt, ALU.mult))
            s.op("vector", lambda e, stc=stc, ga=ga: e.reduce_max(stc(5), a32[:, ga + 64:ga + 72], axis=AX.X))
            cv = s.op("vector", lambda e, scs=scs, stc=stc, ga=ga: e.scalar_tensor_tensor(
                a32[:, ga + 96:ga + 104], scs, stc(5), scs, ALU.is_ge, ALU.mult),
                incs=(("vS", 1),))
            ctr = s.op("tensor", lambda e, ga=ga: e.transpose(
                pf[2][0:8, 0:128], a32[:, ga + 96:ga + 104], identf[:, :]),
                waits=bwait(2, (("vS", cv),)), incs=(("tS", 1),))
            cvc = s.op("vector", lambda e, tt=tt: e.tensor_copy(
                comb_rows[0:8, tt * 128:(tt + 1) * 128], pf[2][0:8, 0:128]),
                waits=(("tS", ctr),), incs=(("vS", 1),))
            bank_free[2] = ("vS", cvc)
        c_comb_v = s.val("vS")

        o_wgb = [0, 2048]
        o_wub = [4096, 6144]
        o_wdb = [8192, 9600]
        o_G = 11008
        o_sil = [22272, 22784]
        o_t1 = [23296, 23808]
        o_cB = [0, 512]
        o_oh = 1024
        bc1 = bar()
        sync_dma(a32[0:8, o_oh:o_oh + E * 128], onehot_d, waits=wall(bc1, ("vS", "sS")))
        c_oh = s.val("dS")

        wgu_free = [None, None]
        wd_free = [None, None]
        sil_free = [None, None]
        moe_dma = {}

        def w_chunk(src, off, ln, waits):
            gp_dma(a16[:, off:off + ln], src, waits=waits)
            return s.val("dG")

        for (ei, hci, b) in ((0, 0, 0), (0, 1, 1)):
            moe_dma[("g", ei, hci)] = w_chunk(wg_a[ei, hci], o_wgb[b], NDC * 128,
                                              wall(bc1, ("dG", "tS")))
            moe_dma[("u", ei, hci)] = w_chunk(wu_a[ei, hci], o_wub[b], NDC * 128, ())

        first_moe = wall(bc1) + (("vS", c_comb_v), ("dS", c_oh))

        def gateup_phase(nchunks, dma_of, prefetch, use_comb=True, first=()):
            lastG = None
            tS0 = ("tS", s.val("tS"))
            gbar = wall(bar())
            for hc in range(nchunks):
                bks = [(4 * hc + i) % 6 for i in range(4)]
                wdma = dma_of(hc)
                for tc in range(2):
                    gb, ub = bks[2 * tc], bks[2 * tc + 1]
                    wfirst = list(wdma) + list(bwait(gb))
                    if hc == 0 and tc == 0:
                        wfirst += list(first) + list(gbar)
                    for dc in range(NDC):
                        s.op("tensor", lambda e, gb=gb, dc=dc, tc=tc, hc=hc: e.matmul(
                            pf[gb], a16[:, o_wgb[hc % 2] + dc * 128:o_wgb[hc % 2] + (dc + 1) * 128],
                            xnT[:, dc * TOK + tc * 512: dc * TOK + tc * 512 + 512],
                            start=(dc == 0), stop=(dc == NDC - 1), skip_group_check=True),
                            waits=tuple(wfirst) if dc == 0 else (),
                            incs=(("tS", 1),) if dc == NDC - 1 else ())
                        wfirst = []
                    ctg = s.val("tS")
                    wfirst = list(bwait(ub))
                    for dc in range(NDC):
                        s.op("tensor", lambda e, ub=ub, dc=dc, tc=tc, hc=hc: e.matmul(
                            pf[ub], a16[:, o_wub[hc % 2] + dc * 128:o_wub[hc % 2] + (dc + 1) * 128],
                            xnT[:, dc * TOK + tc * 512: dc * TOK + tc * 512 + 512],
                            start=(dc == 0), stop=(dc == NDC - 1), skip_group_check=True),
                            waits=tuple(wfirst) if dc == 0 else (),
                            incs=(("tS", 1),) if dc == NDC - 1 else ())
                        wfirst = []
                    ctu = s.val("tS")
                    sb_, tb_ = o_sil[tc], o_t1[tc]
                    wsil = [("tS", ctg)]
                    if sil_free[tc] is not None:
                        wsil.append(sil_free[tc])
                    cs = s.op("scalar", lambda e, gb=gb, sb_=sb_: e.activation(
                        a16[:, sb_:sb_ + 512], pf[gb], AF.Silu),
                        waits=tuple(wsil), incs=(("sS", 1),))
                    bank_free[gb] = ("sS", cs)
                    if use_comb:
                        cv1 = s.op("vector", lambda e, ub=ub, tb_=tb_, tc=tc: e.tensor_tensor(
                            a16[:, tb_:tb_ + 512], pf[ub], a32[:, o_cB[tc]:o_cB[tc] + 512],
                            ALU.mult), waits=(("tS", ctu),), incs=(("vS", 1),))
                        bank_free[ub] = ("vS", cv1)
                        lastG = s.op("vector", lambda e, sb_=sb_, tb_=tb_, hc=hc, tc=tc: e.tensor_mul(
                            a16[:, o_G + hc * 1024 + tc * 512:o_G + hc * 1024 + tc * 512 + 512],
                            a16[:, tb_:tb_ + 512], a16[:, sb_:sb_ + 512]),
                            waits=(("sS", cs), tS0), incs=(("vS", 1),))
                        sil_free[tc] = ("vS", lastG)
                    else:
                        lastG = s.op("vector", lambda e, ub=ub, sb_=sb_, hc=hc, tc=tc: e.tensor_tensor(
                            a16[:, o_G + hc * 1024 + tc * 512:o_G + hc * 1024 + tc * 512 + 512],
                            pf[ub], a16[:, sb_:sb_ + 512], ALU.mult),
                            waits=(("tS", ctu), ("sS", cs), tS0), incs=(("vS", 1),))
                        bank_free[ub] = ("vS", lastG)
                        sil_free[tc] = ("vS", lastG)
                wgu_free[hc % 2] = ("tS", ctu)
                prefetch(hc, wgu_free[hc % 2])
            return lastG

        def down_phase(nchunks, getwd, wd_len, first_copy=False):
            cG = s.val("vS")
            dbar = wall(bar())
            dma_d = {}
            for d2c in range(2):
                w = [wd_free[d2c % 2]] if wd_free[d2c % 2] is not None else []
                dma_d[d2c] = getwd(d2c, d2c % 2, tuple(w))
            for d2c in range(NDC):
                bwf = o_wdb[d2c % 2]
                ctd_last = None
                for tc in range(2):
                    dbk = 4 + tc
                    w = [("dG", dma_d[d2c]), ("vS", cG)] + list(bwait(dbk))
                    if d2c == 0 and tc == 0:
                        w += list(dbar)
                    for hc in range(nchunks):
                        s.op("tensor", lambda e, dbk=dbk, bwf=bwf, hc=hc, tc=tc: e.matmul(
                            pf[dbk], a16[:, bwf + hc * 128:bwf + (hc + 1) * 128],
                            a16[:, o_G + hc * 1024 + tc * 512:o_G + hc * 1024 + tc * 512 + 512],
                            start=(hc == 0), stop=(hc == nchunks - 1), skip_group_check=True),
                            waits=tuple(w) if hc == 0 else (),
                            incs=(("tS", 1),) if hc == nchunks - 1 else ())
                        w = []
                    ctd = s.val("tS")
                    ctd_last = ctd
                    if first_copy:
                        cv = s.op("vector", lambda e, dbk=dbk, d2c=d2c, tc=tc: e.tensor_copy(
                            fa[:, d2c * TOK + tc * 512: d2c * TOK + tc * 512 + 512], pf[dbk]),
                            waits=(("tS", ctd),), incs=(("vS", 1),))
                    else:
                        cv = s.op("vector", lambda e, dbk=dbk, d2c=d2c, tc=tc: e.tensor_add(
                            fa[:, d2c * TOK + tc * 512: d2c * TOK + tc * 512 + 512],
                            pf[dbk], fa[:, d2c * TOK + tc * 512: d2c * TOK + tc * 512 + 512]),
                            waits=(("tS", ctd),), incs=(("vS", 1),))
                    bank_free[dbk] = ("vS", cv)
                wd_free[d2c % 2] = ("tS", ctd_last)
                if d2c + 2 < NDC:
                    dma_d[d2c + 2] = getwd(d2c + 2, d2c % 2, (wd_free[d2c % 2],))

        for e_i in range(E):
            for tc in range(2):
                w = list(bwait(4))
                if e_i == 0 and tc == 0:
                    w += list(first_moe)
                ct = s.op("tensor", lambda e, e_i=e_i, tc=tc: e.matmul(
                    pf[4], a32[0:8, o_oh + e_i * 128:o_oh + (e_i + 1) * 128],
                    comb_rows[0:8, tc * 512:(tc + 1) * 512], start=True, stop=True,
                    skip_group_check=True),
                    waits=tuple(w), incs=(("tS", 1),))
                cv = s.op("vector", lambda e, tc=tc: e.tensor_copy(
                    a32[:, o_cB[tc]:o_cB[tc] + 512], pf[4]),
                    waits=(("tS", ct),), incs=(("vS", 1),))
                bank_free[4] = ("vS", cv)

            def dma_of(hc, e_i=e_i):
                return [("dG", moe_dma[("g", e_i, hc)]), ("dG", moe_dma[("u", e_i, hc)])]

            def prefetch(hc, free_evt, e_i=e_i):
                if hc + 2 < NHC:
                    nh = hc + 2
                    if ("g", e_i, nh) not in moe_dma:
                        moe_dma[("g", e_i, nh)] = w_chunk(wg_a[e_i, nh], o_wgb[nh % 2],
                                                          NDC * 128, (free_evt,))
                        moe_dma[("u", e_i, nh)] = w_chunk(wu_a[e_i, nh], o_wub[nh % 2],
                                                          NDC * 128, ())
                elif hc == NHC - 1 and e_i + 1 < E:
                    for nh in (0, 1):
                        moe_dma[("g", e_i + 1, nh)] = w_chunk(
                            wg_a[e_i + 1, nh], o_wgb[nh % 2], NDC * 128,
                            (wgu_free[nh % 2],))
                        moe_dma[("u", e_i + 1, nh)] = w_chunk(
                            wu_a[e_i + 1, nh], o_wub[nh % 2], NDC * 128, ())

            gateup_phase(NHC, dma_of, prefetch, use_comb=True,
                         first=first_moe if e_i == 0 else ())

            def getwd(d2c, b, w, e_i=e_i):
                return w_chunk(wd_a[e_i, d2c], o_wdb[b], NHC * 128, w)

            down_phase(NHC, getwd, NHC * 128, first_copy=(e_i == 0))

        bc2 = bar()
        o_xg = [0, 4096, 8192]
        o_wq, o_wk, o_wv = 12288, 16384, 20480
        o_qs = [24576, 25088]
        o_ks = [25600, 26112]
        o_vs = [26624, 27136]
        o_vt = [27648, 28160]
        o_cos = [0, 512]
        o_sin = [1024, 1536]
        o_rt1, o_rt2 = 2048, 2560
        gp_dma(a16[:, o_wq:o_wq + NDC * 256], wq_own, waits=wall({"tS": bc2["tS"], "dG": bc2["dG"]}))
        gp_dma(a16[:, o_wk:o_wk + NDC * 256], wk_own)
        gp_dma(a16[:, o_wv:o_wv + NDC * 256], wv_own)
        c_wqkv = s.val("dG")
        cvz = s.op("vector", lambda e: e.memset(kqm[0:1, 0:8], 0.0), waits=wall(bc2, ("sS",)),
                   incs=(("vS", 1),))
        c_kqm0 = s.val("vS")

        NU = 2 * NTT * 2
        xg_dma = {}
        xg_used = {}

        def xg_load(u, waits):
            tc, half = u // 2, u % 2
            r = tc // 2
            l0 = (tc % 2) * 512
            rows0 = r * D + half * 1024
            src = xn_gath.ap()[rows0:rows0 + 1024, l0:l0 + 512].rearrange("(c p) t -> p c t", p=128)
            off = o_xg[u % 3]
            sync_dma(a16[:, off:off + 4096].rearrange("p (c t) -> p c t", c=8), src, waits=waits)
            xg_dma[u] = s.val("dS")

        for u in range(3):
            xg_load(u, wall(bc2, ("vS", "tS")) + (("cc1", 1),))

        bank_map = [("q", 0, 0), ("k", 0, 1), ("v", 0, 2), ("q", 1, 3), ("k", 1, 4), ("v", 1, 5)]
        cos_free = [None, None]
        qs_free = [None, None]
        ks_free = [None, None]
        vs_free = [None, None]
        vt_free = [None, None]
        norm_red = None

        for tc in range(2 * NTT):
            b2 = tc % 2
            w = [("dS", c_consts)]
            if cos_free[b2] is not None:
                w.append(cos_free[b2])
            else:
                w += list(wall(bc2, ("vS", "tS")))
            sync_dma(a32[:, o_cos[b2]:o_cos[b2] + 512], cos_t[:, tc * 512:(tc + 1) * 512],
                     waits=tuple(w))
            sync_dma(a32[:, o_sin[b2]:o_sin[b2] + 512], sin_t[:, tc * 512:(tc + 1) * 512])
            c_cs = s.val("dS")
            for half in range(2):
                u = tc * 2 + half
                whalf = [("dG", c_wqkv), ("dS", xg_dma[u])]
                for wnm, h, bk in bank_map:
                    woff = {"q": o_wq, "k": o_wk, "v": o_wv}[wnm]
                    wfirst = list(whalf)
                    if half == 0:
                        wfirst += list(bwait(bk))
                    for dci in range(8):
                        dc = half * 8 + dci
                        s.op("tensor", lambda e, bk=bk, woff=woff, dc=dc, h=h, u=u: e.matmul(
                            pf[bk], a16[:, woff + dc * 256 + h * 128: woff + dc * 256 + h * 128 + 128],
                            a16[:, o_xg[u % 3] + (dc % 8) * 512: o_xg[u % 3] + (dc % 8) * 512 + 512],
                            start=(dc == 0), stop=(dc == NDC - 1), skip_group_check=True),
                            waits=tuple(wfirst) if dci == 0 else (),
                            incs=(("tS", 1),) if (dci == 7) else ())
                        wfirst = []
                xg_used[u] = s.val("tS")
                if u + 3 < NU:
                    xg_load(u + 3, (("tS", xg_used[u]),))
            ct_all = s.val("tS")
            for wnm, h, bk in bank_map:
                bcol = {"q": 0, "k": 2, "v": 4}[wnm] + h
                if wnm in ("q", "k"):
                    stg = (o_qs if wnm == "q" else o_ks)[b2]
                    sfree = (qs_free if wnm == "q" else ks_free)[b2]
                    wq_ = [("tS", ct_all), ("dS", c_cs)]
                    s.op("vector", lambda e, bk=bk, bcol=bcol, b2=b2: e.scalar_tensor_tensor(
                        a32[:, o_rt1:o_rt1 + 512], pf[bk], bqkv[:, bcol:bcol + 1],
                        a32[:, o_cos[b2]:o_cos[b2] + 512], ALU.add, ALU.mult),
                        waits=tuple(wq_))
                    s.op("vector", lambda e, bk=bk, bcol=bcol, b2=b2: e.scalar_tensor_tensor(
                        a32[0:64, o_rt2:o_rt2 + 512], pf[bk][64:128, :],
                        bqkv[0:64, 6 + bcol:7 + bcol],
                        a32[0:64, o_sin[b2]:o_sin[b2] + 512], ALU.add, ALU.mult))
                    cv = s.op("vector", lambda e, bk=bk, bcol=bcol, b2=b2: e.scalar_tensor_tensor(
                        a32[64:128, o_rt2:o_rt2 + 512], pf[bk][0:64, :],
                        bqkv[64:128, 6 + bcol:7 + bcol],
                        a32[64:128, o_sin[b2]:o_sin[b2] + 512], ALU.add, ALU.mult),
                        incs=(("vS", 1),))
                    cos_free[b2] = ("vS", cv)
                    wadd = [("vS", cv)]
                    if sfree is not None:
                        wadd.append(sfree)
                    cvr = s.op("vector", lambda e, stg=stg: e.tensor_add(
                        a16[:, stg:stg + 512], a32[:, o_rt1:o_rt1 + 512],
                        a32[:, o_rt2:o_rt2 + 512]), waits=tuple(wadd), incs=(("vS", 1),))
                    dst = (q_dram if wnm == "q" else k_dram).ap()[h, :, tc * 512:(tc + 1) * 512]
                    sync_dma(dst, a16[:, stg:stg + 512], waits=(("vS", cvr),))
                    cdma = s.val("dS")
                    cq2 = s.op("vector", lambda e, stg=stg: e.tensor_mul(
                        a32[:, 3072:3584], a16[:, stg:stg + 512], a16[:, stg:stg + 512]),
                        incs=(("vS", 1),))
                    if wnm == "q":
                        qs_free[b2] = ("dS", cdma)
                    else:
                        ks_free[b2] = ("dS", cdma)
                    ctn = s.op("tensor", lambda e, bk=bk: e.matmul(
                        pf[bk][0:1, :], onescolf[:, :], a32[:, 3072:3584],
                        start=True, stop=True, skip_group_check=True),
                        waits=(("vS", cq2),), incs=(("tS", 1),))
                    wred = [("tS", ctn), ("vS", c_kqm0)]
                    if norm_red is not None:
                        wred.append(norm_red)
                    s.op("vector", lambda e, bk=bk: e.reduce_max(
                        st8[0:1, 10:11], pf[bk][0:1, :], axis=AX.X), waits=tuple(wred))
                    kcol = (0 if wnm == "q" else 2) + h
                    cvn = s.op("vector", lambda e, kcol=kcol: e.tensor_max(
                        kqm[0:1, kcol:kcol + 1], kqm[0:1, kcol:kcol + 1], st8[0:1, 10:11]),
                        incs=(("vS", 1),))
                    norm_red = ("vS", cvn)
                    bank_free[bk] = ("vS", cvn)
                else:
                    wv_ = [("tS", ct_all)]
                    if vs_free[b2] is not None:
                        wv_.append(vs_free[b2])
                    cse = s.op("scalar", lambda e, bk=bk, bcol=bcol, b2=b2: e.activation(
                        a16[:, o_vs[b2]:o_vs[b2] + 512], pf[bk], AF.Identity,
                        bias=bqkv[:, bcol:bcol + 1]), waits=tuple(wv_), incs=(("sS", 1),))
                    bank_free[bk] = ("sS", cse)
                    lastv = None
                    for sub in range(4):
                        pb = pt[sub % 2]
                        wt = [("sS", cse)]
                        if pt_free[sub % 2] is not None:
                            wt.append(pt_free[sub % 2])
                        ctv = s.op("tensor", lambda e, pb=pb, sub=sub, b2=b2: e.transpose(
                            pb[:, 0:128], a16[:, o_vs[b2] + sub * 128:o_vs[b2] + (sub + 1) * 128],
                            identbf[:, :]), waits=tuple(wt), incs=(("tS", 1),))
                        wcp = [("tS", ctv)]
                        if sub == 0 and vt_free[b2] is not None:
                            wcp.append(vt_free[b2])
                        lastv = s.op("vector", lambda e, pb=pb, sub=sub, b2=b2: e.tensor_copy(
                            a16[:, o_vt[b2] + sub * 128:o_vt[b2] + (sub + 1) * 128], pb[:, 0:128]),
                            waits=tuple(wcp), incs=(("vS", 1),))
                        pt_free[sub % 2] = ("vS", lastv)
                    dst = v_dram.ap()[h, tc * 512:(tc + 1) * 512, :].rearrange(
                        "(c p) d -> p c d", p=128)
                    sync_dma(dst, a16[:, o_vt[b2]:o_vt[b2] + 512].rearrange("p (c d) -> p c d", c=4),
                             waits=(("vS", lastv),))
                    vt_free[b2] = ("dS", s.val("dS"))
                    vs_free[b2] = ("tS", ctv)
        c_qkv_sp = s.val("dS")
        for h in range(HL):
            cv = s.op("vector", lambda e, h=h: e.tensor_mul(
                st8[0:1, 11:12], kqm[0:1, h:h + 1], kqm[0:1, 2 + h:3 + h]),
                waits=(norm_red,), incs=(("vS", 1),))
            s.op("scalar", lambda e: e.activation(st8[0:1, 12:13], st8[0:1, 11:12], AF.Ln),
                 waits=(("vS", cv),))
            cs = s.op("scalar", lambda e: e.activation(
                st8[0:1, 13:14], st8[0:1, 12:13], AF.Exp, scale=0.5, bias=fco[0:1, 1:2]),
                incs=(("sS", 1),))
            cv = s.op("vector", lambda e: e.tensor_scalar_mul(st8[0:1, 14:15], st8[0:1, 13:14], -1.0),
                      waits=(("sS", cs),), incs=(("vS", 1),))
            ctb = s.op("tensor", lambda e: e.matmul(
                pf[5][:, 0:1], onesrow[:, :], st8[0:1, 14:15],
                start=True, stop=True, skip_group_check=True),
                waits=bwait(5, (("vS", cv),)), incs=(("tS", 1),))
            cvm = s.op("vector", lambda e, h=h: e.tensor_copy(mhat[:, h:h + 1], pf[5][:, 0:1]),
                       waits=(("tS", ctb),), incs=(("vS", 1),))
            bank_free[5] = ("vS", cvm)
            norm_red = ("vS", cvm)
        c_mhat_v = s.val("vS")

        sh_dma = {}

        def sh_dma_of_factory(half):
            def dma_of(hc):
                ghc = half * 11 + hc
                return [("dG", sh_dma[("g", ghc)]), ("dG", sh_dma[("u", ghc)])]
            return dma_of

        def sh_prefetch_factory(half):
            def prefetch(hc, free_evt):
                nh = hc + 2
                if nh < 11 and ("g", half * 11 + nh) not in sh_dma:
                    sh_dma[("g", half * 11 + nh)] = w_chunk(
                        swg_a[half * 11 + nh], o_wgb[hc % 2], NDC * 128, (free_evt,))
                    sh_dma[("u", half * 11 + nh)] = w_chunk(
                        swu_a[half * 11 + nh], o_wub[hc % 2], NDC * 128, ())
            return prefetch

        def sh_getwd_factory(half):
            def getwd(d2c, b, w):
                return w_chunk(swd_a[d2c][:, half * 1408:(half + 1) * 1408], o_wdb[b],
                               11 * 128, w)
            return getwd

        bcP4 = bar()
        for hci, b in ((0, 0), (1, 1)):
            pw = wall(bcP4, ("tS", "dG")) if hci == 0 else ()
            sh_dma[("g", hci)] = w_chunk(swg_a[hci], o_wgb[b], NDC * 128, pw)
            sh_dma[("u", hci)] = w_chunk(swu_a[hci], o_wub[b], NDC * 128, ())
        gateup_phase(11, sh_dma_of_factory(0), sh_prefetch_factory(0), use_comb=False)
        down_phase(11, sh_getwd_factory(0), 11 * 128)

        bc3 = bar()
        o_q = [0, 512]
        o_K = [1024, 1536]
        o_V = [2048, 2560]
        o_PT = [3072, 3584]
        o_mask = 0
        o_B = 1024
        o_rl = 1536
        sync_dma(a32[:, o_mask:o_mask + 1024], mask_d, waits=wall(bc3, ("vS", "sS", "tS")))
        c_mask = s.val("dS")

        q_dma = {}
        q_freed = [None, None]
        kv_dma = {}
        kv_freed = [None, None]
        PT_freed = [None, None]
        ucount = 0
        pv_last = None

        def kv_load(h, qb, kb, waits=()):
            bb = kb % 2
            w = list(waits)
            if kv_freed[bb] is not None:
                w.append(kv_freed[bb])
            sync_dma(a16[:, o_K[bb]:o_K[bb] + 512], k_dram.ap()[h, :, kb * 512:(kb + 1) * 512],
                     waits=tuple(w))
            sync_dma(a16[:, o_V[bb]:o_V[bb] + 512].rearrange("p (c d) -> p c d", c=4),
                     v_dram.ap()[h, kb * 512:(kb + 1) * 512, :].rearrange("(c p) d -> p c d", p=128))
            kv_dma[(h, qb, kb)] = s.val("dS")

        first_att = (("dS", c_mask), ("vS", c_mhat_v)) + wall(bc3, ("sS",))
        for h in range(HL):
            for qb in range(16):
                qbb = qb % 2
                w = []
                if q_freed[qbb] is not None:
                    w.append(q_freed[qbb])
                sync_dma(a16[:, o_q[qbb]:o_q[qbb] + 512],
                         q_dram.ap()[h, :, qb * 512:(qb + 1) * 512], waits=tuple(w))
                q_dma[(h, qb)] = s.val("dS")
                for kb in range(min(2, qb + 1)):
                    kv_load(h, qb, kb)
                for kb in range(qb + 1):
                    diag = (kb == qb)
                    for jc in range(4):
                        u = ucount
                        ucount += 1
                        sbki = u % 2
                        wS = [("dS", kv_dma[(h, qb, kb)]), ("dS", q_dma[(h, qb)])]
                        wS += list(bwait(sbki))
                        if u == 0:
                            wS += list(first_att)
                        ctS = s.op("tensor", lambda e, sbki=sbki, kb=kb, jc=jc, qbb=qbb: e.matmul(
                            pf[sbki], a16[:, o_K[kb % 2] + jc * 128:o_K[kb % 2] + (jc + 1) * 128],
                            a16[:, o_q[qbb]:o_q[qbb] + 512],
                            start=True, stop=True, skip_group_check=True),
                            waits=tuple(wS), incs=(("tS", 1),))
                        wE = [("tS", ctS)]
                        if diag:
                            cmsk = s.op("vector", lambda e, sbki=sbki, jc=jc: e.tensor_add(
                                pf[sbki], pf[sbki],
                                a32[:, o_mask + 512 - 128 * jc:o_mask + 1024 - 128 * jc]),
                                waits=(("tS", ctS),), incs=(("vS", 1),))
                            wE.append(("vS", cmsk))
                        if PT_freed[u % 2] is not None:
                            wE.append(PT_freed[u % 2])
                        cE = s.op("scalar", lambda e, sbki=sbki, u=u, h=h: e.activation(
                            a16[:, o_PT[u % 2]:o_PT[u % 2] + 512], pf[sbki], AF.Exp,
                            bias=mhat[:, h:h + 1], scale=ATT_SCALE),
                            waits=tuple(wE), incs=(("sS", 1),))
                        bank_free[sbki] = ("sS", cE)
                        wPV = [("sS", cE)]
                        if kb == 0 and jc == 0:
                            wPV += list(bwait(2)) + list(bwait(3))
                        s.op("tensor", lambda e, kb=kb, jc=jc, u=u,
                             fst=(kb == 0 and jc == 0), lst=(kb == qb and jc == 3): e.matmul(
                            pf[2], a16[:, o_V[kb % 2] + jc * 128:o_V[kb % 2] + (jc + 1) * 128],
                            a16[:, o_PT[u % 2]:o_PT[u % 2] + 512],
                            start=fst, stop=lst, skip_group_check=True),
                            waits=tuple(wPV))
                        ctL = s.op("tensor", lambda e, u=u,
                                   fst=(kb == 0 and jc == 0), lst=(kb == qb and jc == 3): e.matmul(
                            pf[3][0:1, :], onescol[:, :], a16[:, o_PT[u % 2]:o_PT[u % 2] + 512],
                            start=fst, stop=lst, skip_group_check=True),
                            incs=(("tS", 1),))
                        PT_freed[u % 2] = ("tS", ctL)
                        pv_last = ctL
                    if kb + 2 <= qb:
                        kv_load(h, qb, kb + 2, (("tS", pv_last),))
                    kv_freed[kb % 2] = ("tS", pv_last)
                q_freed[qbb] = ("tS", pv_last)
                crl = s.op("vector", lambda e: e.reciprocal(
                    a32[0:1, o_rl:o_rl + 512], pf[3][0:1, :]),
                    waits=(("tS", pv_last),), incs=(("vS", 1),))
                bank_free[3] = ("vS", crl)
                ctB = s.op("tensor", lambda e: e.matmul(
                    pf[4], onesrow[:, :], a32[0:1, o_rl:o_rl + 512],
                    start=True, stop=True, skip_group_check=True),
                    waits=bwait(4, (("vS", crl),)), incs=(("tS", 1),))
                cvB = s.op("vector", lambda e: e.tensor_copy(a32[:, o_B:o_B + 512], pf[4]),
                           waits=(("tS", ctB),), incs=(("vS", 1),))
                bank_free[4] = ("vS", cvB)
                cvO = s.op("vector", lambda e, h=h, qb=qb: e.tensor_tensor(
                    AT[:, h * N_TOK + qb * 512: h * N_TOK + (qb + 1) * 512],
                    pf[2], a32[:, o_B:o_B + 512], ALU.mult),
                    waits=(("vS", cvB),), incs=(("vS", 1),))
                bank_free[2] = ("vS", cvO)
        c_att_v = s.val("vS")
        sync_dma(a_bounce.ap().rearrange("(c p) t -> p c t", p=128),
                 AT[:, :].rearrange("p (c t) -> p c t", c=HL), waits=(("vS", c_att_v),))
        c_ab = s.val("dS")
        s.op("gpsimd", lambda e: e.collective_compute(
            "AllGather", ALU.bypass, replica_groups=[list(range(NCORES))],
            ins=[a_bounce.ap().opt()], outs=[a_gath.ap().opt()]),
            waits=(("dS", c_ab),), incs=(("cc2", 1),))

        bcP6 = bar()
        for hci, b in ((0, 0), (1, 1)):
            pw = wall(bcP6, ("tS", "dG")) if hci == 0 else ()
            sh_dma[("g", 11 + hci)] = w_chunk(swg_a[11 + hci], o_wgb[b], NDC * 128, pw)
            sh_dma[("u", 11 + hci)] = w_chunk(swu_a[11 + hci], o_wub[b], NDC * 128, ())
        gateup_phase(11, sh_dma_of_factory(1), sh_prefetch_factory(1), use_comb=False)
        down_phase(11, sh_getwd_factory(1), 11 * 128)

        bc4 = bar()
        o_wo = [0, 2048]
        o_ag = [4096, 12288]
        o_ssb = [0, 512]
        o_st4 = [1024, 1536]
        wo_dma = {}
        wo_used = {}
        ag_dma = {}
        ag_free = [None, None]
        ssb_free = [None, None]
        st4_free = [None, None]
        pairs = [(tc, d2c) for tc in range(2) for d2c in range(NDC)]

        def wo_load(i, waits=()):
            bb = i % 2
            w = list(waits)
            if i - 2 in wo_used:
                w.append(wo_used[i - 2])
            gp_dma(a16[:, o_wo[bb]:o_wo[bb] + NDC * 128], wo_a[pairs[i][1]], waits=tuple(w))
            wo_dma[i] = s.val("dG")

        for i in range(2):
            wo_load(i, wall(bc4, ("tS", "dG")) if i == 0 else ())

        def ag_load(tc, pid_mul, waits):
            bb = tc % 2
            w = list(waits)
            if ag_free[bb] is not None:
                w.append(ag_free[bb])

            def fn(e, tc=tc, bb=bb):
                pid = e.partition_id()
                src = a_gath.ap()[:, bass.ts(pid, TOK)][:, tc * 512:(tc + 1) * 512]
                src = src.rearrange("(c p) t -> p c t", p=128)
                return e.dma_start(out=a16[:, o_ag[bb]:o_ag[bb] + NDC * 512].rearrange(
                    "p (c t) -> p c t", c=NDC), in_=src)
            ag_dma[tc] = s.dma("sync", fn, waits=tuple(w))

        ag_load(0, TOK, (("cc2", 1),) + wall(bc4, ("tS",)))
        ag_load(1, TOK, ())

        for i, (tc, d2c) in enumerate(pairs):
            agb = tc % 2
            if True:
                wbk = i % 2
                wfirst = [("dG", wo_dma[i]), ("dS", ag_dma[tc])] + list(bwait(wbk))
                for ic in range(NDC):
                    s.op("tensor", lambda e, wbk=wbk, ic=ic, agb=agb: e.matmul(
                        pf[wbk], a16[:, o_wo[wbk] + ic * 128:o_wo[wbk] + (ic + 1) * 128],
                        a16[:, o_ag[agb] + ic * 512:o_ag[agb] + (ic + 1) * 512],
                        start=(ic == 0), stop=(ic == NDC - 1), skip_group_check=True),
                        waits=tuple(wfirst) if ic == 0 else (),
                        incs=(("tS", 1),) if ic == NDC - 1 else ())
                    wfirst = []
                ctw = s.val("tS")
                wo_used[i] = ("tS", ctw)
                if i + 2 < len(pairs):
                    wo_load(i + 2)
                wsb = [("tS", ctw)]
                if ssb_free[d2c % 2] is not None:
                    wsb.append(ssb_free[d2c % 2])
                csb = s.op("vector", lambda e, wbk=wbk, d2c=d2c, tc=tc: e.scalar_tensor_tensor(
                    a32[:, o_ssb[d2c % 2]:o_ssb[d2c % 2] + 512], pf[wbk], bot[:, d2c:d2c + 1],
                    fa[:, d2c * TOK + tc * 512: d2c * TOK + tc * 512 + 512], ALU.add, ALU.add),
                    waits=tuple(wsb), incs=(("vS", 1),))
                bank_free[wbk] = ("vS", csb)
                lastc = None
                for sub in range(4):
                    tb = 2 + (sub % 2)
                    ctt = s.op("tensor", lambda e, tb=tb, d2c=d2c, sub=sub: e.transpose(
                        pf[tb][:, 0:128], a32[:, o_ssb[d2c % 2] + sub * 128:
                                              o_ssb[d2c % 2] + (sub + 1) * 128], identf[:, :]),
                        waits=bwait(tb, (("vS", csb),)), incs=(("tS", 1),))
                    wcp = [("tS", ctt)]
                    if sub == 0 and st4_free[d2c % 2] is not None:
                        wcp.append(st4_free[d2c % 2])
                    lastc = s.op("vector", lambda e, tb=tb, d2c=d2c, sub=sub: e.tensor_copy(
                        a32[:, o_st4[d2c % 2] + sub * 128:o_st4[d2c % 2] + (sub + 1) * 128],
                        pf[tb][:, 0:128]), waits=tuple(wcp), incs=(("vS", 1),))
                    bank_free[tb] = ("vS", lastc)
                ssb_free[d2c % 2] = ("vS", lastc)
                dst = out_own[tc * 512:(tc + 1) * 512, d2c * 128:(d2c + 1) * 128].rearrange(
                    "(c p) d -> p c d", p=128)
                gp_dma(dst, a32[:, o_st4[d2c % 2]:o_st4[d2c % 2] + 512].rearrange(
                    "p (c d) -> p c d", c=4),
                    waits=(("vS", lastc), ("dS", c_xcopy)), accum=True)
                st4_free[d2c % 2] = ("dG", s.val("dG"))
            ag_free[agb] = ("tS", s.val("tS"))


        if taps:
            bfin = bar()
            sync_dma(tap_AT, AT[:, :], waits=wall(bfin))
            sync_dma(tap_fa, fa[:, :])
            sync_dma(tap_comb, comb_rows[:, :])
            sync_dma(tap_mhat, mhat[:, :])
            sync_dma(tap_kqm, kqm[0:1, 0:8])
            sync_dma(tap_q, q_dram.ap())
            sync_dma(tap_k, k_dram.ap())
            sync_dma(tap_v, v_dram.ap())

        fin = bar()
        s.op("sync", lambda e: None, waits=wall(fin) + (("cc1", 1), ("cc2", 1)))

        with nc.Block() as block:

            @block.sync
            def _(eng):
                s.emit("sync", eng, sems)

            @block.gpsimd
            def _(eng):
                s.emit("gpsimd", eng, sems)

            @block.tensor
            def _(eng):
                s.emit("tensor", eng, sems)

            @block.vector
            def _(eng):
                s.emit("vector", eng, sems)

            @block.scalar
            def _(eng):
                s.emit("scalar", eng, sems)

    return nc


_CACHE = {}


def _host_consts():
    inv = (1.0 / 10000.0 ** (np.arange(0, DH, 2, dtype=np.float32) / DH)).astype(np.float32)
    seq = np.arange(N_TOK, dtype=np.float32)
    fr = seq[:, None] * inv[None, :]
    pos = np.concatenate([fr, fr], axis=1).astype(np.float32)
    cos = np.cos(pos).T.copy()
    sinf = np.sin(pos)
    sign = np.concatenate([-np.ones(64, np.float32), np.ones(64, np.float32)])
    sin = (sinf * sign[None, :]).T.copy()
    mask = np.where(np.arange(1024)[None, :] >= np.arange(128)[:, None] + 512,
                    np.float32(0.0), np.float32(NEG)).astype(np.float32)
    onehot = np.zeros((8, E * 128), np.float32)
    for e in range(E):
        onehot[e, e * 128:(e + 1) * 128] = 1.0
    fconst = np.zeros((128, 4), np.float32)
    fconst[:, 0] = 1e-5
    fconst[:, 1] = math.log(1.02 * ATT_SCALE)
    return {
        "cos_t": cos, "sin_t": sin, "mask_d": mask,
        "identbf_d": np.eye(128, dtype=ml_dtypes.bfloat16),
        "identf_d": np.eye(128, dtype=np.float32),
        "onesrow_d": np.ones((1, 128), np.float32),
        "onescol_d": np.ones((128, 1), ml_dtypes.bfloat16),
        "onescolf_d": np.ones((128, 1), np.float32),
        "onehot_d": onehot, "fconst_d": fconst,
    }


def _pack(w, rows_chunks, cols_chunks):
    R, C = rows_chunks, cols_chunks
    return np.ascontiguousarray(
        w.reshape(R, 128, C, 128).transpose(2, 1, 0, 3).reshape(C, 128, R * 128))


def kernel(L0_attn_x, L0_attn_Wq, L0_attn_bq, L0_attn_Wk, L0_attn_bk, L0_attn_Wv,
           L0_attn_bv, L0_attn_Wo, L0_attn_bo, L0_ffn_gate_w, L0_ffn_w_gate,
           L0_ffn_w_up, L0_ffn_w_down, L0_ffn_sw_gate, L0_ffn_sw_up, L0_ffn_sw_down):
    key = ("prog", TAPS)
    if key not in _CACHE:
        _CACHE[key] = build_program(taps=TAPS)
    nc = _CACHE[key]

    x = np.asarray(L0_attn_x, dtype=np.float32).reshape(N_TOK, D)
    f32 = lambda a: np.asarray(a, dtype=np.float32)
    Wq, Wk, Wv, Wo = f32(L0_attn_Wq), f32(L0_attn_Wk), f32(L0_attn_Wv), f32(L0_attn_Wo)
    bq, bk, bv, bo = f32(L0_attn_bq), f32(L0_attn_bk), f32(L0_attn_bv), f32(L0_attn_bo)
    gate_w = f32(L0_ffn_gate_w)
    w_gate, w_up, w_down = f32(L0_ffn_w_gate), f32(L0_ffn_w_up), f32(L0_ffn_w_down)
    sw_gate, sw_up, sw_down = f32(L0_ffn_sw_gate), f32(L0_ffn_sw_up), f32(L0_ffn_sw_down)

    consts = _host_consts()
    gate_w_a = np.ascontiguousarray(
        gate_w.reshape(NDC, 128, E).transpose(1, 0, 2).reshape(128, NDC * E))
    wg_a = np.ascontiguousarray(
        w_gate.reshape(E, NDC, 128, NHC, 128).transpose(0, 3, 2, 1, 4).reshape(E, NHC, 128, NDC * 128))
    wu_a = np.ascontiguousarray(
        w_up.reshape(E, NDC, 128, NHC, 128).transpose(0, 3, 2, 1, 4).reshape(E, NHC, 128, NDC * 128))
    wd_a = np.ascontiguousarray(
        w_down.reshape(E, NHC, 128, NDC, 128).transpose(0, 3, 2, 1, 4).reshape(E, NDC, 128, NHC * 128))
    swg_a = _pack(sw_gate, NDC, NSC)
    swu_a = _pack(sw_up, NDC, NSC)
    swd_a = _pack(sw_down, NSC, NDC)
    wo_a = _pack(Wo, NDC, NDC)
    bo_t = np.ascontiguousarray(bo.reshape(NDC, 128).T)

    shared = dict(consts)
    shared.update({
        "gate_w_a": gate_w_a, "wg_a": wg_a, "wu_a": wu_a, "wd_a": wd_a,
        "swg_a": swg_a, "swu_a": swu_a, "swd_a": swd_a, "wo_a": wo_a, "bo_t": bo_t,
    })

    in_maps = []
    for c in range(NCORES):
        m = dict(shared)
        m["x_own"] = np.ascontiguousarray(x[c * TOK:(c + 1) * TOK, :])
        hsl = slice(c * 256, (c + 1) * 256)
        for nm, W in (("wq_own", Wq), ("wk_own", Wk), ("wv_own", Wv)):
            m[nm] = np.ascontiguousarray(
                W[:, hsl].reshape(NDC, 128, 256).transpose(1, 0, 2).reshape(128, NDC * 256))
        barr = np.zeros((128, 12), np.float32)
        for i, b in enumerate((bq, bk, bv)):
            bl = b[hsl].reshape(2, 128)
            barr[:, 2 * i + 0] = bl[0]
            barr[:, 2 * i + 1] = bl[1]
        barr[:, 6:12] = np.concatenate([barr[64:128, 0:6], barr[0:64, 0:6]], axis=0)
        m["b_qkv"] = barr
        in_maps.append(m)

    res = run_bass_kernel_spmd(nc, in_maps, core_ids=list(range(NCORES)))
    out = np.concatenate([res.results[c]["out_own"] for c in range(NCORES)], axis=0)
    if TAPS:
        kernel.last_results = res
    return out.reshape(1, N_TOK, D)


# version 07
# speedup vs baseline: 1.2916x; 1.2467x over previous
import math
import contextlib
import numpy as np
import ml_dtypes
import concourse.bass as bass
import concourse.mybir as mybir
from concourse.bass_utils import run_bass_kernel_spmd

F32 = mybir.dt.float32
BF16 = mybir.dt.bfloat16
AF = mybir.ActivationFunctionType
ALU = mybir.AluOpType
AX = mybir.AxisListType

N_TOK = 8192
D = 2048
DH = 128
NCORES = 8
TOK = N_TOK // NCORES
HL = 2
E = 8
NDC = D // 128
NHC = 11
NSC = 22
ATT_SCALE = DH ** -0.5
NEG = -1e30

ENGINES = ("sync", "gpsimd", "tensor", "vector", "scalar")
TAPS = False


DMA_SLOTS = 8


class Sched:
    def __init__(self):
        self.ops = {e: [] for e in ENGINES}
        self.cnt = {}
        self.dma_idx = {"sync": 0, "gpsimd": 0}
        self.dma_hist = {"dS": [], "dG": []}

    def dma(self, eng, fn, waits=()):
        vname = {"sync": "dS", "gpsimd": "dG"}[eng]
        j = self.dma_idx[eng]
        slot = j % DMA_SLOTS
        g = j // DMA_SLOTS + 1
        sname = f"{vname}{slot}"
        w = list(waits)
        if g > 1:
            w.append((sname, 16 * (g - 1)))
        self.dma_idx[eng] = j + 1
        self.dma_hist[vname].append(sname)
        self.cnt[vname] = self.cnt.get(vname, 0) + 16
        self.op(eng, fn, waits=tuple(w), incs=((sname, 16),))
        return self.cnt[vname]

    def _xlate(self, ss, v):
        if ss not in ("dS", "dG"):
            return ((ss, v),)
        k = v // 16
        hist = self.dma_hist[ss][:k]
        per = {}
        for sn in hist:
            per[sn] = per.get(sn, 0) + 16
        return tuple(per.items())

    def val(self, sem):
        return self.cnt.get(sem, 0)

    SELF = {"vector": "vS", "scalar": "sS"}

    def op(self, eng, fn, waits=(), incs=()):
        for ss, v in waits:
            if not (ss[0] == "d" and len(ss) > 2):
                assert v <= self.val(ss), f"wait on future: {ss} {v} > {self.val(ss)}"
        sse = self.SELF.get(eng)
        n_orig = len(tuple(incs))
        if sse is not None and not any(ss == sse for ss, _ in incs):
            incs = tuple(incs) + ((sse, 1),)
        self.ops[eng].append((tuple(waits), fn, tuple(incs)))
        out = []
        for i, (ss, d) in enumerate(incs):
            self.cnt[ss] = self.cnt.get(ss, 0) + d
            if i < n_orig or n_orig == 0:
                out.append(self.cnt[ss])
        return out[0] if len(out) == 1 else out

    def emit(self, eng_name, eng, sems):
        last = {}
        sse = self.SELF.get(eng_name)
        self_c = 0
        for waits, fn, incs in self.ops[eng_name]:
            if sse is not None and self_c > 0:
                eng.wait_ge(sems[sse], self_c)
                last[sse] = max(last.get(sse, 0), self_c)
            for ss0, v0 in waits:
                for ss, v in self._xlate(ss0, v0):
                    if last.get(ss, 0) < v:
                        eng.wait_ge(sems[ss], v)
                        last[ss] = v
            inst = fn(eng)
            for ss, d in incs:
                if ss.startswith("cc"):
                    inst.then_inc(sems[ss])
                else:
                    inst.then_inc(sems[ss], d)
                if ss == sse:
                    self_c += d


def build_program(taps=False):
    nc = bass.Bass("TRN2", target_bir_lowering=False)
    dt = nc.dram_tensor

    x_own = dt("x_own", [TOK, D], F32, kind="ExternalInput").ap()
    wq_own = dt("wq_own", [128, NDC * 256], F32, kind="ExternalInput").ap()
    wk_own = dt("wk_own", [128, NDC * 256], F32, kind="ExternalInput").ap()
    wv_own = dt("wv_own", [128, NDC * 256], F32, kind="ExternalInput").ap()
    b_qkv = dt("b_qkv", [128, 12], F32, kind="ExternalInput").ap()
    gate_w_a = dt("gate_w_a", [128, NDC * E], F32, kind="ExternalInput").ap()
    wg_a = dt("wg_a", [E, NHC, 128, NDC * 128], F32, kind="ExternalInput").ap()
    wu_a = dt("wu_a", [E, NHC, 128, NDC * 128], F32, kind="ExternalInput").ap()
    wd_a = dt("wd_a", [E, NDC, 128, NHC * 128], F32, kind="ExternalInput").ap()
    swg_a = dt("swg_a", [NSC, 128, NDC * 128], F32, kind="ExternalInput").ap()
    swu_a = dt("swu_a", [NSC, 128, NDC * 128], F32, kind="ExternalInput").ap()
    swd_a = dt("swd_a", [NDC, 128, NSC * 128], F32, kind="ExternalInput").ap()
    wo_a = dt("wo_a", [NDC, 128, NDC * 128], F32, kind="ExternalInput").ap()
    bo_t = dt("bo_t", [128, NDC], F32, kind="ExternalInput").ap()
    cos_t = dt("cos_t", [128, N_TOK], F32, kind="ExternalInput").ap()
    sin_t = dt("sin_t", [128, N_TOK], F32, kind="ExternalInput").ap()
    mask_d = dt("mask_d", [128, 1024], F32, kind="ExternalInput").ap()
    identbf_d = dt("identbf_d", [128, 128], BF16, kind="ExternalInput").ap()
    identf_d = dt("identf_d", [128, 128], F32, kind="ExternalInput").ap()
    onesrow_d = dt("onesrow_d", [1, 128], F32, kind="ExternalInput").ap()
    onescol_d = dt("onescol_d", [128, 1], BF16, kind="ExternalInput").ap()
    onescolf_d = dt("onescolf_d", [128, 1], F32, kind="ExternalInput").ap()
    onehot_d = dt("onehot_d", [8, E * 128], F32, kind="ExternalInput").ap()
    fconst_d = dt("fconst_d", [128, 4], F32, kind="ExternalInput").ap()
    out_own = dt("out_own", [TOK, D], F32, kind="ExternalOutput").ap()

    xn_bounce = dt("xn_bounce", [D, TOK], BF16)
    xn_gath = dt("xn_gath", [NCORES * D, TOK], BF16)
    a_bounce = [dt(f"a_bounce{h}", [128, N_TOK], BF16) for h in range(HL)]
    a_gath = [dt(f"a_gath{h}", [NCORES * 128, N_TOK], BF16) for h in range(HL)]
    q_dram = dt("q_dram", [HL, 128, N_TOK], BF16)
    k_dram = dt("k_dram", [HL, 128, N_TOK], BF16)
    v_dram = dt("v_dram", [HL, N_TOK, 128], BF16)
    if taps:
        tap_xnT = dt("tap_xnT", [128, NDC * TOK], BF16, kind="ExternalOutput").ap()
        tap_AT = dt("tap_AT", [128, HL * N_TOK], BF16, kind="ExternalOutput").ap()
        tap_fa = dt("tap_fa", [128, NDC * TOK], BF16, kind="ExternalOutput").ap()
        tap_comb = dt("tap_comb", [8, TOK], F32, kind="ExternalOutput").ap()
        tap_mhat = dt("tap_mhat", [128, HL], F32, kind="ExternalOutput").ap()
        tap_kqm = dt("tap_kqm", [1, 8], F32, kind="ExternalOutput").ap()
        tap_st8 = dt("tap_st8", [128, 40], F32, kind="ExternalOutput").ap()
        tap_q = dt("tap_q", [HL, 128, N_TOK], BF16, kind="ExternalOutput").ap()
        tap_k = dt("tap_k", [HL, 128, N_TOK], BF16, kind="ExternalOutput").ap()
        tap_v = dt("tap_v", [HL, N_TOK, 128], BF16, kind="ExternalOutput").ap()

    s = Sched()

    with contextlib.ExitStack() as es:
        sb = lambda name, shape, dtp: es.enter_context(nc.sbuf_tensor(name, shape, dtp))
        psmk = lambda name, shape, dtp: es.enter_context(nc.psum_tensor(name, shape, dtp))

        xnT = sb("xnT", [128, NDC * TOK], BF16)
        AT = sb("AT", [128, HL * N_TOK], BF16)
        fa = sb("fa", [128, NDC * TOK], BF16)
        comb_rows = sb("comb_rows", [8, TOK], F32)
        identbf = sb("identbf", [128, 128], BF16)
        identf = sb("identf", [128, 128], F32)
        onesrow = sb("onesrow", [1, 128], F32)
        onescol = sb("onescol", [128, 1], BF16)
        onescolf = sb("onescolf", [128, 1], F32)
        bqkv = sb("bqkv", [128, 12], F32)
        bot = sb("bot", [128, NDC], F32)
        gw = sb("gw", [128, NDC * E], BF16)
        mhat = sb("mhat", [128, HL], F32)
        fco = sb("fco", [128, 4], F32)
        st8 = sb("st8", [128, 40], F32)
        kqm = sb("kqm", [1, 8], F32)
        a16 = sb("a16", [128, 29184], BF16)
        a32 = sb("a32", [128, 4608], F32)

        pfh = [psmk(f"pf{i}", [128, 512], F32) for i in range(6)]
        pf = [h[:, :] for h in pfh]
        pt = [psmk(f"pt{i}", [128, 1024], BF16) for i in range(2)]
        bank_free = [None] * 6
        pt_free = [None] * 2

        sems = {}
        for nm in (["cc1", "cc2", "cc3", "tS", "vS", "sS"]
                   + [f"dS{i}" for i in range(DMA_SLOTS)]
                   + [f"dG{i}" for i in range(DMA_SLOTS)]):
            sems[nm] = es.enter_context(nc.semaphore(nm))

        def sync_dma(dst, src, waits=()):
            return s.dma("sync", lambda e, d=dst, sr=src: e.dma_start(out=d, in_=sr),
                         waits=waits)

        def gp_dma(dst, src, waits=(), accum=False):
            if accum:
                return s.dma("gpsimd",
                             lambda e, d=dst, sr=src: e.dma_start(out=d, in_=sr, accum_op=ALU.add),
                             waits=waits)
            return s.dma("gpsimd", lambda e, d=dst, sr=src: e.dma_start(out=d, in_=sr),
                         waits=waits)

        def bar():
            return {nm: s.val(nm) for nm in ("dS", "dG", "tS", "vS", "sS")}

        def wall(bc, only=None):
            return tuple((nm, v) for nm, v in bc.items()
                         if v > 0 and (only is None or nm in only))

        def bwait(i, extra=()):
            w = list(extra)
            if bank_free[i] is not None:
                w.append(bank_free[i])
            return tuple(w)

        sync_dma(identbf[:, :], identbf_d)
        sync_dma(identf[:, :], identf_d)
        sync_dma(onesrow[:, :], onesrow_d)
        sync_dma(onescol[:, :], onescol_d)
        sync_dma(onescolf[:, :], onescolf_d)
        sync_dma(bqkv[:, :], b_qkv)
        sync_dma(bot[:, :], bo_t)
        sync_dma(fco[:, :], fconst_d)
        c_consts = s.val("dS")
        gp_dma(gw[:, :], gate_w_a)
        c_gw = s.val("dG")

        NTT = TOK // 128
        o_xt = [0, 2048]
        o_xn = [24576, 26624]
        x_free = [None, None]
        xn_free = [None, None]
        ln_dma = []
        for tt in range(NTT):
            b = tt % 2
            w = [("dS", c_consts)]
            if x_free[b] is not None:
                w.append(x_free[b])
            sync_dma(a32[:, o_xt[b]:o_xt[b] + 2048], x_own[tt * 128:(tt + 1) * 128, :], waits=tuple(w))
            ln_dma.append(s.val("dS"))
            if tt == 1:
                break
        for tt in range(NTT):
            b = tt % 2
            if tt >= 2:
                w2 = [("dS", c_consts), x_free[b]]
                sync_dma(a32[:, o_xt[b]:o_xt[b] + 2048], x_own[tt * 128:(tt + 1) * 128, :],
                         waits=tuple(w2))
                ln_dma.append(s.val("dS"))
            xt = a32[:, o_xt[b]:o_xt[b] + 2048]
            xn = a16[:, o_xn[b]:o_xn[b] + 2048]
            w = [("dS", ln_dma[tt])]
            if xn_free[b] is not None:
                w.append(xn_free[b])
            B0 = (tt % 2) * 16
            stc = lambda i: st8[:, B0 + i:B0 + i + 1]
            s.op("vector", lambda e, xt=xt, stc=stc: e.reduce_sum(stc(0), xt, axis=AX.X),
                 waits=tuple(w))
            xsq = a16[:, o_xn[b] - 24576 + 20480:o_xn[b] - 24576 + 20480 + 2048]
            cssq = s.op("scalar", lambda e, xt=xt, xsq=xsq: e.activation(
                xsq, xt, AF.Square), waits=tuple(w), incs=(("sS", 1),))
            s.op("vector", lambda e, xsq=xsq, stc=stc: e.reduce_sum(stc(1), xsq, axis=AX.X),
                 waits=(("sS", cssq),))
            s.op("vector", lambda e, stc=stc: e.tensor_scalar_mul(stc(2), stc(0), 1.0 / D))
            s.op("vector", lambda e, stc=stc: e.tensor_mul(stc(3), stc(2), stc(2)))
            cv = s.op("vector", lambda e, stc=stc: e.scalar_tensor_tensor(
                stc(4), stc(1), 1.0 / D, stc(3), ALU.mult, ALU.subtract),
                incs=(("vS", 1),))
            s.op("scalar", lambda e, stc=stc: e.activation(stc(5), stc(4), AF.Ln, bias=fco[:, 0:1]),
                 waits=(("vS", cv),))
            cs = s.op("scalar", lambda e, stc=stc: e.activation(stc(6), stc(5), AF.Exp, scale=-0.5),
                      incs=(("sS", 1),))
            cv2 = s.op("vector", lambda e, stc=stc: e.tensor_mul(stc(8), stc(2), stc(6)),
                       waits=(("sS", cs),))
            cv2 = s.op("vector", lambda e, stc=stc: e.tensor_scalar_mul(stc(9), stc(8), -1.0),
                       incs=(("vS", 1),))
            cs2 = s.op("scalar", lambda e, xt=xt, xn=xn, stc=stc: e.activation(
                xn, xt, AF.Identity, bias=stc(9), scale=stc(6)),
                waits=(("vS", cv2),), incs=(("sS", 1),))
            x_free[b] = ("sS", cs2)
            lastv = None
            for dc in range(NDC):
                pb = pt[dc % 2]
                wt = [("sS", cs2)]
                if pt_free[dc % 2] is not None:
                    wt.append(pt_free[dc % 2])
                ct = s.op("tensor", lambda e, pb=pb, xn=xn, dc=dc: e.transpose(
                    pb[:, 0:128], xn[:, dc * 128:(dc + 1) * 128], identbf[:, :]),
                    waits=tuple(wt), incs=(("tS", 1),))
                lastv = s.op("vector", lambda e, pb=pb, dc=dc, tt=tt: e.tensor_copy(
                    xnT[:, dc * TOK + tt * 128: dc * TOK + (tt + 1) * 128], pb[:, 0:128]),
                    waits=(("tS", ct),), incs=(("vS", 1),))
                pt_free[dc % 2] = ("vS", lastv)
            xn_free[b] = ("vS", lastv)
        c_ln_v = s.val("vS")
        sync_dma(xn_bounce.ap().rearrange("(c p) t -> p c t", p=128),
                 xnT[:, :].rearrange("p (c t) -> p c t", c=NDC), waits=(("vS", c_ln_v),))
        c_xnb = s.val("dS")
        if taps:
            sync_dma(tap_xnT, xnT[:, :])
            sync_dma(tap_st8, st8[:, :])
        s.op("gpsimd", lambda e: e.collective_compute(
            "AllGather", ALU.bypass, replica_groups=[list(range(NCORES))],
            ins=[xn_bounce.ap().opt()], outs=[xn_gath.ap().opt()]),
            waits=(("dS", c_xnb),), incs=(("cc1", 1),))
        sync_dma(out_own, x_own)
        c_xcopy = s.val("dS")

        bc0 = bar()
        for tt in range(NTT):
            bi = tt % 2
            wfirst = bwait(bi, wall(bc0) + (("dG", c_gw),))
            for dc in range(NDC):
                sp = (dc == NDC - 1)
                inc = (("tS", 1),) if sp else ()
                s.op("tensor", lambda e, bi=bi, dc=dc, tt=tt, sp=sp: e.matmul(
                    pf[bi][:, 0:8], xnT[:, dc * TOK + tt * 128: dc * TOK + (tt + 1) * 128],
                    gw[:, dc * 8:(dc + 1) * 8], start=(dc == 0), stop=sp,
                    skip_group_check=True),
                    waits=wfirst if dc == 0 else (), incs=inc)
                if dc == 0:
                    wfirst = ()
            ct = s.val("tS")
            B0 = (tt % 2) * 16
            stc = lambda i: st8[:, B0 + i:B0 + i + 1]
            ga = (tt % 2) * 128
            s.op("vector", lambda e, bi=bi, stc=stc: e.reduce_max(stc(0), pf[bi][:, 0:8], axis=AX.X),
                 waits=(("tS", ct),))
            cm = s.op("vector", lambda e, stc=stc: e.tensor_scalar_mul(stc(1), stc(0), -1.0),
                      incs=(("vS", 1),))
            ce = s.op("scalar", lambda e, bi=bi, stc=stc, ga=ga: e.activation(
                a32[:, ga + 0:ga + 8], pf[bi][:, 0:8], AF.Exp, bias=stc(1), accum_out=stc(2)),
                waits=(("vS", cm),), incs=(("sS", 1),))
            bank_free[bi] = ("sS", ce)
            cr = s.op("vector", lambda e, stc=stc: e.reciprocal(stc(3), stc(2)),
                      waits=(("sS", ce),), incs=(("vS", 1),))
            cn = s.op("scalar", lambda e, stc=stc, ga=ga: e.activation(
                a32[:, ga + 32:ga + 40], a32[:, ga + 0:ga + 8], AF.Identity, scale=stc(3)),
                waits=(("vS", cr),), incs=(("sS", 1),))
            scs = a32[:, ga + 32:ga + 40]
            s.op("vector", lambda e, scs=scs, stc=stc: e.reduce_max(stc(4), scs, axis=AX.X),
                 waits=(("sS", cn),))
            s.op("vector", lambda e, scs=scs, stc=stc, ga=ga: e.scalar_tensor_tensor(
                a32[:, ga + 64:ga + 72], scs, stc(4), scs, ALU.is_lt, ALU.mult))
            s.op("vector", lambda e, stc=stc, ga=ga: e.reduce_max(stc(5), a32[:, ga + 64:ga + 72], axis=AX.X))
            cv = s.op("vector", lambda e, scs=scs, stc=stc, ga=ga: e.scalar_tensor_tensor(
                a32[:, ga + 96:ga + 104], scs, stc(5), scs, ALU.is_ge, ALU.mult),
                incs=(("vS", 1),))
            ctr = s.op("tensor", lambda e, ga=ga: e.transpose(
                pf[2][0:8, 0:128], a32[:, ga + 96:ga + 104], identf[:, :]),
                waits=bwait(2, (("vS", cv),)), incs=(("tS", 1),))
            cvc = s.op("vector", lambda e, tt=tt: e.tensor_copy(
                comb_rows[0:8, tt * 128:(tt + 1) * 128], pf[2][0:8, 0:128]),
                waits=(("tS", ctr),), incs=(("vS", 1),))
            bank_free[2] = ("vS", cvc)
        c_comb_v = s.val("vS")

        o_wgb = [0, 2048]
        o_wub = [4096, 6144]
        o_wdb = [8192, 9600]
        o_G = 11008
        o_sil = [22272, 22784]
        o_t1 = [23296, 23808]
        o_cB = [0, 512]
        o_oh = 1024
        bc1 = bar()
        sync_dma(a32[0:8, o_oh:o_oh + E * 128], onehot_d, waits=wall(bc1, ("vS", "sS")))
        c_oh = s.val("dS")

        wgu_free = [None, None]
        wd_free = [None, None]
        sil_free = [None, None]
        moe_dma = {}

        def w_chunk(src, off, ln, waits):
            gp_dma(a16[:, off:off + ln], src, waits=waits)
            return s.val("dG")

        for (ei, hci, b) in ((0, 0, 0), (0, 1, 1)):
            moe_dma[("g", ei, hci)] = w_chunk(wg_a[ei, hci], o_wgb[b], NDC * 128, ())
            moe_dma[("u", ei, hci)] = w_chunk(wu_a[ei, hci], o_wub[b], NDC * 128, ())

        first_moe = wall(bc1) + (("vS", c_comb_v), ("dS", c_oh))

        def gateup_phase(nchunks, dma_of, prefetch, use_comb=True, first=()):
            lastG = None
            tS0 = ("tS", s.val("tS"))
            for hc in range(nchunks):
                bks = [(4 * hc + i) % 6 for i in range(4)]
                wdma = dma_of(hc)
                for tc in range(2):
                    gb, ub = bks[2 * tc], bks[2 * tc + 1]
                    wfirst = list(wdma) + list(bwait(gb))
                    if hc == 0 and tc == 0:
                        wfirst += list(first)
                    for dc in range(NDC):
                        s.op("tensor", lambda e, gb=gb, dc=dc, tc=tc, hc=hc: e.matmul(
                            pf[gb], a16[:, o_wgb[hc % 2] + dc * 128:o_wgb[hc % 2] + (dc + 1) * 128],
                            xnT[:, dc * TOK + tc * 512: dc * TOK + tc * 512 + 512],
                            start=(dc == 0), stop=(dc == NDC - 1), skip_group_check=True),
                            waits=tuple(wfirst) if dc == 0 else (),
                            incs=(("tS", 1),) if dc == NDC - 1 else ())
                        wfirst = []
                    ctg = s.val("tS")
                    wfirst = list(bwait(ub))
                    for dc in range(NDC):
                        s.op("tensor", lambda e, ub=ub, dc=dc, tc=tc, hc=hc: e.matmul(
                            pf[ub], a16[:, o_wub[hc % 2] + dc * 128:o_wub[hc % 2] + (dc + 1) * 128],
                            xnT[:, dc * TOK + tc * 512: dc * TOK + tc * 512 + 512],
                            start=(dc == 0), stop=(dc == NDC - 1), skip_group_check=True),
                            waits=tuple(wfirst) if dc == 0 else (),
                            incs=(("tS", 1),) if dc == NDC - 1 else ())
                        wfirst = []
                    ctu = s.val("tS")
                    sb_, tb_ = o_sil[tc], o_t1[tc]
                    wsil = [("tS", ctg)]
                    if sil_free[tc] is not None:
                        wsil.append(sil_free[tc])
                    cs = s.op("scalar", lambda e, gb=gb, sb_=sb_: e.activation(
                        a16[:, sb_:sb_ + 512], pf[gb], AF.Silu),
                        waits=tuple(wsil), incs=(("sS", 1),))
                    bank_free[gb] = ("sS", cs)
                    if use_comb:
                        cv1 = s.op("vector", lambda e, ub=ub, tb_=tb_, tc=tc: e.tensor_tensor(
                            a16[:, tb_:tb_ + 512], pf[ub], a32[:, o_cB[tc]:o_cB[tc] + 512],
                            ALU.mult), waits=(("tS", ctu),), incs=(("vS", 1),))
                        bank_free[ub] = ("vS", cv1)
                        lastG = s.op("vector", lambda e, sb_=sb_, tb_=tb_, hc=hc, tc=tc: e.tensor_mul(
                            a16[:, o_G + hc * 1024 + tc * 512:o_G + hc * 1024 + tc * 512 + 512],
                            a16[:, tb_:tb_ + 512], a16[:, sb_:sb_ + 512]),
                            waits=(("sS", cs), tS0), incs=(("vS", 1),))
                        sil_free[tc] = ("vS", lastG)
                    else:
                        lastG = s.op("vector", lambda e, ub=ub, sb_=sb_, hc=hc, tc=tc: e.tensor_tensor(
                            a16[:, o_G + hc * 1024 + tc * 512:o_G + hc * 1024 + tc * 512 + 512],
                            pf[ub], a16[:, sb_:sb_ + 512], ALU.mult),
                            waits=(("tS", ctu), ("sS", cs), tS0), incs=(("vS", 1),))
                        bank_free[ub] = ("vS", lastG)
                        sil_free[tc] = ("vS", lastG)
                wgu_free[hc % 2] = ("tS", ctu)
                prefetch(hc, wgu_free[hc % 2])
            return lastG

        def down_phase(nchunks, getwd, wd_len, first_copy=False):
            cG = s.val("vS")
            dma_d = {}
            for d2c in range(2):
                w = [wd_free[d2c % 2]] if wd_free[d2c % 2] is not None else []
                dma_d[d2c] = getwd(d2c, d2c % 2, tuple(w))
            for d2c in range(NDC):
                bwf = o_wdb[d2c % 2]
                ctd_last = None
                for tc in range(2):
                    dbk = 4 + tc
                    w = [("dG", dma_d[d2c]), ("vS", cG)] + list(bwait(dbk))
                    for hc in range(nchunks):
                        s.op("tensor", lambda e, dbk=dbk, bwf=bwf, hc=hc, tc=tc: e.matmul(
                            pf[dbk], a16[:, bwf + hc * 128:bwf + (hc + 1) * 128],
                            a16[:, o_G + hc * 1024 + tc * 512:o_G + hc * 1024 + tc * 512 + 512],
                            start=(hc == 0), stop=(hc == nchunks - 1), skip_group_check=True),
                            waits=tuple(w) if hc == 0 else (),
                            incs=(("tS", 1),) if hc == nchunks - 1 else ())
                        w = []
                    ctd = s.val("tS")
                    ctd_last = ctd
                    if first_copy:
                        cv = s.op("vector", lambda e, dbk=dbk, d2c=d2c, tc=tc: e.tensor_copy(
                            fa[:, d2c * TOK + tc * 512: d2c * TOK + tc * 512 + 512], pf[dbk]),
                            waits=(("tS", ctd),), incs=(("vS", 1),))
                    else:
                        cv = s.op("vector", lambda e, dbk=dbk, d2c=d2c, tc=tc: e.tensor_add(
                            fa[:, d2c * TOK + tc * 512: d2c * TOK + tc * 512 + 512],
                            pf[dbk], fa[:, d2c * TOK + tc * 512: d2c * TOK + tc * 512 + 512]),
                            waits=(("tS", ctd),), incs=(("vS", 1),))
                    bank_free[dbk] = ("vS", cv)
                wd_free[d2c % 2] = ("tS", ctd_last)
                if d2c + 2 < NDC:
                    dma_d[d2c + 2] = getwd(d2c + 2, d2c % 2, (wd_free[d2c % 2],))

        for e_i in range(E):
            for tc in range(2):
                w = list(bwait(4))
                if e_i == 0 and tc == 0:
                    w += list(first_moe)
                ct = s.op("tensor", lambda e, e_i=e_i, tc=tc: e.matmul(
                    pf[4], a32[0:8, o_oh + e_i * 128:o_oh + (e_i + 1) * 128],
                    comb_rows[0:8, tc * 512:(tc + 1) * 512], start=True, stop=True,
                    skip_group_check=True),
                    waits=tuple(w), incs=(("tS", 1),))
                cv = s.op("vector", lambda e, tc=tc: e.tensor_copy(
                    a32[:, o_cB[tc]:o_cB[tc] + 512], pf[4]),
                    waits=(("tS", ct),), incs=(("vS", 1),))
                bank_free[4] = ("vS", cv)

            def dma_of(hc, e_i=e_i):
                return [("dG", moe_dma[("g", e_i, hc)]), ("dG", moe_dma[("u", e_i, hc)])]

            def prefetch(hc, free_evt, e_i=e_i):
                if hc + 2 < NHC:
                    nh = hc + 2
                    if ("g", e_i, nh) not in moe_dma:
                        moe_dma[("g", e_i, nh)] = w_chunk(wg_a[e_i, nh], o_wgb[nh % 2],
                                                          NDC * 128, (free_evt,))
                        moe_dma[("u", e_i, nh)] = w_chunk(wu_a[e_i, nh], o_wub[nh % 2],
                                                          NDC * 128, ())
                elif hc == NHC - 1 and e_i + 1 < E:
                    for nh in (0, 1):
                        moe_dma[("g", e_i + 1, nh)] = w_chunk(
                            wg_a[e_i + 1, nh], o_wgb[nh % 2], NDC * 128,
                            (wgu_free[nh % 2],))
                        moe_dma[("u", e_i + 1, nh)] = w_chunk(
                            wu_a[e_i + 1, nh], o_wub[nh % 2], NDC * 128, ())

            gateup_phase(NHC, dma_of, prefetch, use_comb=True,
                         first=first_moe if e_i == 0 else ())

            def getwd(d2c, b, w, e_i=e_i):
                return w_chunk(wd_a[e_i, d2c], o_wdb[b], NHC * 128, w)

            down_phase(NHC, getwd, NHC * 128, first_copy=(e_i == 0))

        bc2 = bar()
        o_xg = [0, 4096, 8192]
        o_wq, o_wk, o_wv = 12288, 16384, 20480
        o_qs = [24576, 25088]
        o_ks = [25600, 26112]
        o_vs = [26624, 27136]
        o_vt = [27648, 28160]
        o_cos = [0, 512]
        o_sin = [1024, 1536]
        o_rt1, o_rt2 = 2048, 2560
        gp_dma(a16[:, o_wq:o_wq + NDC * 256], wq_own, waits=wall({"tS": bc2["tS"], "dG": bc2["dG"]}))
        gp_dma(a16[:, o_wk:o_wk + NDC * 256], wk_own)
        gp_dma(a16[:, o_wv:o_wv + NDC * 256], wv_own)
        c_wqkv = s.val("dG")
        cvz = s.op("vector", lambda e: e.memset(kqm[0:1, 0:8], 0.0), waits=wall(bc2, ("sS",)),
                   incs=(("vS", 1),))
        c_kqm0 = s.val("vS")

        NU = 2 * NTT * 2
        xg_dma = {}
        xg_used = {}

        def xg_load(u, waits):
            tc, half = u // 2, u % 2
            r = tc // 2
            l0 = (tc % 2) * 512
            rows0 = r * D + half * 1024
            src = xn_gath.ap()[rows0:rows0 + 1024, l0:l0 + 512].rearrange("(c p) t -> p c t", p=128)
            off = o_xg[u % 3]
            sync_dma(a16[:, off:off + 4096].rearrange("p (c t) -> p c t", c=8), src, waits=waits)
            xg_dma[u] = s.val("dS")

        for u in range(3):
            xg_load(u, wall(bc2, ("vS", "tS")) + (("cc1", 1),))

        bank_map = [("q", 0, 0), ("k", 0, 1), ("v", 0, 2), ("q", 1, 3), ("k", 1, 4), ("v", 1, 5)]
        cos_free = [None, None]
        qs_free = [None, None]
        ks_free = [None, None]
        vs_free = [None, None]
        vt_free = [None, None]
        norm_red = None

        for tc in range(2 * NTT):
            b2 = tc % 2
            w = [("dS", c_consts)]
            if cos_free[b2] is not None:
                w.append(cos_free[b2])
            else:
                w += list(wall(bc2, ("vS", "tS")))
            sync_dma(a32[:, o_cos[b2]:o_cos[b2] + 512], cos_t[:, tc * 512:(tc + 1) * 512],
                     waits=tuple(w))
            sync_dma(a32[:, o_sin[b2]:o_sin[b2] + 512], sin_t[:, tc * 512:(tc + 1) * 512])
            c_cs = s.val("dS")
            for half in range(2):
                u = tc * 2 + half
                whalf = [("dG", c_wqkv), ("dS", xg_dma[u])]
                for wnm, h, bk in bank_map:
                    woff = {"q": o_wq, "k": o_wk, "v": o_wv}[wnm]
                    wfirst = list(whalf)
                    if half == 0:
                        wfirst += list(bwait(bk))
                    for dci in range(8):
                        dc = half * 8 + dci
                        s.op("tensor", lambda e, bk=bk, woff=woff, dc=dc, h=h, u=u: e.matmul(
                            pf[bk], a16[:, woff + dc * 256 + h * 128: woff + dc * 256 + h * 128 + 128],
                            a16[:, o_xg[u % 3] + (dc % 8) * 512: o_xg[u % 3] + (dc % 8) * 512 + 512],
                            start=(dc == 0), stop=(dc == NDC - 1), skip_group_check=True),
                            waits=tuple(wfirst) if dci == 0 else (),
                            incs=(("tS", 1),) if (dci == 7) else ())
                        wfirst = []
                xg_used[u] = s.val("tS")
                if u + 3 < NU:
                    xg_load(u + 3, (("tS", xg_used[u]),))
            ct_all = s.val("tS")
            for wnm, h, bk in bank_map:
                bcol = {"q": 0, "k": 2, "v": 4}[wnm] + h
                if wnm in ("q", "k"):
                    stg = (o_qs if wnm == "q" else o_ks)[b2]
                    sfree = (qs_free if wnm == "q" else ks_free)[b2]
                    wq_ = [("tS", ct_all), ("dS", c_cs)]
                    s.op("vector", lambda e, bk=bk, bcol=bcol, b2=b2: e.scalar_tensor_tensor(
                        a32[:, o_rt1:o_rt1 + 512], pf[bk], bqkv[:, bcol:bcol + 1],
                        a32[:, o_cos[b2]:o_cos[b2] + 512], ALU.add, ALU.mult),
                        waits=tuple(wq_))
                    s.op("vector", lambda e, bk=bk, bcol=bcol, b2=b2: e.scalar_tensor_tensor(
                        a32[0:64, o_rt2:o_rt2 + 512], pf[bk][64:128, :],
                        bqkv[0:64, 6 + bcol:7 + bcol],
                        a32[0:64, o_sin[b2]:o_sin[b2] + 512], ALU.add, ALU.mult))
                    cv = s.op("vector", lambda e, bk=bk, bcol=bcol, b2=b2: e.scalar_tensor_tensor(
                        a32[64:128, o_rt2:o_rt2 + 512], pf[bk][0:64, :],
                        bqkv[64:128, 6 + bcol:7 + bcol],
                        a32[64:128, o_sin[b2]:o_sin[b2] + 512], ALU.add, ALU.mult),
                        incs=(("vS", 1),))
                    cos_free[b2] = ("vS", cv)
                    wadd = [("vS", cv)]
                    if sfree is not None:
                        wadd.append(sfree)
                    cvr = s.op("vector", lambda e, stg=stg: e.tensor_add(
                        a16[:, stg:stg + 512], a32[:, o_rt1:o_rt1 + 512],
                        a32[:, o_rt2:o_rt2 + 512]), waits=tuple(wadd), incs=(("vS", 1),))
                    dst = (q_dram if wnm == "q" else k_dram).ap()[h, :, tc * 512:(tc + 1) * 512]
                    sync_dma(dst, a16[:, stg:stg + 512], waits=(("vS", cvr),))
                    cdma = s.val("dS")
                    if wnm == "q":
                        qs_free[b2] = ("dS", cdma)
                    else:
                        ks_free[b2] = ("dS", cdma)
                    bank_free[bk] = ("vS", cv)
                else:
                    wv_ = [("tS", ct_all)]
                    if vs_free[b2] is not None:
                        wv_.append(vs_free[b2])
                    cse = s.op("scalar", lambda e, bk=bk, bcol=bcol, b2=b2: e.activation(
                        a16[:, o_vs[b2]:o_vs[b2] + 512], pf[bk], AF.Identity,
                        bias=bqkv[:, bcol:bcol + 1]), waits=tuple(wv_), incs=(("sS", 1),))
                    bank_free[bk] = ("sS", cse)
                    lastv = None
                    for sub in range(4):
                        pb = pt[sub % 2]
                        wt = [("sS", cse)]
                        if pt_free[sub % 2] is not None:
                            wt.append(pt_free[sub % 2])
                        ctv = s.op("tensor", lambda e, pb=pb, sub=sub, b2=b2: e.transpose(
                            pb[:, 0:128], a16[:, o_vs[b2] + sub * 128:o_vs[b2] + (sub + 1) * 128],
                            identbf[:, :]), waits=tuple(wt), incs=(("tS", 1),))
                        wcp = [("tS", ctv)]
                        if sub == 0 and vt_free[b2] is not None:
                            wcp.append(vt_free[b2])
                        lastv = s.op("vector", lambda e, pb=pb, sub=sub, b2=b2: e.tensor_copy(
                            a16[:, o_vt[b2] + sub * 128:o_vt[b2] + (sub + 1) * 128], pb[:, 0:128]),
                            waits=tuple(wcp), incs=(("vS", 1),))
                        pt_free[sub % 2] = ("vS", lastv)
                    dst = v_dram.ap()[h, tc * 512:(tc + 1) * 512, :].rearrange(
                        "(c p) d -> p c d", p=128)
                    sync_dma(dst, a16[:, o_vt[b2]:o_vt[b2] + 512].rearrange("p (c d) -> p c d", c=4),
                             waits=(("vS", lastv),))
                    vt_free[b2] = ("dS", s.val("dS"))
                    vs_free[b2] = ("tS", ctv)
        c_qkv_sp = s.val("dS")
        bcn = bar()
        nb_free = [None, None]
        norm_red = None
        nslice = 0
        for h in range(HL):
            for wnm, kcol in (("q", 0), ("k", 2)):
                srcd = (q_dram if wnm == "q" else k_dram).ap()
                for ch in range(2):
                    bb = nslice % 2
                    w = list(wall(bcn, ("tS",))) if nslice == 0 else []
                    if nb_free[bb] is not None:
                        w.append(nb_free[bb])
                    sync_dma(a16[:, bb * 4096:bb * 4096 + 4096],
                             srcd[h, :, ch * 4096:(ch + 1) * 4096], waits=tuple(w))
                    cd = s.val("dS")
                    lastn = None
                    for sub in range(2):
                        csq = s.op("vector", lambda e, bb=bb, sub=sub: e.tensor_mul(
                            a32[:, bb * 2048:bb * 2048 + 2048],
                            a16[:, bb * 4096 + sub * 2048:bb * 4096 + sub * 2048 + 2048],
                            a16[:, bb * 4096 + sub * 2048:bb * 4096 + sub * 2048 + 2048]),
                            waits=(("dS", cd),), incs=(("vS", 1),))
                        for sl in range(4):
                            pb = sl % 2
                            wm = [("vS", csq)] + list(bwait(pb))
                            ctn = s.op("tensor", lambda e, pb=pb, bb=bb, sl=sl: e.matmul(
                                pf[pb][0:1, :], onescolf[:, :],
                                a32[:, bb * 2048 + sl * 512:bb * 2048 + (sl + 1) * 512],
                                start=True, stop=True, skip_group_check=True),
                                waits=tuple(wm), incs=(("tS", 1),))
                            s.op("vector", lambda e, pb=pb: e.reduce_max(
                                st8[0:1, 10:11], pf[pb][0:1, :], axis=AX.X),
                                waits=(("tS", ctn), ("vS", c_kqm0)))
                            cvn = s.op("vector", lambda e, kcol=kcol, h=h: e.tensor_max(
                                kqm[0:1, kcol + h:kcol + h + 1],
                                kqm[0:1, kcol + h:kcol + h + 1], st8[0:1, 10:11]),
                                incs=(("vS", 1),))
                            bank_free[pb] = ("vS", cvn)
                            lastn = cvn
                    nb_free[bb] = ("vS", lastn)
                    norm_red = ("vS", lastn)
                    nslice += 1
        for h in range(HL):
            cv = s.op("vector", lambda e, h=h: e.tensor_mul(
                st8[0:1, 11:12], kqm[0:1, h:h + 1], kqm[0:1, 2 + h:3 + h]),
                waits=(norm_red,), incs=(("vS", 1),))
            s.op("scalar", lambda e: e.activation(st8[0:1, 12:13], st8[0:1, 11:12], AF.Ln),
                 waits=(("vS", cv),))
            cs = s.op("scalar", lambda e: e.activation(
                st8[0:1, 13:14], st8[0:1, 12:13], AF.Exp, scale=0.5, bias=fco[0:1, 1:2]),
                incs=(("sS", 1),))
            cv = s.op("vector", lambda e: e.tensor_scalar_mul(st8[0:1, 14:15], st8[0:1, 13:14], -1.0),
                      waits=(("sS", cs),), incs=(("vS", 1),))
            ctb = s.op("tensor", lambda e: e.matmul(
                pf[5][:, 0:1], onesrow[:, :], st8[0:1, 14:15],
                start=True, stop=True, skip_group_check=True),
                waits=bwait(5, (("vS", cv),)), incs=(("tS", 1),))
            cvm = s.op("vector", lambda e, h=h: e.tensor_copy(mhat[:, h:h + 1], pf[5][:, 0:1]),
                       waits=(("tS", ctb),), incs=(("vS", 1),))
            bank_free[5] = ("vS", cvm)
            norm_red = ("vS", cvm)
        c_mhat_v = s.val("vS")

        sh_dma = {}

        def sh_dma_of_factory(half):
            def dma_of(hc):
                ghc = half * 11 + hc
                return [("dG", sh_dma[("g", ghc)]), ("dG", sh_dma[("u", ghc)])]
            return dma_of

        def sh_prefetch_factory(half):
            def prefetch(hc, free_evt):
                nh = hc + 2
                if nh < 11 and ("g", half * 11 + nh) not in sh_dma:
                    sh_dma[("g", half * 11 + nh)] = w_chunk(
                        swg_a[half * 11 + nh], o_wgb[hc % 2], NDC * 128, (free_evt,))
                    sh_dma[("u", half * 11 + nh)] = w_chunk(
                        swu_a[half * 11 + nh], o_wub[hc % 2], NDC * 128, ())
            return prefetch

        def sh_getwd_factory(half):
            def getwd(d2c, b, w):
                return w_chunk(swd_a[d2c][:, half * 1408:(half + 1) * 1408], o_wdb[b],
                               11 * 128, w)
            return getwd

        bcP4 = bar()
        for hci, b in ((0, 0), (1, 1)):
            pw = wall(bcP4, ("tS", "dG")) if hci == 0 else ()
            sh_dma[("g", hci)] = w_chunk(swg_a[hci], o_wgb[b], NDC * 128, pw)
            sh_dma[("u", hci)] = w_chunk(swu_a[hci], o_wub[b], NDC * 128, ())
        gateup_phase(11, sh_dma_of_factory(0), sh_prefetch_factory(0), use_comb=False)
        down_phase(11, sh_getwd_factory(0), 11 * 128)

        bc3 = bar()
        o_q = [0, 512]
        o_K = [1024, 1536, 2048]
        o_V = [2560, 3072, 3584]
        o_PT = [4096, 4608, 5120, 5632]
        o_mask = 0
        o_B = 1024
        o_rl = 1536
        SBANKS = [0, 1, 5]
        sync_dma(a32[:, o_mask:o_mask + 1024], mask_d, waits=wall(bc3, ("vS", "sS", "tS")))
        c_mask = s.val("dS")

        q_dma = {}
        q_freed = [None, None]
        kv_dma = {}
        kv_freed = [None, None, None]
        PT_freed = [None] * 4
        first_att = (("dS", c_mask), ("vS", c_mhat_v)) + wall(bc3, ("sS",))
        first_att_used = False

        def kv_load(h, kb):
            bb = kb % 3
            w = []
            if kv_freed[bb] is not None:
                w.append(kv_freed[bb])
            sync_dma(a16[:, o_K[bb]:o_K[bb] + 512], k_dram.ap()[h, :, kb * 512:(kb + 1) * 512],
                     waits=tuple(w))
            sync_dma(a16[:, o_V[bb]:o_V[bb] + 512].rearrange("p (c d) -> p c d", c=4),
                     v_dram.ap()[h, kb * 512:(kb + 1) * 512, :].rearrange("(c p) d -> p c d", p=128))
            kv_dma[(h, kb)] = s.val("dS")

        for h in range(HL):
            for qb in range(16):
                qbb = qb % 2
                w = []
                if q_freed[qbb] is not None:
                    w.append(q_freed[qbb])
                sync_dma(a16[:, o_q[qbb]:o_q[qbb] + 512],
                         q_dram.ap()[h, :, qb * 512:(qb + 1) * 512], waits=tuple(w))
                q_dma[(h, qb)] = s.val("dS")
                units = [(kb, jc) for kb in range(qb + 1) for jc in range(4)]
                NUNITS = len(units)
                ctS = {}
                maskv = {}

                kv_dma.clear()

                def emit_S(i, h=h, qb=qb, qbb=qbb, units=units):
                    nonlocal first_att_used
                    kb, jc = units[i]
                    if jc == 0:
                        for kbl in (kb, kb + 1):
                            if kbl <= qb and (h, kbl) not in kv_dma:
                                kv_load(h, kbl)
                    sbk = SBANKS[i % 3]
                    wS = [("dS", kv_dma[(h, kb)]), ("dS", q_dma[(h, qb)])]
                    wS += list(bwait(sbk))
                    if not first_att_used:
                        wS += list(first_att)
                        first_att_used = True
                    c = s.op("tensor", lambda e, sbk=sbk, kb=kb, jc=jc, qbb=qbb: e.matmul(
                        pf[sbk], a16[:, o_K[kb % 3] + jc * 128:o_K[kb % 3] + (jc + 1) * 128],
                        a16[:, o_q[qbb]:o_q[qbb] + 512],
                        start=True, stop=True, skip_group_check=True),
                        waits=tuple(wS), incs=(("tS", 1),))
                    ctS[i] = c
                    if kb == qb:
                        maskv[i] = s.op("vector", lambda e, sbk=sbk, jc=jc: e.tensor_add(
                            pf[sbk], pf[sbk],
                            a32[:, o_mask + 512 - 128 * jc:o_mask + 1024 - 128 * jc]),
                            waits=(("tS", c),), incs=(("vS", 1),))

                def emit_exp(i, h=h):
                    sbk = SBANKS[i % 3]
                    wE = [("tS", ctS[i])]
                    if i in maskv:
                        wE.append(("vS", maskv[i]))
                    if PT_freed[i % 4] is not None:
                        wE.append(PT_freed[i % 4])
                    c = s.op("scalar", lambda e, sbk=sbk, i=i, h=h: e.activation(
                        a16[:, o_PT[i % 4]:o_PT[i % 4] + 512], pf[sbk], AF.Exp,
                        bias=mhat[:, h:h + 1], scale=ATT_SCALE),
                        waits=tuple(wE), incs=(("sS", 1),))
                    bank_free[sbk] = ("sS", c)
                    return c

                exp_of = {}
                pv_last = None

                def emit_PV(i, qb=qb, units=units):
                    nonlocal pv_last
                    kb, jc = units[i]
                    fst = (i == 0)
                    lst = (i == len(units) - 1)
                    wPV = [("sS", exp_of[i])]
                    if fst:
                        wPV += list(bwait(2)) + list(bwait(3))
                    s.op("tensor", lambda e, kb=kb, jc=jc, i=i, fst=fst, lst=lst: e.matmul(
                        pf[2], a16[:, o_V[kb % 3] + jc * 128:o_V[kb % 3] + (jc + 1) * 128],
                        a16[:, o_PT[i % 4]:o_PT[i % 4] + 512],
                        start=fst, stop=lst, skip_group_check=True),
                        waits=tuple(wPV))
                    c = s.op("tensor", lambda e, i=i, fst=fst, lst=lst: e.matmul(
                        pf[3][0:1, :], onescol[:, :], a16[:, o_PT[i % 4]:o_PT[i % 4] + 512],
                        start=fst, stop=lst, skip_group_check=True),
                        incs=(("tS", 1),))
                    PT_freed[i % 4] = ("tS", c)
                    pv_last = c
                    if jc == 3:
                        kv_freed[kb % 3] = ("tS", c)

                AHEAD = 2
                for i in range(min(AHEAD, NUNITS)):
                    emit_S(i)
                    exp_of[i] = emit_exp(i)
                for i in range(NUNITS):
                    if i + AHEAD < NUNITS:
                        emit_S(i + AHEAD)
                        exp_of[i + AHEAD] = emit_exp(i + AHEAD)
                    emit_PV(i)
                q_freed[qbb] = ("tS", pv_last)
                crl = s.op("vector", lambda e: e.reciprocal(
                    a32[0:1, o_rl:o_rl + 512], pf[3][0:1, :]),
                    waits=(("tS", pv_last),), incs=(("vS", 1),))
                bank_free[3] = ("vS", crl)
                ctB = s.op("tensor", lambda e: e.matmul(
                    pf[4], onesrow[:, :], a32[0:1, o_rl:o_rl + 512],
                    start=True, stop=True, skip_group_check=True),
                    waits=bwait(4, (("vS", crl),)), incs=(("tS", 1),))
                cvB = s.op("vector", lambda e: e.tensor_copy(a32[:, o_B:o_B + 512], pf[4]),
                           waits=(("tS", ctB),), incs=(("vS", 1),))
                bank_free[4] = ("vS", cvB)
                cvO = s.op("vector", lambda e, h=h, qb=qb: e.tensor_tensor(
                    AT[:, h * N_TOK + qb * 512: h * N_TOK + (qb + 1) * 512],
                    pf[2], a32[:, o_B:o_B + 512], ALU.mult),
                    waits=(("vS", cvB),), incs=(("vS", 1),))
                bank_free[2] = ("vS", cvO)
            c_ah = s.val("vS")
            sync_dma(a_bounce[h].ap(), AT[:, h * N_TOK:(h + 1) * N_TOK],
                     waits=(("vS", c_ah),))
            c_ab = s.val("dS")
            ccn = "cc2" if h == 0 else "cc3"
            s.op("gpsimd", lambda e, h=h: e.collective_compute(
                "AllGather", ALU.bypass, replica_groups=[list(range(NCORES))],
                ins=[a_bounce[h].ap().opt()], outs=[a_gath[h].ap().opt()]),
                waits=(("dS", c_ab),), incs=((ccn, 1),))
        c_att_v = s.val("vS")

        bcP6 = bar()
        for hci, b in ((0, 0), (1, 1)):
            pw = wall(bcP6, ("tS", "dG")) if hci == 0 else ()
            sh_dma[("g", 11 + hci)] = w_chunk(swg_a[11 + hci], o_wgb[b], NDC * 128, pw)
            sh_dma[("u", 11 + hci)] = w_chunk(swu_a[11 + hci], o_wub[b], NDC * 128, ())
        gateup_phase(11, sh_dma_of_factory(1), sh_prefetch_factory(1), use_comb=False)
        down_phase(11, sh_getwd_factory(1), 11 * 128)

        bc4 = bar()
        o_wo = [0, 2048]
        o_ag = [4096, 12288]
        o_ssb = [0, 512]
        o_st4 = [1024, 1536]
        wo_dma = {}
        wo_used = {}
        ag_dma = {}
        ag_free = [None, None]
        ssb_free = [None, None]
        st4_free = [None, None]
        pairs = [(tc, d2c) for tc in range(2) for d2c in range(NDC)]

        def wo_load(i, waits=()):
            bb = i % 2
            w = list(waits)
            if i - 2 in wo_used:
                w.append(wo_used[i - 2])
            gp_dma(a16[:, o_wo[bb]:o_wo[bb] + NDC * 128], wo_a[pairs[i][1]], waits=tuple(w))
            wo_dma[i] = s.val("dG")

        for i in range(2):
            wo_load(i, wall(bc4, ("tS", "dG")) if i == 0 else ())

        def ag_load(tc, waits):
            bb = tc % 2
            for hl in range(HL):
                w = list(waits) if hl == 0 else [("cc3" if hl else "cc2", 1)]
                if hl == 0 and ag_free[bb] is not None:
                    w.append(ag_free[bb])

                def fn(e, tc=tc, bb=bb, hl=hl):
                    pid = e.partition_id()
                    src = a_gath[hl].ap()[:, bass.ts(pid, TOK)][:, tc * 512:(tc + 1) * 512]
                    src = src.rearrange("(r p) t -> p r t", p=128)
                    dst = a16[:, o_ag[bb]:o_ag[bb] + NDC * 512].rearrange(
                        "p (r l t) -> p r l t", l=HL, t=512)[:, :, hl, :]
                    return e.dma_start(out=dst, in_=src)
                ag_dma[tc] = s.dma("sync", fn, waits=tuple(w))

        ag_load(0, (("cc2", 1), ("cc3", 1)) + wall(bc4, ("tS",)))
        ag_load(1, ())

        for i, (tc, d2c) in enumerate(pairs):
            agb = tc % 2
            if True:
                wbk = i % 2
                wfirst = [("dG", wo_dma[i]), ("dS", ag_dma[tc])] + list(bwait(wbk))
                for ic in range(NDC):
                    s.op("tensor", lambda e, wbk=wbk, ic=ic, agb=agb: e.matmul(
                        pf[wbk], a16[:, o_wo[wbk] + ic * 128:o_wo[wbk] + (ic + 1) * 128],
                        a16[:, o_ag[agb] + ic * 512:o_ag[agb] + (ic + 1) * 512],
                        start=(ic == 0), stop=(ic == NDC - 1), skip_group_check=True),
                        waits=tuple(wfirst) if ic == 0 else (),
                        incs=(("tS", 1),) if ic == NDC - 1 else ())
                    wfirst = []
                ctw = s.val("tS")
                wo_used[i] = ("tS", ctw)
                if i + 2 < len(pairs):
                    wo_load(i + 2)
                wsb = [("tS", ctw)]
                if ssb_free[d2c % 2] is not None:
                    wsb.append(ssb_free[d2c % 2])
                csb = s.op("vector", lambda e, wbk=wbk, d2c=d2c, tc=tc: e.scalar_tensor_tensor(
                    a32[:, o_ssb[d2c % 2]:o_ssb[d2c % 2] + 512], pf[wbk], bot[:, d2c:d2c + 1],
                    fa[:, d2c * TOK + tc * 512: d2c * TOK + tc * 512 + 512], ALU.add, ALU.add),
                    waits=tuple(wsb), incs=(("vS", 1),))
                bank_free[wbk] = ("vS", csb)
                lastc = None
                for sub in range(4):
                    tb = 2 + (sub % 2)
                    ctt = s.op("tensor", lambda e, tb=tb, d2c=d2c, sub=sub: e.transpose(
                        pf[tb][:, 0:128], a32[:, o_ssb[d2c % 2] + sub * 128:
                                              o_ssb[d2c % 2] + (sub + 1) * 128], identf[:, :]),
                        waits=bwait(tb, (("vS", csb),)), incs=(("tS", 1),))
                    wcp = [("tS", ctt)]
                    if sub == 0 and st4_free[d2c % 2] is not None:
                        wcp.append(st4_free[d2c % 2])
                    lastc = s.op("vector", lambda e, tb=tb, d2c=d2c, sub=sub: e.tensor_copy(
                        a32[:, o_st4[d2c % 2] + sub * 128:o_st4[d2c % 2] + (sub + 1) * 128],
                        pf[tb][:, 0:128]), waits=tuple(wcp), incs=(("vS", 1),))
                    bank_free[tb] = ("vS", lastc)
                ssb_free[d2c % 2] = ("vS", lastc)
                dst = out_own[tc * 512:(tc + 1) * 512, d2c * 128:(d2c + 1) * 128].rearrange(
                    "(c p) d -> p c d", p=128)
                gp_dma(dst, a32[:, o_st4[d2c % 2]:o_st4[d2c % 2] + 512].rearrange(
                    "p (c d) -> p c d", c=4),
                    waits=(("vS", lastc), ("dS", c_xcopy)), accum=True)
                st4_free[d2c % 2] = ("dG", s.val("dG"))
            ag_free[agb] = ("tS", s.val("tS"))


        if taps:
            bfin = bar()
            sync_dma(tap_AT, AT[:, :], waits=wall(bfin))
            sync_dma(tap_fa, fa[:, :])
            sync_dma(tap_comb, comb_rows[:, :])
            sync_dma(tap_mhat, mhat[:, :])
            sync_dma(tap_kqm, kqm[0:1, 0:8])
            sync_dma(tap_q, q_dram.ap())
            sync_dma(tap_k, k_dram.ap())
            sync_dma(tap_v, v_dram.ap())

        fin = bar()
        s.op("sync", lambda e: None, waits=wall(fin) + (("cc1", 1), ("cc2", 1), ("cc3", 1)))

        with nc.Block() as block:

            @block.sync
            def _(eng):
                s.emit("sync", eng, sems)

            @block.gpsimd
            def _(eng):
                s.emit("gpsimd", eng, sems)

            @block.tensor
            def _(eng):
                s.emit("tensor", eng, sems)

            @block.vector
            def _(eng):
                s.emit("vector", eng, sems)

            @block.scalar
            def _(eng):
                s.emit("scalar", eng, sems)

    return nc


_CACHE = {}


def _host_consts():
    inv = (1.0 / 10000.0 ** (np.arange(0, DH, 2, dtype=np.float32) / DH)).astype(np.float32)
    seq = np.arange(N_TOK, dtype=np.float32)
    fr = seq[:, None] * inv[None, :]
    pos = np.concatenate([fr, fr], axis=1).astype(np.float32)
    cos = np.cos(pos).T.copy()
    sinf = np.sin(pos)
    sign = np.concatenate([-np.ones(64, np.float32), np.ones(64, np.float32)])
    sin = (sinf * sign[None, :]).T.copy()
    mask = np.where(np.arange(1024)[None, :] >= np.arange(128)[:, None] + 512,
                    np.float32(0.0), np.float32(NEG)).astype(np.float32)
    onehot = np.zeros((8, E * 128), np.float32)
    for e in range(E):
        onehot[e, e * 128:(e + 1) * 128] = 1.0
    fconst = np.zeros((128, 4), np.float32)
    fconst[:, 0] = 1e-5
    fconst[:, 1] = math.log(1.02 * ATT_SCALE)
    return {
        "cos_t": cos, "sin_t": sin, "mask_d": mask,
        "identbf_d": np.eye(128, dtype=ml_dtypes.bfloat16),
        "identf_d": np.eye(128, dtype=np.float32),
        "onesrow_d": np.ones((1, 128), np.float32),
        "onescol_d": np.ones((128, 1), ml_dtypes.bfloat16),
        "onescolf_d": np.ones((128, 1), np.float32),
        "onehot_d": onehot, "fconst_d": fconst,
    }


def _pack(w, rows_chunks, cols_chunks):
    R, C = rows_chunks, cols_chunks
    return np.ascontiguousarray(
        w.reshape(R, 128, C, 128).transpose(2, 1, 0, 3).reshape(C, 128, R * 128))


def kernel(L0_attn_x, L0_attn_Wq, L0_attn_bq, L0_attn_Wk, L0_attn_bk, L0_attn_Wv,
           L0_attn_bv, L0_attn_Wo, L0_attn_bo, L0_ffn_gate_w, L0_ffn_w_gate,
           L0_ffn_w_up, L0_ffn_w_down, L0_ffn_sw_gate, L0_ffn_sw_up, L0_ffn_sw_down):
    key = ("prog", TAPS)
    if key not in _CACHE:
        _CACHE[key] = build_program(taps=TAPS)
    nc = _CACHE[key]

    x = np.asarray(L0_attn_x, dtype=np.float32).reshape(N_TOK, D)
    f32 = lambda a: np.asarray(a, dtype=np.float32)
    Wq, Wk, Wv, Wo = f32(L0_attn_Wq), f32(L0_attn_Wk), f32(L0_attn_Wv), f32(L0_attn_Wo)
    bq, bk, bv, bo = f32(L0_attn_bq), f32(L0_attn_bk), f32(L0_attn_bv), f32(L0_attn_bo)
    gate_w = f32(L0_ffn_gate_w)
    w_gate, w_up, w_down = f32(L0_ffn_w_gate), f32(L0_ffn_w_up), f32(L0_ffn_w_down)
    sw_gate, sw_up, sw_down = f32(L0_ffn_sw_gate), f32(L0_ffn_sw_up), f32(L0_ffn_sw_down)

    consts = _host_consts()
    gate_w_a = np.ascontiguousarray(
        gate_w.reshape(NDC, 128, E).transpose(1, 0, 2).reshape(128, NDC * E))
    wg_a = np.ascontiguousarray(
        w_gate.reshape(E, NDC, 128, NHC, 128).transpose(0, 3, 2, 1, 4).reshape(E, NHC, 128, NDC * 128))
    wu_a = np.ascontiguousarray(
        w_up.reshape(E, NDC, 128, NHC, 128).transpose(0, 3, 2, 1, 4).reshape(E, NHC, 128, NDC * 128))
    wd_a = np.ascontiguousarray(
        w_down.reshape(E, NHC, 128, NDC, 128).transpose(0, 3, 2, 1, 4).reshape(E, NDC, 128, NHC * 128))
    swg_a = _pack(sw_gate, NDC, NSC)
    swu_a = _pack(sw_up, NDC, NSC)
    swd_a = _pack(sw_down, NSC, NDC)
    wo_a = _pack(Wo, NDC, NDC)
    bo_t = np.ascontiguousarray(bo.reshape(NDC, 128).T)

    shared = dict(consts)
    shared.update({
        "gate_w_a": gate_w_a, "wg_a": wg_a, "wu_a": wu_a, "wd_a": wd_a,
        "swg_a": swg_a, "swu_a": swu_a, "swd_a": swd_a, "wo_a": wo_a, "bo_t": bo_t,
    })

    in_maps = []
    for c in range(NCORES):
        m = dict(shared)
        m["x_own"] = np.ascontiguousarray(x[c * TOK:(c + 1) * TOK, :])
        hsl = slice(c * 256, (c + 1) * 256)
        for nm, W in (("wq_own", Wq), ("wk_own", Wk), ("wv_own", Wv)):
            m[nm] = np.ascontiguousarray(
                W[:, hsl].reshape(NDC, 128, 256).transpose(1, 0, 2).reshape(128, NDC * 256))
        barr = np.zeros((128, 12), np.float32)
        for i, b in enumerate((bq, bk, bv)):
            bl = b[hsl].reshape(2, 128)
            barr[:, 2 * i + 0] = bl[0]
            barr[:, 2 * i + 1] = bl[1]
        barr[:, 6:12] = np.concatenate([barr[64:128, 0:6], barr[0:64, 0:6]], axis=0)
        m["b_qkv"] = barr
        in_maps.append(m)

    res = run_bass_kernel_spmd(nc, in_maps, core_ids=list(range(NCORES)))
    out = np.concatenate([res.results[c]["out_own"] for c in range(NCORES)], axis=0)
    if TAPS:
        kernel.last_results = res
    return out.reshape(1, N_TOK, D)


# version 08
# speedup vs baseline: 1.4058x; 1.0884x over previous
import math
import contextlib
import numpy as np
import ml_dtypes
import concourse.bass as bass
import concourse.mybir as mybir
from concourse.bass_utils import run_bass_kernel_spmd

F32 = mybir.dt.float32
BF16 = mybir.dt.bfloat16
AF = mybir.ActivationFunctionType
ALU = mybir.AluOpType
AX = mybir.AxisListType

N_TOK = 8192
D = 2048
DH = 128
NCORES = 8
TOK = N_TOK // NCORES
HL = 2
E = 8
NDC = D // 128
NHC = 11
NSC = 22
ATT_SCALE = DH ** -0.5
NEG = -1e30

ENGINES = ("sync", "gpsimd", "tensor", "vector", "scalar")
TAPS = False


DMA_SLOTS = 8


class Sched:
    def __init__(self):
        self.ops = {e: [] for e in ENGINES}
        self.cnt = {}
        self.dma_idx = {"sync": 0, "gpsimd": 0}
        self.dma_hist = {"dS": [], "dG": []}

    def dma(self, eng, fn, waits=()):
        vname = {"sync": "dS", "gpsimd": "dG"}[eng]
        j = self.dma_idx[eng]
        slot = j % DMA_SLOTS
        g = j // DMA_SLOTS + 1
        sname = f"{vname}{slot}"
        w = list(waits)
        if g > 1:
            w.append((sname, 16 * (g - 1)))
        self.dma_idx[eng] = j + 1
        self.dma_hist[vname].append(sname)
        self.cnt[vname] = self.cnt.get(vname, 0) + 16
        self.op(eng, fn, waits=tuple(w), incs=((sname, 16),))
        return self.cnt[vname]

    def _xlate(self, ss, v):
        if ss not in ("dS", "dG"):
            return ((ss, v),)
        k = v // 16
        hist = self.dma_hist[ss][:k]
        per = {}
        for sn in hist:
            per[sn] = per.get(sn, 0) + 16
        return tuple(per.items())

    def val(self, sem):
        return self.cnt.get(sem, 0)

    SELF = {"vector": "vS", "scalar": "sS"}

    def op(self, eng, fn, waits=(), incs=()):
        for ss, v in waits:
            if not (ss[0] == "d" and len(ss) > 2):
                assert v <= self.val(ss), f"wait on future: {ss} {v} > {self.val(ss)}"
        sse = self.SELF.get(eng)
        n_orig = len(tuple(incs))
        if sse is not None and not any(ss == sse for ss, _ in incs):
            incs = tuple(incs) + ((sse, 1),)
        self.ops[eng].append((tuple(waits), fn, tuple(incs)))
        out = []
        for i, (ss, d) in enumerate(incs):
            self.cnt[ss] = self.cnt.get(ss, 0) + d
            if i < n_orig or n_orig == 0:
                out.append(self.cnt[ss])
        return out[0] if len(out) == 1 else out

    def emit(self, eng_name, eng, sems):
        last = {}
        sse = self.SELF.get(eng_name)
        self_c = 0
        for waits, fn, incs in self.ops[eng_name]:
            if sse is not None and self_c > 0:
                eng.wait_ge(sems[sse], self_c)
                last[sse] = max(last.get(sse, 0), self_c)
            for ss0, v0 in waits:
                for ss, v in self._xlate(ss0, v0):
                    if last.get(ss, 0) < v:
                        eng.wait_ge(sems[ss], v)
                        last[ss] = v
            inst = fn(eng)
            for ss, d in incs:
                if ss.startswith("cc"):
                    inst.then_inc(sems[ss])
                else:
                    inst.then_inc(sems[ss], d)
                if ss == sse:
                    self_c += d


def build_program(taps=False):
    nc = bass.Bass("TRN2", target_bir_lowering=False)
    dt = nc.dram_tensor

    x_own = dt("x_own", [TOK, D], F32, kind="ExternalInput").ap()
    wq_own = dt("wq_own", [128, NDC * 256], F32, kind="ExternalInput").ap()
    wk_own = dt("wk_own", [128, NDC * 256], F32, kind="ExternalInput").ap()
    wv_own = dt("wv_own", [128, NDC * 256], F32, kind="ExternalInput").ap()
    b_qkv = dt("b_qkv", [128, 12], F32, kind="ExternalInput").ap()
    gate_w_a = dt("gate_w_a", [128, NDC * E], F32, kind="ExternalInput").ap()
    wg_a = dt("wg_a", [E, NHC, 128, NDC * 128], F32, kind="ExternalInput").ap()
    wu_a = dt("wu_a", [E, NHC, 128, NDC * 128], F32, kind="ExternalInput").ap()
    wd_a = dt("wd_a", [E, NDC, 128, NHC * 128], F32, kind="ExternalInput").ap()
    swg_a = dt("swg_a", [NSC, 128, NDC * 128], F32, kind="ExternalInput").ap()
    swu_a = dt("swu_a", [NSC, 128, NDC * 128], F32, kind="ExternalInput").ap()
    swd_a = dt("swd_a", [NDC, 128, NSC * 128], F32, kind="ExternalInput").ap()
    wo_a = dt("wo_a", [NDC, 128, NDC * 128], F32, kind="ExternalInput").ap()
    bo_t = dt("bo_t", [128, NDC], F32, kind="ExternalInput").ap()
    cos_t = dt("cos_t", [128, N_TOK], F32, kind="ExternalInput").ap()
    sin_t = dt("sin_t", [128, N_TOK], F32, kind="ExternalInput").ap()
    mask_d = dt("mask_d", [128, 1024], F32, kind="ExternalInput").ap()
    identbf_d = dt("identbf_d", [128, 128], BF16, kind="ExternalInput").ap()
    identf_d = dt("identf_d", [128, 128], F32, kind="ExternalInput").ap()
    onesrow_d = dt("onesrow_d", [1, 128], F32, kind="ExternalInput").ap()
    onescol_d = dt("onescol_d", [128, 1], BF16, kind="ExternalInput").ap()
    onescolf_d = dt("onescolf_d", [128, 1], F32, kind="ExternalInput").ap()
    onehot_d = dt("onehot_d", [8, E * 128], F32, kind="ExternalInput").ap()
    fconst_d = dt("fconst_d", [128, 4], F32, kind="ExternalInput").ap()
    out_own = dt("out_own", [TOK, D], F32, kind="ExternalOutput").ap()

    xn_bounce = dt("xn_bounce", [D, TOK], BF16)
    xn_gath = dt("xn_gath", [NCORES * D, TOK], BF16)
    a_bounce = [dt(f"a_bounce{h}", [128, N_TOK], BF16) for h in range(HL)]
    a_gath = [dt(f"a_gath{h}", [NCORES * 128, N_TOK], BF16) for h in range(HL)]
    q_dram = dt("q_dram", [HL, 128, N_TOK], BF16)
    k_dram = dt("k_dram", [HL, 128, N_TOK], BF16)
    v_dram = dt("v_dram", [HL, N_TOK, 128], BF16)
    if taps:
        tap_xnT = dt("tap_xnT", [128, NDC * TOK], BF16, kind="ExternalOutput").ap()
        tap_AT = dt("tap_AT", [128, HL * N_TOK], BF16, kind="ExternalOutput").ap()
        tap_fa = dt("tap_fa", [128, NDC * TOK], BF16, kind="ExternalOutput").ap()
        tap_comb = dt("tap_comb", [8, TOK], F32, kind="ExternalOutput").ap()
        tap_mhat = dt("tap_mhat", [128, HL], F32, kind="ExternalOutput").ap()
        tap_kqm = dt("tap_kqm", [1, 8], F32, kind="ExternalOutput").ap()
        tap_st8 = dt("tap_st8", [128, 40], F32, kind="ExternalOutput").ap()
        tap_q = dt("tap_q", [HL, 128, N_TOK], BF16, kind="ExternalOutput").ap()
        tap_k = dt("tap_k", [HL, 128, N_TOK], BF16, kind="ExternalOutput").ap()
        tap_v = dt("tap_v", [HL, N_TOK, 128], BF16, kind="ExternalOutput").ap()

    s = Sched()

    with contextlib.ExitStack() as es:
        sb = lambda name, shape, dtp: es.enter_context(nc.sbuf_tensor(name, shape, dtp))
        psmk = lambda name, shape, dtp: es.enter_context(nc.psum_tensor(name, shape, dtp))

        xnT = sb("xnT", [128, NDC * TOK], BF16)
        AT = sb("AT", [128, HL * N_TOK], BF16)
        fa = sb("fa", [128, NDC * TOK], BF16)
        comb_rows = sb("comb_rows", [8, TOK], F32)
        identbf = sb("identbf", [128, 128], BF16)
        identf = sb("identf", [128, 128], F32)
        onesrow = sb("onesrow", [1, 128], F32)
        onescol = sb("onescol", [128, 1], BF16)
        onescolf = sb("onescolf", [128, 1], F32)
        bqkv = sb("bqkv", [128, 12], F32)
        bot = sb("bot", [128, NDC], F32)
        gw = sb("gw", [128, NDC * E], BF16)
        mhat = sb("mhat", [128, HL], F32)
        fco = sb("fco", [128, 4], F32)
        st8 = sb("st8", [128, 40], F32)
        kqm = sb("kqm", [1, 8], F32)
        a16 = sb("a16", [128, 29184], BF16)
        a32 = sb("a32", [128, 4608], F32)

        pfh = [psmk(f"pf{i}", [128, 512], F32) for i in range(6)]
        pf = [h[:, :] for h in pfh]
        pt = [psmk(f"pt{i}", [128, 1024], BF16) for i in range(2)]
        bank_free = [None] * 6
        pt_free = [None] * 2

        sems = {}
        for nm in (["cc1", "cc2", "cc3", "tS", "vS", "sS"]
                   + [f"dS{i}" for i in range(DMA_SLOTS)]
                   + [f"dG{i}" for i in range(DMA_SLOTS)]):
            sems[nm] = es.enter_context(nc.semaphore(nm))

        def sync_dma(dst, src, waits=()):
            return s.dma("sync", lambda e, d=dst, sr=src: e.dma_start(out=d, in_=sr),
                         waits=waits)

        def gp_dma(dst, src, waits=(), accum=False):
            if accum:
                return s.dma("gpsimd",
                             lambda e, d=dst, sr=src: e.dma_start(out=d, in_=sr, accum_op=ALU.add),
                             waits=waits)
            return s.dma("gpsimd", lambda e, d=dst, sr=src: e.dma_start(out=d, in_=sr),
                         waits=waits)

        def bar():
            return {nm: s.val(nm) for nm in ("dS", "dG", "tS", "vS", "sS")}

        def wall(bc, only=None):
            return tuple((nm, v) for nm, v in bc.items()
                         if v > 0 and (only is None or nm in only))

        def bwait(i, extra=()):
            w = list(extra)
            if bank_free[i] is not None:
                w.append(bank_free[i])
            return tuple(w)

        sync_dma(identbf[:, :], identbf_d)
        sync_dma(identf[:, :], identf_d)
        sync_dma(onesrow[:, :], onesrow_d)
        sync_dma(onescol[:, :], onescol_d)
        sync_dma(onescolf[:, :], onescolf_d)
        sync_dma(bqkv[:, :], b_qkv)
        sync_dma(bot[:, :], bo_t)
        sync_dma(fco[:, :], fconst_d)
        c_consts = s.val("dS")
        gp_dma(gw[:, :], gate_w_a)
        c_gw = s.val("dG")

        NTT = TOK // 128
        o_xt = [0, 2048]
        o_xn = [24576, 26624]
        x_free = [None, None]
        xn_free = [None, None]
        ln_dma = []
        for tt in range(NTT):
            b = tt % 2
            w = [("dS", c_consts)]
            if x_free[b] is not None:
                w.append(x_free[b])
            sync_dma(a32[:, o_xt[b]:o_xt[b] + 2048], x_own[tt * 128:(tt + 1) * 128, :], waits=tuple(w))
            ln_dma.append(s.val("dS"))
            if tt == 1:
                break
        for tt in range(NTT):
            b = tt % 2
            if tt >= 2:
                w2 = [("dS", c_consts), x_free[b]]
                sync_dma(a32[:, o_xt[b]:o_xt[b] + 2048], x_own[tt * 128:(tt + 1) * 128, :],
                         waits=tuple(w2))
                ln_dma.append(s.val("dS"))
            xt = a32[:, o_xt[b]:o_xt[b] + 2048]
            xn = a16[:, o_xn[b]:o_xn[b] + 2048]
            w = [("dS", ln_dma[tt])]
            if xn_free[b] is not None:
                w.append(xn_free[b])
            B0 = (tt % 2) * 16
            stc = lambda i: st8[:, B0 + i:B0 + i + 1]
            s.op("vector", lambda e, xt=xt, stc=stc: e.reduce_sum(stc(0), xt, axis=AX.X),
                 waits=tuple(w))
            xsq = a16[:, o_xn[b] - 24576 + 20480:o_xn[b] - 24576 + 20480 + 2048]
            cssq = s.op("scalar", lambda e, xt=xt, xsq=xsq: e.activation(
                xsq, xt, AF.Square), waits=tuple(w), incs=(("sS", 1),))
            s.op("vector", lambda e, xsq=xsq, stc=stc: e.reduce_sum(stc(1), xsq, axis=AX.X),
                 waits=(("sS", cssq),))
            s.op("vector", lambda e, stc=stc: e.tensor_scalar_mul(stc(2), stc(0), 1.0 / D))
            s.op("vector", lambda e, stc=stc: e.tensor_mul(stc(3), stc(2), stc(2)))
            cv = s.op("vector", lambda e, stc=stc: e.scalar_tensor_tensor(
                stc(4), stc(1), 1.0 / D, stc(3), ALU.mult, ALU.subtract),
                incs=(("vS", 1),))
            s.op("scalar", lambda e, stc=stc: e.activation(stc(5), stc(4), AF.Ln, bias=fco[:, 0:1]),
                 waits=(("vS", cv),))
            cs = s.op("scalar", lambda e, stc=stc: e.activation(stc(6), stc(5), AF.Exp, scale=-0.5),
                      incs=(("sS", 1),))
            cv2 = s.op("vector", lambda e, stc=stc: e.tensor_mul(stc(8), stc(2), stc(6)),
                       waits=(("sS", cs),))
            cv2 = s.op("vector", lambda e, stc=stc: e.tensor_scalar_mul(stc(9), stc(8), -1.0),
                       incs=(("vS", 1),))
            cs2 = s.op("scalar", lambda e, xt=xt, xn=xn, stc=stc: e.activation(
                xn, xt, AF.Identity, bias=stc(9), scale=stc(6)),
                waits=(("vS", cv2),), incs=(("sS", 1),))
            x_free[b] = ("sS", cs2)
            lastv = None
            for dc in range(NDC):
                pb = pt[dc % 2]
                wt = [("sS", cs2)]
                if pt_free[dc % 2] is not None:
                    wt.append(pt_free[dc % 2])
                ct = s.op("tensor", lambda e, pb=pb, xn=xn, dc=dc: e.transpose(
                    pb[:, 0:128], xn[:, dc * 128:(dc + 1) * 128], identbf[:, :]),
                    waits=tuple(wt), incs=(("tS", 1),))
                lastv = s.op("vector", lambda e, pb=pb, dc=dc, tt=tt: e.tensor_copy(
                    xnT[:, dc * TOK + tt * 128: dc * TOK + (tt + 1) * 128], pb[:, 0:128]),
                    waits=(("tS", ct),), incs=(("vS", 1),))
                pt_free[dc % 2] = ("vS", lastv)
            xn_free[b] = ("vS", lastv)
        c_ln_v = s.val("vS")
        gp_dma(xn_bounce.ap().rearrange("(c p) t -> p c t", p=128),
               xnT[:, :].rearrange("p (c t) -> p c t", c=NDC), waits=(("vS", c_ln_v),))
        c_xnb = s.val("dG")
        if taps:
            sync_dma(tap_xnT, xnT[:, :])
            sync_dma(tap_st8, st8[:, :])
        s.op("gpsimd", lambda e: e.collective_compute(
            "AllGather", ALU.bypass, replica_groups=[list(range(NCORES))],
            ins=[xn_bounce.ap().opt()], outs=[xn_gath.ap().opt()]),
            waits=(("dG", c_xnb),), incs=(("cc1", 1),))
        sync_dma(out_own, x_own)
        c_xcopy = s.val("dS")

        bc0 = bar()
        for tt in range(NTT):
            bi = tt % 2
            wfirst = bwait(bi, wall(bc0) + (("dG", c_gw),))
            for dc in range(NDC):
                sp = (dc == NDC - 1)
                inc = (("tS", 1),) if sp else ()
                s.op("tensor", lambda e, bi=bi, dc=dc, tt=tt, sp=sp: e.matmul(
                    pf[bi][:, 0:8], xnT[:, dc * TOK + tt * 128: dc * TOK + (tt + 1) * 128],
                    gw[:, dc * 8:(dc + 1) * 8], start=(dc == 0), stop=sp,
                    skip_group_check=True),
                    waits=wfirst if dc == 0 else (), incs=inc)
                if dc == 0:
                    wfirst = ()
            ct = s.val("tS")
            B0 = (tt % 2) * 16
            stc = lambda i: st8[:, B0 + i:B0 + i + 1]
            ga = (tt % 2) * 128
            s.op("vector", lambda e, bi=bi, stc=stc: e.reduce_max(stc(0), pf[bi][:, 0:8], axis=AX.X),
                 waits=(("tS", ct),))
            cm = s.op("vector", lambda e, stc=stc: e.tensor_scalar_mul(stc(1), stc(0), -1.0),
                      incs=(("vS", 1),))
            ce = s.op("scalar", lambda e, bi=bi, stc=stc, ga=ga: e.activation(
                a32[:, ga + 0:ga + 8], pf[bi][:, 0:8], AF.Exp, bias=stc(1), accum_out=stc(2)),
                waits=(("vS", cm),), incs=(("sS", 1),))
            bank_free[bi] = ("sS", ce)
            cr = s.op("vector", lambda e, stc=stc: e.reciprocal(stc(3), stc(2)),
                      waits=(("sS", ce),), incs=(("vS", 1),))
            cn = s.op("scalar", lambda e, stc=stc, ga=ga: e.activation(
                a32[:, ga + 32:ga + 40], a32[:, ga + 0:ga + 8], AF.Identity, scale=stc(3)),
                waits=(("vS", cr),), incs=(("sS", 1),))
            scs = a32[:, ga + 32:ga + 40]
            s.op("vector", lambda e, scs=scs, stc=stc: e.reduce_max(stc(4), scs, axis=AX.X),
                 waits=(("sS", cn),))
            s.op("vector", lambda e, scs=scs, stc=stc, ga=ga: e.scalar_tensor_tensor(
                a32[:, ga + 64:ga + 72], scs, stc(4), scs, ALU.is_lt, ALU.mult))
            s.op("vector", lambda e, stc=stc, ga=ga: e.reduce_max(stc(5), a32[:, ga + 64:ga + 72], axis=AX.X))
            cv = s.op("vector", lambda e, scs=scs, stc=stc, ga=ga: e.scalar_tensor_tensor(
                a32[:, ga + 96:ga + 104], scs, stc(5), scs, ALU.is_ge, ALU.mult),
                incs=(("vS", 1),))
            ctr = s.op("tensor", lambda e, ga=ga: e.transpose(
                pf[2][0:8, 0:128], a32[:, ga + 96:ga + 104], identf[:, :]),
                waits=bwait(2, (("vS", cv),)), incs=(("tS", 1),))
            cvc = s.op("vector", lambda e, tt=tt: e.tensor_copy(
                comb_rows[0:8, tt * 128:(tt + 1) * 128], pf[2][0:8, 0:128]),
                waits=(("tS", ctr),), incs=(("vS", 1),))
            bank_free[2] = ("vS", cvc)
        c_comb_v = s.val("vS")

        o_wgb = [0, 2048]
        o_wub = [4096, 6144]
        o_wdb = [8192, 9600]
        o_G = 11008
        o_sil = [22272, 22784]
        o_t1 = [23296, 23808]
        o_cB = [0, 512]
        o_oh = 1024
        bc1 = bar()
        sync_dma(a32[0:8, o_oh:o_oh + E * 128], onehot_d, waits=wall(bc1, ("vS", "sS")))
        c_oh = s.val("dS")

        wgu_free = [None, None]
        wd_free = [None, None]
        sil_free = [None, None]
        moe_dma = {}

        def w_chunk(src, off, ln, waits):
            gp_dma(a16[:, off:off + ln], src, waits=waits)
            return s.val("dG")

        for (ei, hci, b) in ((0, 0, 0), (0, 1, 1)):
            moe_dma[("g", ei, hci)] = w_chunk(wg_a[ei, hci], o_wgb[b], NDC * 128, ())
            moe_dma[("u", ei, hci)] = w_chunk(wu_a[ei, hci], o_wub[b], NDC * 128, ())

        first_moe = wall(bc1) + (("vS", c_comb_v), ("dS", c_oh))

        def gateup_phase(nchunks, dma_of, prefetch, use_comb=True, first=()):
            lastG = None
            tS0 = ("tS", s.val("tS"))
            for hc in range(nchunks):
                bks = [(4 * hc + i) % 6 for i in range(4)]
                wdma = dma_of(hc)
                for tc in range(2):
                    gb, ub = bks[2 * tc], bks[2 * tc + 1]
                    wfirst = list(wdma) + list(bwait(gb))
                    if hc == 0 and tc == 0:
                        wfirst += list(first)
                    for dc in range(NDC):
                        s.op("tensor", lambda e, gb=gb, dc=dc, tc=tc, hc=hc: e.matmul(
                            pf[gb], a16[:, o_wgb[hc % 2] + dc * 128:o_wgb[hc % 2] + (dc + 1) * 128],
                            xnT[:, dc * TOK + tc * 512: dc * TOK + tc * 512 + 512],
                            start=(dc == 0), stop=(dc == NDC - 1), skip_group_check=True),
                            waits=tuple(wfirst) if dc == 0 else (),
                            incs=(("tS", 1),) if dc == NDC - 1 else ())
                        wfirst = []
                    ctg = s.val("tS")
                    wfirst = list(bwait(ub))
                    for dc in range(NDC):
                        s.op("tensor", lambda e, ub=ub, dc=dc, tc=tc, hc=hc: e.matmul(
                            pf[ub], a16[:, o_wub[hc % 2] + dc * 128:o_wub[hc % 2] + (dc + 1) * 128],
                            xnT[:, dc * TOK + tc * 512: dc * TOK + tc * 512 + 512],
                            start=(dc == 0), stop=(dc == NDC - 1), skip_group_check=True),
                            waits=tuple(wfirst) if dc == 0 else (),
                            incs=(("tS", 1),) if dc == NDC - 1 else ())
                        wfirst = []
                    ctu = s.val("tS")
                    sb_, tb_ = o_sil[tc], o_t1[tc]
                    wsil = [("tS", ctg)]
                    if sil_free[tc] is not None:
                        wsil.append(sil_free[tc])
                    cs = s.op("scalar", lambda e, gb=gb, sb_=sb_: e.activation(
                        a16[:, sb_:sb_ + 512], pf[gb], AF.Silu),
                        waits=tuple(wsil), incs=(("sS", 1),))
                    bank_free[gb] = ("sS", cs)
                    if use_comb:
                        cv1 = s.op("vector", lambda e, ub=ub, tb_=tb_, tc=tc: e.tensor_tensor(
                            a16[:, tb_:tb_ + 512], pf[ub], a32[:, o_cB[tc]:o_cB[tc] + 512],
                            ALU.mult), waits=(("tS", ctu),), incs=(("vS", 1),))
                        bank_free[ub] = ("vS", cv1)
                        lastG = s.op("vector", lambda e, sb_=sb_, tb_=tb_, hc=hc, tc=tc: e.tensor_mul(
                            a16[:, o_G + hc * 1024 + tc * 512:o_G + hc * 1024 + tc * 512 + 512],
                            a16[:, tb_:tb_ + 512], a16[:, sb_:sb_ + 512]),
                            waits=(("sS", cs), tS0), incs=(("vS", 1),))
                        sil_free[tc] = ("vS", lastG)
                    else:
                        lastG = s.op("vector", lambda e, ub=ub, sb_=sb_, hc=hc, tc=tc: e.tensor_tensor(
                            a16[:, o_G + hc * 1024 + tc * 512:o_G + hc * 1024 + tc * 512 + 512],
                            pf[ub], a16[:, sb_:sb_ + 512], ALU.mult),
                            waits=(("tS", ctu), ("sS", cs), tS0), incs=(("vS", 1),))
                        bank_free[ub] = ("vS", lastG)
                        sil_free[tc] = ("vS", lastG)
                wgu_free[hc % 2] = ("tS", ctu)
                prefetch(hc, wgu_free[hc % 2])
            return lastG

        def down_phase(nchunks, getwd, wd_len, first_copy=False):
            cG = s.val("vS")
            dma_d = {}
            for d2c in range(2):
                w = [wd_free[d2c % 2]] if wd_free[d2c % 2] is not None else []
                dma_d[d2c] = getwd(d2c, d2c % 2, tuple(w))
            for d2c in range(NDC):
                bwf = o_wdb[d2c % 2]
                ctd_last = None
                for tc in range(2):
                    dbk = 4 + tc
                    w = [("dG", dma_d[d2c]), ("vS", cG)] + list(bwait(dbk))
                    for hc in range(nchunks):
                        s.op("tensor", lambda e, dbk=dbk, bwf=bwf, hc=hc, tc=tc: e.matmul(
                            pf[dbk], a16[:, bwf + hc * 128:bwf + (hc + 1) * 128],
                            a16[:, o_G + hc * 1024 + tc * 512:o_G + hc * 1024 + tc * 512 + 512],
                            start=(hc == 0), stop=(hc == nchunks - 1), skip_group_check=True),
                            waits=tuple(w) if hc == 0 else (),
                            incs=(("tS", 1),) if hc == nchunks - 1 else ())
                        w = []
                    ctd = s.val("tS")
                    ctd_last = ctd
                    if first_copy:
                        cv = s.op("vector", lambda e, dbk=dbk, d2c=d2c, tc=tc: e.tensor_copy(
                            fa[:, d2c * TOK + tc * 512: d2c * TOK + tc * 512 + 512], pf[dbk]),
                            waits=(("tS", ctd),), incs=(("vS", 1),))
                    else:
                        cv = s.op("vector", lambda e, dbk=dbk, d2c=d2c, tc=tc: e.tensor_add(
                            fa[:, d2c * TOK + tc * 512: d2c * TOK + tc * 512 + 512],
                            pf[dbk], fa[:, d2c * TOK + tc * 512: d2c * TOK + tc * 512 + 512]),
                            waits=(("tS", ctd),), incs=(("vS", 1),))
                    bank_free[dbk] = ("vS", cv)
                wd_free[d2c % 2] = ("tS", ctd_last)
                if d2c + 2 < NDC:
                    dma_d[d2c + 2] = getwd(d2c + 2, d2c % 2, (wd_free[d2c % 2],))

        for e_i in range(E):
            for tc in range(2):
                w = list(bwait(4))
                if e_i == 0 and tc == 0:
                    w += list(first_moe)
                ct = s.op("tensor", lambda e, e_i=e_i, tc=tc: e.matmul(
                    pf[4], a32[0:8, o_oh + e_i * 128:o_oh + (e_i + 1) * 128],
                    comb_rows[0:8, tc * 512:(tc + 1) * 512], start=True, stop=True,
                    skip_group_check=True),
                    waits=tuple(w), incs=(("tS", 1),))
                cv = s.op("vector", lambda e, tc=tc: e.tensor_copy(
                    a32[:, o_cB[tc]:o_cB[tc] + 512], pf[4]),
                    waits=(("tS", ct),), incs=(("vS", 1),))
                bank_free[4] = ("vS", cv)

            def dma_of(hc, e_i=e_i):
                return [("dG", moe_dma[("g", e_i, hc)]), ("dG", moe_dma[("u", e_i, hc)])]

            def prefetch(hc, free_evt, e_i=e_i):
                if hc + 2 < NHC:
                    nh = hc + 2
                    if ("g", e_i, nh) not in moe_dma:
                        moe_dma[("g", e_i, nh)] = w_chunk(wg_a[e_i, nh], o_wgb[nh % 2],
                                                          NDC * 128, (free_evt,))
                        moe_dma[("u", e_i, nh)] = w_chunk(wu_a[e_i, nh], o_wub[nh % 2],
                                                          NDC * 128, ())
                elif hc == NHC - 1 and e_i + 1 < E:
                    for nh in (0, 1):
                        moe_dma[("g", e_i + 1, nh)] = w_chunk(
                            wg_a[e_i + 1, nh], o_wgb[nh % 2], NDC * 128,
                            (wgu_free[nh % 2],))
                        moe_dma[("u", e_i + 1, nh)] = w_chunk(
                            wu_a[e_i + 1, nh], o_wub[nh % 2], NDC * 128, ())

            gateup_phase(NHC, dma_of, prefetch, use_comb=True,
                         first=first_moe if e_i == 0 else ())

            def getwd(d2c, b, w, e_i=e_i):
                return w_chunk(wd_a[e_i, d2c], o_wdb[b], NHC * 128, w)

            down_phase(NHC, getwd, NHC * 128, first_copy=(e_i == 0))

        bc2 = bar()
        o_xg = [0, 4096, 8192]
        o_wq, o_wk, o_wv = 12288, 16384, 20480
        o_qs = [24576, 25088]
        o_ks = [25600, 26112]
        o_vs = [26624, 27136]
        o_vt = [27648, 28160]
        o_cos = [0, 512]
        o_sin = [1024, 1536]
        o_rt1, o_rt2 = 2048, 2560
        gp_dma(a16[:, o_wq:o_wq + NDC * 256], wq_own, waits=wall({"tS": bc2["tS"], "dG": bc2["dG"]}))
        gp_dma(a16[:, o_wk:o_wk + NDC * 256], wk_own)
        gp_dma(a16[:, o_wv:o_wv + NDC * 256], wv_own)
        c_wqkv = s.val("dG")
        cvz = s.op("vector", lambda e: e.memset(kqm[0:1, 0:8], 0.0), waits=wall(bc2, ("sS",)),
                   incs=(("vS", 1),))
        c_kqm0 = s.val("vS")

        NU = 2 * NTT * 2
        xg_dma = {}
        xg_used = {}

        def xg_load(u, waits):
            tc, half = u // 2, u % 2
            r = tc // 2
            l0 = (tc % 2) * 512
            rows0 = r * D + half * 1024
            src = xn_gath.ap()[rows0:rows0 + 1024, l0:l0 + 512].rearrange("(c p) t -> p c t", p=128)
            off = o_xg[u % 3]
            sync_dma(a16[:, off:off + 4096].rearrange("p (c t) -> p c t", c=8), src, waits=waits)
            xg_dma[u] = s.val("dS")

        for u in range(3):
            xg_load(u, wall(bc2, ("vS", "tS")) + (("cc1", 1),))

        bank_map = [("q", 0, 0), ("k", 0, 1), ("v", 0, 2), ("q", 1, 3), ("k", 1, 4), ("v", 1, 5)]
        cos_free = [None, None]
        qs_free = [None, None]
        ks_free = [None, None]
        vs_free = [None, None]
        vt_free = [None, None]
        norm_red = None

        for tc in range(2 * NTT):
            b2 = tc % 2
            w = [("dS", c_consts)]
            if cos_free[b2] is not None:
                w.append(cos_free[b2])
            else:
                w += list(wall(bc2, ("vS", "tS")))
            sync_dma(a32[:, o_cos[b2]:o_cos[b2] + 512], cos_t[:, tc * 512:(tc + 1) * 512],
                     waits=tuple(w))
            sync_dma(a32[:, o_sin[b2]:o_sin[b2] + 512], sin_t[:, tc * 512:(tc + 1) * 512])
            c_cs = s.val("dS")
            for half in range(2):
                u = tc * 2 + half
                whalf = [("dG", c_wqkv), ("dS", xg_dma[u])]
                for wnm, h, bk in bank_map:
                    woff = {"q": o_wq, "k": o_wk, "v": o_wv}[wnm]
                    wfirst = list(whalf)
                    if half == 0:
                        wfirst += list(bwait(bk))
                    for dci in range(8):
                        dc = half * 8 + dci
                        s.op("tensor", lambda e, bk=bk, woff=woff, dc=dc, h=h, u=u: e.matmul(
                            pf[bk], a16[:, woff + dc * 256 + h * 128: woff + dc * 256 + h * 128 + 128],
                            a16[:, o_xg[u % 3] + (dc % 8) * 512: o_xg[u % 3] + (dc % 8) * 512 + 512],
                            start=(dc == 0), stop=(dc == NDC - 1), skip_group_check=True),
                            waits=tuple(wfirst) if dci == 0 else (),
                            incs=(("tS", 1),) if (dci == 7) else ())
                        wfirst = []
                xg_used[u] = s.val("tS")
                if u + 3 < NU:
                    xg_load(u + 3, (("tS", xg_used[u]),))
            ct_all = s.val("tS")
            for wnm, h, bk in bank_map:
                bcol = {"q": 0, "k": 2, "v": 4}[wnm] + h
                if wnm in ("q", "k"):
                    stg = (o_qs if wnm == "q" else o_ks)[b2]
                    sfree = (qs_free if wnm == "q" else ks_free)[b2]
                    wq_ = [("tS", ct_all), ("dS", c_cs)]
                    s.op("vector", lambda e, bk=bk, bcol=bcol, b2=b2: e.scalar_tensor_tensor(
                        a32[:, o_rt1:o_rt1 + 512], pf[bk], bqkv[:, bcol:bcol + 1],
                        a32[:, o_cos[b2]:o_cos[b2] + 512], ALU.add, ALU.mult),
                        waits=tuple(wq_))
                    s.op("vector", lambda e, bk=bk, bcol=bcol, b2=b2: e.scalar_tensor_tensor(
                        a32[0:64, o_rt2:o_rt2 + 512], pf[bk][64:128, :],
                        bqkv[0:64, 6 + bcol:7 + bcol],
                        a32[0:64, o_sin[b2]:o_sin[b2] + 512], ALU.add, ALU.mult))
                    cv = s.op("vector", lambda e, bk=bk, bcol=bcol, b2=b2: e.scalar_tensor_tensor(
                        a32[64:128, o_rt2:o_rt2 + 512], pf[bk][0:64, :],
                        bqkv[64:128, 6 + bcol:7 + bcol],
                        a32[64:128, o_sin[b2]:o_sin[b2] + 512], ALU.add, ALU.mult),
                        incs=(("vS", 1),))
                    cos_free[b2] = ("vS", cv)
                    wadd = [("vS", cv)]
                    if sfree is not None:
                        wadd.append(sfree)
                    cvr = s.op("vector", lambda e, stg=stg: e.tensor_add(
                        a16[:, stg:stg + 512], a32[:, o_rt1:o_rt1 + 512],
                        a32[:, o_rt2:o_rt2 + 512]), waits=tuple(wadd), incs=(("vS", 1),))
                    dst = (q_dram if wnm == "q" else k_dram).ap()[h, :, tc * 512:(tc + 1) * 512]
                    gp_dma(dst, a16[:, stg:stg + 512], waits=(("vS", cvr),))
                    cdma = s.val("dG")
                    if wnm == "q":
                        qs_free[b2] = ("dG", cdma)
                    else:
                        ks_free[b2] = ("dG", cdma)
                    bank_free[bk] = ("vS", cv)
                else:
                    wv_ = [("tS", ct_all)]
                    if vs_free[b2] is not None:
                        wv_.append(vs_free[b2])
                    cse = s.op("scalar", lambda e, bk=bk, bcol=bcol, b2=b2: e.activation(
                        a16[:, o_vs[b2]:o_vs[b2] + 512], pf[bk], AF.Identity,
                        bias=bqkv[:, bcol:bcol + 1]), waits=tuple(wv_), incs=(("sS", 1),))
                    bank_free[bk] = ("sS", cse)
                    lastv = None
                    for sub in range(4):
                        pb = pt[sub % 2]
                        wt = [("sS", cse)]
                        if pt_free[sub % 2] is not None:
                            wt.append(pt_free[sub % 2])
                        ctv = s.op("tensor", lambda e, pb=pb, sub=sub, b2=b2: e.transpose(
                            pb[:, 0:128], a16[:, o_vs[b2] + sub * 128:o_vs[b2] + (sub + 1) * 128],
                            identbf[:, :]), waits=tuple(wt), incs=(("tS", 1),))
                        wcp = [("tS", ctv)]
                        if sub == 0 and vt_free[b2] is not None:
                            wcp.append(vt_free[b2])
                        lastv = s.op("vector", lambda e, pb=pb, sub=sub, b2=b2: e.tensor_copy(
                            a16[:, o_vt[b2] + sub * 128:o_vt[b2] + (sub + 1) * 128], pb[:, 0:128]),
                            waits=tuple(wcp), incs=(("vS", 1),))
                        pt_free[sub % 2] = ("vS", lastv)
                    dst = v_dram.ap()[h, tc * 512:(tc + 1) * 512, :].rearrange(
                        "(c p) d -> p c d", p=128)
                    gp_dma(dst, a16[:, o_vt[b2]:o_vt[b2] + 512].rearrange("p (c d) -> p c d", c=4),
                           waits=(("vS", lastv),))
                    vt_free[b2] = ("dG", s.val("dG"))
                    vs_free[b2] = ("tS", ctv)
        c_qkv_sp = s.val("dS")
        bcn = bar()
        nb_free = [None, None]
        norm_red = None
        nslice = 0
        for h in range(HL):
            for wnm, kcol in (("q", 0), ("k", 2)):
                srcd = (q_dram if wnm == "q" else k_dram).ap()
                for ch in range(2):
                    bb = nslice % 2
                    w = list(wall(bcn, ("tS", "dG"))) if nslice == 0 else []
                    if nb_free[bb] is not None:
                        w.append(nb_free[bb])
                    sync_dma(a16[:, bb * 4096:bb * 4096 + 4096],
                             srcd[h, :, ch * 4096:(ch + 1) * 4096], waits=tuple(w))
                    cd = s.val("dS")
                    lastn = None
                    for sub in range(2):
                        csq = s.op("vector", lambda e, bb=bb, sub=sub: e.tensor_mul(
                            a32[:, bb * 2048:bb * 2048 + 2048],
                            a16[:, bb * 4096 + sub * 2048:bb * 4096 + sub * 2048 + 2048],
                            a16[:, bb * 4096 + sub * 2048:bb * 4096 + sub * 2048 + 2048]),
                            waits=(("dS", cd),), incs=(("vS", 1),))
                        for sl in range(4):
                            pb = sl % 2
                            wm = [("vS", csq)] + list(bwait(pb))
                            ctn = s.op("tensor", lambda e, pb=pb, bb=bb, sl=sl: e.matmul(
                                pf[pb][0:1, :], onescolf[:, :],
                                a32[:, bb * 2048 + sl * 512:bb * 2048 + (sl + 1) * 512],
                                start=True, stop=True, skip_group_check=True),
                                waits=tuple(wm), incs=(("tS", 1),))
                            s.op("vector", lambda e, pb=pb: e.reduce_max(
                                st8[0:1, 10:11], pf[pb][0:1, :], axis=AX.X),
                                waits=(("tS", ctn), ("vS", c_kqm0)))
                            cvn = s.op("vector", lambda e, kcol=kcol, h=h: e.tensor_max(
                                kqm[0:1, kcol + h:kcol + h + 1],
                                kqm[0:1, kcol + h:kcol + h + 1], st8[0:1, 10:11]),
                                incs=(("vS", 1),))
                            bank_free[pb] = ("vS", cvn)
                            lastn = cvn
                    nb_free[bb] = ("vS", lastn)
                    norm_red = ("vS", lastn)
                    nslice += 1
        for h in range(HL):
            cv = s.op("vector", lambda e, h=h: e.tensor_mul(
                st8[0:1, 11:12], kqm[0:1, h:h + 1], kqm[0:1, 2 + h:3 + h]),
                waits=(norm_red,), incs=(("vS", 1),))
            s.op("scalar", lambda e: e.activation(st8[0:1, 12:13], st8[0:1, 11:12], AF.Ln),
                 waits=(("vS", cv),))
            cs = s.op("scalar", lambda e: e.activation(
                st8[0:1, 13:14], st8[0:1, 12:13], AF.Exp, scale=0.5, bias=fco[0:1, 1:2]),
                incs=(("sS", 1),))
            cv = s.op("vector", lambda e: e.tensor_scalar_mul(st8[0:1, 14:15], st8[0:1, 13:14], -1.0),
                      waits=(("sS", cs),), incs=(("vS", 1),))
            ctb = s.op("tensor", lambda e: e.matmul(
                pf[5][:, 0:1], onesrow[:, :], st8[0:1, 14:15],
                start=True, stop=True, skip_group_check=True),
                waits=bwait(5, (("vS", cv),)), incs=(("tS", 1),))
            cvm = s.op("vector", lambda e, h=h: e.tensor_copy(mhat[:, h:h + 1], pf[5][:, 0:1]),
                       waits=(("tS", ctb),), incs=(("vS", 1),))
            bank_free[5] = ("vS", cvm)
            norm_red = ("vS", cvm)
        c_mhat_v = s.val("vS")

        sh_dma = {}

        def sh_dma_of_factory(half):
            def dma_of(hc):
                ghc = half * 11 + hc
                return [("dG", sh_dma[("g", ghc)]), ("dG", sh_dma[("u", ghc)])]
            return dma_of

        def sh_prefetch_factory(half):
            def prefetch(hc, free_evt):
                nh = hc + 2
                if nh < 11 and ("g", half * 11 + nh) not in sh_dma:
                    sh_dma[("g", half * 11 + nh)] = w_chunk(
                        swg_a[half * 11 + nh], o_wgb[hc % 2], NDC * 128, (free_evt,))
                    sh_dma[("u", half * 11 + nh)] = w_chunk(
                        swu_a[half * 11 + nh], o_wub[hc % 2], NDC * 128, ())
            return prefetch

        def sh_getwd_factory(half):
            def getwd(d2c, b, w):
                return w_chunk(swd_a[d2c][:, half * 1408:(half + 1) * 1408], o_wdb[b],
                               11 * 128, w)
            return getwd

        bcP4 = bar()
        for hci, b in ((0, 0), (1, 1)):
            pw = wall(bcP4, ("tS", "dG")) if hci == 0 else ()
            sh_dma[("g", hci)] = w_chunk(swg_a[hci], o_wgb[b], NDC * 128, pw)
            sh_dma[("u", hci)] = w_chunk(swu_a[hci], o_wub[b], NDC * 128, ())
        gateup_phase(11, sh_dma_of_factory(0), sh_prefetch_factory(0), use_comb=False)
        down_phase(11, sh_getwd_factory(0), 11 * 128)

        bc3 = bar()
        o_q = [0, 512]
        o_K = [1024, 1536, 2048]
        o_V = [2560, 3072, 3584]
        o_PT = [4096, 4608, 5120, 5632]
        o_mask = 0
        o_B = 1024
        o_rl = 1536
        SBANKS = [0, 1, 5]
        sync_dma(a32[:, o_mask:o_mask + 1024], mask_d, waits=wall(bc3, ("vS", "sS", "tS")))
        c_mask = s.val("dS")

        q_dma = {}
        q_freed = [None, None]
        kv_dma = {}
        kv_freed = [None, None, None]
        PT_freed = [None] * 4
        first_att = (("dS", c_mask), ("vS", c_mhat_v)) + wall(bc3, ("sS",))
        first_att_used = False

        def kv_load(h, kb):
            bb = kb % 3
            w = []
            if kv_freed[bb] is not None:
                w.append(kv_freed[bb])
            sync_dma(a16[:, o_K[bb]:o_K[bb] + 512], k_dram.ap()[h, :, kb * 512:(kb + 1) * 512],
                     waits=tuple(w))
            sync_dma(a16[:, o_V[bb]:o_V[bb] + 512].rearrange("p (c d) -> p c d", c=4),
                     v_dram.ap()[h, kb * 512:(kb + 1) * 512, :].rearrange("(c p) d -> p c d", p=128))
            kv_dma[(h, kb)] = s.val("dS")

        for h in range(HL):
            for qb in range(16):
                qbb = qb % 2
                w = []
                if q_freed[qbb] is not None:
                    w.append(q_freed[qbb])
                sync_dma(a16[:, o_q[qbb]:o_q[qbb] + 512],
                         q_dram.ap()[h, :, qb * 512:(qb + 1) * 512], waits=tuple(w))
                q_dma[(h, qb)] = s.val("dS")
                units = [(kb, jc) for kb in range(qb + 1) for jc in range(4)]
                NUNITS = len(units)
                ctS = {}
                maskv = {}

                kv_dma.clear()

                def emit_S(i, h=h, qb=qb, qbb=qbb, units=units):
                    nonlocal first_att_used
                    kb, jc = units[i]
                    if jc == 0:
                        for kbl in (kb, kb + 1):
                            if kbl <= qb and (h, kbl) not in kv_dma:
                                kv_load(h, kbl)
                    sbk = SBANKS[i % 3]
                    wS = [("dS", kv_dma[(h, kb)]), ("dS", q_dma[(h, qb)])]
                    wS += list(bwait(sbk))
                    if not first_att_used:
                        wS += list(first_att)
                        first_att_used = True
                    c = s.op("tensor", lambda e, sbk=sbk, kb=kb, jc=jc, qbb=qbb: e.matmul(
                        pf[sbk], a16[:, o_K[kb % 3] + jc * 128:o_K[kb % 3] + (jc + 1) * 128],
                        a16[:, o_q[qbb]:o_q[qbb] + 512],
                        start=True, stop=True, skip_group_check=True),
                        waits=tuple(wS), incs=(("tS", 1),))
                    ctS[i] = c
                    if kb == qb:
                        maskv[i] = s.op("vector", lambda e, sbk=sbk, jc=jc: e.tensor_add(
                            pf[sbk], pf[sbk],
                            a32[:, o_mask + 512 - 128 * jc:o_mask + 1024 - 128 * jc]),
                            waits=(("tS", c),), incs=(("vS", 1),))

                def emit_exp(i, h=h):
                    sbk = SBANKS[i % 3]
                    wE = [("tS", ctS[i])]
                    if i in maskv:
                        wE.append(("vS", maskv[i]))
                    if PT_freed[i % 4] is not None:
                        wE.append(PT_freed[i % 4])
                    c = s.op("scalar", lambda e, sbk=sbk, i=i, h=h: e.activation(
                        a16[:, o_PT[i % 4]:o_PT[i % 4] + 512], pf[sbk], AF.Exp,
                        bias=mhat[:, h:h + 1], scale=ATT_SCALE),
                        waits=tuple(wE), incs=(("sS", 1),))
                    bank_free[sbk] = ("sS", c)
                    return c

                exp_of = {}
                pv_last = None

                def emit_PV(i, qb=qb, units=units):
                    nonlocal pv_last
                    kb, jc = units[i]
                    fst = (i == 0)
                    lst = (i == len(units) - 1)
                    wPV = [("sS", exp_of[i])]
                    if fst:
                        wPV += list(bwait(2)) + list(bwait(3))
                    s.op("tensor", lambda e, kb=kb, jc=jc, i=i, fst=fst, lst=lst: e.matmul(
                        pf[2], a16[:, o_V[kb % 3] + jc * 128:o_V[kb % 3] + (jc + 1) * 128],
                        a16[:, o_PT[i % 4]:o_PT[i % 4] + 512],
                        start=fst, stop=lst, skip_group_check=True),
                        waits=tuple(wPV))
                    c = s.op("tensor", lambda e, i=i, fst=fst, lst=lst: e.matmul(
                        pf[3][0:1, :], onescol[:, :], a16[:, o_PT[i % 4]:o_PT[i % 4] + 512],
                        start=fst, stop=lst, skip_group_check=True),
                        incs=(("tS", 1),))
                    PT_freed[i % 4] = ("tS", c)
                    pv_last = c
                    if jc == 3:
                        kv_freed[kb % 3] = ("tS", c)

                AHEAD = 2
                for i in range(min(AHEAD, NUNITS)):
                    emit_S(i)
                    exp_of[i] = emit_exp(i)
                for i in range(NUNITS):
                    if i + AHEAD < NUNITS:
                        emit_S(i + AHEAD)
                        exp_of[i + AHEAD] = emit_exp(i + AHEAD)
                    emit_PV(i)
                q_freed[qbb] = ("tS", pv_last)
                crl = s.op("vector", lambda e: e.reciprocal(
                    a32[0:1, o_rl:o_rl + 512], pf[3][0:1, :]),
                    waits=(("tS", pv_last),), incs=(("vS", 1),))
                bank_free[3] = ("vS", crl)
                ctB = s.op("tensor", lambda e: e.matmul(
                    pf[4], onesrow[:, :], a32[0:1, o_rl:o_rl + 512],
                    start=True, stop=True, skip_group_check=True),
                    waits=bwait(4, (("vS", crl),)), incs=(("tS", 1),))
                cvB = s.op("vector", lambda e: e.tensor_copy(a32[:, o_B:o_B + 512], pf[4]),
                           waits=(("tS", ctB),), incs=(("vS", 1),))
                bank_free[4] = ("vS", cvB)
                cvO = s.op("vector", lambda e, h=h, qb=qb: e.tensor_tensor(
                    AT[:, h * N_TOK + qb * 512: h * N_TOK + (qb + 1) * 512],
                    pf[2], a32[:, o_B:o_B + 512], ALU.mult),
                    waits=(("vS", cvB),), incs=(("vS", 1),))
                bank_free[2] = ("vS", cvO)
            c_ah = s.val("vS")
            gp_dma(a_bounce[h].ap(), AT[:, h * N_TOK:(h + 1) * N_TOK],
                   waits=(("vS", c_ah),))
            c_ab = s.val("dG")
            ccn = "cc2" if h == 0 else "cc3"
            s.op("gpsimd", lambda e, h=h: e.collective_compute(
                "AllGather", ALU.bypass, replica_groups=[list(range(NCORES))],
                ins=[a_bounce[h].ap().opt()], outs=[a_gath[h].ap().opt()]),
                waits=(("dG", c_ab),), incs=((ccn, 1),))
        c_att_v = s.val("vS")

        bcP6 = bar()
        for hci, b in ((0, 0), (1, 1)):
            pw = wall(bcP6, ("tS", "dG")) if hci == 0 else ()
            sh_dma[("g", 11 + hci)] = w_chunk(swg_a[11 + hci], o_wgb[b], NDC * 128, pw)
            sh_dma[("u", 11 + hci)] = w_chunk(swu_a[11 + hci], o_wub[b], NDC * 128, ())
        gateup_phase(11, sh_dma_of_factory(1), sh_prefetch_factory(1), use_comb=False)
        down_phase(11, sh_getwd_factory(1), 11 * 128)

        bc4 = bar()
        o_wo = [0, 2048]
        o_ag = [4096, 12288]
        o_ssb = [0, 512]
        o_st4 = [1024, 1536]
        wo_dma = {}
        wo_used = {}
        ag_dma = {}
        ag_free = [None, None]
        ssb_free = [None, None]
        st4_free = [None, None]
        pairs = [(tc, d2c) for tc in range(2) for d2c in range(NDC)]

        def wo_load(i, waits=()):
            bb = i % 2
            w = list(waits)
            if i - 2 in wo_used:
                w.append(wo_used[i - 2])
            gp_dma(a16[:, o_wo[bb]:o_wo[bb] + NDC * 128], wo_a[pairs[i][1]], waits=tuple(w))
            wo_dma[i] = s.val("dG")

        for i in range(2):
            wo_load(i, wall(bc4, ("tS", "dG")) if i == 0 else ())

        def ag_load(tc, waits):
            bb = tc % 2
            for hl in range(HL):
                w = list(waits) if hl == 0 else [("cc3" if hl else "cc2", 1)]
                if hl == 0 and ag_free[bb] is not None:
                    w.append(ag_free[bb])

                def fn(e, tc=tc, bb=bb, hl=hl):
                    pid = e.partition_id()
                    src = a_gath[hl].ap()[:, bass.ts(pid, TOK)][:, tc * 512:(tc + 1) * 512]
                    src = src.rearrange("(r p) t -> p r t", p=128)
                    dst = a16[:, o_ag[bb]:o_ag[bb] + NDC * 512].rearrange(
                        "p (r l t) -> p r l t", l=HL, t=512)[:, :, hl, :]
                    return e.dma_start(out=dst, in_=src)
                ag_dma[tc] = s.dma("sync", fn, waits=tuple(w))

        ag_load(0, (("cc2", 1), ("cc3", 1)) + wall(bc4, ("tS",)))
        ag_load(1, ())

        for i, (tc, d2c) in enumerate(pairs):
            agb = tc % 2
            if True:
                wbk = i % 2
                wfirst = [("dG", wo_dma[i]), ("dS", ag_dma[tc])] + list(bwait(wbk))
                for ic in range(NDC):
                    s.op("tensor", lambda e, wbk=wbk, ic=ic, agb=agb: e.matmul(
                        pf[wbk], a16[:, o_wo[wbk] + ic * 128:o_wo[wbk] + (ic + 1) * 128],
                        a16[:, o_ag[agb] + ic * 512:o_ag[agb] + (ic + 1) * 512],
                        start=(ic == 0), stop=(ic == NDC - 1), skip_group_check=True),
                        waits=tuple(wfirst) if ic == 0 else (),
                        incs=(("tS", 1),) if ic == NDC - 1 else ())
                    wfirst = []
                ctw = s.val("tS")
                wo_used[i] = ("tS", ctw)
                if i + 2 < len(pairs):
                    wo_load(i + 2)
                wsb = [("tS", ctw)]
                if ssb_free[d2c % 2] is not None:
                    wsb.append(ssb_free[d2c % 2])
                csb = s.op("vector", lambda e, wbk=wbk, d2c=d2c, tc=tc: e.scalar_tensor_tensor(
                    a32[:, o_ssb[d2c % 2]:o_ssb[d2c % 2] + 512], pf[wbk], bot[:, d2c:d2c + 1],
                    fa[:, d2c * TOK + tc * 512: d2c * TOK + tc * 512 + 512], ALU.add, ALU.add),
                    waits=tuple(wsb), incs=(("vS", 1),))
                bank_free[wbk] = ("vS", csb)
                lastc = None
                for sub in range(4):
                    tb = 2 + (sub % 2)
                    ctt = s.op("tensor", lambda e, tb=tb, d2c=d2c, sub=sub: e.transpose(
                        pf[tb][:, 0:128], a32[:, o_ssb[d2c % 2] + sub * 128:
                                              o_ssb[d2c % 2] + (sub + 1) * 128], identf[:, :]),
                        waits=bwait(tb, (("vS", csb),)), incs=(("tS", 1),))
                    wcp = [("tS", ctt)]
                    if sub == 0 and st4_free[d2c % 2] is not None:
                        wcp.append(st4_free[d2c % 2])
                    lastc = s.op("vector", lambda e, tb=tb, d2c=d2c, sub=sub: e.tensor_copy(
                        a32[:, o_st4[d2c % 2] + sub * 128:o_st4[d2c % 2] + (sub + 1) * 128],
                        pf[tb][:, 0:128]), waits=tuple(wcp), incs=(("vS", 1),))
                    bank_free[tb] = ("vS", lastc)
                ssb_free[d2c % 2] = ("vS", lastc)
                dst = out_own[tc * 512:(tc + 1) * 512, d2c * 128:(d2c + 1) * 128].rearrange(
                    "(c p) d -> p c d", p=128)
                gp_dma(dst, a32[:, o_st4[d2c % 2]:o_st4[d2c % 2] + 512].rearrange(
                    "p (c d) -> p c d", c=4),
                    waits=(("vS", lastc), ("dS", c_xcopy)), accum=True)
                st4_free[d2c % 2] = ("dG", s.val("dG"))
            ag_free[agb] = ("tS", s.val("tS"))


        if taps:
            bfin = bar()
            sync_dma(tap_AT, AT[:, :], waits=wall(bfin))
            sync_dma(tap_fa, fa[:, :])
            sync_dma(tap_comb, comb_rows[:, :])
            sync_dma(tap_mhat, mhat[:, :])
            sync_dma(tap_kqm, kqm[0:1, 0:8])
            sync_dma(tap_q, q_dram.ap())
            sync_dma(tap_k, k_dram.ap())
            sync_dma(tap_v, v_dram.ap())

        fin = bar()
        s.op("sync", lambda e: None, waits=wall(fin) + (("cc1", 1), ("cc2", 1), ("cc3", 1)))

        with nc.Block() as block:

            @block.sync
            def _(eng):
                s.emit("sync", eng, sems)

            @block.gpsimd
            def _(eng):
                s.emit("gpsimd", eng, sems)

            @block.tensor
            def _(eng):
                s.emit("tensor", eng, sems)

            @block.vector
            def _(eng):
                s.emit("vector", eng, sems)

            @block.scalar
            def _(eng):
                s.emit("scalar", eng, sems)

    return nc


_CACHE = {}


def _host_consts():
    inv = (1.0 / 10000.0 ** (np.arange(0, DH, 2, dtype=np.float32) / DH)).astype(np.float32)
    seq = np.arange(N_TOK, dtype=np.float32)
    fr = seq[:, None] * inv[None, :]
    pos = np.concatenate([fr, fr], axis=1).astype(np.float32)
    cos = np.cos(pos).T.copy()
    sinf = np.sin(pos)
    sign = np.concatenate([-np.ones(64, np.float32), np.ones(64, np.float32)])
    sin = (sinf * sign[None, :]).T.copy()
    mask = np.where(np.arange(1024)[None, :] >= np.arange(128)[:, None] + 512,
                    np.float32(0.0), np.float32(NEG)).astype(np.float32)
    onehot = np.zeros((8, E * 128), np.float32)
    for e in range(E):
        onehot[e, e * 128:(e + 1) * 128] = 1.0
    fconst = np.zeros((128, 4), np.float32)
    fconst[:, 0] = 1e-5
    fconst[:, 1] = math.log(1.02 * ATT_SCALE)
    return {
        "cos_t": cos, "sin_t": sin, "mask_d": mask,
        "identbf_d": np.eye(128, dtype=ml_dtypes.bfloat16),
        "identf_d": np.eye(128, dtype=np.float32),
        "onesrow_d": np.ones((1, 128), np.float32),
        "onescol_d": np.ones((128, 1), ml_dtypes.bfloat16),
        "onescolf_d": np.ones((128, 1), np.float32),
        "onehot_d": onehot, "fconst_d": fconst,
    }


def _pack(w, rows_chunks, cols_chunks):
    R, C = rows_chunks, cols_chunks
    return np.ascontiguousarray(
        w.reshape(R, 128, C, 128).transpose(2, 1, 0, 3).reshape(C, 128, R * 128))


def kernel(L0_attn_x, L0_attn_Wq, L0_attn_bq, L0_attn_Wk, L0_attn_bk, L0_attn_Wv,
           L0_attn_bv, L0_attn_Wo, L0_attn_bo, L0_ffn_gate_w, L0_ffn_w_gate,
           L0_ffn_w_up, L0_ffn_w_down, L0_ffn_sw_gate, L0_ffn_sw_up, L0_ffn_sw_down):
    key = ("prog", TAPS)
    if key not in _CACHE:
        _CACHE[key] = build_program(taps=TAPS)
    nc = _CACHE[key]

    x = np.asarray(L0_attn_x, dtype=np.float32).reshape(N_TOK, D)
    f32 = lambda a: np.asarray(a, dtype=np.float32)
    Wq, Wk, Wv, Wo = f32(L0_attn_Wq), f32(L0_attn_Wk), f32(L0_attn_Wv), f32(L0_attn_Wo)
    bq, bk, bv, bo = f32(L0_attn_bq), f32(L0_attn_bk), f32(L0_attn_bv), f32(L0_attn_bo)
    gate_w = f32(L0_ffn_gate_w)
    w_gate, w_up, w_down = f32(L0_ffn_w_gate), f32(L0_ffn_w_up), f32(L0_ffn_w_down)
    sw_gate, sw_up, sw_down = f32(L0_ffn_sw_gate), f32(L0_ffn_sw_up), f32(L0_ffn_sw_down)

    consts = _host_consts()
    gate_w_a = np.ascontiguousarray(
        gate_w.reshape(NDC, 128, E).transpose(1, 0, 2).reshape(128, NDC * E))
    wg_a = np.ascontiguousarray(
        w_gate.reshape(E, NDC, 128, NHC, 128).transpose(0, 3, 2, 1, 4).reshape(E, NHC, 128, NDC * 128))
    wu_a = np.ascontiguousarray(
        w_up.reshape(E, NDC, 128, NHC, 128).transpose(0, 3, 2, 1, 4).reshape(E, NHC, 128, NDC * 128))
    wd_a = np.ascontiguousarray(
        w_down.reshape(E, NHC, 128, NDC, 128).transpose(0, 3, 2, 1, 4).reshape(E, NDC, 128, NHC * 128))
    swg_a = _pack(sw_gate, NDC, NSC)
    swu_a = _pack(sw_up, NDC, NSC)
    swd_a = _pack(sw_down, NSC, NDC)
    wo_a = _pack(Wo, NDC, NDC)
    bo_t = np.ascontiguousarray(bo.reshape(NDC, 128).T)

    shared = dict(consts)
    shared.update({
        "gate_w_a": gate_w_a, "wg_a": wg_a, "wu_a": wu_a, "wd_a": wd_a,
        "swg_a": swg_a, "swu_a": swu_a, "swd_a": swd_a, "wo_a": wo_a, "bo_t": bo_t,
    })

    in_maps = []
    for c in range(NCORES):
        m = dict(shared)
        m["x_own"] = np.ascontiguousarray(x[c * TOK:(c + 1) * TOK, :])
        hsl = slice(c * 256, (c + 1) * 256)
        for nm, W in (("wq_own", Wq), ("wk_own", Wk), ("wv_own", Wv)):
            m[nm] = np.ascontiguousarray(
                W[:, hsl].reshape(NDC, 128, 256).transpose(1, 0, 2).reshape(128, NDC * 256))
        barr = np.zeros((128, 12), np.float32)
        for i, b in enumerate((bq, bk, bv)):
            bl = b[hsl].reshape(2, 128)
            barr[:, 2 * i + 0] = bl[0]
            barr[:, 2 * i + 1] = bl[1]
        barr[:, 6:12] = np.concatenate([barr[64:128, 0:6], barr[0:64, 0:6]], axis=0)
        m["b_qkv"] = barr
        in_maps.append(m)

    res = run_bass_kernel_spmd(nc, in_maps, core_ids=list(range(NCORES)))
    out = np.concatenate([res.results[c]["out_own"] for c in range(NCORES)], axis=0)
    if TAPS:
        kernel.last_results = res
    return out.reshape(1, N_TOK, D)


# version 10
# speedup vs baseline: 1.4191x; 1.0094x over previous
import math
import contextlib
import numpy as np
import ml_dtypes
import concourse.bass as bass
import concourse.mybir as mybir
from concourse.bass_utils import run_bass_kernel_spmd

F32 = mybir.dt.float32
BF16 = mybir.dt.bfloat16
AF = mybir.ActivationFunctionType
ALU = mybir.AluOpType
AX = mybir.AxisListType

N_TOK = 8192
D = 2048
DH = 128
NCORES = 8
TOK = N_TOK // NCORES
HL = 2
E = 8
NDC = D // 128
NHC = 11
NSC = 22
ATT_SCALE = DH ** -0.5
NEG = -1e30

ENGINES = ("sync", "gpsimd", "tensor", "vector", "scalar")
TAPS = False


DMA_SLOTS = 8


class Sched:
    def __init__(self):
        self.ops = {e: [] for e in ENGINES}
        self.cnt = {}
        self.dma_idx = {"sync": 0, "gpsimd": 0}
        self.dma_hist = {"dS": [], "dG": []}

    def dma(self, eng, fn, waits=()):
        vname = {"sync": "dS", "gpsimd": "dG"}[eng]
        j = self.dma_idx[eng]
        slot = j % DMA_SLOTS
        g = j // DMA_SLOTS + 1
        sname = f"{vname}{slot}"
        w = list(waits)
        if g > 1:
            w.append((sname, 16 * (g - 1)))
        self.dma_idx[eng] = j + 1
        self.dma_hist[vname].append(sname)
        self.cnt[vname] = self.cnt.get(vname, 0) + 16
        self.op(eng, fn, waits=tuple(w), incs=((sname, 16),))
        return self.cnt[vname]

    def _xlate(self, ss, v):
        if ss not in ("dS", "dG"):
            return ((ss, v),)
        k = v // 16
        hist = self.dma_hist[ss][:k]
        per = {}
        for sn in hist:
            per[sn] = per.get(sn, 0) + 16
        return tuple(per.items())

    def val(self, sem):
        return self.cnt.get(sem, 0)

    SELF = {"vector": "vS", "scalar": "sS"}

    def op(self, eng, fn, waits=(), incs=()):
        for ss, v in waits:
            if not (ss[0] == "d" and len(ss) > 2):
                assert v <= self.val(ss), f"wait on future: {ss} {v} > {self.val(ss)}"
        sse = self.SELF.get(eng)
        n_orig = len(tuple(incs))
        if sse is not None and not any(ss == sse for ss, _ in incs):
            incs = tuple(incs) + ((sse, 1),)
        self.ops[eng].append((tuple(waits), fn, tuple(incs)))
        out = []
        for i, (ss, d) in enumerate(incs):
            self.cnt[ss] = self.cnt.get(ss, 0) + d
            if i < n_orig or n_orig == 0:
                out.append(self.cnt[ss])
        return out[0] if len(out) == 1 else out

    def emit(self, eng_name, eng, sems):
        last = {}
        sse = self.SELF.get(eng_name)
        self_c = 0
        for waits, fn, incs in self.ops[eng_name]:
            if sse is not None and self_c > 0:
                eng.wait_ge(sems[sse], self_c)
                last[sse] = max(last.get(sse, 0), self_c)
            for ss0, v0 in waits:
                for ss, v in self._xlate(ss0, v0):
                    if last.get(ss, 0) < v:
                        eng.wait_ge(sems[ss], v)
                        last[ss] = v
            inst = fn(eng)
            for ss, d in incs:
                if ss.startswith("cc"):
                    inst.then_inc(sems[ss])
                else:
                    inst.then_inc(sems[ss], d)
                if ss == sse:
                    self_c += d


def build_program(taps=False):
    nc = bass.Bass("TRN2", target_bir_lowering=False)
    dt = nc.dram_tensor

    x_own = dt("x_own", [TOK, D], F32, kind="ExternalInput").ap()
    wq_own = dt("wq_own", [128, NDC * 256], F32, kind="ExternalInput").ap()
    wk_own = dt("wk_own", [128, NDC * 256], F32, kind="ExternalInput").ap()
    wv_own = dt("wv_own", [128, NDC * 256], F32, kind="ExternalInput").ap()
    b_qkv = dt("b_qkv", [128, 12], F32, kind="ExternalInput").ap()
    gate_w_a = dt("gate_w_a", [128, NDC * E], F32, kind="ExternalInput").ap()
    wg_a = dt("wg_a", [E, NHC, 128, NDC * 128], F32, kind="ExternalInput").ap()
    wu_a = dt("wu_a", [E, NHC, 128, NDC * 128], F32, kind="ExternalInput").ap()
    wd_a = dt("wd_a", [E, NDC, 128, NHC * 128], F32, kind="ExternalInput").ap()
    swg_a = dt("swg_a", [NSC, 128, NDC * 128], F32, kind="ExternalInput").ap()
    swu_a = dt("swu_a", [NSC, 128, NDC * 128], F32, kind="ExternalInput").ap()
    swd_a = dt("swd_a", [NDC, 128, NSC * 128], F32, kind="ExternalInput").ap()
    wo_a = dt("wo_a", [NDC, 128, NDC * 128], F32, kind="ExternalInput").ap()
    bo_t = dt("bo_t", [128, NDC], F32, kind="ExternalInput").ap()
    cos_t = dt("cos_t", [128, N_TOK], F32, kind="ExternalInput").ap()
    sin_t = dt("sin_t", [128, N_TOK], F32, kind="ExternalInput").ap()
    mask_d = dt("mask_d", [128, 1024], F32, kind="ExternalInput").ap()
    identbf_d = dt("identbf_d", [128, 128], BF16, kind="ExternalInput").ap()
    identf_d = dt("identf_d", [128, 128], F32, kind="ExternalInput").ap()
    onesrow_d = dt("onesrow_d", [1, 128], F32, kind="ExternalInput").ap()
    onescol_d = dt("onescol_d", [128, 1], BF16, kind="ExternalInput").ap()
    onescolf_d = dt("onescolf_d", [128, 1], F32, kind="ExternalInput").ap()
    onehot_d = dt("onehot_d", [8, E * 128], F32, kind="ExternalInput").ap()
    fconst_d = dt("fconst_d", [128, 4], F32, kind="ExternalInput").ap()
    out_own = dt("out_own", [TOK, D], F32, kind="ExternalOutput").ap()

    xn_bounce = dt("xn_bounce", [D, TOK], BF16)
    xn_gath = dt("xn_gath", [NCORES * D, TOK], BF16)
    a_bounce = [dt(f"a_bounce{h}", [128, N_TOK], BF16) for h in range(HL)]
    a_gath = [dt(f"a_gath{h}", [NCORES * 128, N_TOK], BF16) for h in range(HL)]
    q_dram = dt("q_dram", [HL, 128, N_TOK], BF16)
    k_dram = dt("k_dram", [HL, 128, N_TOK], BF16)
    v_dram = dt("v_dram", [HL, N_TOK, 128], BF16)
    if taps:
        tap_xnT = dt("tap_xnT", [128, NDC * TOK], BF16, kind="ExternalOutput").ap()
        tap_AT = dt("tap_AT", [128, HL * N_TOK], BF16, kind="ExternalOutput").ap()
        tap_fa = dt("tap_fa", [128, NDC * TOK], BF16, kind="ExternalOutput").ap()
        tap_comb = dt("tap_comb", [8, TOK], F32, kind="ExternalOutput").ap()
        tap_mhat = dt("tap_mhat", [128, HL], F32, kind="ExternalOutput").ap()
        tap_kqm = dt("tap_kqm", [1, 8], F32, kind="ExternalOutput").ap()
        tap_st8 = dt("tap_st8", [128, 40], F32, kind="ExternalOutput").ap()
        tap_q = dt("tap_q", [HL, 128, N_TOK], BF16, kind="ExternalOutput").ap()
        tap_k = dt("tap_k", [HL, 128, N_TOK], BF16, kind="ExternalOutput").ap()
        tap_v = dt("tap_v", [HL, N_TOK, 128], BF16, kind="ExternalOutput").ap()

    s = Sched()

    with contextlib.ExitStack() as es:
        sb = lambda name, shape, dtp: es.enter_context(nc.sbuf_tensor(name, shape, dtp))
        psmk = lambda name, shape, dtp: es.enter_context(nc.psum_tensor(name, shape, dtp))

        xnT = sb("xnT", [128, NDC * TOK], BF16)
        AT = sb("AT", [128, HL * N_TOK], BF16)
        fa = sb("fa", [128, NDC * TOK], BF16)
        comb_rows = sb("comb_rows", [8, TOK], F32)
        identbf = sb("identbf", [128, 128], BF16)
        identf = sb("identf", [128, 128], F32)
        onesrow = sb("onesrow", [1, 128], F32)
        onescol = sb("onescol", [128, 1], BF16)
        onescolf = sb("onescolf", [128, 1], F32)
        bqkv = sb("bqkv", [128, 12], F32)
        bot = sb("bot", [128, NDC], F32)
        gw = sb("gw", [128, NDC * E], BF16)
        mhat = sb("mhat", [128, HL], F32)
        fco = sb("fco", [128, 4], F32)
        st8 = sb("st8", [128, 40], F32)
        kqm = sb("kqm", [1, 8], F32)
        a16 = sb("a16", [128, 30720], BF16)
        a32 = sb("a32", [128, 4608], F32)

        pfh = [psmk(f"pf{i}", [128, 512], F32) for i in range(6)]
        pf = [h[:, :] for h in pfh]
        pt = [psmk(f"pt{i}", [128, 1024], BF16) for i in range(2)]
        bank_free = [None] * 6
        pt_free = [None] * 2

        sems = {}
        for nm in (["cc1", "cc2", "cc3", "tS", "vS", "sS"]
                   + [f"dS{i}" for i in range(DMA_SLOTS)]
                   + [f"dG{i}" for i in range(DMA_SLOTS)]):
            sems[nm] = es.enter_context(nc.semaphore(nm))

        def sync_dma(dst, src, waits=()):
            return s.dma("sync", lambda e, d=dst, sr=src: e.dma_start(out=d, in_=sr),
                         waits=waits)

        def gp_dma(dst, src, waits=(), accum=False):
            if accum:
                return s.dma("gpsimd",
                             lambda e, d=dst, sr=src: e.dma_start(out=d, in_=sr, accum_op=ALU.add),
                             waits=waits)
            return s.dma("gpsimd", lambda e, d=dst, sr=src: e.dma_start(out=d, in_=sr),
                         waits=waits)

        def bar():
            return {nm: s.val(nm) for nm in ("dS", "dG", "tS", "vS", "sS")}

        def wall(bc, only=None):
            return tuple((nm, v) for nm, v in bc.items()
                         if v > 0 and (only is None or nm in only))

        def bwait(i, extra=()):
            w = list(extra)
            if bank_free[i] is not None:
                w.append(bank_free[i])
            return tuple(w)

        sync_dma(identbf[:, :], identbf_d)
        sync_dma(identf[:, :], identf_d)
        sync_dma(onesrow[:, :], onesrow_d)
        sync_dma(onescol[:, :], onescol_d)
        sync_dma(onescolf[:, :], onescolf_d)
        sync_dma(bqkv[:, :], b_qkv)
        sync_dma(bot[:, :], bo_t)
        sync_dma(fco[:, :], fconst_d)
        c_consts = s.val("dS")
        gp_dma(gw[:, :], gate_w_a)
        c_gw = s.val("dG")

        NTT = TOK // 128
        o_xt = [0, 2048]
        o_xn = [24576, 26624]
        x_free = [None, None]
        xn_free = [None, None]
        ln_dma = []
        for tt in range(NTT):
            b = tt % 2
            w = [("dS", c_consts)]
            if x_free[b] is not None:
                w.append(x_free[b])
            sync_dma(a32[:, o_xt[b]:o_xt[b] + 2048], x_own[tt * 128:(tt + 1) * 128, :], waits=tuple(w))
            ln_dma.append(s.val("dS"))
            if tt == 1:
                break
        for tt in range(NTT):
            b = tt % 2
            if tt >= 2:
                w2 = [("dS", c_consts), x_free[b]]
                sync_dma(a32[:, o_xt[b]:o_xt[b] + 2048], x_own[tt * 128:(tt + 1) * 128, :],
                         waits=tuple(w2))
                ln_dma.append(s.val("dS"))
            xt = a32[:, o_xt[b]:o_xt[b] + 2048]
            xn = a16[:, o_xn[b]:o_xn[b] + 2048]
            w = [("dS", ln_dma[tt])]
            if xn_free[b] is not None:
                w.append(xn_free[b])
            B0 = (tt % 2) * 16
            stc = lambda i: st8[:, B0 + i:B0 + i + 1]
            s.op("vector", lambda e, xt=xt, stc=stc: e.reduce_sum(stc(0), xt, axis=AX.X),
                 waits=tuple(w))
            xsq = a16[:, o_xn[b] - 24576 + 20480:o_xn[b] - 24576 + 20480 + 2048]
            cssq = s.op("scalar", lambda e, xt=xt, xsq=xsq: e.activation(
                xsq, xt, AF.Square), waits=tuple(w), incs=(("sS", 1),))
            s.op("vector", lambda e, xsq=xsq, stc=stc: e.reduce_sum(stc(1), xsq, axis=AX.X),
                 waits=(("sS", cssq),))
            s.op("vector", lambda e, stc=stc: e.tensor_scalar_mul(stc(2), stc(0), 1.0 / D))
            s.op("vector", lambda e, stc=stc: e.tensor_mul(stc(3), stc(2), stc(2)))
            cv = s.op("vector", lambda e, stc=stc: e.scalar_tensor_tensor(
                stc(4), stc(1), 1.0 / D, stc(3), ALU.mult, ALU.subtract),
                incs=(("vS", 1),))
            s.op("scalar", lambda e, stc=stc: e.activation(stc(5), stc(4), AF.Ln, bias=fco[:, 0:1]),
                 waits=(("vS", cv),))
            cs = s.op("scalar", lambda e, stc=stc: e.activation(stc(6), stc(5), AF.Exp, scale=-0.5),
                      incs=(("sS", 1),))
            cv2 = s.op("vector", lambda e, stc=stc: e.tensor_mul(stc(8), stc(2), stc(6)),
                       waits=(("sS", cs),))
            cv2 = s.op("vector", lambda e, stc=stc: e.tensor_scalar_mul(stc(9), stc(8), -1.0),
                       incs=(("vS", 1),))
            cs2 = s.op("scalar", lambda e, xt=xt, xn=xn, stc=stc: e.activation(
                xn, xt, AF.Identity, bias=stc(9), scale=stc(6)),
                waits=(("vS", cv2),), incs=(("sS", 1),))
            x_free[b] = ("sS", cs2)
            lastv = None
            for dc in range(NDC):
                pb = pt[dc % 2]
                wt = [("sS", cs2)]
                if pt_free[dc % 2] is not None:
                    wt.append(pt_free[dc % 2])
                ct = s.op("tensor", lambda e, pb=pb, xn=xn, dc=dc: e.transpose(
                    pb[:, 0:128], xn[:, dc * 128:(dc + 1) * 128], identbf[:, :]),
                    waits=tuple(wt), incs=(("tS", 1),))
                lastv = s.op("vector", lambda e, pb=pb, dc=dc, tt=tt: e.tensor_copy(
                    xnT[:, dc * TOK + tt * 128: dc * TOK + (tt + 1) * 128], pb[:, 0:128]),
                    waits=(("tS", ct),), incs=(("vS", 1),))
                pt_free[dc % 2] = ("vS", lastv)
            xn_free[b] = ("vS", lastv)
        c_ln_v = s.val("vS")
        gp_dma(xn_bounce.ap().rearrange("(c p) t -> p c t", p=128),
               xnT[:, :].rearrange("p (c t) -> p c t", c=NDC), waits=(("vS", c_ln_v),))
        c_xnb = s.val("dG")
        if taps:
            sync_dma(tap_xnT, xnT[:, :])
            sync_dma(tap_st8, st8[:, :])
        s.op("gpsimd", lambda e: e.collective_compute(
            "AllGather", ALU.bypass, replica_groups=[list(range(NCORES))],
            ins=[xn_bounce.ap().opt()], outs=[xn_gath.ap().opt()]),
            waits=(("dG", c_xnb),), incs=(("cc1", 1),))
        sync_dma(out_own, x_own)
        c_xcopy = s.val("dS")

        bc0 = bar()
        for tt in range(NTT):
            bi = tt % 2
            wfirst = bwait(bi, wall(bc0) + (("dG", c_gw),))
            for dc in range(NDC):
                sp = (dc == NDC - 1)
                inc = (("tS", 1),) if sp else ()
                s.op("tensor", lambda e, bi=bi, dc=dc, tt=tt, sp=sp: e.matmul(
                    pf[bi][:, 0:8], xnT[:, dc * TOK + tt * 128: dc * TOK + (tt + 1) * 128],
                    gw[:, dc * 8:(dc + 1) * 8], start=(dc == 0), stop=sp,
                    skip_group_check=True),
                    waits=wfirst if dc == 0 else (), incs=inc)
                if dc == 0:
                    wfirst = ()
            ct = s.val("tS")
            B0 = (tt % 2) * 16
            stc = lambda i: st8[:, B0 + i:B0 + i + 1]
            ga = (tt % 2) * 128
            s.op("vector", lambda e, bi=bi, stc=stc: e.reduce_max(stc(0), pf[bi][:, 0:8], axis=AX.X),
                 waits=(("tS", ct),))
            cm = s.op("vector", lambda e, stc=stc: e.tensor_scalar_mul(stc(1), stc(0), -1.0),
                      incs=(("vS", 1),))
            ce = s.op("scalar", lambda e, bi=bi, stc=stc, ga=ga: e.activation(
                a32[:, ga + 0:ga + 8], pf[bi][:, 0:8], AF.Exp, bias=stc(1), accum_out=stc(2)),
                waits=(("vS", cm),), incs=(("sS", 1),))
            bank_free[bi] = ("sS", ce)
            cr = s.op("vector", lambda e, stc=stc: e.reciprocal(stc(3), stc(2)),
                      waits=(("sS", ce),), incs=(("vS", 1),))
            cn = s.op("scalar", lambda e, stc=stc, ga=ga: e.activation(
                a32[:, ga + 32:ga + 40], a32[:, ga + 0:ga + 8], AF.Identity, scale=stc(3)),
                waits=(("vS", cr),), incs=(("sS", 1),))
            scs = a32[:, ga + 32:ga + 40]
            s.op("vector", lambda e, scs=scs, stc=stc: e.reduce_max(stc(4), scs, axis=AX.X),
                 waits=(("sS", cn),))
            s.op("vector", lambda e, scs=scs, stc=stc, ga=ga: e.scalar_tensor_tensor(
                a32[:, ga + 64:ga + 72], scs, stc(4), scs, ALU.is_lt, ALU.mult))
            s.op("vector", lambda e, stc=stc, ga=ga: e.reduce_max(stc(5), a32[:, ga + 64:ga + 72], axis=AX.X))
            cv = s.op("vector", lambda e, scs=scs, stc=stc, ga=ga: e.scalar_tensor_tensor(
                a32[:, ga + 96:ga + 104], scs, stc(5), scs, ALU.is_ge, ALU.mult),
                incs=(("vS", 1),))
            ctr = s.op("tensor", lambda e, ga=ga: e.transpose(
                pf[2][0:8, 0:128], a32[:, ga + 96:ga + 104], identf[:, :]),
                waits=bwait(2, (("vS", cv),)), incs=(("tS", 1),))
            cvc = s.op("vector", lambda e, tt=tt: e.tensor_copy(
                comb_rows[0:8, tt * 128:(tt + 1) * 128], pf[2][0:8, 0:128]),
                waits=(("tS", ctr),), incs=(("vS", 1),))
            bank_free[2] = ("vS", cvc)
        c_comb_v = s.val("vS")

        o_wgb = [0, 2048]
        o_wub = [4096, 6144]
        o_wdb = [8192, 9600]
        o_G = 11008
        o_sil = [22272, 22784]
        o_t1 = [23296, 23808]
        o_cB = [0, 512]
        o_oh = 1024
        bc1 = bar()
        sync_dma(a32[0:8, o_oh:o_oh + E * 128], onehot_d, waits=wall(bc1, ("vS", "sS")))
        c_oh = s.val("dS")

        wgu_free = [None, None]
        wd_free = [None, None]
        sil_free = [None, None]
        moe_dma = {}

        def w_chunk(src, off, ln, waits):
            gp_dma(a16[:, off:off + ln], src, waits=waits)
            return s.val("dG")

        for (ei, hci, b) in ((0, 0, 0), (0, 1, 1)):
            moe_dma[("g", ei, hci)] = w_chunk(wg_a[ei, hci], o_wgb[b], NDC * 128, ())
            moe_dma[("u", ei, hci)] = w_chunk(wu_a[ei, hci], o_wub[b], NDC * 128, ())

        first_moe = wall(bc1) + (("vS", c_comb_v), ("dS", c_oh))

        def gateup_phase(nchunks, dma_of, prefetch, use_comb=True, first=()):
            lastG = None
            tS0 = ("tS", s.val("tS"))
            for hc in range(nchunks):
                bks = [(4 * hc + i) % 6 for i in range(4)]
                wdma = dma_of(hc)
                for tc in range(2):
                    gb, ub = bks[2 * tc], bks[2 * tc + 1]
                    wfirst = list(wdma) + list(bwait(gb))
                    if hc == 0 and tc == 0:
                        wfirst += list(first)
                    for dc in range(NDC):
                        s.op("tensor", lambda e, gb=gb, dc=dc, tc=tc, hc=hc: e.matmul(
                            pf[gb], a16[:, o_wgb[hc % 2] + dc * 128:o_wgb[hc % 2] + (dc + 1) * 128],
                            xnT[:, dc * TOK + tc * 512: dc * TOK + tc * 512 + 512],
                            start=(dc == 0), stop=(dc == NDC - 1), skip_group_check=True),
                            waits=tuple(wfirst) if dc == 0 else (),
                            incs=(("tS", 1),) if dc == NDC - 1 else ())
                        wfirst = []
                    ctg = s.val("tS")
                    wfirst = list(bwait(ub))
                    for dc in range(NDC):
                        s.op("tensor", lambda e, ub=ub, dc=dc, tc=tc, hc=hc: e.matmul(
                            pf[ub], a16[:, o_wub[hc % 2] + dc * 128:o_wub[hc % 2] + (dc + 1) * 128],
                            xnT[:, dc * TOK + tc * 512: dc * TOK + tc * 512 + 512],
                            start=(dc == 0), stop=(dc == NDC - 1), skip_group_check=True),
                            waits=tuple(wfirst) if dc == 0 else (),
                            incs=(("tS", 1),) if dc == NDC - 1 else ())
                        wfirst = []
                    ctu = s.val("tS")
                    sb_, tb_ = o_sil[tc], o_t1[tc]
                    wsil = [("tS", ctg)]
                    if sil_free[tc] is not None:
                        wsil.append(sil_free[tc])
                    cs = s.op("scalar", lambda e, gb=gb, sb_=sb_: e.activation(
                        a16[:, sb_:sb_ + 512], pf[gb], AF.Silu),
                        waits=tuple(wsil), incs=(("sS", 1),))
                    bank_free[gb] = ("sS", cs)
                    if use_comb:
                        cv1 = s.op("vector", lambda e, ub=ub, tb_=tb_, tc=tc: e.tensor_tensor(
                            a16[:, tb_:tb_ + 512], pf[ub], a32[:, o_cB[tc]:o_cB[tc] + 512],
                            ALU.mult), waits=(("tS", ctu),), incs=(("vS", 1),))
                        bank_free[ub] = ("vS", cv1)
                        lastG = s.op("vector", lambda e, sb_=sb_, tb_=tb_, hc=hc, tc=tc: e.tensor_mul(
                            a16[:, o_G + hc * 1024 + tc * 512:o_G + hc * 1024 + tc * 512 + 512],
                            a16[:, tb_:tb_ + 512], a16[:, sb_:sb_ + 512]),
                            waits=(("sS", cs), tS0), incs=(("vS", 1),))
                        sil_free[tc] = ("vS", lastG)
                    else:
                        lastG = s.op("vector", lambda e, ub=ub, sb_=sb_, hc=hc, tc=tc: e.tensor_tensor(
                            a16[:, o_G + hc * 1024 + tc * 512:o_G + hc * 1024 + tc * 512 + 512],
                            pf[ub], a16[:, sb_:sb_ + 512], ALU.mult),
                            waits=(("tS", ctu), ("sS", cs), tS0), incs=(("vS", 1),))
                        bank_free[ub] = ("vS", lastG)
                        sil_free[tc] = ("vS", lastG)
                wgu_free[hc % 2] = ("tS", ctu)
                prefetch(hc, wgu_free[hc % 2])
            return lastG

        def down_phase(nchunks, getwd, wd_len, first_copy=False):
            cG = s.val("vS")
            dma_d = {}
            for d2c in range(2):
                w = [wd_free[d2c % 2]] if wd_free[d2c % 2] is not None else []
                dma_d[d2c] = getwd(d2c, d2c % 2, tuple(w))
            for d2c in range(NDC):
                bwf = o_wdb[d2c % 2]
                ctd_last = None
                for tc in range(2):
                    dbk = 4 + tc
                    w = [("dG", dma_d[d2c]), ("vS", cG)] + list(bwait(dbk))
                    for hc in range(nchunks):
                        s.op("tensor", lambda e, dbk=dbk, bwf=bwf, hc=hc, tc=tc: e.matmul(
                            pf[dbk], a16[:, bwf + hc * 128:bwf + (hc + 1) * 128],
                            a16[:, o_G + hc * 1024 + tc * 512:o_G + hc * 1024 + tc * 512 + 512],
                            start=(hc == 0), stop=(hc == nchunks - 1), skip_group_check=True),
                            waits=tuple(w) if hc == 0 else (),
                            incs=(("tS", 1),) if hc == nchunks - 1 else ())
                        w = []
                    ctd = s.val("tS")
                    ctd_last = ctd
                    if first_copy:
                        cv = s.op("vector", lambda e, dbk=dbk, d2c=d2c, tc=tc: e.tensor_copy(
                            fa[:, d2c * TOK + tc * 512: d2c * TOK + tc * 512 + 512], pf[dbk]),
                            waits=(("tS", ctd),), incs=(("vS", 1),))
                    else:
                        cv = s.op("vector", lambda e, dbk=dbk, d2c=d2c, tc=tc: e.tensor_add(
                            fa[:, d2c * TOK + tc * 512: d2c * TOK + tc * 512 + 512],
                            pf[dbk], fa[:, d2c * TOK + tc * 512: d2c * TOK + tc * 512 + 512]),
                            waits=(("tS", ctd),), incs=(("vS", 1),))
                    bank_free[dbk] = ("vS", cv)
                wd_free[d2c % 2] = ("tS", ctd_last)
                if d2c + 2 < NDC:
                    dma_d[d2c + 2] = getwd(d2c + 2, d2c % 2, (wd_free[d2c % 2],))

        for e_i in range(E):
            for tc in range(2):
                w = list(bwait(4))
                if e_i == 0 and tc == 0:
                    w += list(first_moe)
                ct = s.op("tensor", lambda e, e_i=e_i, tc=tc: e.matmul(
                    pf[4], a32[0:8, o_oh + e_i * 128:o_oh + (e_i + 1) * 128],
                    comb_rows[0:8, tc * 512:(tc + 1) * 512], start=True, stop=True,
                    skip_group_check=True),
                    waits=tuple(w), incs=(("tS", 1),))
                cv = s.op("vector", lambda e, tc=tc: e.tensor_copy(
                    a32[:, o_cB[tc]:o_cB[tc] + 512], pf[4]),
                    waits=(("tS", ct),), incs=(("vS", 1),))
                bank_free[4] = ("vS", cv)

            def dma_of(hc, e_i=e_i):
                return [("dG", moe_dma[("g", e_i, hc)]), ("dG", moe_dma[("u", e_i, hc)])]

            def prefetch(hc, free_evt, e_i=e_i):
                if hc + 2 < NHC:
                    nh = hc + 2
                    if ("g", e_i, nh) not in moe_dma:
                        moe_dma[("g", e_i, nh)] = w_chunk(wg_a[e_i, nh], o_wgb[nh % 2],
                                                          NDC * 128, (free_evt,))
                        moe_dma[("u", e_i, nh)] = w_chunk(wu_a[e_i, nh], o_wub[nh % 2],
                                                          NDC * 128, ())
                elif hc == NHC - 1 and e_i + 1 < E:
                    for nh in (0, 1):
                        moe_dma[("g", e_i + 1, nh)] = w_chunk(
                            wg_a[e_i + 1, nh], o_wgb[nh % 2], NDC * 128,
                            (wgu_free[nh % 2],))
                        moe_dma[("u", e_i + 1, nh)] = w_chunk(
                            wu_a[e_i + 1, nh], o_wub[nh % 2], NDC * 128, ())

            gateup_phase(NHC, dma_of, prefetch, use_comb=True,
                         first=first_moe if e_i == 0 else ())

            def getwd(d2c, b, w, e_i=e_i):
                return w_chunk(wd_a[e_i, d2c], o_wdb[b], NHC * 128, w)

            down_phase(NHC, getwd, NHC * 128, first_copy=(e_i == 0))

        bc2 = bar()
        o_xg = [0, 4096, 8192]
        o_wq, o_wk, o_wv = 12288, 16384, 20480
        o_qs = [24576, 25088]
        o_ks = [25600, 26112]
        o_vs = [26624, 27136, 27648, 28160]
        o_vt = [28672, 29184]
        o_cos = [0, 512]
        o_sin = [1024, 1536]
        o_rt1, o_rt2 = 2048, 2560
        gp_dma(a16[:, o_wq:o_wq + NDC * 256], wq_own, waits=wall({"tS": bc2["tS"], "dG": bc2["dG"]}))
        gp_dma(a16[:, o_wk:o_wk + NDC * 256], wk_own)
        gp_dma(a16[:, o_wv:o_wv + NDC * 256], wv_own)
        c_wqkv = s.val("dG")
        cvz = s.op("vector", lambda e: e.memset(kqm[0:1, 0:8], 0.0), waits=wall(bc2, ("sS",)),
                   incs=(("vS", 1),))
        c_kqm0 = s.val("vS")

        NU = 2 * NTT * 2
        xg_dma = {}
        xg_used = {}

        def xg_load(u, waits):
            tc, half = u // 2, u % 2
            r = tc // 2
            l0 = (tc % 2) * 512
            rows0 = r * D + half * 1024
            src = xn_gath.ap()[rows0:rows0 + 1024, l0:l0 + 512].rearrange("(c p) t -> p c t", p=128)
            off = o_xg[u % 3]
            sync_dma(a16[:, off:off + 4096].rearrange("p (c t) -> p c t", c=8), src, waits=waits)
            xg_dma[u] = s.val("dS")

        for u in range(3):
            xg_load(u, wall(bc2, ("vS", "tS")) + (("cc1", 1),))

        bank_map = [("q", 0, 0), ("k", 0, 1), ("v", 0, 2), ("q", 1, 3), ("k", 1, 4), ("v", 1, 5)]
        cos_free = [None, None]
        qs_free = [None, None]
        ks_free = [None, None]
        vs_free = [None, None, None, None]
        vt_free = [None, None]
        norm_red = None
        deferred_v = []

        def emit_deferred_v():
            nonlocal deferred_v
            for (h, tc, vsl, cse) in deferred_v:
                b2 = vsl // 2
                lastv = None
                ctv = None
                for sub in range(4):
                    pb = pt[sub % 2]
                    wt = [("sS", cse)]
                    if pt_free[sub % 2] is not None:
                        wt.append(pt_free[sub % 2])
                    ctv = s.op("tensor", lambda e, pb=pb, sub=sub, vsl=vsl: e.transpose(
                        pb[:, 0:128], a16[:, o_vs[vsl] + sub * 128:o_vs[vsl] + (sub + 1) * 128],
                        identbf[:, :]), waits=tuple(wt), incs=(("tS", 1),))
                    wcp = [("tS", ctv)]
                    if sub == 0 and vt_free[b2] is not None:
                        wcp.append(vt_free[b2])
                    lastv = s.op("vector", lambda e, pb=pb, sub=sub, b2=b2: e.tensor_copy(
                        a16[:, o_vt[b2] + sub * 128:o_vt[b2] + (sub + 1) * 128], pb[:, 0:128]),
                        waits=tuple(wcp), incs=(("vS", 1),))
                    pt_free[sub % 2] = ("vS", lastv)
                dst = v_dram.ap()[h, tc * 512:(tc + 1) * 512, :].rearrange(
                    "(c p) d -> p c d", p=128)
                gp_dma(dst, a16[:, o_vt[b2]:o_vt[b2] + 512].rearrange("p (c d) -> p c d", c=4),
                       waits=(("vS", lastv),))
                vt_free[b2] = ("dG", s.val("dG"))
                vs_free[vsl] = ("tS", ctv)
            deferred_v = []

        for tc in range(2 * NTT):
            b2 = tc % 2
            w = [("dS", c_consts)]
            if cos_free[b2] is not None:
                w.append(cos_free[b2])
            else:
                w += list(wall(bc2, ("vS", "tS")))
            sync_dma(a32[:, o_cos[b2]:o_cos[b2] + 512], cos_t[:, tc * 512:(tc + 1) * 512],
                     waits=tuple(w))
            sync_dma(a32[:, o_sin[b2]:o_sin[b2] + 512], sin_t[:, tc * 512:(tc + 1) * 512])
            c_cs = s.val("dS")
            for half in range(2):
                u = tc * 2 + half
                whalf = [("dG", c_wqkv), ("dS", xg_dma[u])]
                for wnm, h, bk in bank_map:
                    woff = {"q": o_wq, "k": o_wk, "v": o_wv}[wnm]
                    wfirst = list(whalf)
                    if half == 0:
                        wfirst += list(bwait(bk))
                    for dci in range(8):
                        dc = half * 8 + dci
                        s.op("tensor", lambda e, bk=bk, woff=woff, dc=dc, h=h, u=u: e.matmul(
                            pf[bk], a16[:, woff + dc * 256 + h * 128: woff + dc * 256 + h * 128 + 128],
                            a16[:, o_xg[u % 3] + (dc % 8) * 512: o_xg[u % 3] + (dc % 8) * 512 + 512],
                            start=(dc == 0), stop=(dc == NDC - 1), skip_group_check=True),
                            waits=tuple(wfirst) if dci == 0 else (),
                            incs=(("tS", 1),) if (dci == 7) else ())
                        wfirst = []
                xg_used[u] = s.val("tS")
                if u + 3 < NU:
                    xg_load(u + 3, (("tS", xg_used[u]),))
            ct_all = s.val("tS")
            emit_deferred_v()
            for wnm, h, bk in bank_map:
                bcol = {"q": 0, "k": 2, "v": 4}[wnm] + h
                if wnm in ("q", "k"):
                    stg = (o_qs if wnm == "q" else o_ks)[b2]
                    sfree = (qs_free if wnm == "q" else ks_free)[b2]
                    wq_ = [("tS", ct_all), ("dS", c_cs)]
                    s.op("vector", lambda e, bk=bk, bcol=bcol, b2=b2: e.scalar_tensor_tensor(
                        a32[:, o_rt1:o_rt1 + 512], pf[bk], bqkv[:, bcol:bcol + 1],
                        a32[:, o_cos[b2]:o_cos[b2] + 512], ALU.add, ALU.mult),
                        waits=tuple(wq_))
                    s.op("vector", lambda e, bk=bk, bcol=bcol, b2=b2: e.scalar_tensor_tensor(
                        a32[0:64, o_rt2:o_rt2 + 512], pf[bk][64:128, :],
                        bqkv[0:64, 6 + bcol:7 + bcol],
                        a32[0:64, o_sin[b2]:o_sin[b2] + 512], ALU.add, ALU.mult))
                    cv = s.op("vector", lambda e, bk=bk, bcol=bcol, b2=b2: e.scalar_tensor_tensor(
                        a32[64:128, o_rt2:o_rt2 + 512], pf[bk][0:64, :],
                        bqkv[64:128, 6 + bcol:7 + bcol],
                        a32[64:128, o_sin[b2]:o_sin[b2] + 512], ALU.add, ALU.mult),
                        incs=(("vS", 1),))
                    cos_free[b2] = ("vS", cv)
                    wadd = [("vS", cv)]
                    if sfree is not None:
                        wadd.append(sfree)
                    cvr = s.op("vector", lambda e, stg=stg: e.tensor_add(
                        a16[:, stg:stg + 512], a32[:, o_rt1:o_rt1 + 512],
                        a32[:, o_rt2:o_rt2 + 512]), waits=tuple(wadd), incs=(("vS", 1),))
                    dst = (q_dram if wnm == "q" else k_dram).ap()[h, :, tc * 512:(tc + 1) * 512]
                    gp_dma(dst, a16[:, stg:stg + 512], waits=(("vS", cvr),))
                    cdma = s.val("dG")
                    if wnm == "q":
                        qs_free[b2] = ("dG", cdma)
                    else:
                        ks_free[b2] = ("dG", cdma)
                    bank_free[bk] = ("vS", cv)
                else:
                    wv_ = [("tS", ct_all)]
                    if vs_free[2 * b2 + h] is not None:
                        wv_.append(vs_free[2 * b2 + h])
                    vsl = 2 * b2 + h
                    cse = s.op("scalar", lambda e, bk=bk, bcol=bcol, vsl=vsl: e.activation(
                        a16[:, o_vs[vsl]:o_vs[vsl] + 512], pf[bk], AF.Identity,
                        bias=bqkv[:, bcol:bcol + 1]), waits=tuple(wv_), incs=(("sS", 1),))
                    bank_free[bk] = ("sS", cse)
                    deferred_v.append((h, tc, vsl, cse))
        emit_deferred_v()
        c_qkv_sp = s.val("dS")
        bcn = bar()
        nb_free = [None, None]
        norm_red = None
        nslice = 0
        for h in range(HL):
            for wnm, kcol in (("q", 0), ("k", 2)):
                srcd = (q_dram if wnm == "q" else k_dram).ap()
                for ch in range(2):
                    bb = nslice % 2
                    w = list(wall(bcn, ("tS", "dG"))) if nslice == 0 else []
                    if nb_free[bb] is not None:
                        w.append(nb_free[bb])
                    sync_dma(a16[:, bb * 4096:bb * 4096 + 4096],
                             srcd[h, :, ch * 4096:(ch + 1) * 4096], waits=tuple(w))
                    cd = s.val("dS")
                    lastn = None
                    for sub in range(2):
                        csq = s.op("vector", lambda e, bb=bb, sub=sub: e.tensor_mul(
                            a32[:, bb * 2048:bb * 2048 + 2048],
                            a16[:, bb * 4096 + sub * 2048:bb * 4096 + sub * 2048 + 2048],
                            a16[:, bb * 4096 + sub * 2048:bb * 4096 + sub * 2048 + 2048]),
                            waits=(("dS", cd),), incs=(("vS", 1),))
                        for sl in range(4):
                            pb = sl % 2
                            wm = [("vS", csq)] + list(bwait(pb))
                            ctn = s.op("tensor", lambda e, pb=pb, bb=bb, sl=sl: e.matmul(
                                pf[pb][0:1, :], onescolf[:, :],
                                a32[:, bb * 2048 + sl * 512:bb * 2048 + (sl + 1) * 512],
                                start=True, stop=True, skip_group_check=True),
                                waits=tuple(wm), incs=(("tS", 1),))
                            s.op("vector", lambda e, pb=pb: e.reduce_max(
                                st8[0:1, 10:11], pf[pb][0:1, :], axis=AX.X),
                                waits=(("tS", ctn), ("vS", c_kqm0)))
                            cvn = s.op("vector", lambda e, kcol=kcol, h=h: e.tensor_max(
                                kqm[0:1, kcol + h:kcol + h + 1],
                                kqm[0:1, kcol + h:kcol + h + 1], st8[0:1, 10:11]),
                                incs=(("vS", 1),))
                            bank_free[pb] = ("vS", cvn)
                            lastn = cvn
                    nb_free[bb] = ("vS", lastn)
                    norm_red = ("vS", lastn)
                    nslice += 1
        for h in range(HL):
            cv = s.op("vector", lambda e, h=h: e.tensor_mul(
                st8[0:1, 11:12], kqm[0:1, h:h + 1], kqm[0:1, 2 + h:3 + h]),
                waits=(norm_red,), incs=(("vS", 1),))
            s.op("scalar", lambda e: e.activation(st8[0:1, 12:13], st8[0:1, 11:12], AF.Ln),
                 waits=(("vS", cv),))
            cs = s.op("scalar", lambda e: e.activation(
                st8[0:1, 13:14], st8[0:1, 12:13], AF.Exp, scale=0.5, bias=fco[0:1, 1:2]),
                incs=(("sS", 1),))
            cv = s.op("vector", lambda e: e.tensor_scalar_mul(st8[0:1, 14:15], st8[0:1, 13:14], -1.0),
                      waits=(("sS", cs),), incs=(("vS", 1),))
            ctb = s.op("tensor", lambda e: e.matmul(
                pf[5][:, 0:1], onesrow[:, :], st8[0:1, 14:15],
                start=True, stop=True, skip_group_check=True),
                waits=bwait(5, (("vS", cv),)), incs=(("tS", 1),))
            cvm = s.op("vector", lambda e, h=h: e.tensor_copy(mhat[:, h:h + 1], pf[5][:, 0:1]),
                       waits=(("tS", ctb),), incs=(("vS", 1),))
            bank_free[5] = ("vS", cvm)
            norm_red = ("vS", cvm)
        c_mhat_v = s.val("vS")

        sh_dma = {}

        def sh_dma_of_factory(half):
            def dma_of(hc):
                ghc = half * 11 + hc
                return [("dG", sh_dma[("g", ghc)]), ("dG", sh_dma[("u", ghc)])]
            return dma_of

        def sh_prefetch_factory(half):
            def prefetch(hc, free_evt):
                nh = hc + 2
                if nh < 11 and ("g", half * 11 + nh) not in sh_dma:
                    sh_dma[("g", half * 11 + nh)] = w_chunk(
                        swg_a[half * 11 + nh], o_wgb[hc % 2], NDC * 128, (free_evt,))
                    sh_dma[("u", half * 11 + nh)] = w_chunk(
                        swu_a[half * 11 + nh], o_wub[hc % 2], NDC * 128, ())
            return prefetch

        def sh_getwd_factory(half):
            def getwd(d2c, b, w):
                return w_chunk(swd_a[d2c][:, half * 1408:(half + 1) * 1408], o_wdb[b],
                               11 * 128, w)
            return getwd

        bcP4 = bar()
        for hci, b in ((0, 0), (1, 1)):
            pw = wall(bcP4, ("tS", "dG")) if hci == 0 else ()
            sh_dma[("g", hci)] = w_chunk(swg_a[hci], o_wgb[b], NDC * 128, pw)
            sh_dma[("u", hci)] = w_chunk(swu_a[hci], o_wub[b], NDC * 128, ())
        gateup_phase(11, sh_dma_of_factory(0), sh_prefetch_factory(0), use_comb=False)
        down_phase(11, sh_getwd_factory(0), 11 * 128)

        bc3 = bar()
        o_q = [0, 512]
        o_K = [1024, 1536, 2048]
        o_V = [2560, 3072, 3584]
        o_PT = [4096, 4608, 5120, 5632]
        o_mask = 0
        o_B = 1024
        o_rl = 1536
        SBANKS = [0, 1, 5]
        sync_dma(a32[:, o_mask:o_mask + 1024], mask_d, waits=wall(bc3, ("vS", "sS", "tS")))
        c_mask = s.val("dS")

        q_dma = {}
        q_freed = [None, None]
        kv_dma = {}
        kv_freed = [None, None, None]
        PT_freed = [None] * 4
        first_att = (("dS", c_mask), ("vS", c_mhat_v)) + wall(bc3, ("sS",))
        first_att_used = False

        def kv_load(h, kb):
            bb = kb % 3
            w = []
            if kv_freed[bb] is not None:
                w.append(kv_freed[bb])
            sync_dma(a16[:, o_K[bb]:o_K[bb] + 512], k_dram.ap()[h, :, kb * 512:(kb + 1) * 512],
                     waits=tuple(w))
            sync_dma(a16[:, o_V[bb]:o_V[bb] + 512].rearrange("p (c d) -> p c d", c=4),
                     v_dram.ap()[h, kb * 512:(kb + 1) * 512, :].rearrange("(c p) d -> p c d", p=128))
            kv_dma[(h, kb)] = s.val("dS")

        for h in range(HL):
            for qb in range(16):
                qbb = qb % 2
                w = []
                if q_freed[qbb] is not None:
                    w.append(q_freed[qbb])
                sync_dma(a16[:, o_q[qbb]:o_q[qbb] + 512],
                         q_dram.ap()[h, :, qb * 512:(qb + 1) * 512], waits=tuple(w))
                q_dma[(h, qb)] = s.val("dS")
                units = [(kb, jc) for kb in range(qb + 1) for jc in range(4)]
                NUNITS = len(units)
                ctS = {}
                maskv = {}

                kv_dma.clear()

                def emit_S(i, h=h, qb=qb, qbb=qbb, units=units):
                    nonlocal first_att_used
                    kb, jc = units[i]
                    if jc == 0:
                        for kbl in (kb, kb + 1):
                            if kbl <= qb and (h, kbl) not in kv_dma:
                                kv_load(h, kbl)
                    sbk = SBANKS[i % 3]
                    wS = [("dS", kv_dma[(h, kb)]), ("dS", q_dma[(h, qb)])]
                    wS += list(bwait(sbk))
                    if not first_att_used:
                        wS += list(first_att)
                        first_att_used = True
                    c = s.op("tensor", lambda e, sbk=sbk, kb=kb, jc=jc, qbb=qbb: e.matmul(
                        pf[sbk], a16[:, o_K[kb % 3] + jc * 128:o_K[kb % 3] + (jc + 1) * 128],
                        a16[:, o_q[qbb]:o_q[qbb] + 512],
                        start=True, stop=True, skip_group_check=True),
                        waits=tuple(wS), incs=(("tS", 1),))
                    ctS[i] = c
                    if kb == qb:
                        maskv[i] = s.op("vector", lambda e, sbk=sbk, jc=jc: e.tensor_add(
                            pf[sbk], pf[sbk],
                            a32[:, o_mask + 512 - 128 * jc:o_mask + 1024 - 128 * jc]),
                            waits=(("tS", c),), incs=(("vS", 1),))

                def emit_exp(i, h=h):
                    sbk = SBANKS[i % 3]
                    wE = [("tS", ctS[i])]
                    if i in maskv:
                        wE.append(("vS", maskv[i]))
                    if PT_freed[i % 4] is not None:
                        wE.append(PT_freed[i % 4])
                    c = s.op("scalar", lambda e, sbk=sbk, i=i, h=h: e.activation(
                        a16[:, o_PT[i % 4]:o_PT[i % 4] + 512], pf[sbk], AF.Exp,
                        bias=mhat[:, h:h + 1], scale=ATT_SCALE),
                        waits=tuple(wE), incs=(("sS", 1),))
                    bank_free[sbk] = ("sS", c)
                    return c

                exp_of = {}
                pv_last = None

                def emit_PV(i, qb=qb, units=units):
                    nonlocal pv_last
                    kb, jc = units[i]
                    fst = (i == 0)
                    lst = (i == len(units) - 1)
                    wPV = [("sS", exp_of[i])]
                    if fst:
                        wPV += list(bwait(2)) + list(bwait(3))
                    s.op("tensor", lambda e, kb=kb, jc=jc, i=i, fst=fst, lst=lst: e.matmul(
                        pf[2], a16[:, o_V[kb % 3] + jc * 128:o_V[kb % 3] + (jc + 1) * 128],
                        a16[:, o_PT[i % 4]:o_PT[i % 4] + 512],
                        start=fst, stop=lst, skip_group_check=True),
                        waits=tuple(wPV))
                    c = s.op("tensor", lambda e, i=i, fst=fst, lst=lst: e.matmul(
                        pf[3][0:1, :], onescol[:, :], a16[:, o_PT[i % 4]:o_PT[i % 4] + 512],
                        start=fst, stop=lst, skip_group_check=True),
                        incs=(("tS", 1),))
                    PT_freed[i % 4] = ("tS", c)
                    pv_last = c
                    if jc == 3:
                        kv_freed[kb % 3] = ("tS", c)

                AHEAD = 2
                for i in range(min(AHEAD, NUNITS)):
                    emit_S(i)
                    exp_of[i] = emit_exp(i)
                for i in range(NUNITS):
                    if i + AHEAD < NUNITS:
                        emit_S(i + AHEAD)
                        exp_of[i + AHEAD] = emit_exp(i + AHEAD)
                    emit_PV(i)
                q_freed[qbb] = ("tS", pv_last)
                crl = s.op("vector", lambda e: e.reciprocal(
                    a32[0:1, o_rl:o_rl + 512], pf[3][0:1, :]),
                    waits=(("tS", pv_last),), incs=(("vS", 1),))
                bank_free[3] = ("vS", crl)
                ctB = s.op("tensor", lambda e: e.matmul(
                    pf[4], onesrow[:, :], a32[0:1, o_rl:o_rl + 512],
                    start=True, stop=True, skip_group_check=True),
                    waits=bwait(4, (("vS", crl),)), incs=(("tS", 1),))
                cvB = s.op("vector", lambda e: e.tensor_copy(a32[:, o_B:o_B + 512], pf[4]),
                           waits=(("tS", ctB),), incs=(("vS", 1),))
                bank_free[4] = ("vS", cvB)
                cvO = s.op("vector", lambda e, h=h, qb=qb: e.tensor_tensor(
                    AT[:, h * N_TOK + qb * 512: h * N_TOK + (qb + 1) * 512],
                    pf[2], a32[:, o_B:o_B + 512], ALU.mult),
                    waits=(("vS", cvB),), incs=(("vS", 1),))
                bank_free[2] = ("vS", cvO)
            c_ah = s.val("vS")
            gp_dma(a_bounce[h].ap(), AT[:, h * N_TOK:(h + 1) * N_TOK],
                   waits=(("vS", c_ah),))
            c_ab = s.val("dG")
            ccn = "cc2" if h == 0 else "cc3"
            s.op("gpsimd", lambda e, h=h: e.collective_compute(
                "AllGather", ALU.bypass, replica_groups=[list(range(NCORES))],
                ins=[a_bounce[h].ap().opt()], outs=[a_gath[h].ap().opt()]),
                waits=(("dG", c_ab),), incs=((ccn, 1),))
        c_att_v = s.val("vS")

        bcP6 = bar()
        for hci, b in ((0, 0), (1, 1)):
            pw = wall(bcP6, ("tS", "dG")) if hci == 0 else ()
            sh_dma[("g", 11 + hci)] = w_chunk(swg_a[11 + hci], o_wgb[b], NDC * 128, pw)
            sh_dma[("u", 11 + hci)] = w_chunk(swu_a[11 + hci], o_wub[b], NDC * 128, ())
        gateup_phase(11, sh_dma_of_factory(1), sh_prefetch_factory(1), use_comb=False)
        down_phase(11, sh_getwd_factory(1), 11 * 128)

        bc4 = bar()
        o_wo = [0, 2048]
        o_ag = [4096, 12288]
        o_ssb = [0, 512]
        o_st4 = [1024, 1536]
        wo_dma = {}
        wo_used = {}
        ag_dma = {}
        ag_free = [None, None]
        ssb_free = [None, None]
        st4_free = [None, None]
        pairs = [(tc, d2c) for tc in range(2) for d2c in range(NDC)]

        def wo_load(i, waits=()):
            bb = i % 2
            w = list(waits)
            if i - 2 in wo_used:
                w.append(wo_used[i - 2])
            gp_dma(a16[:, o_wo[bb]:o_wo[bb] + NDC * 128], wo_a[pairs[i][1]], waits=tuple(w))
            wo_dma[i] = s.val("dG")

        for i in range(2):
            wo_load(i, wall(bc4, ("tS", "dG")) if i == 0 else ())

        def ag_load(tc, waits):
            bb = tc % 2
            for hl in range(HL):
                w = list(waits) if hl == 0 else [("cc3" if hl else "cc2", 1)]
                if hl == 0 and ag_free[bb] is not None:
                    w.append(ag_free[bb])

                def fn(e, tc=tc, bb=bb, hl=hl):
                    pid = e.partition_id()
                    src = a_gath[hl].ap()[:, bass.ts(pid, TOK)][:, tc * 512:(tc + 1) * 512]
                    src = src.rearrange("(r p) t -> p r t", p=128)
                    dst = a16[:, o_ag[bb]:o_ag[bb] + NDC * 512].rearrange(
                        "p (r l t) -> p r l t", l=HL, t=512)[:, :, hl, :]
                    return e.dma_start(out=dst, in_=src)
                ag_dma[tc] = s.dma("sync", fn, waits=tuple(w))

        ag_load(0, (("cc2", 1), ("cc3", 1)) + wall(bc4, ("tS",)))
        ag_load(1, ())

        for i, (tc, d2c) in enumerate(pairs):
            agb = tc % 2
            if True:
                wbk = i % 2
                wfirst = [("dG", wo_dma[i]), ("dS", ag_dma[tc])] + list(bwait(wbk))
                for ic in range(NDC):
                    s.op("tensor", lambda e, wbk=wbk, ic=ic, agb=agb: e.matmul(
                        pf[wbk], a16[:, o_wo[wbk] + ic * 128:o_wo[wbk] + (ic + 1) * 128],
                        a16[:, o_ag[agb] + ic * 512:o_ag[agb] + (ic + 1) * 512],
                        start=(ic == 0), stop=(ic == NDC - 1), skip_group_check=True),
                        waits=tuple(wfirst) if ic == 0 else (),
                        incs=(("tS", 1),) if ic == NDC - 1 else ())
                    wfirst = []
                ctw = s.val("tS")
                wo_used[i] = ("tS", ctw)
                if i + 2 < len(pairs):
                    wo_load(i + 2)
                wsb = [("tS", ctw)]
                if ssb_free[d2c % 2] is not None:
                    wsb.append(ssb_free[d2c % 2])
                csb = s.op("vector", lambda e, wbk=wbk, d2c=d2c, tc=tc: e.scalar_tensor_tensor(
                    a32[:, o_ssb[d2c % 2]:o_ssb[d2c % 2] + 512], pf[wbk], bot[:, d2c:d2c + 1],
                    fa[:, d2c * TOK + tc * 512: d2c * TOK + tc * 512 + 512], ALU.add, ALU.add),
                    waits=tuple(wsb), incs=(("vS", 1),))
                bank_free[wbk] = ("vS", csb)
                lastc = None
                for sub in range(4):
                    tb = 2 + (sub % 2)
                    ctt = s.op("tensor", lambda e, tb=tb, d2c=d2c, sub=sub: e.transpose(
                        pf[tb][:, 0:128], a32[:, o_ssb[d2c % 2] + sub * 128:
                                              o_ssb[d2c % 2] + (sub + 1) * 128], identf[:, :]),
                        waits=bwait(tb, (("vS", csb),)), incs=(("tS", 1),))
                    wcp = [("tS", ctt)]
                    if sub == 0 and st4_free[d2c % 2] is not None:
                        wcp.append(st4_free[d2c % 2])
                    lastc = s.op("vector", lambda e, tb=tb, d2c=d2c, sub=sub: e.tensor_copy(
                        a32[:, o_st4[d2c % 2] + sub * 128:o_st4[d2c % 2] + (sub + 1) * 128],
                        pf[tb][:, 0:128]), waits=tuple(wcp), incs=(("vS", 1),))
                    bank_free[tb] = ("vS", lastc)
                ssb_free[d2c % 2] = ("vS", lastc)
                dst = out_own[tc * 512:(tc + 1) * 512, d2c * 128:(d2c + 1) * 128].rearrange(
                    "(c p) d -> p c d", p=128)
                gp_dma(dst, a32[:, o_st4[d2c % 2]:o_st4[d2c % 2] + 512].rearrange(
                    "p (c d) -> p c d", c=4),
                    waits=(("vS", lastc), ("dS", c_xcopy)), accum=True)
                st4_free[d2c % 2] = ("dG", s.val("dG"))
            ag_free[agb] = ("tS", s.val("tS"))


        if taps:
            bfin = bar()
            sync_dma(tap_AT, AT[:, :], waits=wall(bfin))
            sync_dma(tap_fa, fa[:, :])
            sync_dma(tap_comb, comb_rows[:, :])
            sync_dma(tap_mhat, mhat[:, :])
            sync_dma(tap_kqm, kqm[0:1, 0:8])
            sync_dma(tap_q, q_dram.ap())
            sync_dma(tap_k, k_dram.ap())
            sync_dma(tap_v, v_dram.ap())

        fin = bar()
        s.op("sync", lambda e: None, waits=wall(fin) + (("cc1", 1), ("cc2", 1), ("cc3", 1)))

        with nc.Block() as block:

            @block.sync
            def _(eng):
                s.emit("sync", eng, sems)

            @block.gpsimd
            def _(eng):
                s.emit("gpsimd", eng, sems)

            @block.tensor
            def _(eng):
                s.emit("tensor", eng, sems)

            @block.vector
            def _(eng):
                s.emit("vector", eng, sems)

            @block.scalar
            def _(eng):
                s.emit("scalar", eng, sems)

    return nc


_CACHE = {}


def _host_consts():
    inv = (1.0 / 10000.0 ** (np.arange(0, DH, 2, dtype=np.float32) / DH)).astype(np.float32)
    seq = np.arange(N_TOK, dtype=np.float32)
    fr = seq[:, None] * inv[None, :]
    pos = np.concatenate([fr, fr], axis=1).astype(np.float32)
    cos = np.cos(pos).T.copy()
    sinf = np.sin(pos)
    sign = np.concatenate([-np.ones(64, np.float32), np.ones(64, np.float32)])
    sin = (sinf * sign[None, :]).T.copy()
    mask = np.where(np.arange(1024)[None, :] >= np.arange(128)[:, None] + 512,
                    np.float32(0.0), np.float32(NEG)).astype(np.float32)
    onehot = np.zeros((8, E * 128), np.float32)
    for e in range(E):
        onehot[e, e * 128:(e + 1) * 128] = 1.0
    fconst = np.zeros((128, 4), np.float32)
    fconst[:, 0] = 1e-5
    fconst[:, 1] = math.log(1.02 * ATT_SCALE)
    return {
        "cos_t": cos, "sin_t": sin, "mask_d": mask,
        "identbf_d": np.eye(128, dtype=ml_dtypes.bfloat16),
        "identf_d": np.eye(128, dtype=np.float32),
        "onesrow_d": np.ones((1, 128), np.float32),
        "onescol_d": np.ones((128, 1), ml_dtypes.bfloat16),
        "onescolf_d": np.ones((128, 1), np.float32),
        "onehot_d": onehot, "fconst_d": fconst,
    }


def _pack(w, rows_chunks, cols_chunks):
    R, C = rows_chunks, cols_chunks
    return np.ascontiguousarray(
        w.reshape(R, 128, C, 128).transpose(2, 1, 0, 3).reshape(C, 128, R * 128))


def kernel(L0_attn_x, L0_attn_Wq, L0_attn_bq, L0_attn_Wk, L0_attn_bk, L0_attn_Wv,
           L0_attn_bv, L0_attn_Wo, L0_attn_bo, L0_ffn_gate_w, L0_ffn_w_gate,
           L0_ffn_w_up, L0_ffn_w_down, L0_ffn_sw_gate, L0_ffn_sw_up, L0_ffn_sw_down):
    key = ("prog", TAPS)
    if key not in _CACHE:
        _CACHE[key] = build_program(taps=TAPS)
    nc = _CACHE[key]

    x = np.asarray(L0_attn_x, dtype=np.float32).reshape(N_TOK, D)
    f32 = lambda a: np.asarray(a, dtype=np.float32)
    Wq, Wk, Wv, Wo = f32(L0_attn_Wq), f32(L0_attn_Wk), f32(L0_attn_Wv), f32(L0_attn_Wo)
    bq, bk, bv, bo = f32(L0_attn_bq), f32(L0_attn_bk), f32(L0_attn_bv), f32(L0_attn_bo)
    gate_w = f32(L0_ffn_gate_w)
    w_gate, w_up, w_down = f32(L0_ffn_w_gate), f32(L0_ffn_w_up), f32(L0_ffn_w_down)
    sw_gate, sw_up, sw_down = f32(L0_ffn_sw_gate), f32(L0_ffn_sw_up), f32(L0_ffn_sw_down)

    consts = _host_consts()
    gate_w_a = np.ascontiguousarray(
        gate_w.reshape(NDC, 128, E).transpose(1, 0, 2).reshape(128, NDC * E))
    wg_a = np.ascontiguousarray(
        w_gate.reshape(E, NDC, 128, NHC, 128).transpose(0, 3, 2, 1, 4).reshape(E, NHC, 128, NDC * 128))
    wu_a = np.ascontiguousarray(
        w_up.reshape(E, NDC, 128, NHC, 128).transpose(0, 3, 2, 1, 4).reshape(E, NHC, 128, NDC * 128))
    wd_a = np.ascontiguousarray(
        w_down.reshape(E, NHC, 128, NDC, 128).transpose(0, 3, 2, 1, 4).reshape(E, NDC, 128, NHC * 128))
    swg_a = _pack(sw_gate, NDC, NSC)
    swu_a = _pack(sw_up, NDC, NSC)
    swd_a = _pack(sw_down, NSC, NDC)
    wo_a = _pack(Wo, NDC, NDC)
    bo_t = np.ascontiguousarray(bo.reshape(NDC, 128).T)

    shared = dict(consts)
    shared.update({
        "gate_w_a": gate_w_a, "wg_a": wg_a, "wu_a": wu_a, "wd_a": wd_a,
        "swg_a": swg_a, "swu_a": swu_a, "swd_a": swd_a, "wo_a": wo_a, "bo_t": bo_t,
    })

    in_maps = []
    for c in range(NCORES):
        m = dict(shared)
        m["x_own"] = np.ascontiguousarray(x[c * TOK:(c + 1) * TOK, :])
        hsl = slice(c * 256, (c + 1) * 256)
        for nm, W in (("wq_own", Wq), ("wk_own", Wk), ("wv_own", Wv)):
            m[nm] = np.ascontiguousarray(
                W[:, hsl].reshape(NDC, 128, 256).transpose(1, 0, 2).reshape(128, NDC * 256))
        barr = np.zeros((128, 12), np.float32)
        for i, b in enumerate((bq, bk, bv)):
            bl = b[hsl].reshape(2, 128)
            barr[:, 2 * i + 0] = bl[0]
            barr[:, 2 * i + 1] = bl[1]
        barr[:, 6:12] = np.concatenate([barr[64:128, 0:6], barr[0:64, 0:6]], axis=0)
        m["b_qkv"] = barr
        in_maps.append(m)

    res = run_bass_kernel_spmd(nc, in_maps, core_ids=list(range(NCORES)))
    out = np.concatenate([res.results[c]["out_own"] for c in range(NCORES)], axis=0)
    if TAPS:
        kernel.last_results = res
    return out.reshape(1, N_TOK, D)


# version 11
# speedup vs baseline: 1.4248x; 1.0040x over previous
import math
import contextlib
import numpy as np
import ml_dtypes
import concourse.bass as bass
import concourse.mybir as mybir
from concourse.bass_utils import run_bass_kernel_spmd

F32 = mybir.dt.float32
BF16 = mybir.dt.bfloat16
AF = mybir.ActivationFunctionType
ALU = mybir.AluOpType
AX = mybir.AxisListType

N_TOK = 8192
D = 2048
DH = 128
NCORES = 8
TOK = N_TOK // NCORES
HL = 2
E = 8
NDC = D // 128
NHC = 11
NSC = 22
ATT_SCALE = DH ** -0.5
NEG = -1e30

ENGINES = ("sync", "gpsimd", "tensor", "vector", "scalar")
TAPS = False


DMA_SLOTS = 8


class Sched:
    def __init__(self):
        self.ops = {e: [] for e in ENGINES}
        self.cnt = {}
        self.dma_idx = {"sync": 0, "gpsimd": 0}
        self.dma_hist = {"dS": [], "dG": []}

    def dma(self, eng, fn, waits=()):
        vname = {"sync": "dS", "gpsimd": "dG"}[eng]
        j = self.dma_idx[eng]
        slot = j % DMA_SLOTS
        g = j // DMA_SLOTS + 1
        sname = f"{vname}{slot}"
        w = list(waits)
        if g > 1:
            w.append((sname, 16 * (g - 1)))
        self.dma_idx[eng] = j + 1
        self.dma_hist[vname].append(sname)
        self.cnt[vname] = self.cnt.get(vname, 0) + 16
        self.op(eng, fn, waits=tuple(w), incs=((sname, 16),))
        return self.cnt[vname]

    def _xlate(self, ss, v):
        if ss not in ("dS", "dG"):
            return ((ss, v),)
        k = v // 16
        hist = self.dma_hist[ss][:k]
        per = {}
        for sn in hist:
            per[sn] = per.get(sn, 0) + 16
        return tuple(per.items())

    def val(self, sem):
        return self.cnt.get(sem, 0)

    SELF = {"vector": "vS", "scalar": "sS"}

    def op(self, eng, fn, waits=(), incs=()):
        for ss, v in waits:
            if not (ss[0] == "d" and len(ss) > 2):
                assert v <= self.val(ss), f"wait on future: {ss} {v} > {self.val(ss)}"
        sse = self.SELF.get(eng)
        n_orig = len(tuple(incs))
        if sse is not None and not any(ss == sse for ss, _ in incs):
            incs = tuple(incs) + ((sse, 1),)
        self.ops[eng].append((tuple(waits), fn, tuple(incs)))
        out = []
        for i, (ss, d) in enumerate(incs):
            self.cnt[ss] = self.cnt.get(ss, 0) + d
            if i < n_orig or n_orig == 0:
                out.append(self.cnt[ss])
        return out[0] if len(out) == 1 else out

    def emit(self, eng_name, eng, sems):
        last = {}
        sse = self.SELF.get(eng_name)
        self_c = 0
        for waits, fn, incs in self.ops[eng_name]:
            if sse is not None and self_c > 0:
                eng.wait_ge(sems[sse], self_c)
                last[sse] = max(last.get(sse, 0), self_c)
            for ss0, v0 in waits:
                for ss, v in self._xlate(ss0, v0):
                    if last.get(ss, 0) < v:
                        eng.wait_ge(sems[ss], v)
                        last[ss] = v
            inst = fn(eng)
            for ss, d in incs:
                if ss.startswith("cc"):
                    inst.then_inc(sems[ss])
                else:
                    inst.then_inc(sems[ss], d)
                if ss == sse:
                    self_c += d


def build_program(taps=False):
    nc = bass.Bass("TRN2", target_bir_lowering=False)
    dt = nc.dram_tensor

    x_own = dt("x_own", [TOK, D], F32, kind="ExternalInput").ap()
    wq_own = dt("wq_own", [128, NDC * 256], F32, kind="ExternalInput").ap()
    wk_own = dt("wk_own", [128, NDC * 256], F32, kind="ExternalInput").ap()
    wv_own = dt("wv_own", [128, NDC * 256], F32, kind="ExternalInput").ap()
    b_qkv = dt("b_qkv", [128, 12], F32, kind="ExternalInput").ap()
    gate_w_a = dt("gate_w_a", [128, NDC * E], F32, kind="ExternalInput").ap()
    wg_a = dt("wg_a", [E, NHC, 128, NDC * 128], F32, kind="ExternalInput").ap()
    wu_a = dt("wu_a", [E, NHC, 128, NDC * 128], F32, kind="ExternalInput").ap()
    wd_a = dt("wd_a", [E, NDC, 128, NHC * 128], F32, kind="ExternalInput").ap()
    swg_a = dt("swg_a", [NSC, 128, NDC * 128], F32, kind="ExternalInput").ap()
    swu_a = dt("swu_a", [NSC, 128, NDC * 128], F32, kind="ExternalInput").ap()
    swd_a = dt("swd_a", [NDC, 128, NSC * 128], F32, kind="ExternalInput").ap()
    wo_a = dt("wo_a", [NDC, 128, NDC * 128], F32, kind="ExternalInput").ap()
    bo_t = dt("bo_t", [128, NDC], F32, kind="ExternalInput").ap()
    cos_t = dt("cos_t", [128, N_TOK], F32, kind="ExternalInput").ap()
    sin_t = dt("sin_t", [128, N_TOK], F32, kind="ExternalInput").ap()
    mask_d = dt("mask_d", [128, 1024], F32, kind="ExternalInput").ap()
    identbf_d = dt("identbf_d", [128, 128], BF16, kind="ExternalInput").ap()
    identf_d = dt("identf_d", [128, 128], F32, kind="ExternalInput").ap()
    onesrow_d = dt("onesrow_d", [1, 128], F32, kind="ExternalInput").ap()
    onescol_d = dt("onescol_d", [128, 1], BF16, kind="ExternalInput").ap()
    onescolf_d = dt("onescolf_d", [128, 1], F32, kind="ExternalInput").ap()
    onehot_d = dt("onehot_d", [8, E * 128], F32, kind="ExternalInput").ap()
    fconst_d = dt("fconst_d", [128, 4], F32, kind="ExternalInput").ap()
    out_own = dt("out_own", [TOK, D], F32, kind="ExternalOutput").ap()

    xn_bounce = dt("xn_bounce", [D, TOK], BF16)
    xn_gath = dt("xn_gath", [NCORES * D, TOK], BF16)
    a_bounce = [dt(f"a_bounce{h}", [128, N_TOK], BF16) for h in range(HL)]
    a_gath = [dt(f"a_gath{h}", [NCORES * 128, N_TOK], BF16) for h in range(HL)]
    q_dram = dt("q_dram", [HL, 128, N_TOK], BF16)
    k_dram = dt("k_dram", [HL, 128, N_TOK], BF16)
    v_dram = dt("v_dram", [HL, N_TOK, 128], BF16)
    if taps:
        tap_xnT = dt("tap_xnT", [128, NDC * TOK], BF16, kind="ExternalOutput").ap()
        tap_AT = dt("tap_AT", [128, HL * N_TOK], BF16, kind="ExternalOutput").ap()
        tap_fa = dt("tap_fa", [128, NDC * TOK], BF16, kind="ExternalOutput").ap()
        tap_comb = dt("tap_comb", [8, TOK], F32, kind="ExternalOutput").ap()
        tap_mhat = dt("tap_mhat", [128, HL], F32, kind="ExternalOutput").ap()
        tap_kqm = dt("tap_kqm", [1, 8], F32, kind="ExternalOutput").ap()
        tap_st8 = dt("tap_st8", [128, 40], F32, kind="ExternalOutput").ap()
        tap_q = dt("tap_q", [HL, 128, N_TOK], BF16, kind="ExternalOutput").ap()
        tap_k = dt("tap_k", [HL, 128, N_TOK], BF16, kind="ExternalOutput").ap()
        tap_v = dt("tap_v", [HL, N_TOK, 128], BF16, kind="ExternalOutput").ap()

    s = Sched()

    with contextlib.ExitStack() as es:
        sb = lambda name, shape, dtp: es.enter_context(nc.sbuf_tensor(name, shape, dtp))
        psmk = lambda name, shape, dtp: es.enter_context(nc.psum_tensor(name, shape, dtp))

        xnT = sb("xnT", [128, NDC * TOK], BF16)
        AT = sb("AT", [128, HL * N_TOK], BF16)
        fa = sb("fa", [128, NDC * TOK], BF16)
        comb_rows = sb("comb_rows", [8, TOK], F32)
        identbf = sb("identbf", [128, 128], BF16)
        identf = sb("identf", [128, 128], F32)
        onesrow = sb("onesrow", [1, 128], F32)
        onescol = sb("onescol", [128, 1], BF16)
        onescolf = sb("onescolf", [128, 1], F32)
        bqkv = sb("bqkv", [128, 12], F32)
        bot = sb("bot", [128, NDC], F32)
        gw = sb("gw", [128, NDC * E], BF16)
        mhat = sb("mhat", [128, HL], F32)
        fco = sb("fco", [128, 4], F32)
        st8 = sb("st8", [128, 40], F32)
        kqm = sb("kqm", [1, 8], F32)
        a16 = sb("a16", [128, 30720], BF16)
        a32 = sb("a32", [128, 4608], F32)

        pfh = [psmk(f"pf{i}", [128, 512], F32) for i in range(6)]
        pf = [h[:, :] for h in pfh]
        pt = [psmk(f"pt{i}", [128, 1024], BF16) for i in range(2)]
        bank_free = [None] * 6
        pt_free = [None] * 2

        sems = {}
        for nm in (["cc1", "cc2", "cc3", "tS", "vS", "sS"]
                   + [f"dS{i}" for i in range(DMA_SLOTS)]
                   + [f"dG{i}" for i in range(DMA_SLOTS)]):
            sems[nm] = es.enter_context(nc.semaphore(nm))

        def sync_dma(dst, src, waits=()):
            return s.dma("sync", lambda e, d=dst, sr=src: e.dma_start(out=d, in_=sr),
                         waits=waits)

        def gp_dma(dst, src, waits=(), accum=False):
            if accum:
                return s.dma("gpsimd",
                             lambda e, d=dst, sr=src: e.dma_start(out=d, in_=sr, accum_op=ALU.add),
                             waits=waits)
            return s.dma("gpsimd", lambda e, d=dst, sr=src: e.dma_start(out=d, in_=sr),
                         waits=waits)

        def bar():
            return {nm: s.val(nm) for nm in ("dS", "dG", "tS", "vS", "sS")}

        def wall(bc, only=None):
            return tuple((nm, v) for nm, v in bc.items()
                         if v > 0 and (only is None or nm in only))

        def bwait(i, extra=()):
            w = list(extra)
            if bank_free[i] is not None:
                w.append(bank_free[i])
            return tuple(w)

        sync_dma(identbf[:, :], identbf_d)
        sync_dma(identf[:, :], identf_d)
        sync_dma(onesrow[:, :], onesrow_d)
        sync_dma(onescol[:, :], onescol_d)
        sync_dma(onescolf[:, :], onescolf_d)
        sync_dma(bqkv[:, :], b_qkv)
        sync_dma(bot[:, :], bo_t)
        sync_dma(fco[:, :], fconst_d)
        c_consts = s.val("dS")
        gp_dma(gw[:, :], gate_w_a)
        c_gw = s.val("dG")

        NTT = TOK // 128
        o_xt = [0, 2048]
        o_xn = [24576, 26624]
        x_free = [None, None]
        xn_free = [None, None]
        ln_dma = []
        for tt in range(NTT):
            b = tt % 2
            w = [("dS", c_consts)]
            if x_free[b] is not None:
                w.append(x_free[b])
            sync_dma(a32[:, o_xt[b]:o_xt[b] + 2048], x_own[tt * 128:(tt + 1) * 128, :], waits=tuple(w))
            ln_dma.append(s.val("dS"))
            if tt == 1:
                break
        for tt in range(NTT):
            b = tt % 2
            if tt >= 2:
                w2 = [("dS", c_consts), x_free[b]]
                sync_dma(a32[:, o_xt[b]:o_xt[b] + 2048], x_own[tt * 128:(tt + 1) * 128, :],
                         waits=tuple(w2))
                ln_dma.append(s.val("dS"))
            xt = a32[:, o_xt[b]:o_xt[b] + 2048]
            xn = a16[:, o_xn[b]:o_xn[b] + 2048]
            w = [("dS", ln_dma[tt])]
            if xn_free[b] is not None:
                w.append(xn_free[b])
            B0 = (tt % 2) * 16
            stc = lambda i: st8[:, B0 + i:B0 + i + 1]
            s.op("vector", lambda e, xt=xt, stc=stc: e.reduce_sum(stc(0), xt, axis=AX.X),
                 waits=tuple(w))
            xsq = a16[:, o_xn[b] - 24576 + 20480:o_xn[b] - 24576 + 20480 + 2048]
            cssq = s.op("scalar", lambda e, xt=xt, xsq=xsq: e.activation(
                xsq, xt, AF.Square), waits=tuple(w), incs=(("sS", 1),))
            s.op("vector", lambda e, xsq=xsq, stc=stc: e.reduce_sum(stc(1), xsq, axis=AX.X),
                 waits=(("sS", cssq),))
            s.op("vector", lambda e, stc=stc: e.tensor_scalar_mul(stc(2), stc(0), 1.0 / D))
            s.op("vector", lambda e, stc=stc: e.tensor_mul(stc(3), stc(2), stc(2)))
            cv = s.op("vector", lambda e, stc=stc: e.scalar_tensor_tensor(
                stc(4), stc(1), 1.0 / D, stc(3), ALU.mult, ALU.subtract),
                incs=(("vS", 1),))
            s.op("scalar", lambda e, stc=stc: e.activation(stc(5), stc(4), AF.Ln, bias=fco[:, 0:1]),
                 waits=(("vS", cv),))
            cs = s.op("scalar", lambda e, stc=stc: e.activation(stc(6), stc(5), AF.Exp, scale=-0.5),
                      incs=(("sS", 1),))
            cv2 = s.op("vector", lambda e, stc=stc: e.tensor_mul(stc(8), stc(2), stc(6)),
                       waits=(("sS", cs),))
            cv2 = s.op("vector", lambda e, stc=stc: e.tensor_scalar_mul(stc(9), stc(8), -1.0),
                       incs=(("vS", 1),))
            cs2 = s.op("scalar", lambda e, xt=xt, xn=xn, stc=stc: e.activation(
                xn, xt, AF.Identity, bias=stc(9), scale=stc(6)),
                waits=(("vS", cv2),), incs=(("sS", 1),))
            x_free[b] = ("sS", cs2)
            lastv = None
            for dc in range(NDC):
                pb = pt[dc % 2]
                wt = [("sS", cs2)]
                if pt_free[dc % 2] is not None:
                    wt.append(pt_free[dc % 2])
                ct = s.op("tensor", lambda e, pb=pb, xn=xn, dc=dc: e.transpose(
                    pb[:, 0:128], xn[:, dc * 128:(dc + 1) * 128], identbf[:, :]),
                    waits=tuple(wt), incs=(("tS", 1),))
                lastv = s.op("vector", lambda e, pb=pb, dc=dc, tt=tt: e.tensor_copy(
                    xnT[:, dc * TOK + tt * 128: dc * TOK + (tt + 1) * 128], pb[:, 0:128]),
                    waits=(("tS", ct),), incs=(("vS", 1),))
                pt_free[dc % 2] = ("vS", lastv)
            xn_free[b] = ("vS", lastv)
        c_ln_v = s.val("vS")
        gp_dma(xn_bounce.ap().rearrange("(c p) t -> p c t", p=128),
               xnT[:, :].rearrange("p (c t) -> p c t", c=NDC), waits=(("vS", c_ln_v),))
        c_xnb = s.val("dG")
        if taps:
            sync_dma(tap_xnT, xnT[:, :])
            sync_dma(tap_st8, st8[:, :])
        s.op("gpsimd", lambda e: e.collective_compute(
            "AllGather", ALU.bypass, replica_groups=[list(range(NCORES))],
            ins=[xn_bounce.ap().opt()], outs=[xn_gath.ap().opt()]),
            waits=(("dG", c_xnb),), incs=(("cc1", 1),))
        sync_dma(out_own, x_own)
        c_xcopy = s.val("dS")

        bc0 = bar()
        for tt in range(NTT):
            bi = tt % 2
            wfirst = bwait(bi, wall(bc0) + (("dG", c_gw),))
            for dc in range(NDC):
                sp = (dc == NDC - 1)
                inc = (("tS", 1),) if sp else ()
                s.op("tensor", lambda e, bi=bi, dc=dc, tt=tt, sp=sp: e.matmul(
                    pf[bi][:, 0:8], xnT[:, dc * TOK + tt * 128: dc * TOK + (tt + 1) * 128],
                    gw[:, dc * 8:(dc + 1) * 8], start=(dc == 0), stop=sp,
                    skip_group_check=True),
                    waits=wfirst if dc == 0 else (), incs=inc)
                if dc == 0:
                    wfirst = ()
            ct = s.val("tS")
            B0 = (tt % 2) * 16
            stc = lambda i: st8[:, B0 + i:B0 + i + 1]
            ga = (tt % 2) * 128
            s.op("vector", lambda e, bi=bi, stc=stc: e.reduce_max(stc(0), pf[bi][:, 0:8], axis=AX.X),
                 waits=(("tS", ct),))
            cm = s.op("vector", lambda e, stc=stc: e.tensor_scalar_mul(stc(1), stc(0), -1.0),
                      incs=(("vS", 1),))
            ce = s.op("scalar", lambda e, bi=bi, stc=stc, ga=ga: e.activation(
                a32[:, ga + 0:ga + 8], pf[bi][:, 0:8], AF.Exp, bias=stc(1), accum_out=stc(2)),
                waits=(("vS", cm),), incs=(("sS", 1),))
            bank_free[bi] = ("sS", ce)
            cr = s.op("vector", lambda e, stc=stc: e.reciprocal(stc(3), stc(2)),
                      waits=(("sS", ce),), incs=(("vS", 1),))
            cn = s.op("scalar", lambda e, stc=stc, ga=ga: e.activation(
                a32[:, ga + 32:ga + 40], a32[:, ga + 0:ga + 8], AF.Identity, scale=stc(3)),
                waits=(("vS", cr),), incs=(("sS", 1),))
            scs = a32[:, ga + 32:ga + 40]
            s.op("vector", lambda e, scs=scs, stc=stc: e.reduce_max(stc(4), scs, axis=AX.X),
                 waits=(("sS", cn),))
            s.op("vector", lambda e, scs=scs, stc=stc, ga=ga: e.scalar_tensor_tensor(
                a32[:, ga + 64:ga + 72], scs, stc(4), scs, ALU.is_lt, ALU.mult))
            s.op("vector", lambda e, stc=stc, ga=ga: e.reduce_max(stc(5), a32[:, ga + 64:ga + 72], axis=AX.X))
            cv = s.op("vector", lambda e, scs=scs, stc=stc, ga=ga: e.scalar_tensor_tensor(
                a32[:, ga + 96:ga + 104], scs, stc(5), scs, ALU.is_ge, ALU.mult),
                incs=(("vS", 1),))
            ctr = s.op("tensor", lambda e, ga=ga: e.transpose(
                pf[2][0:8, 0:128], a32[:, ga + 96:ga + 104], identf[:, :]),
                waits=bwait(2, (("vS", cv),)), incs=(("tS", 1),))
            cvc = s.op("vector", lambda e, tt=tt: e.tensor_copy(
                comb_rows[0:8, tt * 128:(tt + 1) * 128], pf[2][0:8, 0:128]),
                waits=(("tS", ctr),), incs=(("vS", 1),))
            bank_free[2] = ("vS", cvc)
        c_comb_v = s.val("vS")

        o_wgb = [0, 2048]
        o_wub = [4096, 6144]
        o_wdb = [8192, 9600]
        o_G = 11008
        o_sil = [22272, 22784]
        o_t1 = [23296, 23808]
        o_cB = [0, 512]
        o_oh = 1024
        bc1 = bar()
        sync_dma(a32[0:8, o_oh:o_oh + E * 128], onehot_d, waits=wall(bc1, ("vS", "sS")))
        c_oh = s.val("dS")

        wgu_free = [None, None]
        wd_free = [None, None]
        sil_free = [None, None]
        moe_dma = {}

        def w_chunk(src, off, ln, waits):
            gp_dma(a16[:, off:off + ln], src, waits=waits)
            return s.val("dG")

        for (ei, hci, b) in ((0, 0, 0), (0, 1, 1)):
            moe_dma[("g", ei, hci)] = w_chunk(wg_a[ei, hci], o_wgb[b], NDC * 128, ())
            moe_dma[("u", ei, hci)] = w_chunk(wu_a[ei, hci], o_wub[b], NDC * 128, ())

        first_moe = wall(bc1) + (("vS", c_comb_v), ("dS", c_oh))

        def gateup_phase(nchunks, dma_of, prefetch, use_comb=True, first=()):
            lastG = None
            tS0 = ("tS", s.val("tS"))
            for hc in range(nchunks):
                bks = [(4 * hc + i) % 6 for i in range(4)]
                wdma = dma_of(hc)
                for tc in range(2):
                    gb, ub = bks[2 * tc], bks[2 * tc + 1]
                    wfirst = list(wdma) + list(bwait(gb))
                    if hc == 0 and tc == 0:
                        wfirst += list(first)
                    for dc in range(NDC):
                        s.op("tensor", lambda e, gb=gb, dc=dc, tc=tc, hc=hc: e.matmul(
                            pf[gb], a16[:, o_wgb[hc % 2] + dc * 128:o_wgb[hc % 2] + (dc + 1) * 128],
                            xnT[:, dc * TOK + tc * 512: dc * TOK + tc * 512 + 512],
                            start=(dc == 0), stop=(dc == NDC - 1), skip_group_check=True),
                            waits=tuple(wfirst) if dc == 0 else (),
                            incs=(("tS", 1),) if dc == NDC - 1 else ())
                        wfirst = []
                    ctg = s.val("tS")
                    wfirst = list(bwait(ub))
                    for dc in range(NDC):
                        s.op("tensor", lambda e, ub=ub, dc=dc, tc=tc, hc=hc: e.matmul(
                            pf[ub], a16[:, o_wub[hc % 2] + dc * 128:o_wub[hc % 2] + (dc + 1) * 128],
                            xnT[:, dc * TOK + tc * 512: dc * TOK + tc * 512 + 512],
                            start=(dc == 0), stop=(dc == NDC - 1), skip_group_check=True),
                            waits=tuple(wfirst) if dc == 0 else (),
                            incs=(("tS", 1),) if dc == NDC - 1 else ())
                        wfirst = []
                    ctu = s.val("tS")
                    sb_, tb_ = o_sil[tc], o_t1[tc]
                    wsil = [("tS", ctg)]
                    if sil_free[tc] is not None:
                        wsil.append(sil_free[tc])
                    cs = s.op("scalar", lambda e, gb=gb, sb_=sb_: e.activation(
                        a16[:, sb_:sb_ + 512], pf[gb], AF.Silu),
                        waits=tuple(wsil), incs=(("sS", 1),))
                    bank_free[gb] = ("sS", cs)
                    if use_comb:
                        cv1 = s.op("vector", lambda e, ub=ub, tb_=tb_, tc=tc: e.tensor_tensor(
                            a16[:, tb_:tb_ + 512], pf[ub], a32[:, o_cB[tc]:o_cB[tc] + 512],
                            ALU.mult), waits=(("tS", ctu),), incs=(("vS", 1),))
                        bank_free[ub] = ("vS", cv1)
                        lastG = s.op("vector", lambda e, sb_=sb_, tb_=tb_, hc=hc, tc=tc: e.tensor_mul(
                            a16[:, o_G + hc * 1024 + tc * 512:o_G + hc * 1024 + tc * 512 + 512],
                            a16[:, tb_:tb_ + 512], a16[:, sb_:sb_ + 512]),
                            waits=(("sS", cs), tS0), incs=(("vS", 1),))
                        sil_free[tc] = ("vS", lastG)
                    else:
                        lastG = s.op("vector", lambda e, ub=ub, sb_=sb_, hc=hc, tc=tc: e.tensor_tensor(
                            a16[:, o_G + hc * 1024 + tc * 512:o_G + hc * 1024 + tc * 512 + 512],
                            pf[ub], a16[:, sb_:sb_ + 512], ALU.mult),
                            waits=(("tS", ctu), ("sS", cs), tS0), incs=(("vS", 1),))
                        bank_free[ub] = ("vS", lastG)
                        sil_free[tc] = ("vS", lastG)
                wgu_free[hc % 2] = ("tS", ctu)
                prefetch(hc, wgu_free[hc % 2])
            return lastG

        def down_phase(nchunks, getwd, wd_len, first_copy=False):
            cG = s.val("vS")
            dma_d = {}
            for d2c in range(2):
                w = [wd_free[d2c % 2]] if wd_free[d2c % 2] is not None else []
                dma_d[d2c] = getwd(d2c, d2c % 2, tuple(w))
            for d2c in range(NDC):
                bwf = o_wdb[d2c % 2]
                ctd_last = None
                for tc in range(2):
                    dbk = 4 + tc
                    w = [("dG", dma_d[d2c]), ("vS", cG)] + list(bwait(dbk))
                    for hc in range(nchunks):
                        s.op("tensor", lambda e, dbk=dbk, bwf=bwf, hc=hc, tc=tc: e.matmul(
                            pf[dbk], a16[:, bwf + hc * 128:bwf + (hc + 1) * 128],
                            a16[:, o_G + hc * 1024 + tc * 512:o_G + hc * 1024 + tc * 512 + 512],
                            start=(hc == 0), stop=(hc == nchunks - 1), skip_group_check=True),
                            waits=tuple(w) if hc == 0 else (),
                            incs=(("tS", 1),) if hc == nchunks - 1 else ())
                        w = []
                    ctd = s.val("tS")
                    ctd_last = ctd
                    if first_copy:
                        cv = s.op("vector", lambda e, dbk=dbk, d2c=d2c, tc=tc: e.tensor_copy(
                            fa[:, d2c * TOK + tc * 512: d2c * TOK + tc * 512 + 512], pf[dbk]),
                            waits=(("tS", ctd),), incs=(("vS", 1),))
                    else:
                        cv = s.op("vector", lambda e, dbk=dbk, d2c=d2c, tc=tc: e.tensor_add(
                            fa[:, d2c * TOK + tc * 512: d2c * TOK + tc * 512 + 512],
                            pf[dbk], fa[:, d2c * TOK + tc * 512: d2c * TOK + tc * 512 + 512]),
                            waits=(("tS", ctd),), incs=(("vS", 1),))
                    bank_free[dbk] = ("vS", cv)
                wd_free[d2c % 2] = ("tS", ctd_last)
                if d2c + 2 < NDC:
                    dma_d[d2c + 2] = getwd(d2c + 2, d2c % 2, (wd_free[d2c % 2],))

        for e_i in range(E):
            for tc in range(2):
                w = list(bwait(4))
                if e_i == 0 and tc == 0:
                    w += list(first_moe)
                ct = s.op("tensor", lambda e, e_i=e_i, tc=tc: e.matmul(
                    pf[4], a32[0:8, o_oh + e_i * 128:o_oh + (e_i + 1) * 128],
                    comb_rows[0:8, tc * 512:(tc + 1) * 512], start=True, stop=True,
                    skip_group_check=True),
                    waits=tuple(w), incs=(("tS", 1),))
                cv = s.op("vector", lambda e, tc=tc: e.tensor_copy(
                    a32[:, o_cB[tc]:o_cB[tc] + 512], pf[4]),
                    waits=(("tS", ct),), incs=(("vS", 1),))
                bank_free[4] = ("vS", cv)

            def dma_of(hc, e_i=e_i):
                return [("dG", moe_dma[("g", e_i, hc)]), ("dG", moe_dma[("u", e_i, hc)])]

            def prefetch(hc, free_evt, e_i=e_i):
                if hc + 2 < NHC:
                    nh = hc + 2
                    if ("g", e_i, nh) not in moe_dma:
                        moe_dma[("g", e_i, nh)] = w_chunk(wg_a[e_i, nh], o_wgb[nh % 2],
                                                          NDC * 128, (free_evt,))
                        moe_dma[("u", e_i, nh)] = w_chunk(wu_a[e_i, nh], o_wub[nh % 2],
                                                          NDC * 128, ())
                elif hc == NHC - 1 and e_i + 1 < E:
                    for nh in (0, 1):
                        moe_dma[("g", e_i + 1, nh)] = w_chunk(
                            wg_a[e_i + 1, nh], o_wgb[nh % 2], NDC * 128,
                            (wgu_free[nh % 2],))
                        moe_dma[("u", e_i + 1, nh)] = w_chunk(
                            wu_a[e_i + 1, nh], o_wub[nh % 2], NDC * 128, ())

            gateup_phase(NHC, dma_of, prefetch, use_comb=True,
                         first=first_moe if e_i == 0 else ())

            def getwd(d2c, b, w, e_i=e_i):
                return w_chunk(wd_a[e_i, d2c], o_wdb[b], NHC * 128, w)

            down_phase(NHC, getwd, NHC * 128, first_copy=(e_i == 0))

        bc2 = bar()
        o_xg = [0, 4096, 8192]
        o_wq, o_wk, o_wv = 12288, 16384, 20480
        o_qs = [24576, 25088]
        o_ks = [25600, 26112]
        o_vs = [26624, 27136, 27648, 28160]
        o_vt = [28672, 29184]
        o_cos = [0, 512]
        o_sin = [1024, 1536]
        o_rt1, o_rt2 = 2048, 2560
        gp_dma(a16[:, o_wq:o_wq + NDC * 256], wq_own, waits=wall({"tS": bc2["tS"], "dG": bc2["dG"]}))
        gp_dma(a16[:, o_wk:o_wk + NDC * 256], wk_own)
        gp_dma(a16[:, o_wv:o_wv + NDC * 256], wv_own)
        c_wqkv = s.val("dG")
        cvz = s.op("vector", lambda e: e.memset(kqm[0:1, 0:8], 0.0), waits=wall(bc2, ("sS",)),
                   incs=(("vS", 1),))
        c_kqm0 = s.val("vS")

        NU = 2 * NTT * 2
        xg_dma = {}
        xg_used = {}

        def xg_load(u, waits):
            tc, half = u // 2, u % 2
            r = tc // 2
            l0 = (tc % 2) * 512
            rows0 = r * D + half * 1024
            src = xn_gath.ap()[rows0:rows0 + 1024, l0:l0 + 512].rearrange("(c p) t -> p c t", p=128)
            off = o_xg[u % 3]
            sync_dma(a16[:, off:off + 4096].rearrange("p (c t) -> p c t", c=8), src, waits=waits)
            xg_dma[u] = s.val("dS")

        for u in range(3):
            xg_load(u, wall(bc2, ("vS", "tS")) + (("cc1", 1),))

        bank_map = [("q", 0, 0), ("k", 0, 1), ("v", 0, 2), ("q", 1, 3), ("k", 1, 4), ("v", 1, 5)]
        cos_free = [None, None]
        qs_free = [None, None]
        ks_free = [None, None]
        vs_free = [None, None, None, None]
        vt_free = [None, None]
        norm_red = None
        deferred_v = []

        def emit_deferred_v():
            nonlocal deferred_v
            for (h, tc, vsl, cse) in deferred_v:
                b2 = vsl // 2
                lastv = None
                ctv = None
                for sub in range(4):
                    pb = pt[sub % 2]
                    wt = [("sS", cse)]
                    if pt_free[sub % 2] is not None:
                        wt.append(pt_free[sub % 2])
                    ctv = s.op("tensor", lambda e, pb=pb, sub=sub, vsl=vsl: e.transpose(
                        pb[:, 0:128], a16[:, o_vs[vsl] + sub * 128:o_vs[vsl] + (sub + 1) * 128],
                        identbf[:, :]), waits=tuple(wt), incs=(("tS", 1),))
                    wcp = [("tS", ctv)]
                    if sub == 0 and vt_free[b2] is not None:
                        wcp.append(vt_free[b2])
                    lastv = s.op("vector", lambda e, pb=pb, sub=sub, b2=b2: e.tensor_copy(
                        a16[:, o_vt[b2] + sub * 128:o_vt[b2] + (sub + 1) * 128], pb[:, 0:128]),
                        waits=tuple(wcp), incs=(("vS", 1),))
                    pt_free[sub % 2] = ("vS", lastv)
                dst = v_dram.ap()[h, tc * 512:(tc + 1) * 512, :].rearrange(
                    "(c p) d -> p c d", p=128)
                gp_dma(dst, a16[:, o_vt[b2]:o_vt[b2] + 512].rearrange("p (c d) -> p c d", c=4),
                       waits=(("vS", lastv),))
                vt_free[b2] = ("dG", s.val("dG"))
                vs_free[vsl] = ("tS", ctv)
            deferred_v = []

        for tc in range(2 * NTT):
            b2 = tc % 2
            w = [("dS", c_consts)]
            if cos_free[b2] is not None:
                w.append(cos_free[b2])
            else:
                w += list(wall(bc2, ("vS", "tS")))
            sync_dma(a32[:, o_cos[b2]:o_cos[b2] + 512], cos_t[:, tc * 512:(tc + 1) * 512],
                     waits=tuple(w))
            sync_dma(a32[:, o_sin[b2]:o_sin[b2] + 512], sin_t[:, tc * 512:(tc + 1) * 512])
            c_cs = s.val("dS")
            proj_done = {}
            for half in range(2):
                u = tc * 2 + half
                whalf = [("dG", c_wqkv), ("dS", xg_dma[u])]
                for wnm, h, bk in bank_map:
                    woff = {"q": o_wq, "k": o_wk, "v": o_wv}[wnm]
                    wfirst = list(whalf)
                    if half == 0:
                        wfirst += list(bwait(bk))
                    for dci in range(8):
                        dc = half * 8 + dci
                        s.op("tensor", lambda e, bk=bk, woff=woff, dc=dc, h=h, u=u: e.matmul(
                            pf[bk], a16[:, woff + dc * 256 + h * 128: woff + dc * 256 + h * 128 + 128],
                            a16[:, o_xg[u % 3] + (dc % 8) * 512: o_xg[u % 3] + (dc % 8) * 512 + 512],
                            start=(dc == 0), stop=(dc == NDC - 1), skip_group_check=True),
                            waits=tuple(wfirst) if dci == 0 else (),
                            incs=(("tS", 1),) if (dci == 7) else ())
                        wfirst = []
                    if half == 1:
                        proj_done[bk] = s.val("tS")
                xg_used[u] = s.val("tS")
                if u + 3 < NU:
                    xg_load(u + 3, (("tS", xg_used[u]),))
            ct_all = s.val("tS")
            emit_deferred_v()
            for wnm, h, bk in bank_map:
                bcol = {"q": 0, "k": 2, "v": 4}[wnm] + h
                if wnm in ("q", "k"):
                    stg = (o_qs if wnm == "q" else o_ks)[b2]
                    sfree = (qs_free if wnm == "q" else ks_free)[b2]
                    wq_ = [("tS", proj_done[bk]), ("dS", c_cs)]
                    s.op("vector", lambda e, bk=bk, bcol=bcol, b2=b2: e.scalar_tensor_tensor(
                        a32[:, o_rt1:o_rt1 + 512], pf[bk], bqkv[:, bcol:bcol + 1],
                        a32[:, o_cos[b2]:o_cos[b2] + 512], ALU.add, ALU.mult),
                        waits=tuple(wq_))
                    s.op("vector", lambda e, bk=bk, bcol=bcol, b2=b2: e.scalar_tensor_tensor(
                        a32[0:64, o_rt2:o_rt2 + 512], pf[bk][64:128, :],
                        bqkv[0:64, 6 + bcol:7 + bcol],
                        a32[0:64, o_sin[b2]:o_sin[b2] + 512], ALU.add, ALU.mult))
                    cv = s.op("vector", lambda e, bk=bk, bcol=bcol, b2=b2: e.scalar_tensor_tensor(
                        a32[64:128, o_rt2:o_rt2 + 512], pf[bk][0:64, :],
                        bqkv[64:128, 6 + bcol:7 + bcol],
                        a32[64:128, o_sin[b2]:o_sin[b2] + 512], ALU.add, ALU.mult),
                        incs=(("vS", 1),))
                    cos_free[b2] = ("vS", cv)
                    wadd = [("vS", cv)]
                    if sfree is not None:
                        wadd.append(sfree)
                    cvr = s.op("vector", lambda e, stg=stg: e.tensor_add(
                        a16[:, stg:stg + 512], a32[:, o_rt1:o_rt1 + 512],
                        a32[:, o_rt2:o_rt2 + 512]), waits=tuple(wadd), incs=(("vS", 1),))
                    dst = (q_dram if wnm == "q" else k_dram).ap()[h, :, tc * 512:(tc + 1) * 512]
                    gp_dma(dst, a16[:, stg:stg + 512], waits=(("vS", cvr),))
                    cdma = s.val("dG")
                    if wnm == "q":
                        qs_free[b2] = ("dG", cdma)
                    else:
                        ks_free[b2] = ("dG", cdma)
                    bank_free[bk] = ("vS", cv)
                else:
                    wv_ = [("tS", proj_done[bk])]
                    if vs_free[2 * b2 + h] is not None:
                        wv_.append(vs_free[2 * b2 + h])
                    vsl = 2 * b2 + h
                    cse = s.op("scalar", lambda e, bk=bk, bcol=bcol, vsl=vsl: e.activation(
                        a16[:, o_vs[vsl]:o_vs[vsl] + 512], pf[bk], AF.Identity,
                        bias=bqkv[:, bcol:bcol + 1]), waits=tuple(wv_), incs=(("sS", 1),))
                    bank_free[bk] = ("sS", cse)
                    deferred_v.append((h, tc, vsl, cse))
        emit_deferred_v()
        c_qkv_sp = s.val("dS")
        bcn = bar()
        nb_free = [None, None]
        norm_red = None
        nslice = 0
        for h in range(HL):
            for wnm, kcol in (("q", 0), ("k", 2)):
                srcd = (q_dram if wnm == "q" else k_dram).ap()
                for ch in range(2):
                    bb = nslice % 2
                    w = list(wall(bcn, ("tS", "dG"))) if nslice == 0 else []
                    if nb_free[bb] is not None:
                        w.append(nb_free[bb])
                    sync_dma(a16[:, bb * 4096:bb * 4096 + 4096],
                             srcd[h, :, ch * 4096:(ch + 1) * 4096], waits=tuple(w))
                    cd = s.val("dS")
                    lastn = None
                    for sub in range(2):
                        csq = s.op("vector", lambda e, bb=bb, sub=sub: e.tensor_mul(
                            a32[:, bb * 2048:bb * 2048 + 2048],
                            a16[:, bb * 4096 + sub * 2048:bb * 4096 + sub * 2048 + 2048],
                            a16[:, bb * 4096 + sub * 2048:bb * 4096 + sub * 2048 + 2048]),
                            waits=(("dS", cd),), incs=(("vS", 1),))
                        for sl in range(4):
                            pb = sl % 2
                            wm = [("vS", csq)] + list(bwait(pb))
                            ctn = s.op("tensor", lambda e, pb=pb, bb=bb, sl=sl: e.matmul(
                                pf[pb][0:1, :], onescolf[:, :],
                                a32[:, bb * 2048 + sl * 512:bb * 2048 + (sl + 1) * 512],
                                start=True, stop=True, skip_group_check=True),
                                waits=tuple(wm), incs=(("tS", 1),))
                            s.op("vector", lambda e, pb=pb: e.reduce_max(
                                st8[0:1, 10:11], pf[pb][0:1, :], axis=AX.X),
                                waits=(("tS", ctn), ("vS", c_kqm0)))
                            cvn = s.op("vector", lambda e, kcol=kcol, h=h: e.tensor_max(
                                kqm[0:1, kcol + h:kcol + h + 1],
                                kqm[0:1, kcol + h:kcol + h + 1], st8[0:1, 10:11]),
                                incs=(("vS", 1),))
                            bank_free[pb] = ("vS", cvn)
                            lastn = cvn
                    nb_free[bb] = ("vS", lastn)
                    norm_red = ("vS", lastn)
                    nslice += 1
        for h in range(HL):
            cv = s.op("vector", lambda e, h=h: e.tensor_mul(
                st8[0:1, 11:12], kqm[0:1, h:h + 1], kqm[0:1, 2 + h:3 + h]),
                waits=(norm_red,), incs=(("vS", 1),))
            s.op("scalar", lambda e: e.activation(st8[0:1, 12:13], st8[0:1, 11:12], AF.Ln),
                 waits=(("vS", cv),))
            cs = s.op("scalar", lambda e: e.activation(
                st8[0:1, 13:14], st8[0:1, 12:13], AF.Exp, scale=0.5, bias=fco[0:1, 1:2]),
                incs=(("sS", 1),))
            cv = s.op("vector", lambda e: e.tensor_scalar_mul(st8[0:1, 14:15], st8[0:1, 13:14], -1.0),
                      waits=(("sS", cs),), incs=(("vS", 1),))
            ctb = s.op("tensor", lambda e: e.matmul(
                pf[5][:, 0:1], onesrow[:, :], st8[0:1, 14:15],
                start=True, stop=True, skip_group_check=True),
                waits=bwait(5, (("vS", cv),)), incs=(("tS", 1),))
            cvm = s.op("vector", lambda e, h=h: e.tensor_copy(mhat[:, h:h + 1], pf[5][:, 0:1]),
                       waits=(("tS", ctb),), incs=(("vS", 1),))
            bank_free[5] = ("vS", cvm)
            norm_red = ("vS", cvm)
        c_mhat_v = s.val("vS")

        sh_dma = {}

        def sh_dma_of_factory(half):
            def dma_of(hc):
                ghc = half * 11 + hc
                return [("dG", sh_dma[("g", ghc)]), ("dG", sh_dma[("u", ghc)])]
            return dma_of

        def sh_prefetch_factory(half):
            def prefetch(hc, free_evt):
                nh = hc + 2
                if nh < 11 and ("g", half * 11 + nh) not in sh_dma:
                    sh_dma[("g", half * 11 + nh)] = w_chunk(
                        swg_a[half * 11 + nh], o_wgb[hc % 2], NDC * 128, (free_evt,))
                    sh_dma[("u", half * 11 + nh)] = w_chunk(
                        swu_a[half * 11 + nh], o_wub[hc % 2], NDC * 128, ())
            return prefetch

        def sh_getwd_factory(half):
            def getwd(d2c, b, w):
                return w_chunk(swd_a[d2c][:, half * 1408:(half + 1) * 1408], o_wdb[b],
                               11 * 128, w)
            return getwd

        bcP4 = bar()
        for hci, b in ((0, 0), (1, 1)):
            pw = wall(bcP4, ("tS", "dG")) if hci == 0 else ()
            sh_dma[("g", hci)] = w_chunk(swg_a[hci], o_wgb[b], NDC * 128, pw)
            sh_dma[("u", hci)] = w_chunk(swu_a[hci], o_wub[b], NDC * 128, ())
        gateup_phase(11, sh_dma_of_factory(0), sh_prefetch_factory(0), use_comb=False)
        down_phase(11, sh_getwd_factory(0), 11 * 128)

        bc3 = bar()
        o_q = [0, 512]
        o_K = [1024, 1536, 2048]
        o_V = [2560, 3072, 3584]
        o_PT = [4096, 4608, 5120, 5632]
        o_mask = 0
        o_B = 1024
        o_rl = 1536
        SBANKS = [0, 1, 5]
        sync_dma(a32[:, o_mask:o_mask + 1024], mask_d, waits=wall(bc3, ("vS", "sS", "tS")))
        c_mask = s.val("dS")

        q_dma = {}
        q_freed = [None, None]
        kv_dma = {}
        kv_freed = [None, None, None]
        PT_freed = [None] * 4
        first_att = (("dS", c_mask), ("vS", c_mhat_v)) + wall(bc3, ("sS",))
        first_att_used = False

        def kv_load(h, kb):
            bb = kb % 3
            w = []
            if kv_freed[bb] is not None:
                w.append(kv_freed[bb])
            sync_dma(a16[:, o_K[bb]:o_K[bb] + 512], k_dram.ap()[h, :, kb * 512:(kb + 1) * 512],
                     waits=tuple(w))
            sync_dma(a16[:, o_V[bb]:o_V[bb] + 512].rearrange("p (c d) -> p c d", c=4),
                     v_dram.ap()[h, kb * 512:(kb + 1) * 512, :].rearrange("(c p) d -> p c d", p=128))
            kv_dma[(h, kb)] = s.val("dS")

        for h in range(HL):
            for qb in range(16):
                qbb = qb % 2
                w = []
                if q_freed[qbb] is not None:
                    w.append(q_freed[qbb])
                sync_dma(a16[:, o_q[qbb]:o_q[qbb] + 512],
                         q_dram.ap()[h, :, qb * 512:(qb + 1) * 512], waits=tuple(w))
                q_dma[(h, qb)] = s.val("dS")
                units = [(kb, jc) for kb in range(qb + 1) for jc in range(4)]
                NUNITS = len(units)
                ctS = {}
                maskv = {}

                kv_dma.clear()

                def emit_S(i, h=h, qb=qb, qbb=qbb, units=units):
                    nonlocal first_att_used
                    kb, jc = units[i]
                    if jc == 0:
                        for kbl in (kb, kb + 1):
                            if kbl <= qb and (h, kbl) not in kv_dma:
                                kv_load(h, kbl)
                    sbk = SBANKS[i % 3]
                    wS = [("dS", kv_dma[(h, kb)]), ("dS", q_dma[(h, qb)])]
                    wS += list(bwait(sbk))
                    if not first_att_used:
                        wS += list(first_att)
                        first_att_used = True
                    c = s.op("tensor", lambda e, sbk=sbk, kb=kb, jc=jc, qbb=qbb: e.matmul(
                        pf[sbk], a16[:, o_K[kb % 3] + jc * 128:o_K[kb % 3] + (jc + 1) * 128],
                        a16[:, o_q[qbb]:o_q[qbb] + 512],
                        start=True, stop=True, skip_group_check=True),
                        waits=tuple(wS), incs=(("tS", 1),))
                    ctS[i] = c
                    if kb == qb:
                        maskv[i] = s.op("vector", lambda e, sbk=sbk, jc=jc: e.tensor_add(
                            pf[sbk], pf[sbk],
                            a32[:, o_mask + 512 - 128 * jc:o_mask + 1024 - 128 * jc]),
                            waits=(("tS", c),), incs=(("vS", 1),))

                def emit_exp(i, h=h):
                    sbk = SBANKS[i % 3]
                    wE = [("tS", ctS[i])]
                    if i in maskv:
                        wE.append(("vS", maskv[i]))
                    if PT_freed[i % 4] is not None:
                        wE.append(PT_freed[i % 4])
                    c = s.op("scalar", lambda e, sbk=sbk, i=i, h=h: e.activation(
                        a16[:, o_PT[i % 4]:o_PT[i % 4] + 512], pf[sbk], AF.Exp,
                        bias=mhat[:, h:h + 1], scale=ATT_SCALE),
                        waits=tuple(wE), incs=(("sS", 1),))
                    bank_free[sbk] = ("sS", c)
                    return c

                exp_of = {}
                pv_last = None

                def emit_PV(i, qb=qb, units=units):
                    nonlocal pv_last
                    kb, jc = units[i]
                    fst = (i == 0)
                    lst = (i == len(units) - 1)
                    wPV = [("sS", exp_of[i])]
                    if fst:
                        wPV += list(bwait(2)) + list(bwait(3))
                    s.op("tensor", lambda e, kb=kb, jc=jc, i=i, fst=fst, lst=lst: e.matmul(
                        pf[2], a16[:, o_V[kb % 3] + jc * 128:o_V[kb % 3] + (jc + 1) * 128],
                        a16[:, o_PT[i % 4]:o_PT[i % 4] + 512],
                        start=fst, stop=lst, skip_group_check=True),
                        waits=tuple(wPV))
                    c = s.op("tensor", lambda e, i=i, fst=fst, lst=lst: e.matmul(
                        pf[3][0:1, :], onescol[:, :], a16[:, o_PT[i % 4]:o_PT[i % 4] + 512],
                        start=fst, stop=lst, skip_group_check=True),
                        incs=(("tS", 1),))
                    PT_freed[i % 4] = ("tS", c)
                    pv_last = c
                    if jc == 3:
                        kv_freed[kb % 3] = ("tS", c)

                AHEAD = 2
                for i in range(min(AHEAD, NUNITS)):
                    emit_S(i)
                    exp_of[i] = emit_exp(i)
                for i in range(NUNITS):
                    if i + AHEAD < NUNITS:
                        emit_S(i + AHEAD)
                        exp_of[i + AHEAD] = emit_exp(i + AHEAD)
                    emit_PV(i)
                q_freed[qbb] = ("tS", pv_last)
                crl = s.op("vector", lambda e: e.reciprocal(
                    a32[0:1, o_rl:o_rl + 512], pf[3][0:1, :]),
                    waits=(("tS", pv_last),), incs=(("vS", 1),))
                bank_free[3] = ("vS", crl)
                ctB = s.op("tensor", lambda e: e.matmul(
                    pf[4], onesrow[:, :], a32[0:1, o_rl:o_rl + 512],
                    start=True, stop=True, skip_group_check=True),
                    waits=bwait(4, (("vS", crl),)), incs=(("tS", 1),))
                cvB = s.op("vector", lambda e: e.tensor_copy(a32[:, o_B:o_B + 512], pf[4]),
                           waits=(("tS", ctB),), incs=(("vS", 1),))
                bank_free[4] = ("vS", cvB)
                cvO = s.op("vector", lambda e, h=h, qb=qb: e.tensor_tensor(
                    AT[:, h * N_TOK + qb * 512: h * N_TOK + (qb + 1) * 512],
                    pf[2], a32[:, o_B:o_B + 512], ALU.mult),
                    waits=(("vS", cvB),), incs=(("vS", 1),))
                bank_free[2] = ("vS", cvO)
            c_ah = s.val("vS")
            gp_dma(a_bounce[h].ap(), AT[:, h * N_TOK:(h + 1) * N_TOK],
                   waits=(("vS", c_ah),))
            c_ab = s.val("dG")
            ccn = "cc2" if h == 0 else "cc3"
            s.op("gpsimd", lambda e, h=h: e.collective_compute(
                "AllGather", ALU.bypass, replica_groups=[list(range(NCORES))],
                ins=[a_bounce[h].ap().opt()], outs=[a_gath[h].ap().opt()]),
                waits=(("dG", c_ab),), incs=((ccn, 1),))
        c_att_v = s.val("vS")

        bcP6 = bar()
        for hci, b in ((0, 0), (1, 1)):
            pw = wall(bcP6, ("tS", "dG")) if hci == 0 else ()
            sh_dma[("g", 11 + hci)] = w_chunk(swg_a[11 + hci], o_wgb[b], NDC * 128, pw)
            sh_dma[("u", 11 + hci)] = w_chunk(swu_a[11 + hci], o_wub[b], NDC * 128, ())
        gateup_phase(11, sh_dma_of_factory(1), sh_prefetch_factory(1), use_comb=False)
        down_phase(11, sh_getwd_factory(1), 11 * 128)

        bc4 = bar()
        o_wo = [0, 2048]
        o_ag = [4096, 12288]
        o_ssb = [0, 512]
        o_st4 = [1024, 1536]
        wo_dma = {}
        wo_used = {}
        ag_dma = {}
        ag_free = [None, None]
        ssb_free = [None, None]
        st4_free = [None, None]
        pairs = [(tc, d2c) for tc in range(2) for d2c in range(NDC)]

        def wo_load(i, waits=()):
            bb = i % 2
            w = list(waits)
            if i - 2 in wo_used:
                w.append(wo_used[i - 2])
            gp_dma(a16[:, o_wo[bb]:o_wo[bb] + NDC * 128], wo_a[pairs[i][1]], waits=tuple(w))
            wo_dma[i] = s.val("dG")

        for i in range(2):
            wo_load(i, wall(bc4, ("tS", "dG")) if i == 0 else ())

        def ag_load(tc, waits):
            bb = tc % 2
            for hl in range(HL):
                w = list(waits) if hl == 0 else [("cc3" if hl else "cc2", 1)]
                if hl == 0 and ag_free[bb] is not None:
                    w.append(ag_free[bb])

                def fn(e, tc=tc, bb=bb, hl=hl):
                    pid = e.partition_id()
                    src = a_gath[hl].ap()[:, bass.ts(pid, TOK)][:, tc * 512:(tc + 1) * 512]
                    src = src.rearrange("(r p) t -> p r t", p=128)
                    dst = a16[:, o_ag[bb]:o_ag[bb] + NDC * 512].rearrange(
                        "p (r l t) -> p r l t", l=HL, t=512)[:, :, hl, :]
                    return e.dma_start(out=dst, in_=src)
                ag_dma[tc] = s.dma("sync", fn, waits=tuple(w))

        ag_load(0, (("cc2", 1), ("cc3", 1)) + wall(bc4, ("tS",)))
        ag_load(1, ())

        for i, (tc, d2c) in enumerate(pairs):
            agb = tc % 2
            if True:
                wbk = i % 2
                wfirst = [("dG", wo_dma[i]), ("dS", ag_dma[tc])] + list(bwait(wbk))
                for ic in range(NDC):
                    s.op("tensor", lambda e, wbk=wbk, ic=ic, agb=agb: e.matmul(
                        pf[wbk], a16[:, o_wo[wbk] + ic * 128:o_wo[wbk] + (ic + 1) * 128],
                        a16[:, o_ag[agb] + ic * 512:o_ag[agb] + (ic + 1) * 512],
                        start=(ic == 0), stop=(ic == NDC - 1), skip_group_check=True),
                        waits=tuple(wfirst) if ic == 0 else (),
                        incs=(("tS", 1),) if ic == NDC - 1 else ())
                    wfirst = []
                ctw = s.val("tS")
                wo_used[i] = ("tS", ctw)
                if i + 2 < len(pairs):
                    wo_load(i + 2)
                wsb = [("tS", ctw)]
                if ssb_free[d2c % 2] is not None:
                    wsb.append(ssb_free[d2c % 2])
                csb = s.op("vector", lambda e, wbk=wbk, d2c=d2c, tc=tc: e.scalar_tensor_tensor(
                    a32[:, o_ssb[d2c % 2]:o_ssb[d2c % 2] + 512], pf[wbk], bot[:, d2c:d2c + 1],
                    fa[:, d2c * TOK + tc * 512: d2c * TOK + tc * 512 + 512], ALU.add, ALU.add),
                    waits=tuple(wsb), incs=(("vS", 1),))
                bank_free[wbk] = ("vS", csb)
                lastc = None
                for sub in range(4):
                    tb = 2 + (sub % 2)
                    ctt = s.op("tensor", lambda e, tb=tb, d2c=d2c, sub=sub: e.transpose(
                        pf[tb][:, 0:128], a32[:, o_ssb[d2c % 2] + sub * 128:
                                              o_ssb[d2c % 2] + (sub + 1) * 128], identf[:, :]),
                        waits=bwait(tb, (("vS", csb),)), incs=(("tS", 1),))
                    wcp = [("tS", ctt)]
                    if sub == 0 and st4_free[d2c % 2] is not None:
                        wcp.append(st4_free[d2c % 2])
                    lastc = s.op("vector", lambda e, tb=tb, d2c=d2c, sub=sub: e.tensor_copy(
                        a32[:, o_st4[d2c % 2] + sub * 128:o_st4[d2c % 2] + (sub + 1) * 128],
                        pf[tb][:, 0:128]), waits=tuple(wcp), incs=(("vS", 1),))
                    bank_free[tb] = ("vS", lastc)
                ssb_free[d2c % 2] = ("vS", lastc)
                dst = out_own[tc * 512:(tc + 1) * 512, d2c * 128:(d2c + 1) * 128].rearrange(
                    "(c p) d -> p c d", p=128)
                gp_dma(dst, a32[:, o_st4[d2c % 2]:o_st4[d2c % 2] + 512].rearrange(
                    "p (c d) -> p c d", c=4),
                    waits=(("vS", lastc), ("dS", c_xcopy)), accum=True)
                st4_free[d2c % 2] = ("dG", s.val("dG"))
            ag_free[agb] = ("tS", s.val("tS"))


        if taps:
            bfin = bar()
            sync_dma(tap_AT, AT[:, :], waits=wall(bfin))
            sync_dma(tap_fa, fa[:, :])
            sync_dma(tap_comb, comb_rows[:, :])
            sync_dma(tap_mhat, mhat[:, :])
            sync_dma(tap_kqm, kqm[0:1, 0:8])
            sync_dma(tap_q, q_dram.ap())
            sync_dma(tap_k, k_dram.ap())
            sync_dma(tap_v, v_dram.ap())

        fin = bar()
        s.op("sync", lambda e: None, waits=wall(fin) + (("cc1", 1), ("cc2", 1), ("cc3", 1)))

        with nc.Block() as block:

            @block.sync
            def _(eng):
                s.emit("sync", eng, sems)

            @block.gpsimd
            def _(eng):
                s.emit("gpsimd", eng, sems)

            @block.tensor
            def _(eng):
                s.emit("tensor", eng, sems)

            @block.vector
            def _(eng):
                s.emit("vector", eng, sems)

            @block.scalar
            def _(eng):
                s.emit("scalar", eng, sems)

    return nc


_CACHE = {}


def _host_consts():
    inv = (1.0 / 10000.0 ** (np.arange(0, DH, 2, dtype=np.float32) / DH)).astype(np.float32)
    seq = np.arange(N_TOK, dtype=np.float32)
    fr = seq[:, None] * inv[None, :]
    pos = np.concatenate([fr, fr], axis=1).astype(np.float32)
    cos = np.cos(pos).T.copy()
    sinf = np.sin(pos)
    sign = np.concatenate([-np.ones(64, np.float32), np.ones(64, np.float32)])
    sin = (sinf * sign[None, :]).T.copy()
    mask = np.where(np.arange(1024)[None, :] >= np.arange(128)[:, None] + 512,
                    np.float32(0.0), np.float32(NEG)).astype(np.float32)
    onehot = np.zeros((8, E * 128), np.float32)
    for e in range(E):
        onehot[e, e * 128:(e + 1) * 128] = 1.0
    fconst = np.zeros((128, 4), np.float32)
    fconst[:, 0] = 1e-5
    fconst[:, 1] = math.log(1.02 * ATT_SCALE)
    return {
        "cos_t": cos, "sin_t": sin, "mask_d": mask,
        "identbf_d": np.eye(128, dtype=ml_dtypes.bfloat16),
        "identf_d": np.eye(128, dtype=np.float32),
        "onesrow_d": np.ones((1, 128), np.float32),
        "onescol_d": np.ones((128, 1), ml_dtypes.bfloat16),
        "onescolf_d": np.ones((128, 1), np.float32),
        "onehot_d": onehot, "fconst_d": fconst,
    }


def _pack(w, rows_chunks, cols_chunks):
    R, C = rows_chunks, cols_chunks
    return np.ascontiguousarray(
        w.reshape(R, 128, C, 128).transpose(2, 1, 0, 3).reshape(C, 128, R * 128))


def kernel(L0_attn_x, L0_attn_Wq, L0_attn_bq, L0_attn_Wk, L0_attn_bk, L0_attn_Wv,
           L0_attn_bv, L0_attn_Wo, L0_attn_bo, L0_ffn_gate_w, L0_ffn_w_gate,
           L0_ffn_w_up, L0_ffn_w_down, L0_ffn_sw_gate, L0_ffn_sw_up, L0_ffn_sw_down):
    key = ("prog", TAPS)
    if key not in _CACHE:
        _CACHE[key] = build_program(taps=TAPS)
    nc = _CACHE[key]

    x = np.asarray(L0_attn_x, dtype=np.float32).reshape(N_TOK, D)
    f32 = lambda a: np.asarray(a, dtype=np.float32)
    Wq, Wk, Wv, Wo = f32(L0_attn_Wq), f32(L0_attn_Wk), f32(L0_attn_Wv), f32(L0_attn_Wo)
    bq, bk, bv, bo = f32(L0_attn_bq), f32(L0_attn_bk), f32(L0_attn_bv), f32(L0_attn_bo)
    gate_w = f32(L0_ffn_gate_w)
    w_gate, w_up, w_down = f32(L0_ffn_w_gate), f32(L0_ffn_w_up), f32(L0_ffn_w_down)
    sw_gate, sw_up, sw_down = f32(L0_ffn_sw_gate), f32(L0_ffn_sw_up), f32(L0_ffn_sw_down)

    consts = _host_consts()
    gate_w_a = np.ascontiguousarray(
        gate_w.reshape(NDC, 128, E).transpose(1, 0, 2).reshape(128, NDC * E))
    wg_a = np.ascontiguousarray(
        w_gate.reshape(E, NDC, 128, NHC, 128).transpose(0, 3, 2, 1, 4).reshape(E, NHC, 128, NDC * 128))
    wu_a = np.ascontiguousarray(
        w_up.reshape(E, NDC, 128, NHC, 128).transpose(0, 3, 2, 1, 4).reshape(E, NHC, 128, NDC * 128))
    wd_a = np.ascontiguousarray(
        w_down.reshape(E, NHC, 128, NDC, 128).transpose(0, 3, 2, 1, 4).reshape(E, NDC, 128, NHC * 128))
    swg_a = _pack(sw_gate, NDC, NSC)
    swu_a = _pack(sw_up, NDC, NSC)
    swd_a = _pack(sw_down, NSC, NDC)
    wo_a = _pack(Wo, NDC, NDC)
    bo_t = np.ascontiguousarray(bo.reshape(NDC, 128).T)

    shared = dict(consts)
    shared.update({
        "gate_w_a": gate_w_a, "wg_a": wg_a, "wu_a": wu_a, "wd_a": wd_a,
        "swg_a": swg_a, "swu_a": swu_a, "swd_a": swd_a, "wo_a": wo_a, "bo_t": bo_t,
    })

    in_maps = []
    for c in range(NCORES):
        m = dict(shared)
        m["x_own"] = np.ascontiguousarray(x[c * TOK:(c + 1) * TOK, :])
        hsl = slice(c * 256, (c + 1) * 256)
        for nm, W in (("wq_own", Wq), ("wk_own", Wk), ("wv_own", Wv)):
            m[nm] = np.ascontiguousarray(
                W[:, hsl].reshape(NDC, 128, 256).transpose(1, 0, 2).reshape(128, NDC * 256))
        barr = np.zeros((128, 12), np.float32)
        for i, b in enumerate((bq, bk, bv)):
            bl = b[hsl].reshape(2, 128)
            barr[:, 2 * i + 0] = bl[0]
            barr[:, 2 * i + 1] = bl[1]
        barr[:, 6:12] = np.concatenate([barr[64:128, 0:6], barr[0:64, 0:6]], axis=0)
        m["b_qkv"] = barr
        in_maps.append(m)

    res = run_bass_kernel_spmd(nc, in_maps, core_ids=list(range(NCORES)))
    out = np.concatenate([res.results[c]["out_own"] for c in range(NCORES)], axis=0)
    if TAPS:
        kernel.last_results = res
    return out.reshape(1, N_TOK, D)


# version 12
# speedup vs baseline: 1.4510x; 1.0184x over previous
import math
import contextlib
import numpy as np
import ml_dtypes
import concourse.bass as bass
import concourse.mybir as mybir
from concourse.bass_utils import run_bass_kernel_spmd

F32 = mybir.dt.float32
BF16 = mybir.dt.bfloat16
AF = mybir.ActivationFunctionType
ALU = mybir.AluOpType
AX = mybir.AxisListType

N_TOK = 8192
D = 2048
DH = 128
NCORES = 8
TOK = N_TOK // NCORES
HL = 2
E = 8
NDC = D // 128
NHC = 11
NSC = 22
ATT_SCALE = DH ** -0.5
NEG = -1e30

ENGINES = ("sync", "gpsimd", "tensor", "vector", "scalar")
TAPS = False


DMA_SLOTS = 8


class Sched:
    def __init__(self):
        self.ops = {e: [] for e in ENGINES}
        self.cnt = {}
        self.dma_idx = {"sync": 0, "gpsimd": 0}
        self.dma_hist = {"dS": [], "dG": []}

    def dma(self, eng, fn, waits=()):
        vname = {"sync": "dS", "gpsimd": "dG"}[eng]
        j = self.dma_idx[eng]
        slot = j % DMA_SLOTS
        g = j // DMA_SLOTS + 1
        sname = f"{vname}{slot}"
        w = list(waits)
        if g > 1:
            w.append((sname, 16 * (g - 1)))
        self.dma_idx[eng] = j + 1
        self.dma_hist[vname].append(sname)
        self.cnt[vname] = self.cnt.get(vname, 0) + 16
        self.op(eng, fn, waits=tuple(w), incs=((sname, 16),))
        return self.cnt[vname]

    def _xlate(self, ss, v):
        if ss not in ("dS", "dG"):
            return ((ss, v),)
        k = v // 16
        hist = self.dma_hist[ss][:k]
        per = {}
        for sn in hist:
            per[sn] = per.get(sn, 0) + 16
        return tuple(per.items())

    def val(self, sem):
        return self.cnt.get(sem, 0)

    SELF = {"vector": "vS", "scalar": "sS"}

    def op(self, eng, fn, waits=(), incs=()):
        for ss, v in waits:
            if not (ss[0] == "d" and len(ss) > 2):
                assert v <= self.val(ss), f"wait on future: {ss} {v} > {self.val(ss)}"
        sse = self.SELF.get(eng)
        n_orig = len(tuple(incs))
        if sse is not None and not any(ss == sse for ss, _ in incs):
            incs = tuple(incs) + ((sse, 1),)
        self.ops[eng].append((tuple(waits), fn, tuple(incs)))
        out = []
        for i, (ss, d) in enumerate(incs):
            self.cnt[ss] = self.cnt.get(ss, 0) + d
            if i < n_orig or n_orig == 0:
                out.append(self.cnt[ss])
        return out[0] if len(out) == 1 else out

    def emit(self, eng_name, eng, sems):
        last = {}
        sse = self.SELF.get(eng_name)
        self_c = 0
        for waits, fn, incs in self.ops[eng_name]:
            if sse is not None and self_c > 0:
                eng.wait_ge(sems[sse], self_c)
                last[sse] = max(last.get(sse, 0), self_c)
            for ss0, v0 in waits:
                for ss, v in self._xlate(ss0, v0):
                    if last.get(ss, 0) < v:
                        eng.wait_ge(sems[ss], v)
                        last[ss] = v
            inst = fn(eng)
            for ss, d in incs:
                if ss.startswith("cc"):
                    inst.then_inc(sems[ss])
                else:
                    inst.then_inc(sems[ss], d)
                if ss == sse:
                    self_c += d


def build_program(taps=False):
    nc = bass.Bass("TRN2", target_bir_lowering=False)
    dt = nc.dram_tensor

    x_own = dt("x_own", [TOK, D], F32, kind="ExternalInput").ap()
    wq_own = dt("wq_own", [128, NDC * 256], F32, kind="ExternalInput").ap()
    wk_own = dt("wk_own", [128, NDC * 256], F32, kind="ExternalInput").ap()
    wv_own = dt("wv_own", [128, NDC * 256], F32, kind="ExternalInput").ap()
    b_qkv = dt("b_qkv", [128, 12], F32, kind="ExternalInput").ap()
    gate_w_a = dt("gate_w_a", [128, NDC * E], F32, kind="ExternalInput").ap()
    wg_a = dt("wg_a", [E, NHC, 128, NDC * 128], F32, kind="ExternalInput").ap()
    wu_a = dt("wu_a", [E, NHC, 128, NDC * 128], F32, kind="ExternalInput").ap()
    wd_a = dt("wd_a", [E, NDC, 128, NHC * 128], F32, kind="ExternalInput").ap()
    swg_a = dt("swg_a", [NSC, 128, NDC * 128], F32, kind="ExternalInput").ap()
    swu_a = dt("swu_a", [NSC, 128, NDC * 128], F32, kind="ExternalInput").ap()
    swd_a = dt("swd_a", [NDC, 128, NSC * 128], F32, kind="ExternalInput").ap()
    wo_a = dt("wo_a", [NDC, 128, NDC * 128], F32, kind="ExternalInput").ap()
    bo_t = dt("bo_t", [128, NDC], F32, kind="ExternalInput").ap()
    cos_t = dt("cos_t", [128, N_TOK], F32, kind="ExternalInput").ap()
    sin_t = dt("sin_t", [128, N_TOK], F32, kind="ExternalInput").ap()
    mask_d = dt("mask_d", [128, 1024], F32, kind="ExternalInput").ap()
    identbf_d = dt("identbf_d", [128, 128], BF16, kind="ExternalInput").ap()
    identf_d = dt("identf_d", [128, 128], F32, kind="ExternalInput").ap()
    onesrow_d = dt("onesrow_d", [1, 128], F32, kind="ExternalInput").ap()
    onescol_d = dt("onescol_d", [128, 1], BF16, kind="ExternalInput").ap()
    onescolf_d = dt("onescolf_d", [128, 1], F32, kind="ExternalInput").ap()
    onehot_d = dt("onehot_d", [8, E * 128], F32, kind="ExternalInput").ap()
    fconst_d = dt("fconst_d", [128, 4], F32, kind="ExternalInput").ap()
    out_own = dt("out_own", [TOK, D], F32, kind="ExternalOutput").ap()

    xn_bounce = dt("xn_bounce", [D, TOK], BF16)
    xn_gath = dt("xn_gath", [NCORES * D, TOK], BF16)
    a_bounce = [dt(f"a_bounce{h}", [128, N_TOK], BF16) for h in range(HL)]
    a_gath = [dt(f"a_gath{h}", [NCORES * 128, N_TOK], BF16) for h in range(HL)]
    q_dram = dt("q_dram", [HL, 128, N_TOK], BF16)
    k_dram = dt("k_dram", [HL, 128, N_TOK], BF16)
    v_dram = dt("v_dram", [HL, N_TOK, 128], BF16)
    if taps:
        tap_xnT = dt("tap_xnT", [128, NDC * TOK], BF16, kind="ExternalOutput").ap()
        tap_AT = dt("tap_AT", [128, HL * N_TOK], BF16, kind="ExternalOutput").ap()
        tap_fa = dt("tap_fa", [128, NDC * TOK], BF16, kind="ExternalOutput").ap()
        tap_comb = dt("tap_comb", [8, TOK], F32, kind="ExternalOutput").ap()
        tap_mhat = dt("tap_mhat", [128, HL], F32, kind="ExternalOutput").ap()
        tap_kqm = dt("tap_kqm", [1, 8], F32, kind="ExternalOutput").ap()
        tap_st8 = dt("tap_st8", [128, 40], F32, kind="ExternalOutput").ap()
        tap_q = dt("tap_q", [HL, 128, N_TOK], BF16, kind="ExternalOutput").ap()
        tap_k = dt("tap_k", [HL, 128, N_TOK], BF16, kind="ExternalOutput").ap()
        tap_v = dt("tap_v", [HL, N_TOK, 128], BF16, kind="ExternalOutput").ap()

    s = Sched()

    with contextlib.ExitStack() as es:
        sb = lambda name, shape, dtp: es.enter_context(nc.sbuf_tensor(name, shape, dtp))
        psmk = lambda name, shape, dtp: es.enter_context(nc.psum_tensor(name, shape, dtp))

        xnT = sb("xnT", [128, NDC * TOK], BF16)
        AT = sb("AT", [128, HL * N_TOK], BF16)
        fa = sb("fa", [128, NDC * TOK], BF16)
        comb_rows = sb("comb_rows", [8, TOK], F32)
        identbf = sb("identbf", [128, 128], BF16)
        identf = sb("identf", [128, 128], F32)
        onesrow = sb("onesrow", [1, 128], F32)
        onescol = sb("onescol", [128, 1], BF16)
        onescolf = sb("onescolf", [128, 1], F32)
        bqkv = sb("bqkv", [128, 12], F32)
        bot = sb("bot", [128, NDC], F32)
        gw = sb("gw", [128, NDC * E], BF16)
        mhat = sb("mhat", [128, HL], F32)
        fco = sb("fco", [128, 4], F32)
        st8 = sb("st8", [128, 40], F32)
        kqm = sb("kqm", [1, 8], F32)
        a16 = sb("a16", [128, 30720], BF16)
        a32 = sb("a32", [128, 4608], F32)

        pfh = [psmk(f"pf{i}", [128, 512], F32) for i in range(6)]
        pf = [h[:, :] for h in pfh]
        pt = [psmk(f"pt{i}", [128, 1024], BF16) for i in range(2)]
        bank_free = [None] * 6
        pt_free = [None] * 2

        sems = {}
        for nm in (["cc1", "cc2", "cc3", "tS", "vS", "sS"]
                   + [f"dS{i}" for i in range(DMA_SLOTS)]
                   + [f"dG{i}" for i in range(DMA_SLOTS)]):
            sems[nm] = es.enter_context(nc.semaphore(nm))

        def sync_dma(dst, src, waits=()):
            return s.dma("sync", lambda e, d=dst, sr=src: e.dma_start(out=d, in_=sr),
                         waits=waits)

        def gp_dma(dst, src, waits=(), accum=False):
            if accum:
                return s.dma("gpsimd",
                             lambda e, d=dst, sr=src: e.dma_start(out=d, in_=sr, accum_op=ALU.add),
                             waits=waits)
            return s.dma("gpsimd", lambda e, d=dst, sr=src: e.dma_start(out=d, in_=sr),
                         waits=waits)

        def bar():
            return {nm: s.val(nm) for nm in ("dS", "dG", "tS", "vS", "sS")}

        def wall(bc, only=None):
            return tuple((nm, v) for nm, v in bc.items()
                         if v > 0 and (only is None or nm in only))

        def bwait(i, extra=()):
            w = list(extra)
            if bank_free[i] is not None:
                w.append(bank_free[i])
            return tuple(w)

        sync_dma(identbf[:, :], identbf_d)
        sync_dma(identf[:, :], identf_d)
        sync_dma(onesrow[:, :], onesrow_d)
        sync_dma(onescol[:, :], onescol_d)
        sync_dma(onescolf[:, :], onescolf_d)
        sync_dma(bqkv[:, :], b_qkv)
        sync_dma(bot[:, :], bo_t)
        sync_dma(fco[:, :], fconst_d)
        c_consts = s.val("dS")
        gp_dma(gw[:, :], gate_w_a)
        c_gw = s.val("dG")

        NTT = TOK // 128
        o_xt = [0, 2048]
        o_xn = [24576, 26624]
        x_free = [None, None]
        xn_free = [None, None]
        ln_dma = []
        for tt in range(NTT):
            b = tt % 2
            w = [("dS", c_consts)]
            if x_free[b] is not None:
                w.append(x_free[b])
            sync_dma(a32[:, o_xt[b]:o_xt[b] + 2048], x_own[tt * 128:(tt + 1) * 128, :], waits=tuple(w))
            ln_dma.append(s.val("dS"))
            if tt == 1:
                break
        for tt in range(NTT):
            b = tt % 2
            if tt >= 2:
                w2 = [("dS", c_consts), x_free[b]]
                sync_dma(a32[:, o_xt[b]:o_xt[b] + 2048], x_own[tt * 128:(tt + 1) * 128, :],
                         waits=tuple(w2))
                ln_dma.append(s.val("dS"))
            xt = a32[:, o_xt[b]:o_xt[b] + 2048]
            xn = a16[:, o_xn[b]:o_xn[b] + 2048]
            w = [("dS", ln_dma[tt])]
            if xn_free[b] is not None:
                w.append(xn_free[b])
            B0 = (tt % 2) * 16
            stc = lambda i: st8[:, B0 + i:B0 + i + 1]
            s.op("vector", lambda e, xt=xt, stc=stc: e.reduce_sum(stc(0), xt, axis=AX.X),
                 waits=tuple(w))
            xsq = a16[:, o_xn[b] - 24576 + 20480:o_xn[b] - 24576 + 20480 + 2048]
            cssq = s.op("scalar", lambda e, xt=xt, xsq=xsq, stc=stc: e.activation(
                xsq, xt, AF.Square, accum_out=stc(1)), waits=tuple(w), incs=(("sS", 1),))
            s.op("vector", lambda e, stc=stc: e.tensor_scalar_mul(stc(2), stc(0), 1.0 / D))
            s.op("vector", lambda e, stc=stc: e.tensor_mul(stc(3), stc(2), stc(2)))
            cv = s.op("vector", lambda e, stc=stc: e.scalar_tensor_tensor(
                stc(4), stc(1), 1.0 / D, stc(3), ALU.mult, ALU.subtract),
                waits=(("sS", cssq),), incs=(("vS", 1),))
            s.op("scalar", lambda e, stc=stc: e.activation(stc(5), stc(4), AF.Ln, bias=fco[:, 0:1]),
                 waits=(("vS", cv),))
            cs = s.op("scalar", lambda e, stc=stc: e.activation(stc(6), stc(5), AF.Exp, scale=-0.5),
                      incs=(("sS", 1),))
            cv2 = s.op("vector", lambda e, stc=stc: e.tensor_mul(stc(8), stc(2), stc(6)),
                       waits=(("sS", cs),))
            cv2 = s.op("vector", lambda e, stc=stc: e.tensor_scalar_mul(stc(9), stc(8), -1.0),
                       incs=(("vS", 1),))
            cs2 = s.op("scalar", lambda e, xt=xt, xn=xn, stc=stc: e.activation(
                xn, xt, AF.Identity, bias=stc(9), scale=stc(6)),
                waits=(("vS", cv2),), incs=(("sS", 1),))
            x_free[b] = ("sS", cs2)
            lastv = None
            for dc in range(NDC):
                pb = pt[dc % 2]
                wt = [("sS", cs2)]
                if pt_free[dc % 2] is not None:
                    wt.append(pt_free[dc % 2])
                ct = s.op("tensor", lambda e, pb=pb, xn=xn, dc=dc: e.transpose(
                    pb[:, 0:128], xn[:, dc * 128:(dc + 1) * 128], identbf[:, :]),
                    waits=tuple(wt), incs=(("tS", 1),))
                lastv = s.op("vector", lambda e, pb=pb, dc=dc, tt=tt: e.tensor_copy(
                    xnT[:, dc * TOK + tt * 128: dc * TOK + (tt + 1) * 128], pb[:, 0:128]),
                    waits=(("tS", ct),), incs=(("vS", 1),))
                pt_free[dc % 2] = ("vS", lastv)
            xn_free[b] = ("vS", lastv)
        c_ln_v = s.val("vS")
        gp_dma(xn_bounce.ap().rearrange("(c p) t -> p c t", p=128),
               xnT[:, :].rearrange("p (c t) -> p c t", c=NDC), waits=(("vS", c_ln_v),))
        c_xnb = s.val("dG")
        if taps:
            sync_dma(tap_xnT, xnT[:, :])
            sync_dma(tap_st8, st8[:, :])
        s.op("gpsimd", lambda e: e.collective_compute(
            "AllGather", ALU.bypass, replica_groups=[list(range(NCORES))],
            ins=[xn_bounce.ap().opt()], outs=[xn_gath.ap().opt()]),
            waits=(("dG", c_xnb),), incs=(("cc1", 1),))
        sync_dma(out_own, x_own)
        c_xcopy = s.val("dS")

        bc0 = bar()
        for tt in range(NTT):
            bi = tt % 2
            wfirst = bwait(bi, wall(bc0) + (("dG", c_gw),))
            for dc in range(NDC):
                sp = (dc == NDC - 1)
                inc = (("tS", 1),) if sp else ()
                s.op("tensor", lambda e, bi=bi, dc=dc, tt=tt, sp=sp: e.matmul(
                    pf[bi][:, 0:8], xnT[:, dc * TOK + tt * 128: dc * TOK + (tt + 1) * 128],
                    gw[:, dc * 8:(dc + 1) * 8], start=(dc == 0), stop=sp,
                    skip_group_check=True),
                    waits=wfirst if dc == 0 else (), incs=inc)
                if dc == 0:
                    wfirst = ()
            ct = s.val("tS")
            B0 = (tt % 2) * 16
            stc = lambda i: st8[:, B0 + i:B0 + i + 1]
            ga = (tt % 2) * 128
            s.op("vector", lambda e, bi=bi, stc=stc: e.reduce_max(stc(0), pf[bi][:, 0:8], axis=AX.X),
                 waits=(("tS", ct),))
            cm = s.op("vector", lambda e, stc=stc: e.tensor_scalar_mul(stc(1), stc(0), -1.0),
                      incs=(("vS", 1),))
            ce = s.op("scalar", lambda e, bi=bi, stc=stc, ga=ga: e.activation(
                a32[:, ga + 0:ga + 8], pf[bi][:, 0:8], AF.Exp, bias=stc(1), accum_out=stc(2)),
                waits=(("vS", cm),), incs=(("sS", 1),))
            bank_free[bi] = ("sS", ce)
            cr = s.op("vector", lambda e, stc=stc: e.reciprocal(stc(3), stc(2)),
                      waits=(("sS", ce),), incs=(("vS", 1),))
            cn = s.op("scalar", lambda e, stc=stc, ga=ga: e.activation(
                a32[:, ga + 32:ga + 40], a32[:, ga + 0:ga + 8], AF.Identity, scale=stc(3)),
                waits=(("vS", cr),), incs=(("sS", 1),))
            scs = a32[:, ga + 32:ga + 40]
            s.op("vector", lambda e, scs=scs, stc=stc: e.reduce_max(stc(4), scs, axis=AX.X),
                 waits=(("sS", cn),))
            s.op("vector", lambda e, scs=scs, stc=stc, ga=ga: e.scalar_tensor_tensor(
                a32[:, ga + 64:ga + 72], scs, stc(4), scs, ALU.is_lt, ALU.mult))
            s.op("vector", lambda e, stc=stc, ga=ga: e.reduce_max(stc(5), a32[:, ga + 64:ga + 72], axis=AX.X))
            cv = s.op("vector", lambda e, scs=scs, stc=stc, ga=ga: e.scalar_tensor_tensor(
                a32[:, ga + 96:ga + 104], scs, stc(5), scs, ALU.is_ge, ALU.mult),
                incs=(("vS", 1),))
            ctr = s.op("tensor", lambda e, ga=ga: e.transpose(
                pf[2][0:8, 0:128], a32[:, ga + 96:ga + 104], identf[:, :]),
                waits=bwait(2, (("vS", cv),)), incs=(("tS", 1),))
            cvc = s.op("vector", lambda e, tt=tt: e.tensor_copy(
                comb_rows[0:8, tt * 128:(tt + 1) * 128], pf[2][0:8, 0:128]),
                waits=(("tS", ctr),), incs=(("vS", 1),))
            bank_free[2] = ("vS", cvc)
        c_comb_v = s.val("vS")

        o_wgb = [0, 2048]
        o_wub = [4096, 6144]
        o_wdb = [8192, 9600]
        o_G = 11008
        o_sil = [22272, 22784]
        o_t1 = [23296, 23808]
        o_cB = [0, 512]
        o_oh = 1024
        bc1 = bar()
        sync_dma(a32[0:8, o_oh:o_oh + E * 128], onehot_d, waits=wall(bc1, ("vS", "sS")))
        c_oh = s.val("dS")

        wgu_free = [None, None]
        wd_free = [None, None]
        sil_free = [None, None]
        moe_dma = {}

        def w_chunk(src, off, ln, waits):
            gp_dma(a16[:, off:off + ln], src, waits=waits)
            return s.val("dG")

        for (ei, hci, b) in ((0, 0, 0), (0, 1, 1)):
            moe_dma[("g", ei, hci)] = w_chunk(wg_a[ei, hci], o_wgb[b], NDC * 128, ())
            moe_dma[("u", ei, hci)] = w_chunk(wu_a[ei, hci], o_wub[b], NDC * 128, ())

        first_moe = wall(bc1) + (("vS", c_comb_v), ("dS", c_oh))

        def gateup_phase(nchunks, dma_of, prefetch, use_comb=True, first=()):
            lastG = None
            tS0 = ("tS", s.val("tS"))
            for hc in range(nchunks):
                bks = [(4 * hc + i) % 6 for i in range(4)]
                wdma = dma_of(hc)
                for tc in range(2):
                    gb, ub = bks[2 * tc], bks[2 * tc + 1]
                    wfirst = list(wdma) + list(bwait(gb))
                    if hc == 0 and tc == 0:
                        wfirst += list(first)
                    for dc in range(NDC):
                        s.op("tensor", lambda e, gb=gb, dc=dc, tc=tc, hc=hc: e.matmul(
                            pf[gb], a16[:, o_wgb[hc % 2] + dc * 128:o_wgb[hc % 2] + (dc + 1) * 128],
                            xnT[:, dc * TOK + tc * 512: dc * TOK + tc * 512 + 512],
                            start=(dc == 0), stop=(dc == NDC - 1), skip_group_check=True),
                            waits=tuple(wfirst) if dc == 0 else (),
                            incs=(("tS", 1),) if dc == NDC - 1 else ())
                        wfirst = []
                    ctg = s.val("tS")
                    wfirst = list(bwait(ub))
                    for dc in range(NDC):
                        s.op("tensor", lambda e, ub=ub, dc=dc, tc=tc, hc=hc: e.matmul(
                            pf[ub], a16[:, o_wub[hc % 2] + dc * 128:o_wub[hc % 2] + (dc + 1) * 128],
                            xnT[:, dc * TOK + tc * 512: dc * TOK + tc * 512 + 512],
                            start=(dc == 0), stop=(dc == NDC - 1), skip_group_check=True),
                            waits=tuple(wfirst) if dc == 0 else (),
                            incs=(("tS", 1),) if dc == NDC - 1 else ())
                        wfirst = []
                    ctu = s.val("tS")
                    sb_, tb_ = o_sil[tc], o_t1[tc]
                    wsil = [("tS", ctg)]
                    if sil_free[tc] is not None:
                        wsil.append(sil_free[tc])
                    cs = s.op("scalar", lambda e, gb=gb, sb_=sb_: e.activation(
                        a16[:, sb_:sb_ + 512], pf[gb], AF.Silu),
                        waits=tuple(wsil), incs=(("sS", 1),))
                    bank_free[gb] = ("sS", cs)
                    if use_comb:
                        cv1 = s.op("vector", lambda e, ub=ub, tb_=tb_, tc=tc: e.tensor_tensor(
                            a16[:, tb_:tb_ + 512], pf[ub], a32[:, o_cB[tc]:o_cB[tc] + 512],
                            ALU.mult), waits=(("tS", ctu),), incs=(("vS", 1),))
                        bank_free[ub] = ("vS", cv1)
                        lastG = s.op("vector", lambda e, sb_=sb_, tb_=tb_, hc=hc, tc=tc: e.tensor_mul(
                            a16[:, o_G + hc * 1024 + tc * 512:o_G + hc * 1024 + tc * 512 + 512],
                            a16[:, tb_:tb_ + 512], a16[:, sb_:sb_ + 512]),
                            waits=(("sS", cs), tS0), incs=(("vS", 1),))
                        sil_free[tc] = ("vS", lastG)
                    else:
                        lastG = s.op("vector", lambda e, ub=ub, sb_=sb_, hc=hc, tc=tc: e.tensor_tensor(
                            a16[:, o_G + hc * 1024 + tc * 512:o_G + hc * 1024 + tc * 512 + 512],
                            pf[ub], a16[:, sb_:sb_ + 512], ALU.mult),
                            waits=(("tS", ctu), ("sS", cs), tS0), incs=(("vS", 1),))
                        bank_free[ub] = ("vS", lastG)
                        sil_free[tc] = ("vS", lastG)
                wgu_free[hc % 2] = ("tS", ctu)
                prefetch(hc, wgu_free[hc % 2])
            return lastG

        def down_phase(nchunks, getwd, wd_len, first_copy=False, pre_dma=None):
            cG = s.val("vS")
            if pre_dma is not None:
                dma_d = dict(pre_dma)
            else:
                dma_d = {}
                for d2c in range(2):
                    w = [wd_free[d2c % 2]] if wd_free[d2c % 2] is not None else []
                    dma_d[d2c] = getwd(d2c, d2c % 2, tuple(w))
            for d2c in range(NDC):
                bwf = o_wdb[d2c % 2]
                ctd_last = None
                for tc in range(2):
                    dbk = 4 + tc
                    w = [("dG", dma_d[d2c]), ("vS", cG)] + list(bwait(dbk))
                    for hc in range(nchunks):
                        s.op("tensor", lambda e, dbk=dbk, bwf=bwf, hc=hc, tc=tc: e.matmul(
                            pf[dbk], a16[:, bwf + hc * 128:bwf + (hc + 1) * 128],
                            a16[:, o_G + hc * 1024 + tc * 512:o_G + hc * 1024 + tc * 512 + 512],
                            start=(hc == 0), stop=(hc == nchunks - 1), skip_group_check=True),
                            waits=tuple(w) if hc == 0 else (),
                            incs=(("tS", 1),) if hc == nchunks - 1 else ())
                        w = []
                    ctd = s.val("tS")
                    ctd_last = ctd
                    if first_copy:
                        cv = s.op("vector", lambda e, dbk=dbk, d2c=d2c, tc=tc: e.tensor_copy(
                            fa[:, d2c * TOK + tc * 512: d2c * TOK + tc * 512 + 512], pf[dbk]),
                            waits=(("tS", ctd),), incs=(("vS", 1),))
                    else:
                        cv = s.op("vector", lambda e, dbk=dbk, d2c=d2c, tc=tc: e.tensor_add(
                            fa[:, d2c * TOK + tc * 512: d2c * TOK + tc * 512 + 512],
                            pf[dbk], fa[:, d2c * TOK + tc * 512: d2c * TOK + tc * 512 + 512]),
                            waits=(("tS", ctd),), incs=(("vS", 1),))
                    bank_free[dbk] = ("vS", cv)
                wd_free[d2c % 2] = ("tS", ctd_last)
                if d2c + 2 < NDC:
                    dma_d[d2c + 2] = getwd(d2c + 2, d2c % 2, (wd_free[d2c % 2],))

        for e_i in range(E):
            for tc in range(2):
                w = list(bwait(4))
                if e_i == 0 and tc == 0:
                    w += list(first_moe)
                ct = s.op("tensor", lambda e, e_i=e_i, tc=tc: e.matmul(
                    pf[4], a32[0:8, o_oh + e_i * 128:o_oh + (e_i + 1) * 128],
                    comb_rows[0:8, tc * 512:(tc + 1) * 512], start=True, stop=True,
                    skip_group_check=True),
                    waits=tuple(w), incs=(("tS", 1),))
                cv = s.op("vector", lambda e, tc=tc: e.tensor_copy(
                    a32[:, o_cB[tc]:o_cB[tc] + 512], pf[4]),
                    waits=(("tS", ct),), incs=(("vS", 1),))
                bank_free[4] = ("vS", cv)

            def dma_of(hc, e_i=e_i):
                return [("dG", moe_dma[("g", e_i, hc)]), ("dG", moe_dma[("u", e_i, hc)])]

            def prefetch(hc, free_evt, e_i=e_i):
                if hc + 2 < NHC:
                    nh = hc + 2
                    if ("g", e_i, nh) not in moe_dma:
                        moe_dma[("g", e_i, nh)] = w_chunk(wg_a[e_i, nh], o_wgb[nh % 2],
                                                          NDC * 128, (free_evt,))
                        moe_dma[("u", e_i, nh)] = w_chunk(wu_a[e_i, nh], o_wub[nh % 2],
                                                          NDC * 128, ())
                elif hc == NHC - 1 and e_i + 1 < E:
                    for nh in (0, 1):
                        moe_dma[("g", e_i + 1, nh)] = w_chunk(
                            wg_a[e_i + 1, nh], o_wgb[nh % 2], NDC * 128,
                            (wgu_free[nh % 2],))
                        moe_dma[("u", e_i + 1, nh)] = w_chunk(
                            wu_a[e_i + 1, nh], o_wub[nh % 2], NDC * 128, ())

            def getwd(d2c, b, w, e_i=e_i):
                return w_chunk(wd_a[e_i, d2c], o_wdb[b], NHC * 128, w)

            pre_wd = {}
            for d2c in range(2):
                w = [wd_free[d2c % 2]] if wd_free[d2c % 2] is not None else []
                pre_wd[d2c] = getwd(d2c, d2c % 2, tuple(w))
            gateup_phase(NHC, dma_of, prefetch, use_comb=True,
                         first=first_moe if e_i == 0 else ())
            down_phase(NHC, getwd, NHC * 128, first_copy=(e_i == 0), pre_dma=pre_wd)

        bc2 = bar()
        o_xg = [0, 4096, 8192]
        o_wq, o_wk, o_wv = 12288, 16384, 20480
        o_qs = [24576, 25088]
        o_ks = [25600, 26112]
        o_vs = [26624, 27136, 27648, 28160]
        o_vt = [28672, 29184]
        o_cos = [0, 512]
        o_sin = [1024, 1536]
        o_rt1, o_rt2 = 2048, 2560
        gp_dma(a16[:, o_wq:o_wq + NDC * 256], wq_own, waits=wall({"tS": bc2["tS"], "dG": bc2["dG"]}))
        gp_dma(a16[:, o_wk:o_wk + NDC * 256], wk_own)
        gp_dma(a16[:, o_wv:o_wv + NDC * 256], wv_own)
        c_wqkv = s.val("dG")
        cvz = s.op("vector", lambda e: e.memset(kqm[0:1, 0:8], 0.0), waits=wall(bc2, ("sS",)),
                   incs=(("vS", 1),))
        c_kqm0 = s.val("vS")

        NU = 2 * NTT * 2
        xg_dma = {}
        xg_used = {}

        def xg_load(u, waits):
            tc, half = u // 2, u % 2
            r = tc // 2
            l0 = (tc % 2) * 512
            rows0 = r * D + half * 1024
            src = xn_gath.ap()[rows0:rows0 + 1024, l0:l0 + 512].rearrange("(c p) t -> p c t", p=128)
            off = o_xg[u % 3]
            sync_dma(a16[:, off:off + 4096].rearrange("p (c t) -> p c t", c=8), src, waits=waits)
            xg_dma[u] = s.val("dS")

        for u in range(3):
            xg_load(u, wall(bc2, ("vS", "tS")) + (("cc1", 1),))

        bank_map = [("q", 0, 0), ("k", 0, 1), ("v", 0, 2), ("q", 1, 3), ("k", 1, 4), ("v", 1, 5)]
        cos_free = [None, None]
        qs_free = [None, None]
        ks_free = [None, None]
        vs_free = [None, None, None, None]
        vt_free = [None, None]
        norm_red = None
        deferred_v = []

        def emit_deferred_v():
            nonlocal deferred_v
            for (h, tc, vsl, cse) in deferred_v:
                b2 = vsl // 2
                lastv = None
                ctv = None
                for sub in range(4):
                    pb = pt[sub % 2]
                    wt = [("sS", cse)]
                    if pt_free[sub % 2] is not None:
                        wt.append(pt_free[sub % 2])
                    ctv = s.op("tensor", lambda e, pb=pb, sub=sub, vsl=vsl: e.transpose(
                        pb[:, 0:128], a16[:, o_vs[vsl] + sub * 128:o_vs[vsl] + (sub + 1) * 128],
                        identbf[:, :]), waits=tuple(wt), incs=(("tS", 1),))
                    wcp = [("tS", ctv)]
                    if sub == 0 and vt_free[b2] is not None:
                        wcp.append(vt_free[b2])
                    lastv = s.op("vector", lambda e, pb=pb, sub=sub, b2=b2: e.tensor_copy(
                        a16[:, o_vt[b2] + sub * 128:o_vt[b2] + (sub + 1) * 128], pb[:, 0:128]),
                        waits=tuple(wcp), incs=(("vS", 1),))
                    pt_free[sub % 2] = ("vS", lastv)
                dst = v_dram.ap()[h, tc * 512:(tc + 1) * 512, :].rearrange(
                    "(c p) d -> p c d", p=128)
                gp_dma(dst, a16[:, o_vt[b2]:o_vt[b2] + 512].rearrange("p (c d) -> p c d", c=4),
                       waits=(("vS", lastv),))
                vt_free[b2] = ("dG", s.val("dG"))
                vs_free[vsl] = ("tS", ctv)
            deferred_v = []

        for tc in range(2 * NTT):
            b2 = tc % 2
            w = [("dS", c_consts)]
            if cos_free[b2] is not None:
                w.append(cos_free[b2])
            else:
                w += list(wall(bc2, ("vS", "tS")))
            sync_dma(a32[:, o_cos[b2]:o_cos[b2] + 512], cos_t[:, tc * 512:(tc + 1) * 512],
                     waits=tuple(w))
            sync_dma(a32[:, o_sin[b2]:o_sin[b2] + 512], sin_t[:, tc * 512:(tc + 1) * 512])
            c_cs = s.val("dS")
            proj_done = {}
            for half in range(2):
                u = tc * 2 + half
                whalf = [("dG", c_wqkv), ("dS", xg_dma[u])]
                for wnm, h, bk in bank_map:
                    woff = {"q": o_wq, "k": o_wk, "v": o_wv}[wnm]
                    wfirst = list(whalf)
                    if half == 0:
                        wfirst += list(bwait(bk))
                    for dci in range(8):
                        dc = half * 8 + dci
                        s.op("tensor", lambda e, bk=bk, woff=woff, dc=dc, h=h, u=u: e.matmul(
                            pf[bk], a16[:, woff + dc * 256 + h * 128: woff + dc * 256 + h * 128 + 128],
                            a16[:, o_xg[u % 3] + (dc % 8) * 512: o_xg[u % 3] + (dc % 8) * 512 + 512],
                            start=(dc == 0), stop=(dc == NDC - 1), skip_group_check=True),
                            waits=tuple(wfirst) if dci == 0 else (),
                            incs=(("tS", 1),) if (dci == 7) else ())
                        wfirst = []
                    if half == 1:
                        proj_done[bk] = s.val("tS")
                xg_used[u] = s.val("tS")
                if u + 3 < NU:
                    xg_load(u + 3, (("tS", xg_used[u]),))
            ct_all = s.val("tS")
            emit_deferred_v()
            for wnm, h, bk in bank_map:
                bcol = {"q": 0, "k": 2, "v": 4}[wnm] + h
                if wnm in ("q", "k"):
                    stg = (o_qs if wnm == "q" else o_ks)[b2]
                    sfree = (qs_free if wnm == "q" else ks_free)[b2]
                    wq_ = [("tS", proj_done[bk]), ("dS", c_cs)]
                    s.op("vector", lambda e, bk=bk, bcol=bcol, b2=b2: e.scalar_tensor_tensor(
                        a32[:, o_rt1:o_rt1 + 512], pf[bk], bqkv[:, bcol:bcol + 1],
                        a32[:, o_cos[b2]:o_cos[b2] + 512], ALU.add, ALU.mult),
                        waits=tuple(wq_))
                    s.op("vector", lambda e, bk=bk, bcol=bcol, b2=b2: e.scalar_tensor_tensor(
                        a32[0:64, o_rt2:o_rt2 + 512], pf[bk][64:128, :],
                        bqkv[0:64, 6 + bcol:7 + bcol],
                        a32[0:64, o_sin[b2]:o_sin[b2] + 512], ALU.add, ALU.mult))
                    cv = s.op("vector", lambda e, bk=bk, bcol=bcol, b2=b2: e.scalar_tensor_tensor(
                        a32[64:128, o_rt2:o_rt2 + 512], pf[bk][0:64, :],
                        bqkv[64:128, 6 + bcol:7 + bcol],
                        a32[64:128, o_sin[b2]:o_sin[b2] + 512], ALU.add, ALU.mult),
                        incs=(("vS", 1),))
                    cos_free[b2] = ("vS", cv)
                    wadd = [("vS", cv)]
                    if sfree is not None:
                        wadd.append(sfree)
                    cvr = s.op("vector", lambda e, stg=stg: e.tensor_add(
                        a16[:, stg:stg + 512], a32[:, o_rt1:o_rt1 + 512],
                        a32[:, o_rt2:o_rt2 + 512]), waits=tuple(wadd), incs=(("vS", 1),))
                    dst = (q_dram if wnm == "q" else k_dram).ap()[h, :, tc * 512:(tc + 1) * 512]
                    gp_dma(dst, a16[:, stg:stg + 512], waits=(("vS", cvr),))
                    cdma = s.val("dG")
                    if wnm == "q":
                        qs_free[b2] = ("dG", cdma)
                    else:
                        ks_free[b2] = ("dG", cdma)
                    bank_free[bk] = ("vS", cv)
                else:
                    wv_ = [("tS", proj_done[bk])]
                    if vs_free[2 * b2 + h] is not None:
                        wv_.append(vs_free[2 * b2 + h])
                    vsl = 2 * b2 + h
                    cse = s.op("scalar", lambda e, bk=bk, bcol=bcol, vsl=vsl: e.activation(
                        a16[:, o_vs[vsl]:o_vs[vsl] + 512], pf[bk], AF.Identity,
                        bias=bqkv[:, bcol:bcol + 1]), waits=tuple(wv_), incs=(("sS", 1),))
                    bank_free[bk] = ("sS", cse)
                    deferred_v.append((h, tc, vsl, cse))
        emit_deferred_v()
        c_qkv_sp = s.val("dS")
        bcn = bar()
        nb_free = [None, None]
        norm_red = None
        nslice = 0
        for h in range(HL):
            for wnm, kcol in (("q", 0), ("k", 2)):
                srcd = (q_dram if wnm == "q" else k_dram).ap()
                for ch in range(2):
                    bb = nslice % 2
                    w = list(wall(bcn, ("tS", "dG"))) if nslice == 0 else []
                    if nb_free[bb] is not None:
                        w.append(nb_free[bb])
                    sync_dma(a16[:, bb * 4096:bb * 4096 + 4096],
                             srcd[h, :, ch * 4096:(ch + 1) * 4096], waits=tuple(w))
                    cd = s.val("dS")
                    lastn = None
                    for sub in range(2):
                        csq = s.op("vector", lambda e, bb=bb, sub=sub: e.tensor_mul(
                            a32[:, bb * 2048:bb * 2048 + 2048],
                            a16[:, bb * 4096 + sub * 2048:bb * 4096 + sub * 2048 + 2048],
                            a16[:, bb * 4096 + sub * 2048:bb * 4096 + sub * 2048 + 2048]),
                            waits=(("dS", cd),), incs=(("vS", 1),))
                        for sl in range(4):
                            pb = sl % 2
                            wm = [("vS", csq)] + list(bwait(pb))
                            ctn = s.op("tensor", lambda e, pb=pb, bb=bb, sl=sl: e.matmul(
                                pf[pb][0:1, :], onescolf[:, :],
                                a32[:, bb * 2048 + sl * 512:bb * 2048 + (sl + 1) * 512],
                                start=True, stop=True, skip_group_check=True),
                                waits=tuple(wm), incs=(("tS", 1),))
                            s.op("vector", lambda e, pb=pb: e.reduce_max(
                                st8[0:1, 10:11], pf[pb][0:1, :], axis=AX.X),
                                waits=(("tS", ctn), ("vS", c_kqm0)))
                            cvn = s.op("vector", lambda e, kcol=kcol, h=h: e.tensor_max(
                                kqm[0:1, kcol + h:kcol + h + 1],
                                kqm[0:1, kcol + h:kcol + h + 1], st8[0:1, 10:11]),
                                incs=(("vS", 1),))
                            bank_free[pb] = ("vS", cvn)
                            lastn = cvn
                    nb_free[bb] = ("vS", lastn)
                    norm_red = ("vS", lastn)
                    nslice += 1
        for h in range(HL):
            cv = s.op("vector", lambda e, h=h: e.tensor_mul(
                st8[0:1, 11:12], kqm[0:1, h:h + 1], kqm[0:1, 2 + h:3 + h]),
                waits=(norm_red,), incs=(("vS", 1),))
            s.op("scalar", lambda e: e.activation(st8[0:1, 12:13], st8[0:1, 11:12], AF.Ln),
                 waits=(("vS", cv),))
            cs = s.op("scalar", lambda e: e.activation(
                st8[0:1, 13:14], st8[0:1, 12:13], AF.Exp, scale=0.5, bias=fco[0:1, 1:2]),
                incs=(("sS", 1),))
            cv = s.op("vector", lambda e: e.tensor_scalar_mul(st8[0:1, 14:15], st8[0:1, 13:14], -1.0),
                      waits=(("sS", cs),), incs=(("vS", 1),))
            ctb = s.op("tensor", lambda e: e.matmul(
                pf[5][:, 0:1], onesrow[:, :], st8[0:1, 14:15],
                start=True, stop=True, skip_group_check=True),
                waits=bwait(5, (("vS", cv),)), incs=(("tS", 1),))
            cvm = s.op("vector", lambda e, h=h: e.tensor_copy(mhat[:, h:h + 1], pf[5][:, 0:1]),
                       waits=(("tS", ctb),), incs=(("vS", 1),))
            bank_free[5] = ("vS", cvm)
            norm_red = ("vS", cvm)
        c_mhat_v = s.val("vS")

        sh_dma = {}

        def sh_dma_of_factory(half):
            def dma_of(hc):
                ghc = half * 11 + hc
                return [("dG", sh_dma[("g", ghc)]), ("dG", sh_dma[("u", ghc)])]
            return dma_of

        def sh_prefetch_factory(half):
            def prefetch(hc, free_evt):
                nh = hc + 2
                if nh < 11 and ("g", half * 11 + nh) not in sh_dma:
                    sh_dma[("g", half * 11 + nh)] = w_chunk(
                        swg_a[half * 11 + nh], o_wgb[hc % 2], NDC * 128, (free_evt,))
                    sh_dma[("u", half * 11 + nh)] = w_chunk(
                        swu_a[half * 11 + nh], o_wub[hc % 2], NDC * 128, ())
            return prefetch

        def sh_getwd_factory(half):
            def getwd(d2c, b, w):
                return w_chunk(swd_a[d2c][:, half * 1408:(half + 1) * 1408], o_wdb[b],
                               11 * 128, w)
            return getwd

        bcP4 = bar()
        for hci, b in ((0, 0), (1, 1)):
            pw = wall(bcP4, ("tS", "dG")) if hci == 0 else ()
            sh_dma[("g", hci)] = w_chunk(swg_a[hci], o_wgb[b], NDC * 128, pw)
            sh_dma[("u", hci)] = w_chunk(swu_a[hci], o_wub[b], NDC * 128, ())
        gateup_phase(11, sh_dma_of_factory(0), sh_prefetch_factory(0), use_comb=False)
        down_phase(11, sh_getwd_factory(0), 11 * 128)

        bc3 = bar()
        o_q = [0, 512]
        o_K = [1024, 1536, 2048]
        o_V = [2560, 3072, 3584]
        o_PT = [4096, 4608, 5120, 5632]
        o_mask = 0
        o_B = 1024
        o_rl = 1536
        SBANKS = [0, 1, 5]
        sync_dma(a32[:, o_mask:o_mask + 1024], mask_d, waits=wall(bc3, ("vS", "sS", "tS")))
        c_mask = s.val("dS")

        q_dma = {}
        q_freed = [None, None]
        kv_dma = {}
        kv_freed = [None, None, None]
        PT_freed = [None] * 4
        first_att = (("dS", c_mask), ("vS", c_mhat_v)) + wall(bc3, ("sS",))
        first_att_used = False

        def kv_load(h, kb):
            bb = kb % 3
            w = []
            if kv_freed[bb] is not None:
                w.append(kv_freed[bb])
            sync_dma(a16[:, o_K[bb]:o_K[bb] + 512], k_dram.ap()[h, :, kb * 512:(kb + 1) * 512],
                     waits=tuple(w))
            sync_dma(a16[:, o_V[bb]:o_V[bb] + 512].rearrange("p (c d) -> p c d", c=4),
                     v_dram.ap()[h, kb * 512:(kb + 1) * 512, :].rearrange("(c p) d -> p c d", p=128))
            kv_dma[(h, kb)] = s.val("dS")

        for h in range(HL):
            for qb in range(16):
                qbb = qb % 2
                w = []
                if q_freed[qbb] is not None:
                    w.append(q_freed[qbb])
                sync_dma(a16[:, o_q[qbb]:o_q[qbb] + 512],
                         q_dram.ap()[h, :, qb * 512:(qb + 1) * 512], waits=tuple(w))
                q_dma[(h, qb)] = s.val("dS")
                units = [(kb, jc) for kb in range(qb + 1) for jc in range(4)]
                NUNITS = len(units)
                ctS = {}
                maskv = {}

                kv_dma.clear()

                def emit_S(i, h=h, qb=qb, qbb=qbb, units=units):
                    nonlocal first_att_used
                    kb, jc = units[i]
                    if jc == 0:
                        for kbl in (kb, kb + 1):
                            if kbl <= qb and (h, kbl) not in kv_dma:
                                kv_load(h, kbl)
                    sbk = SBANKS[i % 3]
                    wS = [("dS", kv_dma[(h, kb)]), ("dS", q_dma[(h, qb)])]
                    wS += list(bwait(sbk))
                    if not first_att_used:
                        wS += list(first_att)
                        first_att_used = True
                    c = s.op("tensor", lambda e, sbk=sbk, kb=kb, jc=jc, qbb=qbb: e.matmul(
                        pf[sbk], a16[:, o_K[kb % 3] + jc * 128:o_K[kb % 3] + (jc + 1) * 128],
                        a16[:, o_q[qbb]:o_q[qbb] + 512],
                        start=True, stop=True, skip_group_check=True),
                        waits=tuple(wS), incs=(("tS", 1),))
                    ctS[i] = c
                    if kb == qb:
                        maskv[i] = s.op("vector", lambda e, sbk=sbk, jc=jc: e.tensor_add(
                            pf[sbk], pf[sbk],
                            a32[:, o_mask + 512 - 128 * jc:o_mask + 1024 - 128 * jc]),
                            waits=(("tS", c),), incs=(("vS", 1),))

                def emit_exp(i, h=h):
                    sbk = SBANKS[i % 3]
                    wE = [("tS", ctS[i])]
                    if i in maskv:
                        wE.append(("vS", maskv[i]))
                    if PT_freed[i % 4] is not None:
                        wE.append(PT_freed[i % 4])
                    c = s.op("scalar", lambda e, sbk=sbk, i=i, h=h: e.activation(
                        a16[:, o_PT[i % 4]:o_PT[i % 4] + 512], pf[sbk], AF.Exp,
                        bias=mhat[:, h:h + 1], scale=ATT_SCALE),
                        waits=tuple(wE), incs=(("sS", 1),))
                    bank_free[sbk] = ("sS", c)
                    return c

                exp_of = {}
                pv_last = None

                def emit_PV(i, qb=qb, units=units):
                    nonlocal pv_last
                    kb, jc = units[i]
                    fst = (i == 0)
                    lst = (i == len(units) - 1)
                    wPV = [("sS", exp_of[i])]
                    if fst:
                        wPV += list(bwait(2)) + list(bwait(3))
                    s.op("tensor", lambda e, kb=kb, jc=jc, i=i, fst=fst, lst=lst: e.matmul(
                        pf[2], a16[:, o_V[kb % 3] + jc * 128:o_V[kb % 3] + (jc + 1) * 128],
                        a16[:, o_PT[i % 4]:o_PT[i % 4] + 512],
                        start=fst, stop=lst, skip_group_check=True),
                        waits=tuple(wPV))
                    c = s.op("tensor", lambda e, i=i, fst=fst, lst=lst: e.matmul(
                        pf[3][0:1, :], onescol[:, :], a16[:, o_PT[i % 4]:o_PT[i % 4] + 512],
                        start=fst, stop=lst, skip_group_check=True),
                        incs=(("tS", 1),))
                    PT_freed[i % 4] = ("tS", c)
                    pv_last = c
                    if jc == 3:
                        kv_freed[kb % 3] = ("tS", c)

                AHEAD = 2
                for i in range(min(AHEAD, NUNITS)):
                    emit_S(i)
                    exp_of[i] = emit_exp(i)
                for i in range(NUNITS):
                    if i + AHEAD < NUNITS:
                        emit_S(i + AHEAD)
                        exp_of[i + AHEAD] = emit_exp(i + AHEAD)
                    emit_PV(i)
                q_freed[qbb] = ("tS", pv_last)
                crl = s.op("vector", lambda e: e.reciprocal(
                    a32[0:1, o_rl:o_rl + 512], pf[3][0:1, :]),
                    waits=(("tS", pv_last),), incs=(("vS", 1),))
                bank_free[3] = ("vS", crl)
                ctB = s.op("tensor", lambda e: e.matmul(
                    pf[4], onesrow[:, :], a32[0:1, o_rl:o_rl + 512],
                    start=True, stop=True, skip_group_check=True),
                    waits=bwait(4, (("vS", crl),)), incs=(("tS", 1),))
                cvB = s.op("vector", lambda e: e.tensor_copy(a32[:, o_B:o_B + 512], pf[4]),
                           waits=(("tS", ctB),), incs=(("vS", 1),))
                bank_free[4] = ("vS", cvB)
                cvO = s.op("vector", lambda e, h=h, qb=qb: e.tensor_tensor(
                    AT[:, h * N_TOK + qb * 512: h * N_TOK + (qb + 1) * 512],
                    pf[2], a32[:, o_B:o_B + 512], ALU.mult),
                    waits=(("vS", cvB),), incs=(("vS", 1),))
                bank_free[2] = ("vS", cvO)
            c_ah = s.val("vS")
            gp_dma(a_bounce[h].ap(), AT[:, h * N_TOK:(h + 1) * N_TOK],
                   waits=(("vS", c_ah),))
            c_ab = s.val("dG")
            ccn = "cc2" if h == 0 else "cc3"
            s.op("gpsimd", lambda e, h=h: e.collective_compute(
                "AllGather", ALU.bypass, replica_groups=[list(range(NCORES))],
                ins=[a_bounce[h].ap().opt()], outs=[a_gath[h].ap().opt()]),
                waits=(("dG", c_ab),), incs=((ccn, 1),))
        c_att_v = s.val("vS")

        bcP6 = bar()
        for hci, b in ((0, 0), (1, 1)):
            pw = wall(bcP6, ("tS", "dG")) if hci == 0 else ()
            sh_dma[("g", 11 + hci)] = w_chunk(swg_a[11 + hci], o_wgb[b], NDC * 128, pw)
            sh_dma[("u", 11 + hci)] = w_chunk(swu_a[11 + hci], o_wub[b], NDC * 128, ())
        gateup_phase(11, sh_dma_of_factory(1), sh_prefetch_factory(1), use_comb=False)
        down_phase(11, sh_getwd_factory(1), 11 * 128)

        bc4 = bar()
        o_wo = [0, 2048]
        o_ag = [4096, 12288]
        o_ssb = [0, 512]
        o_st4 = [1024, 1536]
        wo_dma = {}
        wo_used = {}
        ag_dma = {}
        ag_free = [None, None]
        ssb_free = [None, None]
        st4_free = [None, None]
        pairs = [(tc, d2c) for tc in range(2) for d2c in range(NDC)]

        def wo_load(i, waits=()):
            bb = i % 2
            w = list(waits)
            if i - 2 in wo_used:
                w.append(wo_used[i - 2])
            gp_dma(a16[:, o_wo[bb]:o_wo[bb] + NDC * 128], wo_a[pairs[i][1]], waits=tuple(w))
            wo_dma[i] = s.val("dG")

        for i in range(2):
            wo_load(i, wall(bc4, ("tS", "dG")) if i == 0 else ())

        def ag_load(tc, waits):
            bb = tc % 2
            for hl in range(HL):
                w = list(waits) if hl == 0 else [("cc3" if hl else "cc2", 1)]
                if hl == 0 and ag_free[bb] is not None:
                    w.append(ag_free[bb])

                def fn(e, tc=tc, bb=bb, hl=hl):
                    pid = e.partition_id()
                    src = a_gath[hl].ap()[:, bass.ts(pid, TOK)][:, tc * 512:(tc + 1) * 512]
                    src = src.rearrange("(r p) t -> p r t", p=128)
                    dst = a16[:, o_ag[bb]:o_ag[bb] + NDC * 512].rearrange(
                        "p (r l t) -> p r l t", l=HL, t=512)[:, :, hl, :]
                    return e.dma_start(out=dst, in_=src)
                ag_dma[tc] = s.dma("sync", fn, waits=tuple(w))

        ag_load(0, (("cc2", 1), ("cc3", 1)) + wall(bc4, ("tS",)))
        ag_load(1, ())

        for i, (tc, d2c) in enumerate(pairs):
            agb = tc % 2
            if True:
                wbk = i % 2
                wfirst = [("dG", wo_dma[i]), ("dS", ag_dma[tc])] + list(bwait(wbk))
                for ic in range(NDC):
                    s.op("tensor", lambda e, wbk=wbk, ic=ic, agb=agb: e.matmul(
                        pf[wbk], a16[:, o_wo[wbk] + ic * 128:o_wo[wbk] + (ic + 1) * 128],
                        a16[:, o_ag[agb] + ic * 512:o_ag[agb] + (ic + 1) * 512],
                        start=(ic == 0), stop=(ic == NDC - 1), skip_group_check=True),
                        waits=tuple(wfirst) if ic == 0 else (),
                        incs=(("tS", 1),) if ic == NDC - 1 else ())
                    wfirst = []
                ctw = s.val("tS")
                wo_used[i] = ("tS", ctw)
                if i + 2 < len(pairs):
                    wo_load(i + 2)
                wsb = [("tS", ctw)]
                if ssb_free[d2c % 2] is not None:
                    wsb.append(ssb_free[d2c % 2])
                csb = s.op("vector", lambda e, wbk=wbk, d2c=d2c, tc=tc: e.scalar_tensor_tensor(
                    a32[:, o_ssb[d2c % 2]:o_ssb[d2c % 2] + 512], pf[wbk], bot[:, d2c:d2c + 1],
                    fa[:, d2c * TOK + tc * 512: d2c * TOK + tc * 512 + 512], ALU.add, ALU.add),
                    waits=tuple(wsb), incs=(("vS", 1),))
                bank_free[wbk] = ("vS", csb)
                lastc = None
                for sub in range(4):
                    tb = 2 + (sub % 2)
                    ctt = s.op("tensor", lambda e, tb=tb, d2c=d2c, sub=sub: e.transpose(
                        pf[tb][:, 0:128], a32[:, o_ssb[d2c % 2] + sub * 128:
                                              o_ssb[d2c % 2] + (sub + 1) * 128], identf[:, :]),
                        waits=bwait(tb, (("vS", csb),)), incs=(("tS", 1),))
                    wcp = [("tS", ctt)]
                    if sub == 0 and st4_free[d2c % 2] is not None:
                        wcp.append(st4_free[d2c % 2])
                    lastc = s.op("vector", lambda e, tb=tb, d2c=d2c, sub=sub: e.tensor_copy(
                        a32[:, o_st4[d2c % 2] + sub * 128:o_st4[d2c % 2] + (sub + 1) * 128],
                        pf[tb][:, 0:128]), waits=tuple(wcp), incs=(("vS", 1),))
                    bank_free[tb] = ("vS", lastc)
                ssb_free[d2c % 2] = ("vS", lastc)
                dst = out_own[tc * 512:(tc + 1) * 512, d2c * 128:(d2c + 1) * 128].rearrange(
                    "(c p) d -> p c d", p=128)
                gp_dma(dst, a32[:, o_st4[d2c % 2]:o_st4[d2c % 2] + 512].rearrange(
                    "p (c d) -> p c d", c=4),
                    waits=(("vS", lastc), ("dS", c_xcopy)), accum=True)
                st4_free[d2c % 2] = ("dG", s.val("dG"))
            ag_free[agb] = ("tS", s.val("tS"))


        if taps:
            bfin = bar()
            sync_dma(tap_AT, AT[:, :], waits=wall(bfin))
            sync_dma(tap_fa, fa[:, :])
            sync_dma(tap_comb, comb_rows[:, :])
            sync_dma(tap_mhat, mhat[:, :])
            sync_dma(tap_kqm, kqm[0:1, 0:8])
            sync_dma(tap_q, q_dram.ap())
            sync_dma(tap_k, k_dram.ap())
            sync_dma(tap_v, v_dram.ap())

        fin = bar()
        s.op("sync", lambda e: None, waits=wall(fin) + (("cc1", 1), ("cc2", 1), ("cc3", 1)))

        with nc.Block() as block:

            @block.sync
            def _(eng):
                s.emit("sync", eng, sems)

            @block.gpsimd
            def _(eng):
                s.emit("gpsimd", eng, sems)

            @block.tensor
            def _(eng):
                s.emit("tensor", eng, sems)

            @block.vector
            def _(eng):
                s.emit("vector", eng, sems)

            @block.scalar
            def _(eng):
                s.emit("scalar", eng, sems)

    return nc


_CACHE = {}


def _host_consts():
    inv = (1.0 / 10000.0 ** (np.arange(0, DH, 2, dtype=np.float32) / DH)).astype(np.float32)
    seq = np.arange(N_TOK, dtype=np.float32)
    fr = seq[:, None] * inv[None, :]
    pos = np.concatenate([fr, fr], axis=1).astype(np.float32)
    cos = np.cos(pos).T.copy()
    sinf = np.sin(pos)
    sign = np.concatenate([-np.ones(64, np.float32), np.ones(64, np.float32)])
    sin = (sinf * sign[None, :]).T.copy()
    mask = np.where(np.arange(1024)[None, :] >= np.arange(128)[:, None] + 512,
                    np.float32(0.0), np.float32(NEG)).astype(np.float32)
    onehot = np.zeros((8, E * 128), np.float32)
    for e in range(E):
        onehot[e, e * 128:(e + 1) * 128] = 1.0
    fconst = np.zeros((128, 4), np.float32)
    fconst[:, 0] = 1e-5
    fconst[:, 1] = math.log(1.02 * ATT_SCALE)
    return {
        "cos_t": cos, "sin_t": sin, "mask_d": mask,
        "identbf_d": np.eye(128, dtype=ml_dtypes.bfloat16),
        "identf_d": np.eye(128, dtype=np.float32),
        "onesrow_d": np.ones((1, 128), np.float32),
        "onescol_d": np.ones((128, 1), ml_dtypes.bfloat16),
        "onescolf_d": np.ones((128, 1), np.float32),
        "onehot_d": onehot, "fconst_d": fconst,
    }


def _pack(w, rows_chunks, cols_chunks):
    R, C = rows_chunks, cols_chunks
    return np.ascontiguousarray(
        w.reshape(R, 128, C, 128).transpose(2, 1, 0, 3).reshape(C, 128, R * 128))


def kernel(L0_attn_x, L0_attn_Wq, L0_attn_bq, L0_attn_Wk, L0_attn_bk, L0_attn_Wv,
           L0_attn_bv, L0_attn_Wo, L0_attn_bo, L0_ffn_gate_w, L0_ffn_w_gate,
           L0_ffn_w_up, L0_ffn_w_down, L0_ffn_sw_gate, L0_ffn_sw_up, L0_ffn_sw_down):
    key = ("prog", TAPS)
    if key not in _CACHE:
        _CACHE[key] = build_program(taps=TAPS)
    nc = _CACHE[key]

    x = np.asarray(L0_attn_x, dtype=np.float32).reshape(N_TOK, D)
    f32 = lambda a: np.asarray(a, dtype=np.float32)
    Wq, Wk, Wv, Wo = f32(L0_attn_Wq), f32(L0_attn_Wk), f32(L0_attn_Wv), f32(L0_attn_Wo)
    bq, bk, bv, bo = f32(L0_attn_bq), f32(L0_attn_bk), f32(L0_attn_bv), f32(L0_attn_bo)
    gate_w = f32(L0_ffn_gate_w)
    w_gate, w_up, w_down = f32(L0_ffn_w_gate), f32(L0_ffn_w_up), f32(L0_ffn_w_down)
    sw_gate, sw_up, sw_down = f32(L0_ffn_sw_gate), f32(L0_ffn_sw_up), f32(L0_ffn_sw_down)

    consts = _host_consts()
    gate_w_a = np.ascontiguousarray(
        gate_w.reshape(NDC, 128, E).transpose(1, 0, 2).reshape(128, NDC * E))
    wg_a = np.ascontiguousarray(
        w_gate.reshape(E, NDC, 128, NHC, 128).transpose(0, 3, 2, 1, 4).reshape(E, NHC, 128, NDC * 128))
    wu_a = np.ascontiguousarray(
        w_up.reshape(E, NDC, 128, NHC, 128).transpose(0, 3, 2, 1, 4).reshape(E, NHC, 128, NDC * 128))
    wd_a = np.ascontiguousarray(
        w_down.reshape(E, NHC, 128, NDC, 128).transpose(0, 3, 2, 1, 4).reshape(E, NDC, 128, NHC * 128))
    swg_a = _pack(sw_gate, NDC, NSC)
    swu_a = _pack(sw_up, NDC, NSC)
    swd_a = _pack(sw_down, NSC, NDC)
    wo_a = _pack(Wo, NDC, NDC)
    bo_t = np.ascontiguousarray(bo.reshape(NDC, 128).T)

    shared = dict(consts)
    shared.update({
        "gate_w_a": gate_w_a, "wg_a": wg_a, "wu_a": wu_a, "wd_a": wd_a,
        "swg_a": swg_a, "swu_a": swu_a, "swd_a": swd_a, "wo_a": wo_a, "bo_t": bo_t,
    })

    in_maps = []
    for c in range(NCORES):
        m = dict(shared)
        m["x_own"] = np.ascontiguousarray(x[c * TOK:(c + 1) * TOK, :])
        hsl = slice(c * 256, (c + 1) * 256)
        for nm, W in (("wq_own", Wq), ("wk_own", Wk), ("wv_own", Wv)):
            m[nm] = np.ascontiguousarray(
                W[:, hsl].reshape(NDC, 128, 256).transpose(1, 0, 2).reshape(128, NDC * 256))
        barr = np.zeros((128, 12), np.float32)
        for i, b in enumerate((bq, bk, bv)):
            bl = b[hsl].reshape(2, 128)
            barr[:, 2 * i + 0] = bl[0]
            barr[:, 2 * i + 1] = bl[1]
        barr[:, 6:12] = np.concatenate([barr[64:128, 0:6], barr[0:64, 0:6]], axis=0)
        m["b_qkv"] = barr
        in_maps.append(m)

    res = run_bass_kernel_spmd(nc, in_maps, core_ids=list(range(NCORES)))
    out = np.concatenate([res.results[c]["out_own"] for c in range(NCORES)], axis=0)
    if TAPS:
        kernel.last_results = res
    return out.reshape(1, N_TOK, D)
